# version 03
# baseline (speedup 1.0000x reference)
import contextlib
import numpy as np
import concourse.bass as bass
import concourse.mybir as mybir
from concourse.bass_utils import run_bass_kernel_spmd
from concourse.alu_op_type import AluOpType as OP

F32 = mybir.dt.float32
F16 = mybir.dt.float16
BF16 = mybir.dt.bfloat16
AF = mybir.ActivationFunctionType
AX = mybir.AxisListType.X

L = 2048
H = 4096
DS = 4096
PROJ = 8512
CONVD = 4352
NH = 64
TT = 1024
NB = 8
NT = L // TT
EPS = 1e-5
NCORES = 8
EXT = TT + 3

ALL_ENG = ["sync", "gpsimd", "tensor", "vector", "scalar"]

_DRAIN_OPS = frozenset({
    "tensor_scalar", "tensor_tensor", "scalar_tensor_tensor", "tensor_copy",
    "tensor_tensor_scan", "reduce_max", "reduce_sum", "tensor_reduce",
    "bn_stats", "bn_aggr", "reciprocal", "memset", "activation", "select",
    "copy",
})


class _AutoDrain:
    def __init__(self, eng):
        self._e = eng

    def __getattr__(self, name):
        attr = getattr(self._e, name)
        if name in _DRAIN_OPS:
            e = self._e

            def w(*a, **k):
                e.drain()
                return attr(*a, **k)
            return w
        return attr


def bcast_ap(dram_ap, npart, cols):
    return bass.AP(dram_ap.tensor, 0, [[0, npart], [1, cols]])


class Ctx:

    def __init__(self, nc, es):
        self.nc = nc
        self.sems = {}
        self.cnt = {}
        for name in ["dw", "dwa", "dwb", "dx", "do", "doa", "dob", "tE", "vE",
                     "sE", "gE"]:
            self.sems[name] = es.enter_context(nc.semaphore(name))
            self.cnt[name] = 0
        self.bank_rd = {}

    def inc(self, instr, name, n=1):
        instr.then_inc(self.sems[name], n)
        self.cnt[name] += n
        return self.cnt[name]

    def dma(self, instr, name):
        return self.inc(instr, name, 16)

    def wait(self, eng, name, val=None):
        v = self.cnt[name] if val is None else val
        if v:
            eng.wait_ge(self.sems[name], v)

    def bank_guard(self, eng, key):
        if key in self.bank_rd:
            sem, val = self.bank_rd[key]
            if val:
                eng.wait_ge(self.sems[sem], val)

    def bank_mark(self, key, sem):
        self.bank_rd[key] = (sem, self.cnt[sem])


def build(debug=()):
    nc = bass.Bass("TRN2", target_bir_lowering=False)
    debug = set(debug)

    x_in = nc.dram_tensor("x", [L, H], F32, kind="ExternalInput").ap()
    w_in = nc.dram_tensor("w_in", [H, PROJ], F32, kind="ExternalInput").ap()
    conv_w = nc.dram_tensor("conv_w", [CONVD, 4], F32, kind="ExternalInput").ap()
    a_log = nc.dram_tensor("a_log", [1, NH], F32, kind="ExternalInput").ap()
    dt_bias = nc.dram_tensor("dt_bias", [1, NH], F32, kind="ExternalInput").ap()
    d_vec = nc.dram_tensor("d_vec", [1, NH], F32, kind="ExternalInput").ap()
    ln_w = nc.dram_tensor("ln_w", [1, H], F32, kind="ExternalInput").ap()
    norm_w = nc.dram_tensor("norm_w", [1, DS], F32, kind="ExternalInput").ap()
    mup = nc.dram_tensor("mup", [1, PROJ], F32, kind="ExternalInput").ap()
    w_out = nc.dram_tensor("w_out", [DS, H], F32, kind="ExternalInput").ap()
    ffn_a = nc.dram_tensor("ffn_a", [512, H], F32, kind="ExternalInput").ap()
    ffn_b = nc.dram_tensor("ffn_b", [H, 512], F32, kind="ExternalInput").ap()
    ffn_wq = nc.dram_tensor("ffn_wq", [16, H], F32, kind="ExternalInput").ap()
    ffn_keys = nc.dram_tensor("ffn_keys", [NH, 16], F32, kind="ExternalInput").ap()

    out_t = nc.dram_tensor("out", [L, H], F32, kind="ExternalOutput").ap()

    def scratch(name, shape, dtype):
        if name in debug:
            return nc.dram_tensor(name, shape, dtype, kind="ExternalOutput").ap()
        return nc.dram_tensor(name, shape, dtype).ap()

    zspill = scratch("zspill", [L, DS], BF16)
    xsspill = scratch("xsspill", [DS, L], BF16)
    x1spill = scratch("x1spill", [L, H], BF16)

    es = contextlib.ExitStack()
    with es:
        C = Ctx(nc, es)
        block = es.enter_context(nc.Block())

        def sec(eng_name, fn):
            if eng_name in ("vector", "scalar"):
                getattr(block, eng_name)(lambda e: fn(_AutoDrain(e)))
            else:
                getattr(block, eng_name)(fn)

        def barrier():
            snap = dict(C.cnt)
            for e in ALL_ENG:
                def f(eng, snap=snap):
                    for name, v in snap.items():
                        if v:
                            eng.wait_ge(C.sems[name], v)
                sec(e, f)

        A = lambda n, sh, dt: es.enter_context(nc.sbuf_tensor(n, sh, dt))

        ones_f32 = A("ones_f32", [128, 128], F32)
        ones_f16 = A("ones_f16", [128, 128], F16)
        ident_bf = A("ident_bf", [128, 128], BF16)
        ident_f32 = A("ident_f32", [128, 128], F32)
        tri_f16 = A("tri_f16", [128, 128], F16)
        su_f16 = A("su_f16", [128, 128], F16)
        tri_f32 = A("tri_f32", [128, 128], F32)
        e2_f16 = A("e2_f16", [64, DS], F16)
        e512_bf = A("e512_bf", [64, 512], BF16)
        dbias_bc = A("dbias_bc", [128, NH], F32)
        nega_bc = A("nega_bc", [128, NH], F32)
        dcol_bc = A("dcol_bc", [128, NH], F32)
        convwF = A("convwF", [128, 34 * 4], F32)
        iota_f = A("iota_f", [128, 128], F32)
        iota_c = A("iota_c", [128, 1], F32)
        smalls = A("smalls", [128, 32], F32)
        sm64 = A("sm64", [64, 16], F32)
        dtT = A("dtT", [128, NB * NH], F32)
        h_bf = A("h_bf", [128, DS], BF16)
        tailsave = A("tailsave", [128, 34 * 3], BF16)
        bmcf = A("bmcf", [128, 2 * TT], BF16)
        wbuf = [A(f"wbuf{i}", [128, 8192], BF16) for i in range(2)]
        bnb = [A(f"bnb{i}", [128, 24], F32) for i in range(2)]

        pa = [es.enter_context(nc.psum_tensor(f"pa{i}", [128, 512], F32)) for i in range(6)]
        pb = [es.enter_context(nc.psum_tensor(f"pb{i}", [128, 1024], BF16)) for i in range(2)]

        with contextlib.ExitStack() as sp:
            e2scr = sp.enter_context(nc.sbuf_tensor("e2scr", [64, DS], F32))

            def setup_sync(s):
                C.dma(s.dma_start(convwF[:].rearrange("p (c k) -> p c k", k=4),
                                  conv_w.rearrange("(c p) k -> p c k", p=128)), "dx")
                C.dma(s.dma_start(dbias_bc[:], bcast_ap(dt_bias, 128, NH)), "dx")
                C.dma(s.dma_start(nega_bc[:], bcast_ap(a_log, 128, NH)), "dx")
                C.dma(s.dma_start(dcol_bc[:], bcast_ap(d_vec, 128, NH)), "dx")
            sec("sync", setup_sync)

            def setup_gp(g):
                g.iota(iota_f[:], [[1, 128]], base=0, channel_multiplier=0,
                       allow_small_or_imprecise_dtypes=True)
                g.iota(iota_c[:], [[0, 1]], base=0, channel_multiplier=1,
                       allow_small_or_imprecise_dtypes=True)
                C.inc(g.iota(e2scr[:, :].rearrange("p (a b) -> p a b", b=64),
                             [[1, 64], [0, 64]], base=0, channel_multiplier=0,
                             allow_small_or_imprecise_dtypes=True), "gE")
            sec("gpsimd", setup_gp)

            def setup_v(v):
                C.wait(v, "gE")
                C.wait(v, "dx")
                v.memset(ones_f32[:], 1.0)
                v.memset(ones_f16[:], 1.0)
                v.memset(h_bf[:], 0.0)
                v.memset(tailsave[:], 0.0)
                v.tensor_scalar(ident_f32[:], iota_f[:], iota_c[:], None, OP.is_equal)
                v.tensor_copy(ident_bf[:], ident_f32[:])
                v.tensor_scalar(tri_f32[:], iota_f[:], iota_c[:], None, OP.is_ge)
                v.tensor_copy(tri_f16[:], tri_f32[:])
                v.tensor_scalar(su_f16[:], iota_f[:], iota_c[:], None, OP.is_lt)
                v.tensor_scalar(e2scr[:, :], e2scr[:, :], iota_c[0:64, :],
                                None, OP.is_equal)
                C.inc(v.tensor_copy(e2_f16[:], e2scr[:, :]), "vE")
            sec("vector", setup_v)
            v_e2 = C.cnt["vE"]

            def setup_gp2(g):
                C.wait(g, "vE", v_e2)
                C.inc(g.iota(e2scr[:, 0:512].rearrange("p (a b) -> p a b", b=8),
                             [[1, 64], [0, 8]], base=0, channel_multiplier=0,
                             allow_small_or_imprecise_dtypes=True), "gE")
            sec("gpsimd", setup_gp2)

            def setup_v2(v):
                C.wait(v, "gE")
                v.tensor_scalar(e2scr[:, 0:512], e2scr[:, 0:512], iota_c[0:64, :],
                                None, OP.is_equal)
                C.inc(v.tensor_copy(e512_bf[:], e2scr[:, 0:512]), "vE")
            sec("vector", setup_v2)
            v_e5 = C.cnt["vE"]

            def setup_s(s):
                C.wait(s, "vE", v_e5)
                C.inc(s.activation(nega_bc[:], nega_bc[:], AF.Exp), "sE")
            sec("scalar", setup_s)

            def setup_v3(v):
                C.wait(v, "sE")
                C.inc(v.tensor_scalar(nega_bc[:], nega_bc[:], -1.0, None, OP.mult),
                      "vE")
            sec("vector", setup_v3)

            barrier()

        for tile in range(NT):
            tok0 = tile * TT

            with contextlib.ExitStack() as ph:
                Aph = lambda n, sh, dt: ph.enter_context(
                    nc.sbuf_tensor(f"{n}_t{tile}", sh, dt))
                xhatF = Aph("xhatF", [128, 32 * TT], BF16)

                with contextlib.ExitStack() as p0s:
                    A0 = lambda n, sh, dt: p0s.enter_context(
                        nc.sbuf_tensor(f"{n}_t{tile}", sh, dt))
                    lnw_bc = A0("lnw_bc", [128, H], F16)
                    xblk = [A0(f"xblk{i}", [128, H], F32) for i in range(2)]
                    xhat_tok = A0("xhat_tok", [128, H], BF16)
                    _p0(nc, C, sec, tok0, x_in, ln_w, lnw_bc, xblk, xhat_tok,
                        xhatF, bnb, smalls, ident_bf, pa, pb)
                    barrier()

                with contextlib.ExitStack() as a1s:
                    A1 = lambda n, sh, dt: a1s.enter_context(
                        nc.sbuf_tensor(f"{n}_t{tile}", sh, dt))
                    mup_bc = A1("mup_bc", [128, PROJ], F16)
                    xbct = [A1(f"xbct{i}", [128, EXT], BF16) for i in range(8)]
                    ztmp = A1("ztmp", [128, 512], F32)
                    bst = A1("bst", [128, 512], BF16)
                    accb = A1("accb", [128, TT], F32)
                    zstg = [A1(f"zstg{i}", [128, 512], BF16) for i in range(2)]
                    xstg = [A1(f"xstg{i}", [128, TT], BF16) for i in range(2)]

                    def mup_load(g):
                        C.dma(g.dma_start(mup_bc[:], bcast_ap(mup, 128, PROJ)), "dw")
                    sec("gpsimd", mup_load)

                    _a1(nc, C, sec, tok0, w_in, xhatF, wbuf, mup_bc, ztmp, bst,
                        xbct, accb, convwF, tailsave, bmcf, dtT, xstg, xsspill,
                        ident_bf, pa, pb)
                    barrier()
                    _a2(nc, C, sec, tok0, w_in, xhatF, wbuf, mup_bc, ztmp,
                        zstg, zspill, pa)
                    barrier()

            with contextlib.ExitStack() as ph:
                Aph = lambda n, sh, dt: ph.enter_context(
                    nc.sbuf_tensor(f"{n}_t{tile}", sh, dt))
                yhatF = Aph("yhatF", [128, 32 * TT], BF16)
                nw_bc = Aph("nw_bc", [128, DS], F16)
                ysb = Aph("ysb", [128, DS], F32)
                sc = {n: Aph(n, sh, dt) for n, sh, dt in [
                    ("xsT", [128, DS], BF16),
                    ("yhat_tok", [128, DS], BF16),
                    ("vexp", [128, 512], F32),
                    ("ytmp", [128, 512], BF16),
                    ("dxst", [128, 512], BF16),
                    ("xtld", [128, 512], BF16),
                    ("hdec", [128, 512], BF16),
                    ("decay_sb", [128, 512], F32),
                    ("cbtm", [128, 128], F32),
                    ("bmt", [128, 128], BF16),
                    ("dtc", [128, NH], F32),
                    ("dta32", [128, NH], F32),
                    ("dta16", [128, NH], F16),
                    ("wtok", [128, NH], F32),
                    ("vf16", [64, 128], F16),
                    ("sf16", [64, 128], F16),
                    ("t64a", [128, NH], F32),
                    ("t64b", [128, NH], F32),
                    ("t64c", [128, NH], F32),
                ]}
                sc["expd"] = [Aph(f"expd{i}", [128, 512], F32) for i in range(2)]
                sc["rhsd"] = [Aph(f"rhsd{i}", [128, 512], F16) for i in range(2)]
                sc["mtil"] = [Aph(f"mtil{i}", [128, 128], BF16) for i in range(2)]
                x1stg = [Aph(f"x1stg{i}", [128, 512], BF16) for i in range(2)]

                def nw_load(g):
                    C.dma(g.dma_start(nw_bc[:], bcast_ap(norm_w, 128, DS)), "dw")
                sec("gpsimd", nw_load)

                for ch in range(NB):
                    _scan_chunk(nc, C, sec, tok0, ch, sc, wbuf, xsspill,
                                zspill, dtT, dbias_bc, nega_bc, dcol_bc, h_bf,
                                bmcf, nw_bc, ysb, yhatF, ones_f16, ones_f32,
                                tri_f16, su_f16, tri_f32, ident_bf, e2_f16,
                                smalls, sm64, bnb, pa, pb)
                    barrier()
                _e_outproj(nc, C, sec, tok0, w_out, yhatF, wbuf, x1stg,
                           x1spill, pa)
                barrier()

            with contextlib.ExitStack() as ph:
                Aph = lambda n, sh, dt: ph.enter_context(
                    nc.sbuf_tensor(f"{n}_t{tile}", sh, dt))
                awT = Aph("awT", [128, 32 * 512], BF16)
                bwT = Aph("bwT", [128, 4 * H], BF16)
                wqT = Aph("wqT", [128, 32 * 16], BF16)
                keysT = Aph("keysT", [16, 64], BF16)
                x2F = Aph("x2F", [128, 32 * 128], BF16)
                x3f = Aph("x3f", [128, H], F32)
                xblk2 = Aph("xblk2", [128, H], F32)
                ysb2 = Aph("ysb2", [128, H], F32)
                fscr = Aph("fscr", [128, 512], F32)
                fscr2 = Aph("fscr2", [128, 64], F32)
                gq = Aph("gq", [128, 64], F32)
                gwork = Aph("gwork", [128, 64], F32)
                gate_bf = Aph("gate_bf", [128, 64], BF16)
                gateT = Aph("gateT", [64, 128], BF16)
                zhat = Aph("zhat", [128, 512], BF16)
                zhT = Aph("zhT", [128, 512], BF16)
                qtok = Aph("qtok", [128, 16], BF16)
                qT = Aph("qT", [16, 128], BF16)
                rcols = [Aph(f"rcol{i}", [128, 1], F32) for i in range(4)]

                _f_prep(nc, C, sec, ffn_a, ffn_b, ffn_wq, ffn_keys, wbuf,
                        awT, bwT, wqT, keysT, ident_bf, pa, pb)
                barrier()
                do_at = {}
                for b in range(NB):
                    _f_block(nc, C, sec, tok0, b, x_in, x1spill, out_t,
                             xblk2, ysb2, x3f, x2F, awT, bwT, wqT, keysT,
                             e512_bf, fscr, fscr2, gq, gwork, gate_bf, gateT,
                             zhat, zhT, qtok, qT, bnb, smalls, rcols, ident_bf,
                             ident_f32, pa, pb, do_at)
                    barrier()

        def fin(s):
            C.wait(s, "do")
            C.wait(s, "doa")
            C.wait(s, "dob")
        sec("sync", fin)

    return nc



def _ln_stats(C, v, bnb, src_ap, ncols, mu_col, m2_col, sm, c0, want_meansq):
    nst = ncols // 512
    for i in range(nst):
        v.bn_stats(bnb[i % 2][:, (i // 2) * 6:(i // 2) * 6 + 6],
                   src_ap[:, i * 512:(i + 1) * 512])
    v.bn_aggr(sm[:, c0:c0 + 2], bnb[0][:, 0:(nst // 2) * 6])
    v.bn_aggr(bnb[1][:, 18:20], bnb[1][:, 0:(nst // 2) * 6])
    mA, vA = sm[:, c0:c0 + 1], sm[:, c0 + 1:c0 + 2]
    mB, vB = bnb[1][:, 18:19], bnb[1][:, 19:20]
    t0, t1 = sm[:, c0 + 4:c0 + 5], sm[:, c0 + 5:c0 + 6]
    v.tensor_tensor(mu_col, mA, mB, OP.add)
    v.tensor_scalar(mu_col, mu_col, 0.5, None, OP.mult)
    v.tensor_tensor(t0, mA, mB, OP.subtract)
    v.tensor_tensor(t0, t0, t0, OP.mult)
    v.tensor_tensor(t1, vA, vB, OP.add)
    v.tensor_scalar(t1, t1, 0.5, None, OP.mult)
    v.scalar_tensor_tensor(m2_col, t0, 0.25, t1, OP.mult, OP.add)
    if want_meansq:
        v.tensor_tensor(t0, mu_col, mu_col, OP.mult)
        v.tensor_tensor(m2_col, m2_col, t0, OP.add)
    v.tensor_scalar(m2_col, m2_col, EPS, None, OP.add)


def _rstd_finish(s, src_col, dst_col, tmp_col):
    s.activation(tmp_col, src_col, AF.Ln)
    return s.activation(dst_col, tmp_col, AF.Exp, scale=-0.5)


def _p0(nc, C, sec, tok0, x_in, ln_w, lnw_bc, xblk, xhat_tok, xhatF, bnb,
        smalls, ident_bf, pa, pb):
    dx_at = {}
    v_stt = {}
    t_done = {}

    def g_lnw(g):
        C.dma(g.dma_start(lnw_bc[:], bcast_ap(ln_w, 128, H)), "dw")
    sec("gpsimd", g_lnw)
    dw_lnw = C.cnt["dw"]

    def d_pre(s):
        for b in range(2):
            dx_at[b] = C.dma(s.dma_start(
                xblk[b % 2][:], x_in[tok0 + b * 128:tok0 + (b + 1) * 128, :]), "dx")
    sec("sync", d_pre)

    for b in range(NB):
        def vf(v, b=b, dxf=None):
            C.wait(v, "dx")
            if b == 0:
                C.wait(v, "dw", dw_lnw)
            _ln_stats(C, v, bnb, xblk[b % 2][:], H, smalls[:, 8:9], smalls[:, 9:10],
                      smalls, 12, want_meansq=True)
            C.inc(v.tensor_copy(smalls[:, 18:19], smalls[:, 9:10]), "vE")
        sec("vector", vf)
        v_bn = C.cnt["vE"]

        def sf(s, b=b):
            C.wait(s, "vE", v_bn)
            C.inc(_rstd_finish(s, smalls[:, 18:19], smalls[:, 10:11],
                               smalls[:, 11:12]), "sE")
        sec("scalar", sf)
        s_rs = C.cnt["sE"]

        def vg(v, b=b):
            C.wait(v, "sE", s_rs)
            if b >= 1:
                C.wait(v, "tE", t_done[b - 1])
            C.inc(v.scalar_tensor_tensor(xhat_tok[:], xblk[b % 2][:],
                                         smalls[:, 10:11], lnw_bc[:],
                                         OP.mult, OP.mult), "vE")
        sec("vector", vg)
        v_stt[b] = C.cnt["vE"]

        if b + 2 < NB:
            def df(s, b=b):
                C.wait(s, "vE", v_stt[b])
                dx_at[b + 2] = C.dma(s.dma_start(
                    xblk[b % 2][:],
                    x_in[tok0 + (b + 2) * 128:tok0 + (b + 3) * 128, :]), "dx")
            sec("sync", df)

        for jb in range(4):
            def tf(t, b=b, jb=jb):
                if jb == 0:
                    C.wait(t, "vE", v_stt[b])
                key = f"pb{jb % 2}"
                C.bank_guard(t, key)
                for i in range(8):
                    c = jb * 8 + i
                    mm = t.transpose(pb[jb % 2][:, i * 128:(i + 1) * 128],
                                     xhat_tok[:, c * 128:(c + 1) * 128], ident_bf[:])
                C.inc(mm, "tE")
            sec("tensor", tf)
            t_tr = C.cnt["tE"]

            def sg(s, b=b, jb=jb, t_tr=t_tr):
                s.wait_ge(C.sems["tE"], t_tr)
                dstap = xhatF[:, jb * 8 * TT:(jb * 8 + 8) * TT] \
                    .rearrange("p (c t) -> p c t", t=TT)[:, :, b * 128:(b + 1) * 128]
                C.inc(s.activation(
                    dstap, pb[jb % 2][:, 0:1024].rearrange("p (c t) -> p c t", t=128),
                    AF.Copy), "sE")
                C.bank_mark(f"pb{jb % 2}", "sE")
            sec("scalar", sg)
        t_done[b] = C.cnt["tE"]


def _a1(nc, C, sec, tok0, w_in, xhatF, wbuf, mup_bc, ztmp, bst, xbct, accb,
        convwF, tailsave, bmcf, dtT, xstg, xsspill, ident_bf, pa, pb):
    n_slab = 18
    dw_at = {}
    mm_done = {}
    ev_done = {}
    tg_at = {}
    conv_tap = {}
    spill_at = {}
    s_silu_at = {}

    def pre(g):
        def f(gp, g=g):
            col0 = 4096 + g * 256
            w = 64 if g == 17 else 256
            if g >= 2:
                gp.wait_ge(C.sems["tE"], mm_done[g - 2])
            dw_at[g] = C.dma(gp.dma_start(
                wbuf[g % 2][:, 0:32 * w].rearrange("p (c f) -> p c f", f=w),
                w_in.rearrange("(c p) f -> p c f", p=128)[:, :, col0:col0 + w]),
                "dwa" if g % 2 == 0 else "dwb")
        sec("gpsimd", f)

    pre(0)
    pre(1)

    for g in range(n_slab):
        col0 = 4096 + g * 256
        w = 64 if g == 17 else 256

        if g < 16:
            def gtail(gp, g=g):
                if 2 * g - 8 >= 0:
                    gp.wait_ge(C.sems["vE"], conv_tap[2 * g + 1 - 8])
                cc0 = 2 * g
                gp.tensor_copy(xbct[cc0 % 8][:, 0:3],
                               tailsave[:, cc0 * 3:cc0 * 3 + 3])
                C.inc(gp.tensor_copy(xbct[(cc0 + 1) % 8][:, 0:3],
                                     tailsave[:, (cc0 + 1) * 3:(cc0 + 1) * 3 + 3]),
                      "gE")
            sec("gpsimd", gtail)
        g_tail_in = C.cnt["gE"]

        for b in range(NB):
            idx = g * NB + b

            def tf(t, g=g, b=b, w=w, idx=idx):
                if b == 0:
                    C.wait(t, "dwa" if g % 2 == 0 else "dwb")
                C.bank_guard(t, f"pa{idx % 3}")
                for c in range(32):
                    mm = t.matmul(pa[idx % 3][:, 0:w],
                                  xhatF[:, c * TT + b * 128:c * TT + (b + 1) * 128],
                                  wbuf[g % 2][:, c * w:(c + 1) * w],
                                  start=(c == 0), stop=(c == 31))
                C.inc(mm, "tE")
                mm_done[g] = C.cnt["tE"]
            sec("tensor", tf)
            t_mm = C.cnt["tE"]

            def vf(v, g=g, b=b, w=w, col0=col0, idx=idx, t_mm=t_mm):
                C.wait(v, "tE", t_mm)
                if g < 17 and idx - 1 in tg_at:
                    C.wait(v, "tE", tg_at[idx - 1])
                if g == 17:
                    C.inc(v.tensor_tensor(dtT[:, b * 64:(b + 1) * 64],
                                          pa[idx % 3][:, 0:64],
                                          mup_bc[:, col0:col0 + 64], OP.mult), "vE")
                else:
                    C.inc(v.tensor_tensor(bst[:, 0:w], pa[idx % 3][:, 0:w],
                                          mup_bc[:, col0:col0 + w], OP.mult), "vE")
                C.bank_mark(f"pa{idx % 3}", "vE")
                ev_done[idx] = C.cnt["vE"]
            sec("vector", vf)
            v_ev = C.cnt["vE"]

            if g < 17:
                def tg(t, g=g, b=b, idx=idx, v_ev=v_ev):
                    C.wait(t, "vE", v_ev)
                    key = f"pb{idx % 2}"
                    C.bank_guard(t, key)
                    for i in range(2):
                        mm = t.transpose(pb[idx % 2][:, i * 128:(i + 1) * 128],
                                         bst[:, i * 128:(i + 1) * 128], ident_bf[:])
                    C.inc(mm, "tE")
                    tg_at[idx] = C.cnt["tE"]
                sec("tensor", tg)
                t_tr = C.cnt["tE"]

                def sg(s, g=g, b=b, idx=idx, t_tr=t_tr):
                    C.wait(s, "tE", t_tr)
                    if 2 * g - 8 >= 0:
                        C.wait(s, "vE", conv_tap[2 * g + 1 - 8])
                    last = None
                    for i in range(2):
                        cc = 2 * g + i
                        if cc < 32:
                            dst = xbct[cc % 8][:, 3 + b * 128:3 + (b + 1) * 128]
                        else:
                            dst = bmcf[:, (cc - 32) * TT + b * 128:
                                       (cc - 32) * TT + (b + 1) * 128]
                        last = s.activation(dst, pb[idx % 2][:, i * 128:(i + 1) * 128],
                                            AF.Copy)
                    C.inc(last, "sE")
                    C.bank_mark(f"pb{idx % 2}", "sE")
                sec("scalar", sg)

        if g + 2 < n_slab:
            pre(g + 2)

        if g < 16:
            for i in range(2):
                cc = 2 * g + i
                _conv_chunk(nc, C, sec, tok0, cc, xbct[cc % 8], accb, convwF,
                            tailsave, xstg, xsspill, bmcf, conv_tap, spill_at)
        elif g == 16:
            for i in range(2):
                cc = 32 + i
                _conv_chunk(nc, C, sec, tok0, cc, None, accb, convwF,
                            tailsave, xstg, xsspill, bmcf, conv_tap, spill_at)


def _conv_chunk(nc, C, sec, tok0, cc, buf, accb, convwF, tailsave, xstg,
                xsspill, bmcf, conv_tap, spill_at):
    if cc >= 32:
        data = bmcf[:, (cc - 32) * TT:(cc - 32 + 1) * TT]
    else:
        data = buf[:, 3:3 + TT]
    sE_now = C.cnt["sE"]
    gE_now = C.cnt["gE"]

    def vf(v, cc=cc, data=data, sE_now=sE_now, gE_now=gE_now):
        C.wait(v, "sE", sE_now)
        C.wait(v, "gE", gE_now)
        w = lambda k: convwF[:, cc * 4 + k:cc * 4 + k + 1]
        if cc < 32:
            ext = buf
            v.tensor_scalar(accb[:], ext[:, 0:TT], w(0), None, OP.mult)
            for k in range(1, 3):
                v.scalar_tensor_tensor(accb[:], ext[:, k:k + TT], w(k), accb[:],
                                       OP.mult, OP.add)
            last = v.scalar_tensor_tensor(accb[:], ext[:, 3:3 + TT], w(3), accb[:],
                                          OP.mult, OP.add)
        else:
            ts3 = tailsave[:, cc * 3:cc * 3 + 3]
            v.tensor_scalar(accb[:], data[:], w(3), None, OP.mult)
            for k in range(3):
                sh = 3 - k
                v.scalar_tensor_tensor(accb[:, sh:TT], data[:, 0:TT - sh], w(k),
                                       accb[:, sh:TT], OP.mult, OP.add)
                last = v.scalar_tensor_tensor(accb[:, 0:sh], ts3[:, 3 - sh:3], w(k),
                                              accb[:, 0:sh], OP.mult, OP.add)
        C.inc(last, "vE")
        conv_tap[cc] = C.cnt["vE"]
    sec("vector", vf)
    v_tap = C.cnt["vE"]

    def gp2(g, cc=cc, data=data, v_tap=v_tap):
        C.wait(g, "vE", v_tap)
        C.inc(g.tensor_copy(tailsave[:, cc * 3:cc * 3 + 3], data[:, TT - 3:TT]), "gE")
    sec("gpsimd", gp2)
    g_tail = C.cnt["gE"]

    def sf(s, cc=cc, data=data, v_tap=v_tap, g_tail=g_tail):
        C.wait(s, "vE", v_tap)
        C.wait(s, "gE", g_tail)
        if cc < 32:
            if cc >= 2:
                C.wait(s, "doa" if cc % 2 == 0 else "dob")
            dst = xstg[cc % 2][:]
        else:
            dst = data
        C.inc(s.activation(dst, accb[:], AF.Silu), "sE")
    sec("scalar", sf)
    s_silu = C.cnt["sE"]

    if cc < 32:
        def df(sy, cc=cc, s_silu=s_silu):
            C.wait(sy, "sE", s_silu)
            spill_at[cc] = C.dma(sy.dma_start(
                xsspill[cc * 128:(cc + 1) * 128, tok0:tok0 + TT], xstg[cc % 2][:]),
                "doa" if cc % 2 == 0 else "dob")
        sec("sync", df)


def _a2(nc, C, sec, tok0, w_in, xhatF, wbuf, mup_bc, ztmp, zstg, zspill, pa):
    dw_at = {}
    mm_done = {}
    do_at = {}
    sz_at = {}

    def pre(g):
        def f(gp, g=g):
            col0 = g * 256
            if g >= 2:
                gp.wait_ge(C.sems["tE"], mm_done[g - 2])
            dw_at[g] = C.dma(gp.dma_start(
                wbuf[g % 2][:, 0:32 * 256].rearrange("p (c f) -> p c f", f=256),
                w_in.rearrange("(c p) f -> p c f", p=128)[:, :, col0:col0 + 256]),
                "dwa" if g % 2 == 0 else "dwb")
        sec("gpsimd", f)

    pre(0)
    pre(1)
    for g in range(16):
        col0 = g * 256
        for b in range(NB):
            idx = g * NB + b

            def tf(t, g=g, b=b, idx=idx):
                if b == 0:
                    C.wait(t, "dwa" if g % 2 == 0 else "dwb")
                C.bank_guard(t, f"pa{idx % 3}")
                for c in range(32):
                    mm = t.matmul(pa[idx % 3][:, 0:256],
                                  xhatF[:, c * TT + b * 128:c * TT + (b + 1) * 128],
                                  wbuf[g % 2][:, c * 256:(c + 1) * 256],
                                  start=(c == 0), stop=(c == 31))
                C.inc(mm, "tE")
                mm_done[g] = C.cnt["tE"]
            sec("tensor", tf)
            t_mm = C.cnt["tE"]

            def vf(v, col0=col0, idx=idx, t_mm=t_mm):
                C.wait(v, "tE", t_mm)
                if idx - 1 in sz_at:
                    C.wait(v, "sE", sz_at[idx - 1])
                C.inc(v.tensor_tensor(ztmp[:, 0:256], pa[idx % 3][:, 0:256],
                                      mup_bc[:, col0:col0 + 256], OP.mult), "vE")
                C.bank_mark(f"pa{idx % 3}", "vE")
            sec("vector", vf)
            v_ev = C.cnt["vE"]

            def sf(s, idx=idx, v_ev=v_ev):
                C.wait(s, "vE", v_ev)
                if idx >= 2:
                    C.wait(s, "doa" if idx % 2 == 0 else "dob")
                C.inc(s.activation(zstg[idx % 2][:, 0:256], ztmp[:, 0:256],
                                   AF.Silu), "sE")
                sz_at[idx] = C.cnt["sE"]
            sec("scalar", sf)
            s_z = C.cnt["sE"]

            def df(sy, b=b, col0=col0, idx=idx, s_z=s_z):
                C.wait(sy, "sE", s_z)
                do_at[idx] = C.dma(sy.dma_start(
                    zspill[tok0 + b * 128:tok0 + (b + 1) * 128, col0:col0 + 256],
                    zstg[idx % 2][:, 0:256]), "doa" if idx % 2 == 0 else "dob")
            sec("sync", df)
        if g + 2 < 16:
            pre(g + 2)


def _scan_chunk(nc, C, sec, tok0, ch, sc, wbuf, xsspill, zspill, dtT,
                dbias_bc, nega_bc, dcol_bc, h_bf, bmcf, nw_bc, ysb, yhatF,
                ones_f16, ones_f32, tri_f16, su_f16, tri_f32, ident_bf,
                e2_f16, smalls, sm64, bnb, pa, pb):
    t0c = tok0 + ch * 128
    xrd = wbuf[0][:, (ch % 2) * 4096:(ch % 2) * 4096 + 4096]
    zrd = wbuf[1][:, (ch % 2) * 4096:(ch % 2) * 4096 + 4096]

    def d0(sy):
        C.dma(sy.dma_start(
            xrd.rearrange("p (c t) -> p c t", t=128),
            xsspill.rearrange("(c p) t -> p c t", p=128)[:, :, t0c:t0c + 128]), "dx")
        C.dma(sy.dma_start(zrd, zspill[t0c:t0c + 128, :]), "dx")
    sec("sync", d0)
    dx_rd = C.cnt["dx"]

    for jb in range(4):
        def tf(t, jb=jb):
            if jb == 0:
                C.wait(t, "dx")
            key = f"pb{jb % 2}"
            C.bank_guard(t, key)
            for i in range(8):
                c = jb * 8 + i
                mm = t.transpose(pb[jb % 2][:, i * 128:(i + 1) * 128],
                                 xrd[:, c * 128:(c + 1) * 128], ident_bf[:])
            C.inc(mm, "tE")
        sec("tensor", tf)
        t_tr = C.cnt["tE"]

        def sg(s, jb=jb, t_tr=t_tr):
            s.wait_ge(C.sems["tE"], t_tr)
            C.inc(s.activation(sc["xsT"][:, jb * 1024:(jb + 1) * 1024],
                               pb[jb % 2][:, 0:1024], AF.Copy), "sE")
            C.bank_mark(f"pb{jb % 2}", "sE")
        sec("scalar", sg)
    s_xst = C.cnt["sE"]

    def v_dt(v):
        dts = dtT[:, ch * 64:(ch + 1) * 64]
        v.tensor_tensor(sc["t64a"][:], dts, dbias_bc[:], OP.add)
        C.inc(v.scalar_tensor_tensor(sc["t64b"][:], sc["t64a"][:], -1.0,
                                     sc["t64a"][:], OP.mult, OP.min), "vE")
    sec("vector", v_dt)
    v_1 = C.cnt["vE"]

    def s_dt(s, v_1=v_1):
        C.wait(s, "vE", v_1)
        s.activation(sc["t64c"][:], sc["t64b"][:], AF.Exp)
        C.inc(s.activation(sc["t64b"][:], sc["t64c"][:], AF.Ln, bias=1.0), "sE")
    sec("scalar", s_dt)
    s_1 = C.cnt["sE"]

    def v_dt2(v, s_1=s_1):
        C.wait(v, "sE", s_1)
        v.tensor_scalar(sc["t64c"][:], sc["t64a"][:], 0.0, None, OP.max)
        v.tensor_tensor(sc["dtc"][:], sc["t64b"][:], sc["t64c"][:], OP.add)
        v.tensor_tensor(sc["dta32"][:], sc["dtc"][:], nega_bc[:], OP.mult)
        C.inc(v.tensor_copy(sc["dta16"][:], sc["dta32"][:]), "vE")
    sec("vector", v_dt2)
    v_2 = C.cnt["vE"]

    def t_cum(t, v_2=v_2):
        C.wait(t, "vE", v_2)
        C.bank_guard(t, "pa5")
        t.matmul(pa[5][0:64, 0:128], sc["dta16"][:], tri_f16[:], start=True,
                 stop=True)
        t.matmul(pa[5][:, 128:192], su_f16[:], sc["dta16"][:], start=True,
                 stop=True)
        C.inc(t.matmul(pa[5][0:64, 192:193], sc["dta16"][:], ones_f16[:, 0:1],
                       start=True, stop=True), "tE")
    sec("tensor", t_cum)
    t_cu = C.cnt["tE"]

    def s_cum(s, t_cu=t_cu):
        C.wait(s, "tE", t_cu)
        s.activation(sc["vf16"][:], pa[5][0:64, 0:128], AF.Exp)
        s.activation(sc["t64a"][:], pa[5][:, 128:192], AF.Exp)
        C.inc(s.activation(sm64[:, 2:3], pa[5][0:64, 192:193], AF.Exp), "sE")
        C.bank_mark("pa5", "sE")
    sec("scalar", s_cum)
    s_2 = C.cnt["sE"]

    def v_w(v, s_2=s_2):
        C.wait(v, "sE", s_2)
        v.tensor_tensor(sc["wtok"][:], sc["t64a"][:], sc["dtc"][:], OP.mult)
        C.inc(v.tensor_scalar(sc["sf16"][:], ones_f32[0:64, 0:128], sm64[:, 2:3],
                              None, OP.mult), "vE")
    sec("vector", v_w)
    v_3 = C.cnt["vE"]

    def t_cb(t, s_xst=s_xst):
        C.wait(t, "sE", s_xst)
        C.bank_guard(t, "pa5")
        t.matmul(pa[5][:, 256:384], bmcf[:, ch * 128:(ch + 1) * 128],
                 bmcf[:, TT + ch * 128:TT + (ch + 1) * 128], start=True, stop=True)
        C.bank_guard(t, "pb0")
        C.inc(t.transpose(pb[0][:, 0:128], bmcf[:, ch * 128:(ch + 1) * 128],
                          ident_bf[:]), "tE")
    sec("tensor", t_cb)
    t_cb_at = C.cnt["tE"]

    def v_cb(v, t_cb_at=t_cb_at):
        C.wait(v, "tE", t_cb_at)
        v.tensor_tensor(sc["cbtm"][:], pa[5][:, 256:384], tri_f32[:], OP.mult)
        C.bank_mark("pa5", "vE")
        C.inc(v.tensor_copy(sc["bmt"][:], pb[0][:, 0:128]), "vE")
        C.bank_mark("pb0", "vE")
    sec("vector", v_cb)
    v_cb_at = C.cnt["vE"]

    dmm_at = {}
    exp_at = {}
    expd_free = {}
    intra_at = {}
    dxs_at = {}
    st_at = {}
    prev_vyt = {}
    for g in range(8):
        heads = list(range(8 * g, 8 * g + 8))

        for half in range(2):
            hh = 2 * g + half
            hlist = heads[half * 4:half * 4 + 4]

            def gp_r(gp, hh=hh, hlist=hlist):
                gp.wait_ge(C.sems["vE"], v_2)
                if hh >= 2:
                    gp.wait_ge(C.sems["tE"], dmm_at[hh - 2])
                last = None
                for i, r in enumerate(hlist):
                    last = gp.tensor_scalar(
                        sc["rhsd"][hh % 2][:, i * 128:(i + 1) * 128],
                        tri_f16[:], sc["dta32"][:, r:r + 1], None, OP.mult)
                C.inc(last, "gE")
            sec("gpsimd", gp_r)
            g_r = C.cnt["gE"]

            def t_d(t, hh=hh, g_r=g_r):
                C.wait(t, "gE", g_r)
                key = f"pa{3 + hh % 2}"
                C.bank_guard(t, key)
                for i in range(4):
                    mm = t.matmul(pa[3 + hh % 2][:, i * 128:(i + 1) * 128],
                                  su_f16[:],
                                  sc["rhsd"][hh % 2][:, i * 128:(i + 1) * 128],
                                  start=(i == 0), stop=(i == 3))
                C.inc(mm, "tE")
            sec("tensor", t_d)
            dmm_at[hh] = C.cnt["tE"]

            def s_e(s, hh=hh):
                C.wait(s, "tE", dmm_at[hh])
                if hh >= 2:
                    C.wait(s, "vE", expd_free[hh - 2])
                C.inc(s.activation(sc["expd"][hh % 2][:], pa[3 + hh % 2][:],
                                   AF.Exp), "sE")
                C.bank_mark(f"pa{3 + hh % 2}", "sE")
            sec("scalar", s_e)
            exp_at[hh] = C.cnt["sE"]

        first_y2 = True
        for i, r in enumerate(heads):
            hh = 2 * g + i // 4

            def v_m(v, r=r, i=i, hh=hh):
                C.wait(v, "sE", exp_at[hh])
                C.wait(v, "vE", v_cb_at)
                if r >= 2:
                    C.wait(v, "tE", intra_at[r - 2])
                C.inc(v.scalar_tensor_tensor(
                    sc["mtil"][r % 2][:],
                    sc["expd"][hh % 2][:, (i % 4) * 128:(i % 4 + 1) * 128],
                    sc["dtc"][:, r:r + 1], sc["cbtm"][:], OP.mult, OP.mult), "vE")
                if i % 4 == 3:
                    expd_free[hh] = C.cnt["vE"]
            sec("vector", v_m)
            v_mt = C.cnt["vE"]

            def t_i(t, r=r, i=i, first=first_y2, v_mt=v_mt):
                C.wait(t, "vE", v_mt)
                if first:
                    C.bank_guard(t, "pa1")
                C.inc(t.matmul(pa[1][:, i * 64:(i + 1) * 64], sc["mtil"][r % 2][:],
                               sc["xsT"][:, r * 64:(r + 1) * 64],
                               start=first, stop=False), "tE")
                intra_at[r] = C.cnt["tE"]
            sec("tensor", t_i)
            first_y2 = False

        def t_inter(t, g=g):
            C.bank_guard(t, "pa0")
            last = None
            for i, r in enumerate(heads):
                last = t.matmul(pa[0][:, i * 64:(i + 1) * 64],
                                bmcf[:, TT + ch * 128:TT + (ch + 1) * 128],
                                h_bf[:, r * 64:(r + 1) * 64],
                                start=(i == 0), stop=(i == 7))
            C.inc(last, "tE")
        sec("tensor", t_inter)
        t_int = C.cnt["tE"]

        def t_v(t, g=g, v_3=v_3):
            C.wait(t, "vE", v_3)
            C.bank_guard(t, "pa5")
            C.inc(t.matmul(pa[5][:, 0:512], sc["vf16"][:],
                           e2_f16[:, g * 512:(g + 1) * 512], start=True,
                           stop=True), "tE")
        sec("tensor", t_v)
        t_ve = C.cnt["tE"]

        def s_v(s, g=g, t_ve=t_ve):
            C.wait(s, "tE", t_ve)
            if g - 1 in prev_vyt:
                C.wait(s, "vE", prev_vyt[g - 1])
            C.inc(s.activation(sc["vexp"][:], pa[5][:, 0:512], AF.Copy), "sE")
            C.bank_mark("pa5", "sE")
        sec("scalar", s_v)
        s_ve = C.cnt["sE"]

        def v_y1(v, g=g, s_ve=s_ve, t_int=t_int):
            C.wait(v, "sE", s_ve)
            C.wait(v, "tE", t_int)
            C.inc(v.scalar_tensor_tensor(sc["ytmp"][:], pa[0][:, 0:512], 1.0,
                                         sc["vexp"][:], OP.mult, OP.mult), "vE")
            C.bank_mark("pa0", "vE")
            prev_vyt[g] = C.cnt["vE"]
        sec("vector", v_y1)
        v_yt = C.cnt["vE"]

        def g_dx(gp, g=g, s_xst=s_xst):
            gp.wait_ge(C.sems["sE"], s_xst)
            if g >= 1:
                gp.wait_ge(C.sems["tE"], dxs_at[g - 1])
            last = None
            for i, r in enumerate(range(8 * g, 8 * g + 8)):
                last = gp.tensor_scalar(sc["dxst"][:, i * 64:(i + 1) * 64],
                                        sc["xsT"][:, r * 64:(r + 1) * 64],
                                        dcol_bc[:, r:r + 1], None, OP.mult)
            C.inc(last, "gE")
        sec("gpsimd", g_dx)
        g_dxs = C.cnt["gE"]

        def t_inj(t, g=g, v_yt=v_yt, g_dxs=g_dxs):
            C.wait(t, "vE", v_yt)
            t.matmul(pa[1][:, 0:512], ident_bf[:], sc["ytmp"][:], start=False,
                     stop=False)
            C.wait(t, "gE", g_dxs)
            C.inc(t.matmul(pa[1][:, 0:512], ident_bf[:], sc["dxst"][:],
                           start=False, stop=True), "tE")
            dxs_at[g] = C.cnt["tE"]
        sec("tensor", t_inj)
        t_y2 = C.cnt["tE"]

        def s_y(s, g=g, t_y2=t_y2):
            C.wait(s, "tE", t_y2)
            C.inc(s.activation(ysb[:, g * 512:(g + 1) * 512], pa[1][:, 0:512],
                               AF.Copy), "sE")
            C.bank_mark("pa1", "sE")
        sec("scalar", s_y)

        def g_xt(gp, g=g, v_3=v_3, s_xst=s_xst):
            gp.wait_ge(C.sems["vE"], v_3)
            gp.wait_ge(C.sems["sE"], s_xst)
            if g >= 1:
                gp.wait_ge(C.sems["tE"], st_at[g - 1])
            last = None
            for i, r in enumerate(range(8 * g, 8 * g + 8)):
                last = gp.tensor_scalar(sc["xtld"][:, i * 64:(i + 1) * 64],
                                        sc["xsT"][:, r * 64:(r + 1) * 64],
                                        sc["wtok"][:, r:r + 1], None, OP.mult)
            C.inc(last, "gE")
        sec("gpsimd", g_xt)
        g_xt_at = C.cnt["gE"]

        def t_st(t, g=g, g_xt_at=g_xt_at):
            C.wait(t, "gE", g_xt_at)
            C.bank_guard(t, "pa2")
            for i in range(8):
                t.matmul(pa[2][:, i * 64:(i + 1) * 64], sc["bmt"][:],
                         sc["xtld"][:, i * 64:(i + 1) * 64],
                         start=(i == 0), stop=False)
            C.bank_guard(t, "pa5")
            C.inc(t.matmul(pa[5][:, 0:512], sc["sf16"][:],
                           e2_f16[:, g * 512:(g + 1) * 512], start=True,
                           stop=True), "tE")
        sec("tensor", t_st)
        t_sd = C.cnt["tE"]

        def s_d(s, g=g, t_sd=t_sd):
            C.wait(s, "tE", t_sd)
            C.inc(s.activation(sc["decay_sb"][:], pa[5][:, 0:512], AF.Copy), "sE")
            C.bank_mark("pa5", "sE")
        sec("scalar", s_d)
        s_dec = C.cnt["sE"]

        def g_hd(gp, g=g, s_dec=s_dec):
            gp.wait_ge(C.sems["sE"], s_dec)
            C.inc(gp.tensor_tensor(sc["hdec"][:], h_bf[:, g * 512:(g + 1) * 512],
                                   sc["decay_sb"][:], OP.mult), "gE")
        sec("gpsimd", g_hd)
        g_hd_at = C.cnt["gE"]

        def t_hinj(t, g=g, g_hd_at=g_hd_at):
            C.wait(t, "gE", g_hd_at)
            C.inc(t.matmul(pa[2][:, 0:512], ident_bf[:], sc["hdec"][:],
                           start=False, stop=True), "tE")
            st_at[g] = C.cnt["tE"]
        sec("tensor", t_hinj)
        t_hu = C.cnt["tE"]

        def s_h(s, g=g, t_hu=t_hu):
            C.wait(s, "tE", t_hu)
            C.inc(s.activation(h_bf[:, g * 512:(g + 1) * 512], pa[2][:, 0:512],
                               AF.Copy), "sE")
            C.bank_mark("pa2", "sE")
        sec("scalar", s_h)

    def v_tail(v, dx_rd=dx_rd):
        C.wait(v, "sE")
        C.wait(v, "dx")
        v.tensor_tensor(ysb[:], ysb[:], zrd, OP.mult)
        _ln_stats(C, v, bnb, ysb[:], DS, smalls[:, 8:9], smalls[:, 9:10],
                  smalls, 12, want_meansq=True)
        C.inc(v.tensor_copy(smalls[:, 18:19], smalls[:, 9:10]), "vE")
    sec("vector", v_tail)
    v_t1 = C.cnt["vE"]

    def s_tail(s, v_t1=v_t1):
        C.wait(s, "vE", v_t1)
        C.inc(_rstd_finish(s, smalls[:, 18:19], smalls[:, 10:11],
                           smalls[:, 11:12]), "sE")
    sec("scalar", s_tail)
    s_t1 = C.cnt["sE"]

    def v_tail2(v, s_t1=s_t1):
        C.wait(v, "sE", s_t1)
        C.inc(v.scalar_tensor_tensor(sc["yhat_tok"][:], ysb[:], smalls[:, 10:11],
                                     nw_bc[:], OP.mult, OP.mult), "vE")
    sec("vector", v_tail2)
    v_t2 = C.cnt["vE"]

    for jb in range(4):
        def tf2(t, jb=jb, v_t2=v_t2):
            if jb == 0:
                C.wait(t, "vE", v_t2)
            key = f"pb{jb % 2}"
            C.bank_guard(t, key)
            for i in range(8):
                c = jb * 8 + i
                mm = t.transpose(pb[jb % 2][:, i * 128:(i + 1) * 128],
                                 sc["yhat_tok"][:, c * 128:(c + 1) * 128],
                                 ident_bf[:])
            C.inc(mm, "tE")
        sec("tensor", tf2)
        t_tr2 = C.cnt["tE"]

        def sg2(s, jb=jb, t_tr2=t_tr2):
            s.wait_ge(C.sems["tE"], t_tr2)
            dstap = yhatF[:, jb * 8 * TT:(jb * 8 + 8) * TT] \
                .rearrange("p (c t) -> p c t", t=TT)[:, :, ch * 128:(ch + 1) * 128]
            C.inc(s.activation(
                dstap, pb[jb % 2][:, 0:1024].rearrange("p (c t) -> p c t", t=128),
                AF.Copy), "sE")
            C.bank_mark(f"pb{jb % 2}", "sE")
        sec("scalar", sg2)


def _e_outproj(nc, C, sec, tok0, w_out, yhatF, wbuf, x1stg, x1spill, pa):
    dw_at = {}
    mm_done = {}
    do_at = {}

    def pre(g):
        def f(gp, g=g):
            col0 = g * 256
            if g >= 2:
                gp.wait_ge(C.sems["tE"], mm_done[g - 2])
            dw_at[g] = C.dma(gp.dma_start(
                wbuf[g % 2][:, 0:32 * 256].rearrange("p (c f) -> p c f", f=256),
                w_out.rearrange("(c p) f -> p c f", p=128)[:, :, col0:col0 + 256]),
                "dwa" if g % 2 == 0 else "dwb")
        sec("gpsimd", f)

    pre(0)
    pre(1)
    for g in range(16):
        col0 = g * 256
        for b in range(NB):
            idx = g * NB + b

            def tf(t, g=g, b=b, idx=idx):
                if b == 0:
                    C.wait(t, "dwa" if g % 2 == 0 else "dwb")
                C.bank_guard(t, f"pa{idx % 3}")
                for c in range(32):
                    mm = t.matmul(pa[idx % 3][:, 0:256],
                                  yhatF[:, c * TT + b * 128:c * TT + (b + 1) * 128],
                                  wbuf[g % 2][:, c * 256:(c + 1) * 256],
                                  start=(c == 0), stop=(c == 31))
                C.inc(mm, "tE")
                mm_done[g] = C.cnt["tE"]
            sec("tensor", tf)
            t_mm = C.cnt["tE"]

            def sf(s, idx=idx, t_mm=t_mm):
                C.wait(s, "tE", t_mm)
                if idx >= 2:
                    C.wait(s, "doa" if idx % 2 == 0 else "dob")
                C.inc(s.activation(x1stg[idx % 2][:, 0:256], pa[idx % 3][:, 0:256],
                                   AF.Copy), "sE")
                C.bank_mark(f"pa{idx % 3}", "sE")
            sec("scalar", sf)
            s_x1 = C.cnt["sE"]

            def df(sy, b=b, col0=col0, idx=idx, s_x1=s_x1):
                C.wait(sy, "sE", s_x1)
                do_at[idx] = C.dma(sy.dma_start(
                    x1spill[tok0 + b * 128:tok0 + (b + 1) * 128, col0:col0 + 256],
                    x1stg[idx % 2][:, 0:256]), "doa" if idx % 2 == 0 else "dob")
            sec("sync", df)
        if g + 2 < 16:
            pre(g + 2)


def _f_prep(nc, C, sec, ffn_a, ffn_b, ffn_wq, ffn_keys, wbuf, awT, bwT,
            wqT, keysT, ident_bf, pa, pb):
    def d0(gp):
        for h in range(2):
            C.dma(gp.dma_start(
                wbuf[h][:, 0:8192].rearrange("p (j d) -> p j d", d=4096),
                ffn_a.rearrange("(j p) d -> p j d", p=128)[:, 2 * h:2 * h + 2, :]),
                "dw")
    sec("gpsimd", d0)
    dw_a = C.cnt["dw"]

    for cb in range(16):
        def tf(t, cb=cb):
            if cb == 0:
                C.wait(t, "dw", dw_a)
            key = f"pb{cb % 2}"
            C.bank_guard(t, key)
            for cc in range(2):
                c = cb * 2 + cc
                for j in range(4):
                    mm = t.transpose(
                        pb[cb % 2][:, cc * 512 + j * 128:cc * 512 + (j + 1) * 128],
                        wbuf[j // 2][:, (j % 2) * 4096 + c * 128:
                                     (j % 2) * 4096 + (c + 1) * 128],
                        ident_bf[:])
            C.inc(mm, "tE")
        sec("tensor", tf)
        t_tr = C.cnt["tE"]

        def sg(s, cb=cb, t_tr=t_tr):
            s.wait_ge(C.sems["tE"], t_tr)
            C.inc(s.activation(awT[:, cb * 1024:(cb + 1) * 1024],
                               pb[cb % 2][:, 0:1024], AF.Copy), "sE")
            C.bank_mark(f"pb{cb % 2}", "sE")
        sec("scalar", sg)
    t_a_done = C.cnt["tE"]

    def d1(gp, t_a_done=t_a_done):
        gp.wait_ge(C.sems["tE"], t_a_done)
        for h in range(2):
            C.dma(gp.dma_start(
                wbuf[h][:, 0:8192].rearrange("p (j n) -> p j n", n=512),
                ffn_b.rearrange("(j p) n -> p j n", p=128)[:, 16 * h:16 * h + 16, :]),
                "dw")
    sec("gpsimd", d1)
    dw_b = C.cnt["dw"]

    for nch in range(4):
        for jb in range(4):
            def tf2(t, nch=nch, jb=jb):
                if nch == 0 and jb == 0:
                    C.wait(t, "dw", dw_b)
                key = f"pb{jb % 2}"
                C.bank_guard(t, key)
                for i in range(8):
                    j = jb * 8 + i
                    mm = t.transpose(pb[jb % 2][:, i * 128:(i + 1) * 128],
                                     wbuf[j // 16][:, (j % 16) * 512 + nch * 128:
                                                   (j % 16) * 512 + (nch + 1) * 128],
                                     ident_bf[:])
                C.inc(mm, "tE")
            sec("tensor", tf2)
            t_tr2 = C.cnt["tE"]

            def sg2(s, nch=nch, jb=jb, t_tr2=t_tr2):
                s.wait_ge(C.sems["tE"], t_tr2)
                C.inc(s.activation(
                    bwT[:, nch * 4096 + jb * 1024:nch * 4096 + (jb + 1) * 1024],
                    pb[jb % 2][:, 0:1024], AF.Copy), "sE")
                C.bank_mark(f"pb{jb % 2}", "sE")
            sec("scalar", sg2)
    t_b_done = C.cnt["tE"]

    def d2(gp, t_b_done=t_b_done):
        gp.wait_ge(C.sems["tE"], t_b_done)
        C.dma(gp.dma_start(wbuf[0][0:16, 0:4096], ffn_wq), "dw")
        C.dma(gp.dma_start(wbuf[0][0:64, 4096:4112], ffn_keys), "dw")
    sec("gpsimd", d2)
    dw_q = C.cnt["dw"]

    for jb in range(4):
        def tf3(t, jb=jb):
            if jb == 0:
                C.wait(t, "dw", dw_q)
            key = f"pb{jb % 2}"
            C.bank_guard(t, key)
            for i in range(8):
                c = jb * 8 + i
                mm = t.transpose(pb[jb % 2][:, i * 16:(i + 1) * 16],
                                 wbuf[0][0:16, c * 128:(c + 1) * 128],
                                 ident_bf[0:16, 0:16])
            C.inc(mm, "tE")
        sec("tensor", tf3)
        t_tr3 = C.cnt["tE"]

        def sg3(s, jb=jb, t_tr3=t_tr3):
            s.wait_ge(C.sems["tE"], t_tr3)
            C.inc(s.activation(wqT[:, jb * 128:(jb + 1) * 128],
                               pb[jb % 2][:, 0:128], AF.Copy), "sE")
            C.bank_mark(f"pb{jb % 2}", "sE")
        sec("scalar", sg3)

    def tf4(t):
        C.bank_guard(t, "pb0")
        C.inc(t.transpose(pb[0][0:16, 0:64], wbuf[0][0:64, 4096:4112],
                          ident_bf[0:64, 0:64]), "tE")
    sec("tensor", tf4)
    t_k = C.cnt["tE"]

    def sg4(s, t_k=t_k):
        s.wait_ge(C.sems["tE"], t_k)
        C.inc(s.activation(keysT[:], pb[0][0:16, 0:64], AF.Copy), "sE")
        C.bank_mark("pb0", "sE")
    sec("scalar", sg4)


def _f_block(nc, C, sec, tok0, b, x_in, x1spill, out_t, xblk2, ysb2, x3f,
             x2F, awT, bwT, wqT, keysT, e512_bf, fscr, fscr2, gq, gwork,
             gate_bf, gateT, zhat, zhT, qtok, qT, bnb, smalls, rcols,
             ident_bf, ident_f32, pa, pb, do_at):
    t0b = tok0 + b * 128

    def d0(sy, b=b):
        C.dma(sy.dma_start(xblk2[:], x_in[t0b:t0b + 128, :]), "dx")
        C.dma(sy.dma_start(x2F[:], x1spill[t0b:t0b + 128, :]), "dx")
    sec("sync", d0)
    dx_b = C.cnt["dx"]

    def v0(v, dx_b=dx_b):
        C.wait(v, "dx")
        v.tensor_tensor(ysb2[:], x2F[:, 0:H], xblk2[:], OP.add)
        _ln_stats(C, v, bnb, ysb2[:], H, smalls[:, 8:9], smalls[:, 9:10],
                  smalls, 12, want_meansq=False)
        C.inc(v.tensor_copy(smalls[:, 18:19], smalls[:, 9:10]), "vE")
    sec("vector", v0)
    v_ln = C.cnt["vE"]

    def s0(s, v_ln=v_ln):
        C.wait(s, "vE", v_ln)
        C.inc(_rstd_finish(s, smalls[:, 18:19], smalls[:, 10:11],
                           smalls[:, 11:12]), "sE")
    sec("scalar", s0)
    s_ln = C.cnt["sE"]

    def v1(v, s_ln=s_ln):
        C.wait(v, "sE", s_ln)
        C.inc(v.tensor_scalar(ysb2[:], ysb2[:], smalls[:, 8:9], smalls[:, 10:11],
                              OP.subtract, OP.mult), "vE")
    sec("vector", v1)
    v_x2 = C.cnt["vE"]

    for jb in range(8):
        def tf(t, jb=jb, v_x2=v_x2):
            if jb == 0:
                C.wait(t, "vE", v_x2)
            key = f"pa{4 + jb % 2}"
            C.bank_guard(t, key)
            for i in range(4):
                c = jb * 4 + i
                mm = t.transpose(pa[4 + jb % 2][:, i * 128:(i + 1) * 128],
                                 ysb2[:, c * 128:(c + 1) * 128], ident_f32[:])
            C.inc(mm, "tE")
        sec("tensor", tf)
        t_tr = C.cnt["tE"]

        def sg(s, jb=jb, t_tr=t_tr):
            s.wait_ge(C.sems["tE"], t_tr)
            C.inc(s.activation(x2F[:, jb * 512:(jb + 1) * 512],
                               pa[4 + jb % 2][:, 0:512], AF.Copy), "sE")
            C.bank_mark(f"pa{4 + jb % 2}", "sE")
        sec("scalar", sg)
    s_x2f = C.cnt["sE"]

    def t_a(t, s_x2f=s_x2f):
        C.wait(t, "sE", s_x2f)
        C.bank_guard(t, "pa0")
        C.bank_guard(t, "pa1")
        last = None
        for c in range(32):
            t.matmul(pa[0][:, 0:512], x2F[:, c * 128:(c + 1) * 128],
                     awT[:, c * 512:(c + 1) * 512], start=(c == 0), stop=(c == 31))
            last = t.matmul(pa[1][:, 0:16], x2F[:, c * 128:(c + 1) * 128],
                            wqT[:, c * 16:(c + 1) * 16], start=(c == 0),
                            stop=(c == 31))
        C.inc(last, "tE")
    sec("tensor", t_a)
    t_aq = C.cnt["tE"]

    def s_q(s, t_aq=t_aq):
        C.wait(s, "tE", t_aq)
        C.inc(s.activation(qtok[:], pa[1][:, 0:16], AF.Copy), "sE")
        C.bank_mark("pa1", "sE")
    sec("scalar", s_q)
    s_qe = C.cnt["sE"]

    def t_q(t, s_qe=s_qe):
        C.wait(t, "sE", s_qe)
        C.bank_guard(t, "pb0")
        C.inc(t.transpose(pb[0][0:16, 0:128], qtok[:], ident_bf[:]), "tE")
    sec("tensor", t_q)
    t_qt = C.cnt["tE"]

    def s_q2(s, t_qt=t_qt):
        C.wait(s, "tE", t_qt)
        C.inc(s.activation(qT[:], pb[0][0:16, 0:128], AF.Copy), "sE")
        C.bank_mark("pb0", "sE")
    sec("scalar", s_q2)
    s_qt = C.cnt["sE"]

    def t_sc(t, s_qt=s_qt):
        C.wait(t, "sE", s_qt)
        C.bank_guard(t, "pa2")
        C.inc(t.matmul(pa[2][:, 0:64], qT[:], keysT[:], start=True, stop=True),
              "tE")
    sec("tensor", t_sc)
    t_s = C.cnt["tE"]

    def v_r(v, t_s=t_s):
        C.wait(v, "tE", t_s)
        v.tensor_copy(gq[:], pa[2][:, 0:64])
        C.bank_mark("pa2", "vE")
        v.tensor_copy(gwork[:], gq[:])
        for it in range(15):
            v.reduce_max(rcols[0][:], gwork[:], axis=AX)
            if it == 0:
                v.tensor_copy(rcols[1][:], rcols[0][:])
            v.tensor_scalar(fscr2[:, 0:64], gwork[:], rcols[0][:], None, OP.is_ge)
            v.scalar_tensor_tensor(gwork[:], fscr2[:, 0:64], -1e30, gwork[:],
                                   OP.mult, OP.add)
        v.reduce_max(rcols[2][:], gwork[:], axis=AX)
        C.inc(v.tensor_scalar(rcols[1][:], rcols[1][:], -1.0, None, OP.mult), "vE")
    sec("vector", v_r)
    v_rt = C.cnt["vE"]

    def s_r(s, v_rt=v_rt):
        C.wait(s, "vE", v_rt)
        C.inc(s.activation(fscr[:, 0:64], gq[:], AF.Exp, bias=rcols[1][:]), "sE")
    sec("scalar", s_r)
    s_re = C.cnt["sE"]

    def v_g(v, s_re=s_re):
        C.wait(v, "sE", s_re)
        v.tensor_scalar(gwork[:], gq[:], rcols[2][:], None, OP.is_ge)
        v.tensor_tensor(gwork[:], gwork[:], fscr[:, 0:64], OP.mult)
        v.reduce_sum(rcols[3][:], gwork[:], axis=AX)
        v.reciprocal(rcols[0][:], rcols[3][:])
        C.inc(v.tensor_scalar(gate_bf[:], gwork[:], rcols[0][:], 2.0,
                              OP.mult, OP.mult), "vE")
    sec("vector", v_g)
    v_gt = C.cnt["vE"]

    def t_g(t, v_gt=v_gt):
        C.wait(t, "vE", v_gt)
        C.bank_guard(t, "pb1")
        C.inc(t.transpose(pb[1][0:64, 0:128], gate_bf[:], ident_bf[:]), "tE")
    sec("tensor", t_g)
    t_gt = C.cnt["tE"]

    def s_g(s, t_gt=t_gt):
        C.wait(s, "tE", t_gt)
        C.inc(s.activation(gateT[:], pb[1][0:64, 0:128], AF.Copy), "sE")
        C.bank_mark("pb1", "sE")
    sec("scalar", s_g)
    s_gT = C.cnt["sE"]

    def t_ge(t, s_gT=s_gT):
        C.wait(t, "sE", s_gT)
        C.bank_guard(t, "pa2")
        C.inc(t.matmul(pa[2][:, 0:512], gateT[:], e512_bf[:], start=True,
                       stop=True), "tE")
    sec("tensor", t_ge)
    t_gx = C.cnt["tE"]

    def s_ge(s, t_gx=t_gx):
        C.wait(s, "tE", t_gx)
        C.inc(s.activation(fscr[:, 0:512], pa[2][:, 0:512], AF.Copy), "sE")
        C.bank_mark("pa2", "sE")
    sec("scalar", s_ge)
    s_gx = C.cnt["sE"]

    def v_z(v, s_gx=s_gx):
        C.wait(v, "sE", s_gx)
        C.inc(v.scalar_tensor_tensor(zhat[:], pa[0][:, 0:512], 1.0,
                                     fscr[:, 0:512], OP.mult, OP.mult), "vE")
        C.bank_mark("pa0", "vE")
    sec("vector", v_z)
    v_zh = C.cnt["vE"]

    def t_zt(t, v_zh=v_zh):
        C.wait(t, "vE", v_zh)
        C.bank_guard(t, "pb0")
        last = None
        for i in range(4):
            last = t.transpose(pb[0][:, i * 128:(i + 1) * 128],
                               zhat[:, i * 128:(i + 1) * 128], ident_bf[:])
        C.inc(last, "tE")
    sec("tensor", t_zt)
    t_zT = C.cnt["tE"]

    def s_zt(s, t_zT=t_zT):
        C.wait(s, "tE", t_zT)
        C.inc(s.activation(zhT[:], pb[0][:, 0:512], AF.Copy), "sE")
        C.bank_mark("pb0", "sE")
    sec("scalar", s_zt)
    s_zTT = C.cnt["sE"]

    for og in range(8):
        def t_b(t, og=og, s_zTT=s_zTT):
            if og == 0:
                C.wait(t, "sE", s_zTT)
            key = f"pa{3 + og % 2}"
            C.bank_guard(t, key)
            last = None
            for nch in range(4):
                last = t.matmul(pa[3 + og % 2][:, 0:512],
                                zhT[:, nch * 128:(nch + 1) * 128],
                                bwT[:, nch * 4096 + og * 512:
                                    nch * 4096 + (og + 1) * 512],
                                start=(nch == 0), stop=(nch == 3))
            C.inc(last, "tE")
        sec("tensor", t_b)
        t_bo = C.cnt["tE"]

        def v_o(v, og=og, t_bo=t_bo):
            C.wait(v, "tE", t_bo)
            C.inc(v.scalar_tensor_tensor(x3f[:, og * 512:(og + 1) * 512],
                                         pa[3 + og % 2][:, 0:512], 1.0,
                                         ysb2[:, og * 512:(og + 1) * 512],
                                         OP.mult, OP.add), "vE")
            C.bank_mark(f"pa{3 + og % 2}", "vE")
        sec("vector", v_o)

    def v_l2(v):
        _ln_stats(C, v, bnb, x3f[:], H, smalls[:, 8:9], smalls[:, 9:10],
                  smalls, 12, want_meansq=False)
        C.inc(v.tensor_copy(smalls[:, 18:19], smalls[:, 9:10]), "vE")
    sec("vector", v_l2)
    v_l2a = C.cnt["vE"]

    def s_l2(s, v_l2a=v_l2a):
        C.wait(s, "vE", v_l2a)
        C.inc(_rstd_finish(s, smalls[:, 18:19], smalls[:, 10:11],
                           smalls[:, 11:12]), "sE")
    sec("scalar", s_l2)
    s_l2a = C.cnt["sE"]

    def v_out(v, b=b, s_l2a=s_l2a):
        C.wait(v, "sE", s_l2a)
        C.inc(v.tensor_scalar(x3f[:], x3f[:], smalls[:, 8:9], smalls[:, 10:11],
                              OP.subtract, OP.mult), "vE")
    sec("vector", v_out)
    v_ot = C.cnt["vE"]

    def d_out(sy, b=b, v_ot=v_ot):
        C.wait(sy, "vE", v_ot)
        do_at[b] = C.dma(sy.dma_start(out_t[t0b:t0b + 128, :], x3f[:]), "do")
    sec("sync", d_out)


def kernel(**inputs):
    _ = inputs.get("L0_attn_residual")
    x = np.ascontiguousarray(inputs["L0_attn_hidden_states"], dtype=np.float32)
    nc = build()
    base = {
        "w_in": np.ascontiguousarray(inputs["L0_attn_in_proj_w"], np.float32),
        "conv_w": np.ascontiguousarray(inputs["L0_attn_conv_w"], np.float32),
        "a_log": np.ascontiguousarray(inputs["L0_attn_A_log"], np.float32).reshape(1, NH),
        "dt_bias": np.ascontiguousarray(inputs["L0_attn_dt_bias"], np.float32).reshape(1, NH),
        "d_vec": np.ascontiguousarray(inputs["L0_attn_D"], np.float32).reshape(1, NH),
        "ln_w": np.ascontiguousarray(inputs["L0_attn_ln_w"], np.float32).reshape(1, H),
        "norm_w": np.ascontiguousarray(inputs["L0_attn_norm_w"], np.float32).reshape(1, DS),
        "mup": np.ascontiguousarray(inputs["L0_attn_mup_vector"], np.float32).reshape(1, PROJ),
        "w_out": np.ascontiguousarray(inputs["L0_attn_out_proj_w"], np.float32),
        "ffn_a": np.ascontiguousarray(inputs["L0_ffn_A_w"], np.float32),
        "ffn_b": np.ascontiguousarray(inputs["L0_ffn_B_w"], np.float32),
        "ffn_wq": np.ascontiguousarray(inputs["L0_ffn_Wq_w"], np.float32),
        "ffn_keys": np.ascontiguousarray(inputs["L0_ffn_keys"], np.float32),
    }
    in_maps = [dict(base, x=np.ascontiguousarray(x[c])) for c in range(NCORES)]
    res = run_bass_kernel_spmd(nc, in_maps, core_ids=list(range(NCORES)))
    return np.stack([res.results[c]["out"] for c in range(NCORES)], axis=0)


# version 04
# speedup vs baseline: 1.2364x; 1.2364x over previous
import contextlib
import numpy as np
import concourse.bass as bass
import concourse.mybir as mybir
from concourse.bass_utils import run_bass_kernel_spmd
from concourse.alu_op_type import AluOpType as OP

F32 = mybir.dt.float32
F16 = mybir.dt.float16
BF16 = mybir.dt.bfloat16
AF = mybir.ActivationFunctionType
AX = mybir.AxisListType.X

L = 2048
H = 4096
DS = 4096
PROJ = 8512
CONVD = 4352
NH = 64
TT = 1024
NB = 8
NT = L // TT
EPS = 1e-5
NCORES = 8
EXT = TT + 3

ALL_ENG = ["sync", "gpsimd", "tensor", "vector", "scalar"]

_DRAIN_OPS = frozenset({
    "tensor_scalar", "tensor_tensor", "scalar_tensor_tensor", "tensor_copy",
    "tensor_tensor_scan", "reduce_max", "reduce_sum", "tensor_reduce",
    "bn_stats", "bn_aggr", "reciprocal", "memset", "activation", "select",
    "copy",
})


class _AutoDrain:

    def __init__(self, eng):
        self._e = eng
        self._w = set()
        self._r = set()

    def __getattr__(self, name):
        attr = getattr(self._e, name)
        if name in _DRAIN_OPS:
            def w(*a, **k):
                aps = [x for x in list(a) + list(k.values())
                       if isinstance(x, bass.AP)]
                names = [ap.tensor.name for ap in aps]
                out, ins = set(names[:1]), set(names[1:])
                if (ins & self._w) or (out & self._w) or (out & self._r):
                    self._e.drain()
                    self._w.clear()
                    self._r.clear()
                self._w |= out
                self._r |= ins
                return attr(*a, **k)
            return w
        return attr


def bcast_ap(dram_ap, npart, cols):
    return bass.AP(dram_ap.tensor, 0, [[0, npart], [1, cols]])


class Ctx:

    def __init__(self, nc, es):
        self.nc = nc
        self.sems = {}
        self.cnt = {}
        for name in ["dw", "dwa", "dwb", "dx", "do", "doa", "dob", "tE", "vE",
                     "sE", "gE"]:
            self.sems[name] = es.enter_context(nc.semaphore(name))
            self.cnt[name] = 0
        self.bank_rd = {}

    def inc(self, instr, name, n=1):
        instr.then_inc(self.sems[name], n)
        self.cnt[name] += n
        return self.cnt[name]

    def dma(self, instr, name):
        return self.inc(instr, name, 16)

    def wait(self, eng, name, val=None):
        v = self.cnt[name] if val is None else val
        if v:
            eng.wait_ge(self.sems[name], v)

    def bank_guard(self, eng, key):
        if key in self.bank_rd:
            sem, val = self.bank_rd[key]
            if val:
                eng.wait_ge(self.sems[sem], val)

    def bank_mark(self, key, sem):
        self.bank_rd[key] = (sem, self.cnt[sem])


def build(debug=()):
    nc = bass.Bass("TRN2", target_bir_lowering=False)
    debug = set(debug)

    x_in = nc.dram_tensor("x", [L, H], F32, kind="ExternalInput").ap()
    w_in = nc.dram_tensor("w_in", [H, PROJ], F32, kind="ExternalInput").ap()
    conv_w = nc.dram_tensor("conv_w", [CONVD, 4], F32, kind="ExternalInput").ap()
    a_log = nc.dram_tensor("a_log", [1, NH], F32, kind="ExternalInput").ap()
    dt_bias = nc.dram_tensor("dt_bias", [1, NH], F32, kind="ExternalInput").ap()
    d_vec = nc.dram_tensor("d_vec", [1, NH], F32, kind="ExternalInput").ap()
    ln_w = nc.dram_tensor("ln_w", [1, H], F32, kind="ExternalInput").ap()
    norm_w = nc.dram_tensor("norm_w", [1, DS], F32, kind="ExternalInput").ap()
    mup = nc.dram_tensor("mup", [1, PROJ], F32, kind="ExternalInput").ap()
    w_out = nc.dram_tensor("w_out", [DS, H], F32, kind="ExternalInput").ap()
    ffn_a = nc.dram_tensor("ffn_a", [512, H], F32, kind="ExternalInput").ap()
    ffn_b = nc.dram_tensor("ffn_b", [H, 512], F32, kind="ExternalInput").ap()
    ffn_wq = nc.dram_tensor("ffn_wq", [16, H], F32, kind="ExternalInput").ap()
    ffn_keys = nc.dram_tensor("ffn_keys", [NH, 16], F32, kind="ExternalInput").ap()

    out_t = nc.dram_tensor("out", [L, H], F32, kind="ExternalOutput").ap()

    def scratch(name, shape, dtype):
        if name in debug:
            return nc.dram_tensor(name, shape, dtype, kind="ExternalOutput").ap()
        return nc.dram_tensor(name, shape, dtype).ap()

    zspill = scratch("zspill", [L, DS], BF16)
    xsspill = scratch("xsspill", [DS, L], BF16)
    x1spill = scratch("x1spill", [L, H], BF16)

    es = contextlib.ExitStack()
    with es:
        C = Ctx(nc, es)
        block = es.enter_context(nc.Block())

        def sec(eng_name, fn):
            if eng_name in ("vector", "scalar"):
                getattr(block, eng_name)(lambda e: fn(_AutoDrain(e)))
            else:
                getattr(block, eng_name)(fn)

        def barrier():
            snap = dict(C.cnt)
            for e in ALL_ENG:
                def f(eng, snap=snap):
                    for name, v in snap.items():
                        if v:
                            eng.wait_ge(C.sems[name], v)
                sec(e, f)

        A = lambda n, sh, dt: es.enter_context(nc.sbuf_tensor(n, sh, dt))

        ones_f32 = A("ones_f32", [128, 128], F32)
        ones_f16 = A("ones_f16", [128, 128], F16)
        ident_bf = A("ident_bf", [128, 128], BF16)
        ident_f32 = A("ident_f32", [128, 128], F32)
        tri_f16 = A("tri_f16", [128, 128], F16)
        su_f16 = A("su_f16", [128, 128], F16)
        tri_f32 = A("tri_f32", [128, 128], F32)
        e2_f16 = A("e2_f16", [64, DS], F16)
        e512_bf = A("e512_bf", [64, 512], BF16)
        dbias_bc = A("dbias_bc", [128, NH], F32)
        nega_bc = A("nega_bc", [128, NH], F32)
        dcol_bc = A("dcol_bc", [128, NH], F32)
        convwF = A("convwF", [128, 34 * 4], F32)
        iota_f = A("iota_f", [128, 128], F32)
        iota_c = A("iota_c", [128, 1], F32)
        smalls = A("smalls", [128, 32], F32)
        sm64 = A("sm64", [64, 16], F32)
        dtT = A("dtT", [128, NB * NH], F32)
        h_bf = A("h_bf", [128, DS], BF16)
        tailsave = A("tailsave", [128, 34 * 3], BF16)
        bmcf = A("bmcf", [128, 2 * TT], BF16)
        wbuf = [A(f"wbuf{i}", [128, 8192], BF16) for i in range(2)]
        bnb = [A(f"bnb{i}", [128, 24], F32) for i in range(2)]

        pa = [es.enter_context(nc.psum_tensor(f"pa{i}", [128, 512], F32)) for i in range(6)]
        pb = [es.enter_context(nc.psum_tensor(f"pb{i}", [128, 1024], BF16)) for i in range(2)]

        with contextlib.ExitStack() as sp:
            e2scr = sp.enter_context(nc.sbuf_tensor("e2scr", [64, DS], F32))

            def setup_sync(s):
                C.dma(s.dma_start(convwF[:].rearrange("p (c k) -> p c k", k=4),
                                  conv_w.rearrange("(c p) k -> p c k", p=128)), "dx")
                C.dma(s.dma_start(dbias_bc[:], bcast_ap(dt_bias, 128, NH)), "dx")
                C.dma(s.dma_start(nega_bc[:], bcast_ap(a_log, 128, NH)), "dx")
                C.dma(s.dma_start(dcol_bc[:], bcast_ap(d_vec, 128, NH)), "dx")
            sec("sync", setup_sync)

            def setup_gp(g):
                g.iota(iota_f[:], [[1, 128]], base=0, channel_multiplier=0,
                       allow_small_or_imprecise_dtypes=True)
                g.iota(iota_c[:], [[0, 1]], base=0, channel_multiplier=1,
                       allow_small_or_imprecise_dtypes=True)
                C.inc(g.iota(e2scr[:, :].rearrange("p (a b) -> p a b", b=64),
                             [[1, 64], [0, 64]], base=0, channel_multiplier=0,
                             allow_small_or_imprecise_dtypes=True), "gE")
            sec("gpsimd", setup_gp)

            def setup_v(v):
                C.wait(v, "gE")
                C.wait(v, "dx")
                v.memset(ones_f32[:], 1.0)
                v.memset(ones_f16[:], 1.0)
                v.memset(h_bf[:], 0.0)
                v.memset(tailsave[:], 0.0)
                v.tensor_scalar(ident_f32[:], iota_f[:], iota_c[:], None, OP.is_equal)
                v.tensor_copy(ident_bf[:], ident_f32[:])
                v.tensor_scalar(tri_f32[:], iota_f[:], iota_c[:], None, OP.is_ge)
                v.tensor_copy(tri_f16[:], tri_f32[:])
                v.tensor_scalar(su_f16[:], iota_f[:], iota_c[:], None, OP.is_lt)
                v.tensor_scalar(e2scr[:, :], e2scr[:, :], iota_c[0:64, :],
                                None, OP.is_equal)
                C.inc(v.tensor_copy(e2_f16[:], e2scr[:, :]), "vE")
            sec("vector", setup_v)
            v_e2 = C.cnt["vE"]

            def setup_gp2(g):
                C.wait(g, "vE", v_e2)
                C.inc(g.iota(e2scr[:, 0:512].rearrange("p (a b) -> p a b", b=8),
                             [[1, 64], [0, 8]], base=0, channel_multiplier=0,
                             allow_small_or_imprecise_dtypes=True), "gE")
            sec("gpsimd", setup_gp2)

            def setup_v2(v):
                C.wait(v, "gE")
                v.tensor_scalar(e2scr[:, 0:512], e2scr[:, 0:512], iota_c[0:64, :],
                                None, OP.is_equal)
                C.inc(v.tensor_copy(e512_bf[:], e2scr[:, 0:512]), "vE")
            sec("vector", setup_v2)
            v_e5 = C.cnt["vE"]

            def setup_s(s):
                C.wait(s, "vE", v_e5)
                C.inc(s.activation(nega_bc[:], nega_bc[:], AF.Exp), "sE")
            sec("scalar", setup_s)

            def setup_v3(v):
                C.wait(v, "sE")
                C.inc(v.tensor_scalar(nega_bc[:], nega_bc[:], -1.0, None, OP.mult),
                      "vE")
            sec("vector", setup_v3)

            barrier()

        for tile in range(NT):
            tok0 = tile * TT

            with contextlib.ExitStack() as ph:
                Aph = lambda n, sh, dt: ph.enter_context(
                    nc.sbuf_tensor(f"{n}_t{tile}", sh, dt))
                xhatF = Aph("xhatF", [128, 32 * TT], BF16)

                with contextlib.ExitStack() as p0s:
                    A0 = lambda n, sh, dt: p0s.enter_context(
                        nc.sbuf_tensor(f"{n}_t{tile}", sh, dt))
                    lnw_bc = A0("lnw_bc", [128, H], F16)
                    xblk = [A0(f"xblk{i}", [128, H], F32) for i in range(2)]
                    xhat_tok = A0("xhat_tok", [128, H], BF16)
                    _p0(nc, C, sec, tok0, x_in, ln_w, lnw_bc, xblk, xhat_tok,
                        xhatF, bnb, smalls, ident_bf, pa, pb)
                    barrier()

                with contextlib.ExitStack() as a1s:
                    A1 = lambda n, sh, dt: a1s.enter_context(
                        nc.sbuf_tensor(f"{n}_t{tile}", sh, dt))
                    mup_bc = A1("mup_bc", [128, PROJ], F16)
                    xbct = [A1(f"xbct{i}", [128, EXT], BF16) for i in range(8)]
                    ztmp = A1("ztmp", [128, 512], F32)
                    bst = A1("bst", [128, 512], BF16)
                    accb = A1("accb", [128, TT], F32)
                    zstg = [A1(f"zstg{i}", [128, 512], BF16) for i in range(2)]
                    xstg = [A1(f"xstg{i}", [128, TT], BF16) for i in range(2)]

                    def mup_load(g):
                        C.dma(g.dma_start(mup_bc[:], bcast_ap(mup, 128, PROJ)), "dw")
                    sec("gpsimd", mup_load)

                    _a1(nc, C, sec, tok0, w_in, xhatF, wbuf, mup_bc, ztmp, bst,
                        xbct, accb, convwF, tailsave, bmcf, dtT, xstg, xsspill,
                        ident_bf, pa, pb)
                    barrier()
                    _a2(nc, C, sec, tok0, w_in, xhatF, wbuf, mup_bc, ztmp,
                        zstg, zspill, pa)
                    barrier()

            with contextlib.ExitStack() as ph:
                Aph = lambda n, sh, dt: ph.enter_context(
                    nc.sbuf_tensor(f"{n}_t{tile}", sh, dt))
                yhatF = Aph("yhatF", [128, 32 * TT], BF16)
                nw_bc = Aph("nw_bc", [128, DS], F16)
                ysb = Aph("ysb", [128, DS], F32)
                sc = {n: Aph(n, sh, dt) for n, sh, dt in [
                    ("xsT", [128, DS], BF16),
                    ("yhat_tok", [128, DS], BF16),
                    ("vexp", [128, 512], F32),
                    ("ytmp", [128, 512], BF16),
                    ("dxst", [128, 512], BF16),
                    ("xtld", [128, 512], BF16),
                    ("hdec", [128, 512], BF16),
                    ("decay_sb", [128, 512], F32),
                    ("cbtm", [128, 128], F32),
                    ("bmt", [128, 128], BF16),
                    ("dtc", [128, NH], F32),
                    ("dta32", [128, NH], F32),
                    ("dta16", [128, NH], F16),
                    ("wtok", [128, NH], F32),
                    ("vf16", [64, 128], F16),
                    ("sf16", [64, 128], F16),
                    ("t64a", [128, NH], F32),
                    ("t64b", [128, NH], F32),
                    ("t64c", [128, NH], F32),
                ]}
                sc["expd"] = [Aph(f"expd{i}", [128, 512], F32) for i in range(2)]
                sc["rhsd"] = [Aph(f"rhsd{i}", [128, 512], F16) for i in range(2)]
                sc["mtil"] = [Aph(f"mtil{i}", [128, 128], BF16) for i in range(2)]
                x1stg = [Aph(f"x1stg{i}", [128, 512], BF16) for i in range(2)]

                def nw_load(g):
                    C.dma(g.dma_start(nw_bc[:], bcast_ap(norm_w, 128, DS)), "dw")
                sec("gpsimd", nw_load)

                for ch in range(NB):
                    _scan_chunk(nc, C, sec, tok0, ch, sc, wbuf, xsspill,
                                zspill, dtT, dbias_bc, nega_bc, dcol_bc, h_bf,
                                bmcf, nw_bc, ysb, yhatF, ones_f16, ones_f32,
                                tri_f16, su_f16, tri_f32, ident_bf, e2_f16,
                                smalls, sm64, bnb, pa, pb)
                    barrier()
                _e_outproj(nc, C, sec, tok0, w_out, yhatF, wbuf, x1stg,
                           x1spill, pa)
                barrier()

            with contextlib.ExitStack() as ph:
                Aph = lambda n, sh, dt: ph.enter_context(
                    nc.sbuf_tensor(f"{n}_t{tile}", sh, dt))
                awT = Aph("awT", [128, 32 * 512], BF16)
                bwT = Aph("bwT", [128, 4 * H], BF16)
                wqT = Aph("wqT", [128, 32 * 16], BF16)
                keysT = Aph("keysT", [16, 64], BF16)
                x2F = Aph("x2F", [128, 32 * 128], BF16)
                x3f = Aph("x3f", [128, H], F32)
                xblk2 = Aph("xblk2", [128, H], F32)
                ysb2 = Aph("ysb2", [128, H], F32)
                fscr = Aph("fscr", [128, 512], F32)
                fscr2 = Aph("fscr2", [128, 64], F32)
                gq = Aph("gq", [128, 64], F32)
                gwork = Aph("gwork", [128, 64], F32)
                gate_bf = Aph("gate_bf", [128, 64], BF16)
                gateT = Aph("gateT", [64, 128], BF16)
                zhat = Aph("zhat", [128, 512], BF16)
                zhT = Aph("zhT", [128, 512], BF16)
                qtok = Aph("qtok", [128, 16], BF16)
                qT = Aph("qT", [16, 128], BF16)
                rcols = [Aph(f"rcol{i}", [128, 1], F32) for i in range(4)]

                _f_prep(nc, C, sec, ffn_a, ffn_b, ffn_wq, ffn_keys, wbuf,
                        awT, bwT, wqT, keysT, ident_bf, pa, pb)
                barrier()
                do_at = {}
                for b in range(NB):
                    _f_block(nc, C, sec, tok0, b, x_in, x1spill, out_t,
                             xblk2, ysb2, x3f, x2F, awT, bwT, wqT, keysT,
                             e512_bf, fscr, fscr2, gq, gwork, gate_bf, gateT,
                             zhat, zhT, qtok, qT, bnb, smalls, rcols, ident_bf,
                             ident_f32, pa, pb, do_at)
                    barrier()

        def fin(s):
            C.wait(s, "do")
            C.wait(s, "doa")
            C.wait(s, "dob")
        sec("sync", fin)

    return nc



def _ln_stats(C, v, bnb, src_ap, ncols, mu_col, m2_col, sm, c0, want_meansq):
    nst = ncols // 512
    for i in range(nst):
        v.bn_stats(bnb[i % 2][:, (i // 2) * 6:(i // 2) * 6 + 6],
                   src_ap[:, i * 512:(i + 1) * 512])
    v.bn_aggr(sm[:, c0:c0 + 2], bnb[0][:, 0:(nst // 2) * 6])
    v.bn_aggr(bnb[1][:, 18:20], bnb[1][:, 0:(nst // 2) * 6])
    mA, vA = sm[:, c0:c0 + 1], sm[:, c0 + 1:c0 + 2]
    mB, vB = bnb[1][:, 18:19], bnb[1][:, 19:20]
    t0, t1 = sm[:, c0 + 4:c0 + 5], sm[:, c0 + 5:c0 + 6]
    v.tensor_tensor(mu_col, mA, mB, OP.add)
    v.tensor_scalar(mu_col, mu_col, 0.5, None, OP.mult)
    v.tensor_tensor(t0, mA, mB, OP.subtract)
    v.tensor_tensor(t0, t0, t0, OP.mult)
    v.tensor_tensor(t1, vA, vB, OP.add)
    v.tensor_scalar(t1, t1, 0.5, None, OP.mult)
    v.scalar_tensor_tensor(m2_col, t0, 0.25, t1, OP.mult, OP.add)
    if want_meansq:
        v.tensor_tensor(t0, mu_col, mu_col, OP.mult)
        v.tensor_tensor(m2_col, m2_col, t0, OP.add)
    v.tensor_scalar(m2_col, m2_col, EPS, None, OP.add)


def _rstd_finish(s, src_col, dst_col, tmp_col):
    s.activation(tmp_col, src_col, AF.Ln)
    return s.activation(dst_col, tmp_col, AF.Exp, scale=-0.5)


def _p0(nc, C, sec, tok0, x_in, ln_w, lnw_bc, xblk, xhat_tok, xhatF, bnb,
        smalls, ident_bf, pa, pb):
    dx_at = {}
    v_stt = {}
    t_done = {}

    def g_lnw(g):
        C.dma(g.dma_start(lnw_bc[:], bcast_ap(ln_w, 128, H)), "dw")
    sec("gpsimd", g_lnw)
    dw_lnw = C.cnt["dw"]

    def d_pre(s):
        for b in range(2):
            dx_at[b] = C.dma(s.dma_start(
                xblk[b % 2][:], x_in[tok0 + b * 128:tok0 + (b + 1) * 128, :]), "dx")
    sec("sync", d_pre)

    for b in range(NB):
        def vf(v, b=b, dxf=None):
            C.wait(v, "dx")
            if b == 0:
                C.wait(v, "dw", dw_lnw)
            _ln_stats(C, v, bnb, xblk[b % 2][:], H, smalls[:, 8:9], smalls[:, 9:10],
                      smalls, 12, want_meansq=True)
            C.inc(v.tensor_copy(smalls[:, 18:19], smalls[:, 9:10]), "vE")
        sec("vector", vf)
        v_bn = C.cnt["vE"]

        def sf(s, b=b):
            C.wait(s, "vE", v_bn)
            C.inc(_rstd_finish(s, smalls[:, 18:19], smalls[:, 10:11],
                               smalls[:, 11:12]), "sE")
        sec("scalar", sf)
        s_rs = C.cnt["sE"]

        def vg(v, b=b):
            C.wait(v, "sE", s_rs)
            if b >= 1:
                C.wait(v, "tE", t_done[b - 1])
            C.inc(v.scalar_tensor_tensor(xhat_tok[:], xblk[b % 2][:],
                                         smalls[:, 10:11], lnw_bc[:],
                                         OP.mult, OP.mult), "vE")
        sec("vector", vg)
        v_stt[b] = C.cnt["vE"]

        if b + 2 < NB:
            def df(s, b=b):
                C.wait(s, "vE", v_stt[b])
                dx_at[b + 2] = C.dma(s.dma_start(
                    xblk[b % 2][:],
                    x_in[tok0 + (b + 2) * 128:tok0 + (b + 3) * 128, :]), "dx")
            sec("sync", df)

        for jb in range(4):
            def tf(t, b=b, jb=jb):
                if jb == 0:
                    C.wait(t, "vE", v_stt[b])
                key = f"pb{jb % 2}"
                C.bank_guard(t, key)
                for i in range(8):
                    c = jb * 8 + i
                    mm = t.transpose(pb[jb % 2][:, i * 128:(i + 1) * 128],
                                     xhat_tok[:, c * 128:(c + 1) * 128], ident_bf[:])
                C.inc(mm, "tE")
            sec("tensor", tf)
            t_tr = C.cnt["tE"]

            def sg(v, b=b, jb=jb, t_tr=t_tr):
                v.wait_ge(C.sems["tE"], t_tr)
                dstap = xhatF[:, jb * 8 * TT:(jb * 8 + 8) * TT] \
                    .rearrange("p (c t) -> p c t", t=TT)[:, :, b * 128:(b + 1) * 128]
                C.inc(v.tensor_copy(
                    dstap, pb[jb % 2][:, 0:1024].rearrange("p (c t) -> p c t", t=128)),
                    "vE")
                C.bank_mark(f"pb{jb % 2}", "vE")
            sec("vector", sg)
        t_done[b] = C.cnt["tE"]


def _a1(nc, C, sec, tok0, w_in, xhatF, wbuf, mup_bc, ztmp, bst, xbct, accb,
        convwF, tailsave, bmcf, dtT, xstg, xsspill, ident_bf, pa, pb):
    n_slab = 18
    dw_at = {}
    mm_done = {}
    ev_done = {}
    tg_at = {}
    conv_tap = {}
    spill_at = {}
    s_silu_at = {}

    def pre(g):
        def f(gp, g=g):
            col0 = 4096 + g * 256
            w = 64 if g == 17 else 256
            if g >= 2:
                gp.wait_ge(C.sems["tE"], mm_done[g - 2])
            dw_at[g] = C.dma(gp.dma_start(
                wbuf[g % 2][:, 0:32 * w].rearrange("p (c f) -> p c f", f=w),
                w_in.rearrange("(c p) f -> p c f", p=128)[:, :, col0:col0 + w]),
                "dwa" if g % 2 == 0 else "dwb")
        sec("gpsimd", f)

    pre(0)
    pre(1)

    for g in range(n_slab):
        col0 = 4096 + g * 256
        w = 64 if g == 17 else 256

        if g < 16:
            def gtail(gp, g=g):
                if 2 * g - 8 >= 0:
                    gp.wait_ge(C.sems["vE"], conv_tap[2 * g + 1 - 8])
                cc0 = 2 * g
                gp.tensor_copy(xbct[cc0 % 8][:, 0:3],
                               tailsave[:, cc0 * 3:cc0 * 3 + 3])
                C.inc(gp.tensor_copy(xbct[(cc0 + 1) % 8][:, 0:3],
                                     tailsave[:, (cc0 + 1) * 3:(cc0 + 1) * 3 + 3]),
                      "gE")
            sec("gpsimd", gtail)
        g_tail_in = C.cnt["gE"]

        for b in range(NB):
            idx = g * NB + b

            def tf(t, g=g, b=b, w=w, idx=idx):
                if b == 0:
                    C.wait(t, "dwa" if g % 2 == 0 else "dwb")
                C.bank_guard(t, f"pa{idx % 3}")
                for c in range(32):
                    mm = t.matmul(pa[idx % 3][:, 0:w],
                                  xhatF[:, c * TT + b * 128:c * TT + (b + 1) * 128],
                                  wbuf[g % 2][:, c * w:(c + 1) * w],
                                  start=(c == 0), stop=(c == 31))
                C.inc(mm, "tE")
                mm_done[g] = C.cnt["tE"]
            sec("tensor", tf)
            t_mm = C.cnt["tE"]

            def vf(v, g=g, b=b, w=w, col0=col0, idx=idx, t_mm=t_mm):
                C.wait(v, "tE", t_mm)
                if g < 17 and idx - 1 in tg_at:
                    C.wait(v, "tE", tg_at[idx - 1])
                if g == 17:
                    C.inc(v.tensor_tensor(dtT[:, b * 64:(b + 1) * 64],
                                          pa[idx % 3][:, 0:64],
                                          mup_bc[:, col0:col0 + 64], OP.mult), "vE")
                else:
                    C.inc(v.tensor_tensor(bst[:, 0:w], pa[idx % 3][:, 0:w],
                                          mup_bc[:, col0:col0 + w], OP.mult), "vE")
                C.bank_mark(f"pa{idx % 3}", "vE")
                ev_done[idx] = C.cnt["vE"]
            sec("vector", vf)
            v_ev = C.cnt["vE"]

            if g < 17:
                def tg(t, g=g, b=b, idx=idx, v_ev=v_ev):
                    C.wait(t, "vE", v_ev)
                    key = f"pb{idx % 2}"
                    C.bank_guard(t, key)
                    for i in range(2):
                        mm = t.transpose(pb[idx % 2][:, i * 128:(i + 1) * 128],
                                         bst[:, i * 128:(i + 1) * 128], ident_bf[:])
                    C.inc(mm, "tE")
                    tg_at[idx] = C.cnt["tE"]
                sec("tensor", tg)
                t_tr = C.cnt["tE"]

                def sg(v, g=g, b=b, idx=idx, t_tr=t_tr):
                    v.wait_ge(C.sems["tE"], t_tr)
                    last = None
                    for i in range(2):
                        cc = 2 * g + i
                        if cc < 32:
                            dst = xbct[cc % 8][:, 3 + b * 128:3 + (b + 1) * 128]
                        else:
                            dst = bmcf[:, (cc - 32) * TT + b * 128:
                                       (cc - 32) * TT + (b + 1) * 128]
                        last = v.tensor_copy(dst, pb[idx % 2][:, i * 128:(i + 1) * 128])
                    C.inc(last, "vE")
                    C.bank_mark(f"pb{idx % 2}", "vE")
                sec("vector", sg)

        if g + 2 < n_slab:
            pre(g + 2)

        if g < 16:
            for i in range(2):
                cc = 2 * g + i
                _conv_chunk(nc, C, sec, tok0, cc, xbct[cc % 8], accb, convwF,
                            tailsave, xstg, xsspill, bmcf, conv_tap, spill_at)
        elif g == 16:
            for i in range(2):
                cc = 32 + i
                _conv_chunk(nc, C, sec, tok0, cc, None, accb, convwF,
                            tailsave, xstg, xsspill, bmcf, conv_tap, spill_at)


def _conv_chunk(nc, C, sec, tok0, cc, buf, accb, convwF, tailsave, xstg,
                xsspill, bmcf, conv_tap, spill_at):
    if cc >= 32:
        data = bmcf[:, (cc - 32) * TT:(cc - 32 + 1) * TT]
    else:
        data = buf[:, 3:3 + TT]
    sE_now = C.cnt["sE"]
    vE_now = C.cnt["vE"]
    gE_now = C.cnt["gE"]

    def vf(v, cc=cc, data=data, sE_now=sE_now, vE_now=vE_now, gE_now=gE_now):
        C.wait(v, "sE", sE_now)
        C.wait(v, "vE", vE_now)
        C.wait(v, "gE", gE_now)
        w = lambda k: convwF[:, cc * 4 + k:cc * 4 + k + 1]
        if cc < 32:
            ext = buf
            v.tensor_scalar(accb[:], ext[:, 0:TT], w(0), None, OP.mult)
            for k in range(1, 3):
                v.scalar_tensor_tensor(accb[:], ext[:, k:k + TT], w(k), accb[:],
                                       OP.mult, OP.add)
            last = v.scalar_tensor_tensor(accb[:], ext[:, 3:3 + TT], w(3), accb[:],
                                          OP.mult, OP.add)
        else:
            ts3 = tailsave[:, cc * 3:cc * 3 + 3]
            v.tensor_scalar(accb[:], data[:], w(3), None, OP.mult)
            for k in range(3):
                sh = 3 - k
                v.scalar_tensor_tensor(accb[:, sh:TT], data[:, 0:TT - sh], w(k),
                                       accb[:, sh:TT], OP.mult, OP.add)
                last = v.scalar_tensor_tensor(accb[:, 0:sh], ts3[:, 3 - sh:3], w(k),
                                              accb[:, 0:sh], OP.mult, OP.add)
        C.inc(last, "vE")
        conv_tap[cc] = C.cnt["vE"]
    sec("vector", vf)
    v_tap = C.cnt["vE"]

    def gp2(g, cc=cc, data=data, v_tap=v_tap):
        C.wait(g, "vE", v_tap)
        C.inc(g.tensor_copy(tailsave[:, cc * 3:cc * 3 + 3], data[:, TT - 3:TT]), "gE")
    sec("gpsimd", gp2)
    g_tail = C.cnt["gE"]

    def sf(s, cc=cc, data=data, v_tap=v_tap, g_tail=g_tail):
        C.wait(s, "vE", v_tap)
        C.wait(s, "gE", g_tail)
        if cc < 32:
            if cc >= 2:
                C.wait(s, "doa" if cc % 2 == 0 else "dob")
            dst = xstg[cc % 2][:]
        else:
            dst = data
        C.inc(s.activation(dst, accb[:], AF.Silu), "sE")
    sec("scalar", sf)
    s_silu = C.cnt["sE"]

    if cc < 32:
        def df(sy, cc=cc, s_silu=s_silu):
            C.wait(sy, "sE", s_silu)
            spill_at[cc] = C.dma(sy.dma_start(
                xsspill[cc * 128:(cc + 1) * 128, tok0:tok0 + TT], xstg[cc % 2][:]),
                "doa" if cc % 2 == 0 else "dob")
        sec("sync", df)


def _a2(nc, C, sec, tok0, w_in, xhatF, wbuf, mup_bc, ztmp, zstg, zspill, pa):
    dw_at = {}
    mm_done = {}
    do_at = {}
    sz_at = {}

    def pre(g):
        def f(gp, g=g):
            col0 = g * 256
            if g >= 2:
                gp.wait_ge(C.sems["tE"], mm_done[g - 2])
            dw_at[g] = C.dma(gp.dma_start(
                wbuf[g % 2][:, 0:32 * 256].rearrange("p (c f) -> p c f", f=256),
                w_in.rearrange("(c p) f -> p c f", p=128)[:, :, col0:col0 + 256]),
                "dwa" if g % 2 == 0 else "dwb")
        sec("gpsimd", f)

    pre(0)
    pre(1)
    for g in range(16):
        col0 = g * 256
        for b in range(NB):
            idx = g * NB + b

            def tf(t, g=g, b=b, idx=idx):
                if b == 0:
                    C.wait(t, "dwa" if g % 2 == 0 else "dwb")
                C.bank_guard(t, f"pa{idx % 3}")
                for c in range(32):
                    mm = t.matmul(pa[idx % 3][:, 0:256],
                                  xhatF[:, c * TT + b * 128:c * TT + (b + 1) * 128],
                                  wbuf[g % 2][:, c * 256:(c + 1) * 256],
                                  start=(c == 0), stop=(c == 31))
                C.inc(mm, "tE")
                mm_done[g] = C.cnt["tE"]
            sec("tensor", tf)
            t_mm = C.cnt["tE"]

            def vf(v, col0=col0, idx=idx, t_mm=t_mm):
                C.wait(v, "tE", t_mm)
                if idx - 1 in sz_at:
                    C.wait(v, "sE", sz_at[idx - 1])
                C.inc(v.tensor_tensor(ztmp[:, 0:256], pa[idx % 3][:, 0:256],
                                      mup_bc[:, col0:col0 + 256], OP.mult), "vE")
                C.bank_mark(f"pa{idx % 3}", "vE")
            sec("vector", vf)
            v_ev = C.cnt["vE"]

            def sf(s, idx=idx, v_ev=v_ev):
                C.wait(s, "vE", v_ev)
                if idx >= 2:
                    C.wait(s, "doa" if idx % 2 == 0 else "dob")
                C.inc(s.activation(zstg[idx % 2][:, 0:256], ztmp[:, 0:256],
                                   AF.Silu), "sE")
                sz_at[idx] = C.cnt["sE"]
            sec("scalar", sf)
            s_z = C.cnt["sE"]

            def df(sy, b=b, col0=col0, idx=idx, s_z=s_z):
                C.wait(sy, "sE", s_z)
                do_at[idx] = C.dma(sy.dma_start(
                    zspill[tok0 + b * 128:tok0 + (b + 1) * 128, col0:col0 + 256],
                    zstg[idx % 2][:, 0:256]), "doa" if idx % 2 == 0 else "dob")
            sec("sync", df)
        if g + 2 < 16:
            pre(g + 2)


def _scan_chunk(nc, C, sec, tok0, ch, sc, wbuf, xsspill, zspill, dtT,
                dbias_bc, nega_bc, dcol_bc, h_bf, bmcf, nw_bc, ysb, yhatF,
                ones_f16, ones_f32, tri_f16, su_f16, tri_f32, ident_bf,
                e2_f16, smalls, sm64, bnb, pa, pb):
    t0c = tok0 + ch * 128
    xrd = wbuf[0][:, (ch % 2) * 4096:(ch % 2) * 4096 + 4096]
    zrd = wbuf[1][:, (ch % 2) * 4096:(ch % 2) * 4096 + 4096]

    def d0(sy):
        C.dma(sy.dma_start(
            xrd.rearrange("p (c t) -> p c t", t=128),
            xsspill.rearrange("(c p) t -> p c t", p=128)[:, :, t0c:t0c + 128]), "dx")
        C.dma(sy.dma_start(zrd, zspill[t0c:t0c + 128, :]), "dx")
    sec("sync", d0)

    for jb in range(4):
        def tf(t, jb=jb):
            if jb == 0:
                C.wait(t, "dx")
            key = f"pb{jb % 2}"
            C.bank_guard(t, key)
            for i in range(8):
                c = jb * 8 + i
                mm = t.transpose(pb[jb % 2][:, i * 128:(i + 1) * 128],
                                 xrd[:, c * 128:(c + 1) * 128], ident_bf[:])
            C.inc(mm, "tE")
        sec("tensor", tf)
        t_tr = C.cnt["tE"]

        def sg(s, jb=jb, t_tr=t_tr):
            s.wait_ge(C.sems["tE"], t_tr)
            C.inc(s.activation(sc["xsT"][:, jb * 1024:(jb + 1) * 1024],
                               pb[jb % 2][:, 0:1024], AF.Copy), "sE")
            C.bank_mark(f"pb{jb % 2}", "sE")
        sec("scalar", sg)
    s_xst = C.cnt["sE"]

    def v_dt(v):
        dts = dtT[:, ch * 64:(ch + 1) * 64]
        v.tensor_tensor(sc["t64a"][:], dts, dbias_bc[:], OP.add)
        C.inc(v.scalar_tensor_tensor(sc["t64b"][:], sc["t64a"][:], -1.0,
                                     sc["t64a"][:], OP.mult, OP.min), "vE")
    sec("vector", v_dt)
    v_1 = C.cnt["vE"]

    def s_dt(s, v_1=v_1):
        C.wait(s, "vE", v_1)
        s.activation(sc["t64c"][:], sc["t64b"][:], AF.Exp)
        C.inc(s.activation(sc["t64b"][:], sc["t64c"][:], AF.Ln, bias=1.0), "sE")
    sec("scalar", s_dt)
    s_1 = C.cnt["sE"]

    def v_dt2(v, s_1=s_1):
        C.wait(v, "sE", s_1)
        v.tensor_scalar(sc["t64c"][:], sc["t64a"][:], 0.0, None, OP.max)
        v.tensor_tensor(sc["dtc"][:], sc["t64b"][:], sc["t64c"][:], OP.add)
        v.tensor_tensor(sc["dta32"][:], sc["dtc"][:], nega_bc[:], OP.mult)
        C.inc(v.tensor_copy(sc["dta16"][:], sc["dta32"][:]), "vE")
    sec("vector", v_dt2)
    v_2 = C.cnt["vE"]

    def t_cum(t, v_2=v_2):
        C.wait(t, "vE", v_2)
        C.bank_guard(t, "pa5")
        t.matmul(pa[5][0:64, 0:128], sc["dta16"][:], tri_f16[:], start=True,
                 stop=True)
        t.matmul(pa[5][:, 128:192], su_f16[:], sc["dta16"][:], start=True,
                 stop=True)
        C.inc(t.matmul(pa[5][0:64, 192:193], sc["dta16"][:], ones_f16[:, 0:1],
                       start=True, stop=True), "tE")
    sec("tensor", t_cum)
    t_cu = C.cnt["tE"]

    def s_cum(s, t_cu=t_cu):
        C.wait(s, "tE", t_cu)
        s.activation(sc["vf16"][:], pa[5][0:64, 0:128], AF.Exp)
        s.activation(sc["t64a"][:], pa[5][:, 128:192], AF.Exp)
        C.inc(s.activation(sm64[:, 2:3], pa[5][0:64, 192:193], AF.Exp), "sE")
        C.bank_mark("pa5", "sE")
    sec("scalar", s_cum)
    s_2 = C.cnt["sE"]

    def v_w(v, s_2=s_2):
        C.wait(v, "sE", s_2)
        v.tensor_tensor(sc["wtok"][:], sc["t64a"][:], sc["dtc"][:], OP.mult)
        C.inc(v.tensor_scalar(sc["sf16"][:], ones_f32[0:64, 0:128], sm64[:, 2:3],
                              None, OP.mult), "vE")
    sec("vector", v_w)
    v_3 = C.cnt["vE"]

    def t_cb(t, s_xst=s_xst):
        C.wait(t, "sE", s_xst)
        C.bank_guard(t, "pa5")
        t.matmul(pa[5][:, 256:384], bmcf[:, ch * 128:(ch + 1) * 128],
                 bmcf[:, TT + ch * 128:TT + (ch + 1) * 128], start=True, stop=True)
        C.bank_guard(t, "pb0")
        C.inc(t.transpose(pb[0][:, 0:128], bmcf[:, ch * 128:(ch + 1) * 128],
                          ident_bf[:]), "tE")
    sec("tensor", t_cb)
    t_cb_at = C.cnt["tE"]

    def v_cb(v, t_cb_at=t_cb_at):
        C.wait(v, "tE", t_cb_at)
        v.tensor_tensor(sc["cbtm"][:], pa[5][:, 256:384], tri_f32[:], OP.mult)
        C.bank_mark("pa5", "vE")
        C.inc(v.tensor_copy(sc["bmt"][:], pb[0][:, 0:128]), "vE")
        C.bank_mark("pb0", "vE")
    sec("vector", v_cb)
    v_cb_at = C.cnt["vE"]

    dmm_at = {}
    exp_at = {}
    expd_free = {}
    intra_at = {}
    dxs_at = {}
    st_at = {}
    for g in range(8):
        heads = list(range(8 * g, 8 * g + 8))

        for half in range(2):
            hh = 2 * g + half
            hlist = heads[half * 4:half * 4 + 4]

            def v_r(v, hh=hh, hlist=hlist):
                if hh >= 2:
                    v.wait_ge(C.sems["tE"], dmm_at[hh - 2])
                last = None
                for i, r in enumerate(hlist):
                    last = v.tensor_scalar(
                        sc["rhsd"][hh % 2][:, i * 128:(i + 1) * 128],
                        tri_f16[:], sc["dta32"][:, r:r + 1], None, OP.mult)
                C.inc(last, "vE")
            sec("vector", v_r)
            v_rh = C.cnt["vE"]

            def t_d(t, hh=hh, v_rh=v_rh):
                C.wait(t, "vE", v_rh)
                key = f"pa{3 + hh % 2}"
                C.bank_guard(t, key)
                for i in range(4):
                    mm = t.matmul(pa[3 + hh % 2][:, i * 128:(i + 1) * 128],
                                  su_f16[:],
                                  sc["rhsd"][hh % 2][:, i * 128:(i + 1) * 128],
                                  start=(i == 0), stop=(i == 3))
                C.inc(mm, "tE")
            sec("tensor", t_d)
            dmm_at[hh] = C.cnt["tE"]

            def s_e(s, hh=hh):
                C.wait(s, "tE", dmm_at[hh])
                if hh >= 2:
                    C.wait(s, "vE", expd_free[hh - 2])
                C.inc(s.activation(sc["expd"][hh % 2][:], pa[3 + hh % 2][:],
                                   AF.Exp), "sE")
                C.bank_mark(f"pa{3 + hh % 2}", "sE")
            sec("scalar", s_e)
            exp_at[hh] = C.cnt["sE"]

        first_y2 = True
        for i, r in enumerate(heads):
            hh = 2 * g + i // 4

            def v_m(v, r=r, i=i, hh=hh):
                C.wait(v, "sE", exp_at[hh])
                if r >= 2:
                    v.wait_ge(C.sems["tE"], intra_at[r - 2])
                C.inc(v.scalar_tensor_tensor(
                    sc["mtil"][r % 2][:],
                    sc["expd"][hh % 2][:, (i % 4) * 128:(i % 4 + 1) * 128],
                    sc["dtc"][:, r:r + 1], sc["cbtm"][:], OP.mult, OP.mult), "vE")
                if i % 4 == 3:
                    expd_free[hh] = C.cnt["vE"]
            sec("vector", v_m)
            v_mt = C.cnt["vE"]

            def t_i(t, r=r, i=i, first=first_y2, v_mt=v_mt):
                C.wait(t, "vE", v_mt)
                if first:
                    C.bank_guard(t, "pa1")
                C.inc(t.matmul(pa[1][:, i * 64:(i + 1) * 64], sc["mtil"][r % 2][:],
                               sc["xsT"][:, r * 64:(r + 1) * 64],
                               start=first, stop=False), "tE")
                intra_at[r] = C.cnt["tE"]
            sec("tensor", t_i)
            first_y2 = False

        def t_inter(t, g=g):
            C.bank_guard(t, "pa0")
            last = None
            for i, r in enumerate(heads):
                last = t.matmul(pa[0][:, i * 64:(i + 1) * 64],
                                bmcf[:, TT + ch * 128:TT + (ch + 1) * 128],
                                h_bf[:, r * 64:(r + 1) * 64],
                                start=(i == 0), stop=(i == 7))
            C.inc(last, "tE")
        sec("tensor", t_inter)
        t_int = C.cnt["tE"]

        def t_v(t, g=g, v_3=v_3):
            C.wait(t, "vE", v_3)
            C.bank_guard(t, "pa5")
            C.inc(t.matmul(pa[5][:, 0:512], sc["vf16"][:],
                           e2_f16[:, g * 512:(g + 1) * 512], start=True,
                           stop=True), "tE")
        sec("tensor", t_v)
        t_ve = C.cnt["tE"]

        def v_grp(v, g=g, t_ve=t_ve, t_int=t_int, s_xst=s_xst):
            C.wait(v, "sE", s_xst)
            if g >= 1:
                v.wait_ge(C.sems["tE"], dxs_at[g - 1])
            for i, r in enumerate(range(8 * g, 8 * g + 8)):
                v.tensor_scalar(sc["dxst"][:, i * 64:(i + 1) * 64],
                                sc["xsT"][:, r * 64:(r + 1) * 64],
                                dcol_bc[:, r:r + 1], None, OP.mult)
            v.wait_ge(C.sems["tE"], t_ve)
            v.tensor_copy(sc["vexp"][:], pa[5][:, 0:512])
            C.bank_mark("pa5", "vE")
            v.wait_ge(C.sems["tE"], t_int)
            C.inc(v.scalar_tensor_tensor(sc["ytmp"][:], pa[0][:, 0:512], 1.0,
                                         sc["vexp"][:], OP.mult, OP.mult), "vE")
            C.bank_mark("pa0", "vE")
        sec("vector", v_grp)
        v_yt = C.cnt["vE"]

        def t_inj(t, g=g, v_yt=v_yt):
            C.wait(t, "vE", v_yt)
            t.matmul(pa[1][:, 0:512], ident_bf[:], sc["ytmp"][:], start=False,
                     stop=False)
            C.inc(t.matmul(pa[1][:, 0:512], ident_bf[:], sc["dxst"][:],
                           start=False, stop=True), "tE")
            dxs_at[g] = C.cnt["tE"]
        sec("tensor", t_inj)
        t_y2 = C.cnt["tE"]

        def v_yx(v, g=g, t_y2=t_y2, v_3=v_3, s_xst=s_xst):
            C.wait(v, "tE", t_y2)
            v.tensor_copy(ysb[:, g * 512:(g + 1) * 512], pa[1][:, 0:512])
            C.bank_mark("pa1", "vE")
            if g >= 1:
                v.wait_ge(C.sems["tE"], st_at[g - 1])
            last = None
            for i, r in enumerate(range(8 * g, 8 * g + 8)):
                last = v.tensor_scalar(sc["xtld"][:, i * 64:(i + 1) * 64],
                                       sc["xsT"][:, r * 64:(r + 1) * 64],
                                       sc["wtok"][:, r:r + 1], None, OP.mult)
            C.inc(last, "vE")
        sec("vector", v_yx)
        v_xt = C.cnt["vE"]

        def t_st(t, g=g, v_xt=v_xt):
            C.wait(t, "vE", v_xt)
            C.bank_guard(t, "pa2")
            for i in range(8):
                t.matmul(pa[2][:, i * 64:(i + 1) * 64], sc["bmt"][:],
                         sc["xtld"][:, i * 64:(i + 1) * 64],
                         start=(i == 0), stop=False)
            C.bank_guard(t, "pa5")
            C.inc(t.matmul(pa[5][:, 0:512], sc["sf16"][:],
                           e2_f16[:, g * 512:(g + 1) * 512], start=True,
                           stop=True), "tE")
        sec("tensor", t_st)
        t_sd = C.cnt["tE"]

        def v_d(v, g=g, t_sd=t_sd):
            C.wait(v, "tE", t_sd)
            C.inc(v.tensor_copy(sc["decay_sb"][:], pa[5][:, 0:512]), "vE")
            C.bank_mark("pa5", "vE")
        sec("vector", v_d)
        v_dec = C.cnt["vE"]

        def g_hd(gp, g=g, v_dec=v_dec):
            gp.wait_ge(C.sems["vE"], v_dec)
            C.inc(gp.tensor_tensor(sc["hdec"][:], h_bf[:, g * 512:(g + 1) * 512],
                                   sc["decay_sb"][:], OP.mult), "gE")
        sec("gpsimd", g_hd)
        g_hd_at = C.cnt["gE"]

        def t_hinj(t, g=g, g_hd_at=g_hd_at):
            C.wait(t, "gE", g_hd_at)
            C.inc(t.matmul(pa[2][:, 0:512], ident_bf[:], sc["hdec"][:],
                           start=False, stop=True), "tE")
            st_at[g] = C.cnt["tE"]
        sec("tensor", t_hinj)
        t_hu = C.cnt["tE"]

        def v_h(v, g=g, t_hu=t_hu):
            C.wait(v, "tE", t_hu)
            C.inc(v.tensor_copy(h_bf[:, g * 512:(g + 1) * 512], pa[2][:, 0:512]),
                  "vE")
            C.bank_mark("pa2", "vE")
        sec("vector", v_h)

    def v_tail(v):
        C.wait(v, "dx")
        v.tensor_tensor(ysb[:], ysb[:], zrd, OP.mult)
        _ln_stats(C, v, bnb, ysb[:], DS, smalls[:, 8:9], smalls[:, 9:10],
                  smalls, 12, want_meansq=True)
        C.inc(v.tensor_copy(smalls[:, 18:19], smalls[:, 9:10]), "vE")
    sec("vector", v_tail)
    v_t1 = C.cnt["vE"]

    def s_tail(s, v_t1=v_t1):
        C.wait(s, "vE", v_t1)
        C.inc(_rstd_finish(s, smalls[:, 18:19], smalls[:, 10:11],
                           smalls[:, 11:12]), "sE")
    sec("scalar", s_tail)
    s_t1 = C.cnt["sE"]

    def v_tail2(v, s_t1=s_t1):
        C.wait(v, "sE", s_t1)
        C.inc(v.scalar_tensor_tensor(sc["yhat_tok"][:], ysb[:], smalls[:, 10:11],
                                     nw_bc[:], OP.mult, OP.mult), "vE")
    sec("vector", v_tail2)
    v_t2 = C.cnt["vE"]

    for jb in range(4):
        def tf2(t, jb=jb, v_t2=v_t2):
            if jb == 0:
                C.wait(t, "vE", v_t2)
            key = f"pb{jb % 2}"
            C.bank_guard(t, key)
            for i in range(8):
                c = jb * 8 + i
                mm = t.transpose(pb[jb % 2][:, i * 128:(i + 1) * 128],
                                 sc["yhat_tok"][:, c * 128:(c + 1) * 128],
                                 ident_bf[:])
            C.inc(mm, "tE")
        sec("tensor", tf2)
        t_tr2 = C.cnt["tE"]

        def sg2(s, jb=jb, t_tr2=t_tr2):
            s.wait_ge(C.sems["tE"], t_tr2)
            dstap = yhatF[:, jb * 8 * TT:(jb * 8 + 8) * TT] \
                .rearrange("p (c t) -> p c t", t=TT)[:, :, ch * 128:(ch + 1) * 128]
            C.inc(s.activation(
                dstap, pb[jb % 2][:, 0:1024].rearrange("p (c t) -> p c t", t=128),
                AF.Copy), "sE")
            C.bank_mark(f"pb{jb % 2}", "sE")
        sec("scalar", sg2)


def _e_outproj(nc, C, sec, tok0, w_out, yhatF, wbuf, x1stg, x1spill, pa):
    dw_at = {}
    mm_done = {}
    do_at = {}

    def pre(g):
        def f(gp, g=g):
            col0 = g * 256
            if g >= 2:
                gp.wait_ge(C.sems["tE"], mm_done[g - 2])
            dw_at[g] = C.dma(gp.dma_start(
                wbuf[g % 2][:, 0:32 * 256].rearrange("p (c f) -> p c f", f=256),
                w_out.rearrange("(c p) f -> p c f", p=128)[:, :, col0:col0 + 256]),
                "dwa" if g % 2 == 0 else "dwb")
        sec("gpsimd", f)

    pre(0)
    pre(1)
    for g in range(16):
        col0 = g * 256
        for b in range(NB):
            idx = g * NB + b

            def tf(t, g=g, b=b, idx=idx):
                if b == 0:
                    C.wait(t, "dwa" if g % 2 == 0 else "dwb")
                C.bank_guard(t, f"pa{idx % 3}")
                for c in range(32):
                    mm = t.matmul(pa[idx % 3][:, 0:256],
                                  yhatF[:, c * TT + b * 128:c * TT + (b + 1) * 128],
                                  wbuf[g % 2][:, c * 256:(c + 1) * 256],
                                  start=(c == 0), stop=(c == 31))
                C.inc(mm, "tE")
                mm_done[g] = C.cnt["tE"]
            sec("tensor", tf)
            t_mm = C.cnt["tE"]

            def sf(s, idx=idx, t_mm=t_mm):
                C.wait(s, "tE", t_mm)
                if idx >= 2:
                    C.wait(s, "doa" if idx % 2 == 0 else "dob")
                C.inc(s.activation(x1stg[idx % 2][:, 0:256], pa[idx % 3][:, 0:256],
                                   AF.Copy), "sE")
                C.bank_mark(f"pa{idx % 3}", "sE")
            sec("scalar", sf)
            s_x1 = C.cnt["sE"]

            def df(sy, b=b, col0=col0, idx=idx, s_x1=s_x1):
                C.wait(sy, "sE", s_x1)
                do_at[idx] = C.dma(sy.dma_start(
                    x1spill[tok0 + b * 128:tok0 + (b + 1) * 128, col0:col0 + 256],
                    x1stg[idx % 2][:, 0:256]), "doa" if idx % 2 == 0 else "dob")
            sec("sync", df)
        if g + 2 < 16:
            pre(g + 2)


def _f_prep(nc, C, sec, ffn_a, ffn_b, ffn_wq, ffn_keys, wbuf, awT, bwT,
            wqT, keysT, ident_bf, pa, pb):
    def d0(gp):
        for h in range(2):
            C.dma(gp.dma_start(
                wbuf[h][:, 0:8192].rearrange("p (j d) -> p j d", d=4096),
                ffn_a.rearrange("(j p) d -> p j d", p=128)[:, 2 * h:2 * h + 2, :]),
                "dw")
    sec("gpsimd", d0)
    dw_a = C.cnt["dw"]

    for cb in range(16):
        def tf(t, cb=cb):
            if cb == 0:
                C.wait(t, "dw", dw_a)
            key = f"pb{cb % 2}"
            C.bank_guard(t, key)
            for cc in range(2):
                c = cb * 2 + cc
                for j in range(4):
                    mm = t.transpose(
                        pb[cb % 2][:, cc * 512 + j * 128:cc * 512 + (j + 1) * 128],
                        wbuf[j // 2][:, (j % 2) * 4096 + c * 128:
                                     (j % 2) * 4096 + (c + 1) * 128],
                        ident_bf[:])
            C.inc(mm, "tE")
        sec("tensor", tf)
        t_tr = C.cnt["tE"]

        def sg(s, cb=cb, t_tr=t_tr):
            s.wait_ge(C.sems["tE"], t_tr)
            C.inc(s.activation(awT[:, cb * 1024:(cb + 1) * 1024],
                               pb[cb % 2][:, 0:1024], AF.Copy), "sE")
            C.bank_mark(f"pb{cb % 2}", "sE")
        sec("scalar", sg)
    t_a_done = C.cnt["tE"]

    def d1(gp, t_a_done=t_a_done):
        gp.wait_ge(C.sems["tE"], t_a_done)
        for h in range(2):
            C.dma(gp.dma_start(
                wbuf[h][:, 0:8192].rearrange("p (j n) -> p j n", n=512),
                ffn_b.rearrange("(j p) n -> p j n", p=128)[:, 16 * h:16 * h + 16, :]),
                "dw")
    sec("gpsimd", d1)
    dw_b = C.cnt["dw"]

    for nch in range(4):
        for jb in range(4):
            def tf2(t, nch=nch, jb=jb):
                if nch == 0 and jb == 0:
                    C.wait(t, "dw", dw_b)
                key = f"pb{jb % 2}"
                C.bank_guard(t, key)
                for i in range(8):
                    j = jb * 8 + i
                    mm = t.transpose(pb[jb % 2][:, i * 128:(i + 1) * 128],
                                     wbuf[j // 16][:, (j % 16) * 512 + nch * 128:
                                                   (j % 16) * 512 + (nch + 1) * 128],
                                     ident_bf[:])
                C.inc(mm, "tE")
            sec("tensor", tf2)
            t_tr2 = C.cnt["tE"]

            def sg2(s, nch=nch, jb=jb, t_tr2=t_tr2):
                s.wait_ge(C.sems["tE"], t_tr2)
                C.inc(s.activation(
                    bwT[:, nch * 4096 + jb * 1024:nch * 4096 + (jb + 1) * 1024],
                    pb[jb % 2][:, 0:1024], AF.Copy), "sE")
                C.bank_mark(f"pb{jb % 2}", "sE")
            sec("scalar", sg2)
    t_b_done = C.cnt["tE"]

    def d2(gp, t_b_done=t_b_done):
        gp.wait_ge(C.sems["tE"], t_b_done)
        C.dma(gp.dma_start(wbuf[0][0:16, 0:4096], ffn_wq), "dw")
        C.dma(gp.dma_start(wbuf[0][0:64, 4096:4112], ffn_keys), "dw")
    sec("gpsimd", d2)
    dw_q = C.cnt["dw"]

    for jb in range(4):
        def tf3(t, jb=jb):
            if jb == 0:
                C.wait(t, "dw", dw_q)
            key = f"pb{jb % 2}"
            C.bank_guard(t, key)
            for i in range(8):
                c = jb * 8 + i
                mm = t.transpose(pb[jb % 2][:, i * 16:(i + 1) * 16],
                                 wbuf[0][0:16, c * 128:(c + 1) * 128],
                                 ident_bf[0:16, 0:16])
            C.inc(mm, "tE")
        sec("tensor", tf3)
        t_tr3 = C.cnt["tE"]

        def sg3(s, jb=jb, t_tr3=t_tr3):
            s.wait_ge(C.sems["tE"], t_tr3)
            C.inc(s.activation(wqT[:, jb * 128:(jb + 1) * 128],
                               pb[jb % 2][:, 0:128], AF.Copy), "sE")
            C.bank_mark(f"pb{jb % 2}", "sE")
        sec("scalar", sg3)

    def tf4(t):
        C.bank_guard(t, "pb0")
        C.inc(t.transpose(pb[0][0:16, 0:64], wbuf[0][0:64, 4096:4112],
                          ident_bf[0:64, 0:64]), "tE")
    sec("tensor", tf4)
    t_k = C.cnt["tE"]

    def sg4(s, t_k=t_k):
        s.wait_ge(C.sems["tE"], t_k)
        C.inc(s.activation(keysT[:], pb[0][0:16, 0:64], AF.Copy), "sE")
        C.bank_mark("pb0", "sE")
    sec("scalar", sg4)


def _f_block(nc, C, sec, tok0, b, x_in, x1spill, out_t, xblk2, ysb2, x3f,
             x2F, awT, bwT, wqT, keysT, e512_bf, fscr, fscr2, gq, gwork,
             gate_bf, gateT, zhat, zhT, qtok, qT, bnb, smalls, rcols,
             ident_bf, ident_f32, pa, pb, do_at):
    t0b = tok0 + b * 128

    def d0(sy, b=b):
        C.dma(sy.dma_start(xblk2[:], x_in[t0b:t0b + 128, :]), "dx")
        C.dma(sy.dma_start(x2F[:], x1spill[t0b:t0b + 128, :]), "dx")
    sec("sync", d0)
    dx_b = C.cnt["dx"]

    def v0(v, dx_b=dx_b):
        C.wait(v, "dx")
        v.tensor_tensor(ysb2[:], x2F[:, 0:H], xblk2[:], OP.add)
        _ln_stats(C, v, bnb, ysb2[:], H, smalls[:, 8:9], smalls[:, 9:10],
                  smalls, 12, want_meansq=False)
        C.inc(v.tensor_copy(smalls[:, 18:19], smalls[:, 9:10]), "vE")
    sec("vector", v0)
    v_ln = C.cnt["vE"]

    def s0(s, v_ln=v_ln):
        C.wait(s, "vE", v_ln)
        C.inc(_rstd_finish(s, smalls[:, 18:19], smalls[:, 10:11],
                           smalls[:, 11:12]), "sE")
    sec("scalar", s0)
    s_ln = C.cnt["sE"]

    def v1(v, s_ln=s_ln):
        C.wait(v, "sE", s_ln)
        C.inc(v.tensor_scalar(ysb2[:], ysb2[:], smalls[:, 8:9], smalls[:, 10:11],
                              OP.subtract, OP.mult), "vE")
    sec("vector", v1)
    v_x2 = C.cnt["vE"]

    for jb in range(8):
        def tf(t, jb=jb, v_x2=v_x2):
            if jb == 0:
                C.wait(t, "vE", v_x2)
            key = f"pa{4 + jb % 2}"
            C.bank_guard(t, key)
            for i in range(4):
                c = jb * 4 + i
                mm = t.transpose(pa[4 + jb % 2][:, i * 128:(i + 1) * 128],
                                 ysb2[:, c * 128:(c + 1) * 128], ident_f32[:])
            C.inc(mm, "tE")
        sec("tensor", tf)
        t_tr = C.cnt["tE"]

        def sg(s, jb=jb, t_tr=t_tr):
            s.wait_ge(C.sems["tE"], t_tr)
            C.inc(s.activation(x2F[:, jb * 512:(jb + 1) * 512],
                               pa[4 + jb % 2][:, 0:512], AF.Copy), "sE")
            C.bank_mark(f"pa{4 + jb % 2}", "sE")
        sec("scalar", sg)
    s_x2f = C.cnt["sE"]

    def t_a(t, s_x2f=s_x2f):
        C.wait(t, "sE", s_x2f)
        C.bank_guard(t, "pa0")
        C.bank_guard(t, "pa1")
        last = None
        for c in range(32):
            t.matmul(pa[0][:, 0:512], x2F[:, c * 128:(c + 1) * 128],
                     awT[:, c * 512:(c + 1) * 512], start=(c == 0), stop=(c == 31))
            last = t.matmul(pa[1][:, 0:16], x2F[:, c * 128:(c + 1) * 128],
                            wqT[:, c * 16:(c + 1) * 16], start=(c == 0),
                            stop=(c == 31))
        C.inc(last, "tE")
    sec("tensor", t_a)
    t_aq = C.cnt["tE"]

    def s_q(v, t_aq=t_aq):
        C.wait(v, "tE", t_aq)
        C.inc(v.tensor_copy(qtok[:], pa[1][:, 0:16]), "vE")
        C.bank_mark("pa1", "vE")
    sec("vector", s_q)
    s_qe = C.cnt["vE"]

    def t_q(t, s_qe=s_qe):
        C.wait(t, "vE", s_qe)
        C.bank_guard(t, "pb0")
        C.inc(t.transpose(pb[0][0:16, 0:128], qtok[:], ident_bf[:]), "tE")
    sec("tensor", t_q)
    t_qt = C.cnt["tE"]

    def s_q2(v, t_qt=t_qt):
        C.wait(v, "tE", t_qt)
        C.inc(v.tensor_copy(qT[:], pb[0][0:16, 0:128]), "vE")
        C.bank_mark("pb0", "vE")
    sec("vector", s_q2)
    s_qt = C.cnt["vE"]

    def t_sc(t, s_qt=s_qt):
        C.wait(t, "vE", s_qt)
        C.bank_guard(t, "pa2")
        C.inc(t.matmul(pa[2][:, 0:64], qT[:], keysT[:], start=True, stop=True),
              "tE")
    sec("tensor", t_sc)
    t_s = C.cnt["tE"]

    def v_r(v, t_s=t_s):
        C.wait(v, "tE", t_s)
        v.tensor_copy(gq[:], pa[2][:, 0:64])
        C.bank_mark("pa2", "vE")
        v.tensor_copy(gwork[:], gq[:])
        for it in range(15):
            v.reduce_max(rcols[0][:], gwork[:], axis=AX)
            if it == 0:
                v.tensor_copy(rcols[1][:], rcols[0][:])
            v.tensor_scalar(fscr2[:, 0:64], gwork[:], rcols[0][:], None, OP.is_ge)
            v.scalar_tensor_tensor(gwork[:], fscr2[:, 0:64], -1e30, gwork[:],
                                   OP.mult, OP.add)
        v.reduce_max(rcols[2][:], gwork[:], axis=AX)
        C.inc(v.tensor_scalar(rcols[1][:], rcols[1][:], -1.0, None, OP.mult), "vE")
    sec("vector", v_r)
    v_rt = C.cnt["vE"]

    def s_r(s, v_rt=v_rt):
        C.wait(s, "vE", v_rt)
        C.inc(s.activation(fscr[:, 0:64], gq[:], AF.Exp, bias=rcols[1][:]), "sE")
    sec("scalar", s_r)
    s_re = C.cnt["sE"]

    def v_g(v, s_re=s_re):
        C.wait(v, "sE", s_re)
        v.tensor_scalar(gwork[:], gq[:], rcols[2][:], None, OP.is_ge)
        v.tensor_tensor(gwork[:], gwork[:], fscr[:, 0:64], OP.mult)
        v.reduce_sum(rcols[3][:], gwork[:], axis=AX)
        v.reciprocal(rcols[0][:], rcols[3][:])
        C.inc(v.tensor_scalar(gate_bf[:], gwork[:], rcols[0][:], 2.0,
                              OP.mult, OP.mult), "vE")
    sec("vector", v_g)
    v_gt = C.cnt["vE"]

    def t_g(t, v_gt=v_gt):
        C.wait(t, "vE", v_gt)
        C.bank_guard(t, "pb1")
        C.inc(t.transpose(pb[1][0:64, 0:128], gate_bf[:], ident_bf[:]), "tE")
    sec("tensor", t_g)
    t_gt = C.cnt["tE"]

    def s_g(v, t_gt=t_gt):
        C.wait(v, "tE", t_gt)
        C.inc(v.tensor_copy(gateT[:], pb[1][0:64, 0:128]), "vE")
        C.bank_mark("pb1", "vE")
    sec("vector", s_g)
    s_gT = C.cnt["vE"]

    def t_ge(t, s_gT=s_gT):
        C.wait(t, "vE", s_gT)
        C.bank_guard(t, "pa2")
        C.inc(t.matmul(pa[2][:, 0:512], gateT[:], e512_bf[:], start=True,
                       stop=True), "tE")
    sec("tensor", t_ge)
    t_gx = C.cnt["tE"]

    def s_ge(v, t_gx=t_gx):
        C.wait(v, "tE", t_gx)
        C.inc(v.tensor_copy(fscr[:, 0:512], pa[2][:, 0:512]), "vE")
        C.bank_mark("pa2", "vE")
    sec("vector", s_ge)
    s_gx = C.cnt["vE"]

    def v_z(v, s_gx=s_gx):
        C.wait(v, "vE", s_gx)
        C.inc(v.scalar_tensor_tensor(zhat[:], pa[0][:, 0:512], 1.0,
                                     fscr[:, 0:512], OP.mult, OP.mult), "vE")
        C.bank_mark("pa0", "vE")
    sec("vector", v_z)
    v_zh = C.cnt["vE"]

    def t_zt(t, v_zh=v_zh):
        C.wait(t, "vE", v_zh)
        C.bank_guard(t, "pb0")
        last = None
        for i in range(4):
            last = t.transpose(pb[0][:, i * 128:(i + 1) * 128],
                               zhat[:, i * 128:(i + 1) * 128], ident_bf[:])
        C.inc(last, "tE")
    sec("tensor", t_zt)
    t_zT = C.cnt["tE"]

    def s_zt(v, t_zT=t_zT):
        C.wait(v, "tE", t_zT)
        C.inc(v.tensor_copy(zhT[:], pb[0][:, 0:512]), "vE")
        C.bank_mark("pb0", "vE")
    sec("vector", s_zt)
    s_zTT = C.cnt["vE"]

    for og in range(8):
        def t_b(t, og=og, s_zTT=s_zTT):
            if og == 0:
                C.wait(t, "vE", s_zTT)
            key = f"pa{3 + og % 2}"
            C.bank_guard(t, key)
            last = None
            for nch in range(4):
                last = t.matmul(pa[3 + og % 2][:, 0:512],
                                zhT[:, nch * 128:(nch + 1) * 128],
                                bwT[:, nch * 4096 + og * 512:
                                    nch * 4096 + (og + 1) * 512],
                                start=(nch == 0), stop=(nch == 3))
            C.inc(last, "tE")
        sec("tensor", t_b)
        t_bo = C.cnt["tE"]

        def v_o(v, og=og, t_bo=t_bo):
            C.wait(v, "tE", t_bo)
            C.inc(v.scalar_tensor_tensor(x3f[:, og * 512:(og + 1) * 512],
                                         pa[3 + og % 2][:, 0:512], 1.0,
                                         ysb2[:, og * 512:(og + 1) * 512],
                                         OP.mult, OP.add), "vE")
            C.bank_mark(f"pa{3 + og % 2}", "vE")
        sec("vector", v_o)

    def v_l2(v):
        _ln_stats(C, v, bnb, x3f[:], H, smalls[:, 8:9], smalls[:, 9:10],
                  smalls, 12, want_meansq=False)
        C.inc(v.tensor_copy(smalls[:, 18:19], smalls[:, 9:10]), "vE")
    sec("vector", v_l2)
    v_l2a = C.cnt["vE"]

    def s_l2(s, v_l2a=v_l2a):
        C.wait(s, "vE", v_l2a)
        C.inc(_rstd_finish(s, smalls[:, 18:19], smalls[:, 10:11],
                           smalls[:, 11:12]), "sE")
    sec("scalar", s_l2)
    s_l2a = C.cnt["sE"]

    def v_out(v, b=b, s_l2a=s_l2a):
        C.wait(v, "sE", s_l2a)
        C.inc(v.tensor_scalar(x3f[:], x3f[:], smalls[:, 8:9], smalls[:, 10:11],
                              OP.subtract, OP.mult), "vE")
    sec("vector", v_out)
    v_ot = C.cnt["vE"]

    def d_out(sy, b=b, v_ot=v_ot):
        C.wait(sy, "vE", v_ot)
        do_at[b] = C.dma(sy.dma_start(out_t[t0b:t0b + 128, :], x3f[:]), "do")
    sec("sync", d_out)


def kernel(**inputs):
    _ = inputs.get("L0_attn_residual")
    x = np.ascontiguousarray(inputs["L0_attn_hidden_states"], dtype=np.float32)
    nc = build()
    base = {
        "w_in": np.ascontiguousarray(inputs["L0_attn_in_proj_w"], np.float32),
        "conv_w": np.ascontiguousarray(inputs["L0_attn_conv_w"], np.float32),
        "a_log": np.ascontiguousarray(inputs["L0_attn_A_log"], np.float32).reshape(1, NH),
        "dt_bias": np.ascontiguousarray(inputs["L0_attn_dt_bias"], np.float32).reshape(1, NH),
        "d_vec": np.ascontiguousarray(inputs["L0_attn_D"], np.float32).reshape(1, NH),
        "ln_w": np.ascontiguousarray(inputs["L0_attn_ln_w"], np.float32).reshape(1, H),
        "norm_w": np.ascontiguousarray(inputs["L0_attn_norm_w"], np.float32).reshape(1, DS),
        "mup": np.ascontiguousarray(inputs["L0_attn_mup_vector"], np.float32).reshape(1, PROJ),
        "w_out": np.ascontiguousarray(inputs["L0_attn_out_proj_w"], np.float32),
        "ffn_a": np.ascontiguousarray(inputs["L0_ffn_A_w"], np.float32),
        "ffn_b": np.ascontiguousarray(inputs["L0_ffn_B_w"], np.float32),
        "ffn_wq": np.ascontiguousarray(inputs["L0_ffn_Wq_w"], np.float32),
        "ffn_keys": np.ascontiguousarray(inputs["L0_ffn_keys"], np.float32),
    }
    in_maps = [dict(base, x=np.ascontiguousarray(x[c])) for c in range(NCORES)]
    res = run_bass_kernel_spmd(nc, in_maps, core_ids=list(range(NCORES)))
    return np.stack([res.results[c]["out"] for c in range(NCORES)], axis=0)


# version 05
# speedup vs baseline: 1.2404x; 1.0032x over previous
import contextlib
import numpy as np
import concourse.bass as bass
import concourse.mybir as mybir
from concourse.bass_utils import run_bass_kernel_spmd
from concourse.alu_op_type import AluOpType as OP

F32 = mybir.dt.float32
F16 = mybir.dt.float16
BF16 = mybir.dt.bfloat16
AF = mybir.ActivationFunctionType
AX = mybir.AxisListType.X

L = 2048
H = 4096
DS = 4096
PROJ = 8512
CONVD = 4352
NH = 64
TT = 1024
NB = 8
NT = L // TT
EPS = 1e-5
NCORES = 8
EXT = TT + 3

ALL_ENG = ["sync", "gpsimd", "tensor", "vector", "scalar"]

_DRAIN_OPS = frozenset({
    "tensor_scalar", "tensor_tensor", "scalar_tensor_tensor", "tensor_copy",
    "tensor_tensor_scan", "reduce_max", "reduce_sum", "tensor_reduce",
    "bn_stats", "bn_aggr", "reciprocal", "memset", "activation", "select",
    "copy",
})


class _AutoDrain:

    def __init__(self, eng):
        self._e = eng
        self._w = set()
        self._r = set()

    def __getattr__(self, name):
        attr = getattr(self._e, name)
        if name in _DRAIN_OPS:
            def w(*a, **k):
                aps = [x for x in list(a) + list(k.values())
                       if isinstance(x, bass.AP)]
                names = [ap.tensor.name for ap in aps]
                out, ins = set(names[:1]), set(names[1:])
                if (ins & self._w) or (out & self._w) or (out & self._r):
                    self._e.drain()
                    self._w.clear()
                    self._r.clear()
                self._w |= out
                self._r |= ins
                return attr(*a, **k)
            return w
        return attr


def bcast_ap(dram_ap, npart, cols):
    return bass.AP(dram_ap.tensor, 0, [[0, npart], [1, cols]])


class Ctx:

    def __init__(self, nc, es):
        self.nc = nc
        self.sems = {}
        self.cnt = {}
        for name in ["dw", "dwa", "dwb", "dx", "do", "doa", "dob", "tE", "vE",
                     "sE", "gE"]:
            self.sems[name] = es.enter_context(nc.semaphore(name))
            self.cnt[name] = 0
        self.bank_rd = {}

    def inc(self, instr, name, n=1):
        instr.then_inc(self.sems[name], n)
        self.cnt[name] += n
        return self.cnt[name]

    def dma(self, instr, name):
        return self.inc(instr, name, 16)

    def wait(self, eng, name, val=None):
        v = self.cnt[name] if val is None else val
        if v:
            eng.wait_ge(self.sems[name], v)

    def bank_guard(self, eng, key):
        if key in self.bank_rd:
            sem, val = self.bank_rd[key]
            if val:
                eng.wait_ge(self.sems[sem], val)

    def bank_mark(self, key, sem):
        self.bank_rd[key] = (sem, self.cnt[sem])


def build(debug=()):
    nc = bass.Bass("TRN2", target_bir_lowering=False)
    debug = set(debug)

    x_in = nc.dram_tensor("x", [L, H], F32, kind="ExternalInput").ap()
    w_in = nc.dram_tensor("w_in", [H, PROJ], F32, kind="ExternalInput").ap()
    conv_w = nc.dram_tensor("conv_w", [CONVD, 4], F32, kind="ExternalInput").ap()
    a_log = nc.dram_tensor("a_log", [1, NH], F32, kind="ExternalInput").ap()
    dt_bias = nc.dram_tensor("dt_bias", [1, NH], F32, kind="ExternalInput").ap()
    d_vec = nc.dram_tensor("d_vec", [1, NH], F32, kind="ExternalInput").ap()
    ln_w = nc.dram_tensor("ln_w", [1, H], F32, kind="ExternalInput").ap()
    norm_w = nc.dram_tensor("norm_w", [1, DS], F32, kind="ExternalInput").ap()
    mup = nc.dram_tensor("mup", [1, PROJ], F32, kind="ExternalInput").ap()
    w_out = nc.dram_tensor("w_out", [DS, H], F32, kind="ExternalInput").ap()
    ffn_a = nc.dram_tensor("ffn_a", [512, H], F32, kind="ExternalInput").ap()
    ffn_b = nc.dram_tensor("ffn_b", [H, 512], F32, kind="ExternalInput").ap()
    ffn_wq = nc.dram_tensor("ffn_wq", [16, H], F32, kind="ExternalInput").ap()
    ffn_keys = nc.dram_tensor("ffn_keys", [NH, 16], F32, kind="ExternalInput").ap()

    out_t = nc.dram_tensor("out", [L, H], F32, kind="ExternalOutput").ap()

    def scratch(name, shape, dtype):
        if name in debug:
            return nc.dram_tensor(name, shape, dtype, kind="ExternalOutput").ap()
        return nc.dram_tensor(name, shape, dtype).ap()

    zspill = scratch("zspill", [L, DS], BF16)
    xsspill = scratch("xsspill", [DS, L], BF16)
    x1spill = scratch("x1spill", [L, H], BF16)

    es = contextlib.ExitStack()
    with es:
        C = Ctx(nc, es)
        block = es.enter_context(nc.Block())

        def sec(eng_name, fn):
            if eng_name in ("vector", "scalar"):
                getattr(block, eng_name)(lambda e: fn(_AutoDrain(e)))
            else:
                getattr(block, eng_name)(fn)

        def barrier():
            snap = dict(C.cnt)
            for e in ALL_ENG:
                def f(eng, snap=snap):
                    for name, v in snap.items():
                        if v:
                            eng.wait_ge(C.sems[name], v)
                sec(e, f)

        A = lambda n, sh, dt: es.enter_context(nc.sbuf_tensor(n, sh, dt))

        ones_f32 = A("ones_f32", [128, 128], F32)
        ones_f16 = A("ones_f16", [128, 128], F16)
        ident_bf = A("ident_bf", [128, 128], BF16)
        ident_f32 = A("ident_f32", [128, 128], F32)
        tri_f16 = A("tri_f16", [128, 128], F16)
        su_f16 = A("su_f16", [128, 128], F16)
        tri_f32 = A("tri_f32", [128, 128], F32)
        e2_f16 = A("e2_f16", [64, DS], F16)
        e512_bf = A("e512_bf", [64, 512], BF16)
        dbias_bc = A("dbias_bc", [128, NH], F32)
        nega_bc = A("nega_bc", [128, NH], F32)
        dcol_bc = A("dcol_bc", [128, NH], F32)
        convwF = A("convwF", [128, 34 * 4], F32)
        iota_f = A("iota_f", [128, 128], F32)
        iota_c = A("iota_c", [128, 1], F32)
        smalls = A("smalls", [128, 32], F32)
        sm64 = A("sm64", [64, 16], F32)
        dtT = A("dtT", [128, NB * NH], F32)
        h_bf = A("h_bf", [128, DS], BF16)
        tailsave = A("tailsave", [128, 34 * 3], BF16)
        bmcf = A("bmcf", [128, 2 * TT], BF16)
        wbuf = [A(f"wbuf{i}", [128, 8192], BF16) for i in range(2)]
        bnb = [A(f"bnb{i}", [128, 24], F32) for i in range(2)]

        pa = [es.enter_context(nc.psum_tensor(f"pa{i}", [128, 512], F32)) for i in range(4)]
        pdm = es.enter_context(nc.psum_tensor("pdm", [128, 1024], F32))
        pb = [es.enter_context(nc.psum_tensor(f"pb{i}", [128, 1024], BF16)) for i in range(2)]

        with contextlib.ExitStack() as sp:
            e2scr = sp.enter_context(nc.sbuf_tensor("e2scr", [64, DS], F32))

            def setup_sync(s):
                C.dma(s.dma_start(convwF[:].rearrange("p (c k) -> p c k", k=4),
                                  conv_w.rearrange("(c p) k -> p c k", p=128)), "dx")
                C.dma(s.dma_start(dbias_bc[:], bcast_ap(dt_bias, 128, NH)), "dx")
                C.dma(s.dma_start(nega_bc[:], bcast_ap(a_log, 128, NH)), "dx")
                C.dma(s.dma_start(dcol_bc[:], bcast_ap(d_vec, 128, NH)), "dx")
            sec("sync", setup_sync)

            def setup_gp(g):
                g.iota(iota_f[:], [[1, 128]], base=0, channel_multiplier=0,
                       allow_small_or_imprecise_dtypes=True)
                g.iota(iota_c[:], [[0, 1]], base=0, channel_multiplier=1,
                       allow_small_or_imprecise_dtypes=True)
                C.inc(g.iota(e2scr[:, :].rearrange("p (a b) -> p a b", b=64),
                             [[1, 64], [0, 64]], base=0, channel_multiplier=0,
                             allow_small_or_imprecise_dtypes=True), "gE")
            sec("gpsimd", setup_gp)

            def setup_v(v):
                C.wait(v, "gE")
                C.wait(v, "dx")
                v.memset(ones_f32[:], 1.0)
                v.memset(ones_f16[:], 1.0)
                v.memset(h_bf[:], 0.0)
                v.memset(tailsave[:], 0.0)
                v.tensor_scalar(ident_f32[:], iota_f[:], iota_c[:], None, OP.is_equal)
                v.tensor_copy(ident_bf[:], ident_f32[:])
                v.tensor_scalar(tri_f32[:], iota_f[:], iota_c[:], None, OP.is_ge)
                v.tensor_copy(tri_f16[:], tri_f32[:])
                v.tensor_scalar(su_f16[:], iota_f[:], iota_c[:], None, OP.is_lt)
                v.tensor_scalar(e2scr[:, :], e2scr[:, :], iota_c[0:64, :],
                                None, OP.is_equal)
                C.inc(v.tensor_copy(e2_f16[:], e2scr[:, :]), "vE")
            sec("vector", setup_v)
            v_e2 = C.cnt["vE"]

            def setup_gp2(g):
                C.wait(g, "vE", v_e2)
                C.inc(g.iota(e2scr[:, 0:512].rearrange("p (a b) -> p a b", b=8),
                             [[1, 64], [0, 8]], base=0, channel_multiplier=0,
                             allow_small_or_imprecise_dtypes=True), "gE")
            sec("gpsimd", setup_gp2)

            def setup_v2(v):
                C.wait(v, "gE")
                v.tensor_scalar(e2scr[:, 0:512], e2scr[:, 0:512], iota_c[0:64, :],
                                None, OP.is_equal)
                C.inc(v.tensor_copy(e512_bf[:], e2scr[:, 0:512]), "vE")
            sec("vector", setup_v2)
            v_e5 = C.cnt["vE"]

            def setup_s(s):
                C.wait(s, "vE", v_e5)
                C.inc(s.activation(nega_bc[:], nega_bc[:], AF.Exp), "sE")
            sec("scalar", setup_s)

            def setup_v3(v):
                C.wait(v, "sE")
                C.inc(v.tensor_scalar(nega_bc[:], nega_bc[:], -1.0, None, OP.mult),
                      "vE")
            sec("vector", setup_v3)

            barrier()

        for tile in range(NT):
            tok0 = tile * TT

            with contextlib.ExitStack() as ph:
                Aph = lambda n, sh, dt: ph.enter_context(
                    nc.sbuf_tensor(f"{n}_t{tile}", sh, dt))
                xhatF = Aph("xhatF", [128, 32 * TT], BF16)

                with contextlib.ExitStack() as p0s:
                    A0 = lambda n, sh, dt: p0s.enter_context(
                        nc.sbuf_tensor(f"{n}_t{tile}", sh, dt))
                    lnw_bc = A0("lnw_bc", [128, H], F16)
                    xblk = [A0(f"xblk{i}", [128, H], F32) for i in range(2)]
                    xhat_tok = A0("xhat_tok", [128, H], BF16)
                    with nc.named_scope(f"P0_t{tile}"):
                        _p0(nc, C, sec, tok0, x_in, ln_w, lnw_bc, xblk, xhat_tok,
                            xhatF, bnb, smalls, ident_bf, pa, pb)
                        barrier()

                with contextlib.ExitStack() as a1s:
                    A1 = lambda n, sh, dt: a1s.enter_context(
                        nc.sbuf_tensor(f"{n}_t{tile}", sh, dt))
                    mup_bc = A1("mup_bc", [128, PROJ], F16)
                    xbct = [A1(f"xbct{i}", [128, EXT], BF16) for i in range(8)]
                    ztmp = A1("ztmp", [128, 512], F32)
                    bst = A1("bst", [128, 512], BF16)
                    accb = A1("accb", [128, TT], F32)
                    zstg = [A1(f"zstg{i}", [128, 512], BF16) for i in range(2)]
                    xstg = [A1(f"xstg{i}", [128, TT], BF16) for i in range(2)]

                    def mup_load(g):
                        C.dma(g.dma_start(mup_bc[:], bcast_ap(mup, 128, PROJ)), "dw")
                    sec("gpsimd", mup_load)

                    with nc.named_scope(f"A1_t{tile}"):
                        _a1(nc, C, sec, tok0, w_in, xhatF, wbuf, mup_bc, ztmp, bst,
                            xbct, accb, convwF, tailsave, bmcf, dtT, xstg, xsspill,
                            ident_bf, pa, pb)
                        barrier()
                    with nc.named_scope(f"A2_t{tile}"):
                        _a2(nc, C, sec, tok0, w_in, xhatF, wbuf, mup_bc, ztmp,
                            zstg, zspill, pa)
                        barrier()

            with contextlib.ExitStack() as ph:
                Aph = lambda n, sh, dt: ph.enter_context(
                    nc.sbuf_tensor(f"{n}_t{tile}", sh, dt))
                yhatF = Aph("yhatF", [128, 32 * TT], BF16)
                nw_bc = Aph("nw_bc", [128, DS], F16)
                ysb = Aph("ysb", [128, DS], F32)
                sc = {n: Aph(n, sh, dt) for n, sh, dt in [
                    ("xsT", [128, DS], BF16),
                    ("yhat_tok", [128, DS], BF16),
                    ("vexp", [128, 512], F32),
                    ("ytmp", [128, 512], BF16),
                    ("dxst", [128, 512], BF16),
                    ("xtld", [128, 512], BF16),
                    ("hdec", [128, 512], BF16),
                    ("decay_sb", [128, 512], F32),
                    ("cbtm", [128, 128], F32),
                    ("bmt", [128, 128], BF16),
                    ("dtc", [128, NH], F32),
                    ("dta32", [128, NH], F32),
                    ("dta16", [128, NH], F16),
                    ("wtok", [128, NH], F32),
                    ("vf16", [64, 128], F16),
                    ("srow", [1, 64], F32),
                    ("sbc", [128, 64], F32),
                    ("t64a", [128, NH], F32),
                    ("t64b", [128, NH], F32),
                    ("t64c", [128, NH], F32),
                ]}
                sc["expd"] = [Aph(f"expd{i}", [128, 1024], F32) for i in range(2)]
                sc["rhsd"] = [Aph(f"rhsd{i}", [128, 512], F16) for i in range(2)]
                sc["mtil"] = [Aph(f"mtil{i}", [128, 128], BF16) for i in range(2)]
                x1stg = [Aph(f"x1stg{i}", [128, 512], BF16) for i in range(2)]

                def nw_load(g):
                    C.dma(g.dma_start(nw_bc[:], bcast_ap(norm_w, 128, DS)), "dw")
                sec("gpsimd", nw_load)

                with nc.named_scope(f"SCAN_t{tile}"):
                    X = {"xrd_free": {0: 0, 1: 0}, "zrd_free": {0: 0, 1: 0},
                         "st_last": 0, "scum": 0, "v3": 0, "tv_last": 0,
                         "vt2": 0, "ytr": 0, "base_g": 0, "base_r": 0,
                         "dmm": {}, "expfree": {}, "intra": {}, "dxs": {},
                         "st": {}, "hev": {}}
                    for ch in range(NB):
                        _scan_chunk(nc, C, sec, tok0, ch, sc, wbuf, xsspill,
                                    zspill, dtT, dbias_bc, nega_bc, dcol_bc, h_bf,
                                    bmcf, nw_bc, ysb, yhatF, ones_f16, ones_f32,
                                    tri_f16, su_f16, tri_f32, ident_bf, ident_f32,
                                    e2_f16, smalls, sm64, bnb, pa, pdm, pb, X)
                    barrier()
                with nc.named_scope(f"E_t{tile}"):
                    _e_outproj(nc, C, sec, tok0, w_out, yhatF, wbuf, x1stg,
                               x1spill, pa)
                    barrier()

            with contextlib.ExitStack() as ph:
                Aph = lambda n, sh, dt: ph.enter_context(
                    nc.sbuf_tensor(f"{n}_t{tile}", sh, dt))
                awT = Aph("awT", [128, 32 * 512], BF16)
                bwT = Aph("bwT", [128, 4 * H], BF16)
                wqT = Aph("wqT", [128, 32 * 16], BF16)
                keysT = Aph("keysT", [16, 64], BF16)
                x2F = Aph("x2F", [128, 32 * 128], BF16)
                x3f = Aph("x3f", [128, H], F32)
                xblk2 = Aph("xblk2", [128, H], F32)
                ysb2 = Aph("ysb2", [128, H], F32)
                fscr = Aph("fscr", [128, 512], F32)
                fscr2 = Aph("fscr2", [128, 64], F32)
                gq = Aph("gq", [128, 64], F32)
                gwork = Aph("gwork", [128, 64], F32)
                gate_bf = Aph("gate_bf", [128, 64], BF16)
                gateT = Aph("gateT", [64, 128], BF16)
                zhat = Aph("zhat", [128, 512], BF16)
                zhT = Aph("zhT", [128, 512], BF16)
                qtok = Aph("qtok", [128, 16], BF16)
                qT = Aph("qT", [16, 128], BF16)
                rcols = [Aph(f"rcol{i}", [128, 1], F32) for i in range(4)]

                with nc.named_scope(f"FP_t{tile}"):
                    _f_prep(nc, C, sec, ffn_a, ffn_b, ffn_wq, ffn_keys, wbuf,
                            awT, bwT, wqT, keysT, ident_bf, pa, pb)
                    barrier()
                do_at = {}
                Xf = {"aq": 0, "vln": 0, "pe_last": 0}
                with nc.named_scope(f"F_t{tile}"):
                    for b in range(NB):
                        _f_block(nc, C, sec, tok0, b, x_in, x1spill, out_t,
                                 xblk2, ysb2, x3f, x2F, awT, bwT, wqT, keysT,
                                 e512_bf, fscr, fscr2, gq, gwork, gate_bf, gateT,
                                 zhat, zhT, qtok, qT, bnb, smalls, rcols, ident_bf,
                                 ident_f32, pa, pdm, pb, do_at, Xf)
                    barrier()

        def fin(s):
            C.wait(s, "do")
            C.wait(s, "doa")
            C.wait(s, "dob")
        sec("sync", fin)

    return nc



def _ln_stats(C, v, bnb, src_ap, ncols, mu_col, m2_col, sm, c0, want_meansq):
    nst = ncols // 512
    for i in range(nst):
        v.bn_stats(bnb[i % 2][:, (i // 2) * 6:(i // 2) * 6 + 6],
                   src_ap[:, i * 512:(i + 1) * 512])
    v.bn_aggr(sm[:, c0:c0 + 2], bnb[0][:, 0:(nst // 2) * 6])
    v.bn_aggr(bnb[1][:, 18:20], bnb[1][:, 0:(nst // 2) * 6])
    mA, vA = sm[:, c0:c0 + 1], sm[:, c0 + 1:c0 + 2]
    mB, vB = bnb[1][:, 18:19], bnb[1][:, 19:20]
    t0, t1 = sm[:, c0 + 4:c0 + 5], sm[:, c0 + 5:c0 + 6]
    v.tensor_tensor(mu_col, mA, mB, OP.add)
    v.tensor_scalar(mu_col, mu_col, 0.5, None, OP.mult)
    v.tensor_tensor(t0, mA, mB, OP.subtract)
    v.tensor_tensor(t0, t0, t0, OP.mult)
    v.tensor_tensor(t1, vA, vB, OP.add)
    v.tensor_scalar(t1, t1, 0.5, None, OP.mult)
    v.scalar_tensor_tensor(m2_col, t0, 0.25, t1, OP.mult, OP.add)
    if want_meansq:
        v.tensor_tensor(t0, mu_col, mu_col, OP.mult)
        v.tensor_tensor(m2_col, m2_col, t0, OP.add)
    v.tensor_scalar(m2_col, m2_col, EPS, None, OP.add)


def _rstd_finish(s, src_col, dst_col, tmp_col):
    s.activation(tmp_col, src_col, AF.Ln)
    return s.activation(dst_col, tmp_col, AF.Exp, scale=-0.5)


def _p0(nc, C, sec, tok0, x_in, ln_w, lnw_bc, xblk, xhat_tok, xhatF, bnb,
        smalls, ident_bf, pa, pb):
    dx_at = {}
    v_stt = {}
    t_done = {}

    def g_lnw(g):
        C.dma(g.dma_start(lnw_bc[:], bcast_ap(ln_w, 128, H)), "dw")
    sec("gpsimd", g_lnw)
    dw_lnw = C.cnt["dw"]

    def d_pre(s):
        for b in range(2):
            dx_at[b] = C.dma(s.dma_start(
                xblk[b % 2][:], x_in[tok0 + b * 128:tok0 + (b + 1) * 128, :]), "dx")
    sec("sync", d_pre)

    for b in range(NB):
        def vf(v, b=b, dxf=None):
            C.wait(v, "dx")
            if b == 0:
                C.wait(v, "dw", dw_lnw)
            _ln_stats(C, v, bnb, xblk[b % 2][:], H, smalls[:, 8:9], smalls[:, 9:10],
                      smalls, 12, want_meansq=True)
            C.inc(v.tensor_copy(smalls[:, 18:19], smalls[:, 9:10]), "vE")
        sec("vector", vf)
        v_bn = C.cnt["vE"]

        def sf(s, b=b):
            C.wait(s, "vE", v_bn)
            C.inc(_rstd_finish(s, smalls[:, 18:19], smalls[:, 10:11],
                               smalls[:, 11:12]), "sE")
        sec("scalar", sf)
        s_rs = C.cnt["sE"]

        def vg(v, b=b):
            C.wait(v, "sE", s_rs)
            if b >= 1:
                C.wait(v, "tE", t_done[b - 1])
            C.inc(v.scalar_tensor_tensor(xhat_tok[:], xblk[b % 2][:],
                                         smalls[:, 10:11], lnw_bc[:],
                                         OP.mult, OP.mult), "vE")
        sec("vector", vg)
        v_stt[b] = C.cnt["vE"]

        if b + 2 < NB:
            def df(s, b=b):
                C.wait(s, "vE", v_stt[b])
                dx_at[b + 2] = C.dma(s.dma_start(
                    xblk[b % 2][:],
                    x_in[tok0 + (b + 2) * 128:tok0 + (b + 3) * 128, :]), "dx")
            sec("sync", df)

        for jb in range(4):
            def tf(t, b=b, jb=jb):
                if jb == 0:
                    C.wait(t, "vE", v_stt[b])
                key = f"pb{jb % 2}"
                C.bank_guard(t, key)
                for i in range(8):
                    c = jb * 8 + i
                    mm = t.transpose(pb[jb % 2][:, i * 128:(i + 1) * 128],
                                     xhat_tok[:, c * 128:(c + 1) * 128], ident_bf[:])
                C.inc(mm, "tE")
            sec("tensor", tf)
            t_tr = C.cnt["tE"]

            def sg(v, b=b, jb=jb, t_tr=t_tr):
                v.wait_ge(C.sems["tE"], t_tr)
                dstap = xhatF[:, jb * 8 * TT:(jb * 8 + 8) * TT] \
                    .rearrange("p (c t) -> p c t", t=TT)[:, :, b * 128:(b + 1) * 128]
                C.inc(v.tensor_copy(
                    dstap, pb[jb % 2][:, 0:1024].rearrange("p (c t) -> p c t", t=128)),
                    "vE")
                C.bank_mark(f"pb{jb % 2}", "vE")
            sec("vector", sg)
        t_done[b] = C.cnt["tE"]


def _a1(nc, C, sec, tok0, w_in, xhatF, wbuf, mup_bc, ztmp, bst, xbct, accb,
        convwF, tailsave, bmcf, dtT, xstg, xsspill, ident_bf, pa, pb):
    n_slab = 18
    dw_at = {}
    mm_done = {}
    ev_done = {}
    tg_at = {}
    conv_tap = {}
    spill_at = {}
    s_silu_at = {}

    def pre(g):
        def f(gp, g=g):
            col0 = 4096 + g * 256
            w = 64 if g == 17 else 256
            if g >= 2:
                gp.wait_ge(C.sems["tE"], mm_done[g - 2])
            dw_at[g] = C.dma(gp.dma_start(
                wbuf[g % 2][:, 0:32 * w].rearrange("p (c f) -> p c f", f=w),
                w_in.rearrange("(c p) f -> p c f", p=128)[:, :, col0:col0 + w]),
                "dwa" if g % 2 == 0 else "dwb")
        sec("gpsimd", f)

    pre(0)
    pre(1)

    for g in range(n_slab):
        col0 = 4096 + g * 256
        w = 64 if g == 17 else 256

        if g < 16:
            def gtail(gp, g=g):
                if 2 * g - 8 >= 0:
                    gp.wait_ge(C.sems["vE"], conv_tap[2 * g + 1 - 8])
                cc0 = 2 * g
                gp.tensor_copy(xbct[cc0 % 8][:, 0:3],
                               tailsave[:, cc0 * 3:cc0 * 3 + 3])
                C.inc(gp.tensor_copy(xbct[(cc0 + 1) % 8][:, 0:3],
                                     tailsave[:, (cc0 + 1) * 3:(cc0 + 1) * 3 + 3]),
                      "gE")
            sec("gpsimd", gtail)
        g_tail_in = C.cnt["gE"]

        for b in range(NB):
            idx = g * NB + b

            def tf(t, g=g, b=b, w=w, idx=idx):
                if b == 0:
                    C.wait(t, "dwa" if g % 2 == 0 else "dwb")
                C.bank_guard(t, f"pa{idx % 3}")
                for c in range(32):
                    mm = t.matmul(pa[idx % 3][:, 0:w],
                                  xhatF[:, c * TT + b * 128:c * TT + (b + 1) * 128],
                                  wbuf[g % 2][:, c * w:(c + 1) * w],
                                  start=(c == 0), stop=(c == 31))
                C.inc(mm, "tE")
                mm_done[g] = C.cnt["tE"]
            sec("tensor", tf)
            t_mm = C.cnt["tE"]

            def vf(v, g=g, b=b, w=w, col0=col0, idx=idx, t_mm=t_mm):
                C.wait(v, "tE", t_mm)
                if g < 17 and idx - 1 in tg_at:
                    C.wait(v, "tE", tg_at[idx - 1])
                if g == 17:
                    C.inc(v.tensor_tensor(dtT[:, b * 64:(b + 1) * 64],
                                          pa[idx % 3][:, 0:64],
                                          mup_bc[:, col0:col0 + 64], OP.mult), "vE")
                else:
                    C.inc(v.tensor_tensor(bst[:, 0:w], pa[idx % 3][:, 0:w],
                                          mup_bc[:, col0:col0 + w], OP.mult), "vE")
                C.bank_mark(f"pa{idx % 3}", "vE")
                ev_done[idx] = C.cnt["vE"]
            sec("vector", vf)
            v_ev = C.cnt["vE"]

            if g < 17:
                def tg(t, g=g, b=b, idx=idx, v_ev=v_ev):
                    C.wait(t, "vE", v_ev)
                    key = f"pb{idx % 2}"
                    C.bank_guard(t, key)
                    for i in range(2):
                        mm = t.transpose(pb[idx % 2][:, i * 128:(i + 1) * 128],
                                         bst[:, i * 128:(i + 1) * 128], ident_bf[:])
                    C.inc(mm, "tE")
                    tg_at[idx] = C.cnt["tE"]
                sec("tensor", tg)
                t_tr = C.cnt["tE"]

                def sg(v, g=g, b=b, idx=idx, t_tr=t_tr):
                    v.wait_ge(C.sems["tE"], t_tr)
                    last = None
                    for i in range(2):
                        cc = 2 * g + i
                        if cc < 32:
                            dst = xbct[cc % 8][:, 3 + b * 128:3 + (b + 1) * 128]
                        else:
                            dst = bmcf[:, (cc - 32) * TT + b * 128:
                                       (cc - 32) * TT + (b + 1) * 128]
                        last = v.tensor_copy(dst, pb[idx % 2][:, i * 128:(i + 1) * 128])
                    C.inc(last, "vE")
                    C.bank_mark(f"pb{idx % 2}", "vE")
                sec("vector", sg)

        if g + 2 < n_slab:
            pre(g + 2)

        if g < 16:
            for i in range(2):
                cc = 2 * g + i
                _conv_chunk(nc, C, sec, tok0, cc, xbct[cc % 8], accb, convwF,
                            tailsave, xstg, xsspill, bmcf, conv_tap, spill_at)
        elif g == 16:
            for i in range(2):
                cc = 32 + i
                _conv_chunk(nc, C, sec, tok0, cc, None, accb, convwF,
                            tailsave, xstg, xsspill, bmcf, conv_tap, spill_at)


def _conv_chunk(nc, C, sec, tok0, cc, buf, accb, convwF, tailsave, xstg,
                xsspill, bmcf, conv_tap, spill_at):
    if cc >= 32:
        data = bmcf[:, (cc - 32) * TT:(cc - 32 + 1) * TT]
    else:
        data = buf[:, 3:3 + TT]
    sE_now = C.cnt["sE"]
    vE_now = C.cnt["vE"]
    gE_now = C.cnt["gE"]

    def vf(v, cc=cc, data=data, sE_now=sE_now, vE_now=vE_now, gE_now=gE_now):
        C.wait(v, "sE", sE_now)
        C.wait(v, "vE", vE_now)
        C.wait(v, "gE", gE_now)
        w = lambda k: convwF[:, cc * 4 + k:cc * 4 + k + 1]
        if cc < 32:
            ext = buf
            v.tensor_scalar(accb[:], ext[:, 0:TT], w(0), None, OP.mult)
            for k in range(1, 3):
                v.scalar_tensor_tensor(accb[:], ext[:, k:k + TT], w(k), accb[:],
                                       OP.mult, OP.add)
            last = v.scalar_tensor_tensor(accb[:], ext[:, 3:3 + TT], w(3), accb[:],
                                          OP.mult, OP.add)
        else:
            ts3 = tailsave[:, cc * 3:cc * 3 + 3]
            v.tensor_scalar(accb[:], data[:], w(3), None, OP.mult)
            for k in range(3):
                sh = 3 - k
                v.scalar_tensor_tensor(accb[:, sh:TT], data[:, 0:TT - sh], w(k),
                                       accb[:, sh:TT], OP.mult, OP.add)
                last = v.scalar_tensor_tensor(accb[:, 0:sh], ts3[:, 3 - sh:3], w(k),
                                              accb[:, 0:sh], OP.mult, OP.add)
        C.inc(last, "vE")
        conv_tap[cc] = C.cnt["vE"]
    sec("vector", vf)
    v_tap = C.cnt["vE"]

    def gp2(g, cc=cc, data=data, v_tap=v_tap):
        C.wait(g, "vE", v_tap)
        C.inc(g.tensor_copy(tailsave[:, cc * 3:cc * 3 + 3], data[:, TT - 3:TT]), "gE")
    sec("gpsimd", gp2)
    g_tail = C.cnt["gE"]

    def sf(s, cc=cc, data=data, v_tap=v_tap, g_tail=g_tail):
        C.wait(s, "vE", v_tap)
        C.wait(s, "gE", g_tail)
        if cc < 32:
            if cc >= 2:
                C.wait(s, "doa" if cc % 2 == 0 else "dob")
            dst = xstg[cc % 2][:]
        else:
            dst = data
        C.inc(s.activation(dst, accb[:], AF.Silu), "sE")
    sec("scalar", sf)
    s_silu = C.cnt["sE"]

    if cc < 32:
        def df(sy, cc=cc, s_silu=s_silu):
            C.wait(sy, "sE", s_silu)
            spill_at[cc] = C.dma(sy.dma_start(
                xsspill[cc * 128:(cc + 1) * 128, tok0:tok0 + TT], xstg[cc % 2][:]),
                "doa" if cc % 2 == 0 else "dob")
        sec("sync", df)


def _a2(nc, C, sec, tok0, w_in, xhatF, wbuf, mup_bc, ztmp, zstg, zspill, pa):
    dw_at = {}
    mm_done = {}
    do_at = {}
    sz_at = {}

    def pre(g):
        def f(gp, g=g):
            col0 = g * 256
            if g >= 2:
                gp.wait_ge(C.sems["tE"], mm_done[g - 2])
            dw_at[g] = C.dma(gp.dma_start(
                wbuf[g % 2][:, 0:32 * 256].rearrange("p (c f) -> p c f", f=256),
                w_in.rearrange("(c p) f -> p c f", p=128)[:, :, col0:col0 + 256]),
                "dwa" if g % 2 == 0 else "dwb")
        sec("gpsimd", f)

    pre(0)
    pre(1)
    for g in range(16):
        col0 = g * 256
        for b in range(NB):
            idx = g * NB + b

            def tf(t, g=g, b=b, idx=idx):
                if b == 0:
                    C.wait(t, "dwa" if g % 2 == 0 else "dwb")
                C.bank_guard(t, f"pa{idx % 3}")
                for c in range(32):
                    mm = t.matmul(pa[idx % 3][:, 0:256],
                                  xhatF[:, c * TT + b * 128:c * TT + (b + 1) * 128],
                                  wbuf[g % 2][:, c * 256:(c + 1) * 256],
                                  start=(c == 0), stop=(c == 31))
                C.inc(mm, "tE")
                mm_done[g] = C.cnt["tE"]
            sec("tensor", tf)
            t_mm = C.cnt["tE"]

            def vf(v, col0=col0, idx=idx, t_mm=t_mm):
                C.wait(v, "tE", t_mm)
                if idx - 1 in sz_at:
                    C.wait(v, "sE", sz_at[idx - 1])
                C.inc(v.tensor_tensor(ztmp[:, 0:256], pa[idx % 3][:, 0:256],
                                      mup_bc[:, col0:col0 + 256], OP.mult), "vE")
                C.bank_mark(f"pa{idx % 3}", "vE")
            sec("vector", vf)
            v_ev = C.cnt["vE"]

            def sf(s, idx=idx, v_ev=v_ev):
                C.wait(s, "vE", v_ev)
                if idx >= 2:
                    C.wait(s, "doa" if idx % 2 == 0 else "dob")
                C.inc(s.activation(zstg[idx % 2][:, 0:256], ztmp[:, 0:256],
                                   AF.Silu), "sE")
                sz_at[idx] = C.cnt["sE"]
            sec("scalar", sf)
            s_z = C.cnt["sE"]

            def df(sy, b=b, col0=col0, idx=idx, s_z=s_z):
                C.wait(sy, "sE", s_z)
                do_at[idx] = C.dma(sy.dma_start(
                    zspill[tok0 + b * 128:tok0 + (b + 1) * 128, col0:col0 + 256],
                    zstg[idx % 2][:, 0:256]), "doa" if idx % 2 == 0 else "dob")
            sec("sync", df)
        if g + 2 < 16:
            pre(g + 2)


def _scan_chunk(nc, C, sec, tok0, ch, sc, wbuf, xsspill, zspill, dtT,
                dbias_bc, nega_bc, dcol_bc, h_bf, bmcf, nw_bc, ysb, yhatF,
                ones_f16, ones_f32, tri_f16, su_f16, tri_f32, ident_bf,
                ident_f32, e2_f16, smalls, sm64, bnb, pa, pdm, pb, X):
    t0c = tok0 + ch * 128
    par = ch % 2
    xrd = wbuf[0][:, par * 4096:par * 4096 + 4096]
    zrd = wbuf[1][:, par * 4096:par * 4096 + 4096]

    def d0(sy, par=par):
        if X["xrd_free"][par]:
            sy.wait_ge(C.sems["tE"], X["xrd_free"][par])
        if X["zrd_free"][par]:
            sy.wait_ge(C.sems["vE"], X["zrd_free"][par])
        C.dma(sy.dma_start(
            xrd.rearrange("p (c t) -> p c t", t=128),
            xsspill.rearrange("(c p) t -> p c t", p=128)[:, :, t0c:t0c + 128]), "dx")
        C.dma(sy.dma_start(zrd, zspill[t0c:t0c + 128, :]), "dx")
    sec("sync", d0)

    for jb in range(4):
        def tf(t, jb=jb):
            if jb == 0:
                C.wait(t, "dx")
            key = f"pb{jb % 2}"
            C.bank_guard(t, key)
            for i in range(8):
                c = jb * 8 + i
                mm = t.transpose(pb[jb % 2][:, i * 128:(i + 1) * 128],
                                 xrd[:, c * 128:(c + 1) * 128], ident_bf[:])
            C.inc(mm, "tE")
        sec("tensor", tf)
        t_tr = C.cnt["tE"]

        def sg(v, jb=jb, t_tr=t_tr):
            v.wait_ge(C.sems["tE"], t_tr)
            if jb == 0 and X["st_last"]:
                v.wait_ge(C.sems["tE"], X["st_last"])
            C.inc(v.tensor_copy(sc["xsT"][:, jb * 1024:(jb + 1) * 1024],
                                pb[jb % 2][:, 0:1024]), "vE")
            C.bank_mark(f"pb{jb % 2}", "vE")
        sec("vector", sg)
    X["xrd_free"][par] = C.cnt["tE"]
    v_xst = C.cnt["vE"]

    def v_dt(v):
        dts = dtT[:, ch * 64:(ch + 1) * 64]
        if X["scum"]:
            v.wait_ge(C.sems["sE"], X["scum"])
        v.tensor_tensor(sc["t64a"][:], dts, dbias_bc[:], OP.add)
        C.inc(v.scalar_tensor_tensor(sc["t64b"][:], sc["t64a"][:], -1.0,
                                     sc["t64a"][:], OP.mult, OP.min), "vE")
    sec("vector", v_dt)
    v_1 = C.cnt["vE"]

    def s_dt(s, v_1=v_1):
        C.wait(s, "vE", v_1)
        s.activation(sc["t64c"][:], sc["t64b"][:], AF.Exp)
        C.inc(s.activation(sc["t64b"][:], sc["t64c"][:], AF.Ln, bias=1.0), "sE")
    sec("scalar", s_dt)
    s_1 = C.cnt["sE"]

    def v_dt2(v, s_1=s_1):
        C.wait(v, "sE", s_1)
        v.tensor_scalar(sc["t64c"][:], sc["t64a"][:], 0.0, None, OP.max)
        v.tensor_tensor(sc["dtc"][:], sc["t64b"][:], sc["t64c"][:], OP.add)
        v.tensor_tensor(sc["dta32"][:], sc["dtc"][:], nega_bc[:], OP.mult)
        C.inc(v.tensor_copy(sc["dta16"][:], sc["dta32"][:]), "vE")
    sec("vector", v_dt2)
    v_2 = C.cnt["vE"]

    def t_cum(t, v_2=v_2):
        C.wait(t, "vE", v_2)
        C.bank_guard(t, "pa3")
        t.matmul(pa[3][0:64, 0:128], sc["dta16"][:], tri_f16[:], start=True,
                 stop=True)
        t.matmul(pa[3][:, 128:192], su_f16[:], sc["dta16"][:], start=True,
                 stop=True)
        C.inc(t.matmul(pa[3][0:64, 192:193], sc["dta16"][:], ones_f16[:, 0:1],
                       start=True, stop=True), "tE")
    sec("tensor", t_cum)
    t_cu = C.cnt["tE"]

    def s_cum(s, t_cu=t_cu):
        C.wait(s, "tE", t_cu)
        if X["v3"]:
            C.wait(s, "vE", X["v3"])
        if X["tv_last"]:
            s.wait_ge(C.sems["tE"], X["tv_last"])
        s.activation(sc["vf16"][:], pa[3][0:64, 0:128], AF.Exp)
        s.activation(sc["t64a"][:], pa[3][:, 128:192], AF.Exp)
        C.inc(s.activation(sm64[:, 2:3], pa[3][0:64, 192:193], AF.Exp), "sE")
        C.bank_mark("pa3", "sE")
    sec("scalar", s_cum)
    s_2 = C.cnt["sE"]
    X["scum"] = s_2

    def v_w(v, s_2=s_2):
        C.wait(v, "sE", s_2)
        C.inc(v.tensor_tensor(sc["wtok"][:], sc["t64a"][:], sc["dtc"][:],
                              OP.mult), "vE")
    sec("vector", v_w)
    v_3 = C.cnt["vE"]
    X["v3"] = v_3

    def t_sb(t, s_2=s_2):
        C.wait(t, "sE", s_2)
        C.bank_guard(t, "pa3")
        C.inc(t.transpose(pa[3][0:1, 384:448], sm64[0:64, 2:3],
                          ident_f32[0:64, 0:64]), "tE")
    sec("tensor", t_sb)
    t_sb1 = C.cnt["tE"]

    def v_sb(v, t_sb1=t_sb1):
        v.wait_ge(C.sems["tE"], t_sb1)
        C.inc(v.tensor_copy(sc["srow"][:], pa[3][0:1, 384:448]), "vE")
    sec("vector", v_sb)
    v_sr = C.cnt["vE"]

    def t_sb2(t, v_sr=v_sr):
        C.wait(t, "vE", v_sr)
        C.inc(t.matmul(pa[3][:, 448:512], ones_f32[0:1, :], sc["srow"][:],
                       start=True, stop=True), "tE")
    sec("tensor", t_sb2)
    t_sb2a = C.cnt["tE"]

    def v_sb2(v, t_sb2a=t_sb2a):
        v.wait_ge(C.sems["tE"], t_sb2a)
        C.inc(v.tensor_copy(sc["sbc"][:], pa[3][:, 448:512]), "vE")
        C.bank_mark("pa3", "vE")
    sec("vector", v_sb2)

    def t_cb(t, v_xst=v_xst):
        C.wait(t, "vE", v_xst)
        C.bank_guard(t, "pa3")
        t.matmul(pa[3][:, 256:384], bmcf[:, ch * 128:(ch + 1) * 128],
                 bmcf[:, TT + ch * 128:TT + (ch + 1) * 128], start=True, stop=True)
        C.bank_guard(t, "pb0")
        C.inc(t.transpose(pb[0][:, 0:128], bmcf[:, ch * 128:(ch + 1) * 128],
                          ident_bf[:]), "tE")
    sec("tensor", t_cb)
    t_cb_at = C.cnt["tE"]

    def v_cb(v, t_cb_at=t_cb_at):
        v.wait_ge(C.sems["tE"], t_cb_at)
        if X["st_last"]:
            v.wait_ge(C.sems["tE"], X["st_last"])
        v.tensor_tensor(sc["cbtm"][:], pa[3][:, 256:384], tri_f32[:], OP.mult)
        C.bank_mark("pa3", "vE")
        C.inc(v.tensor_copy(sc["bmt"][:], pb[0][:, 0:128]), "vE")
        C.bank_mark("pb0", "vE")
    sec("vector", v_cb)

    for g in range(8):
        gg = X["base_g"] + g
        heads = list(range(8 * g, 8 * g + 8))

        for half in range(2):
            hh = 2 * gg + half
            hlist = heads[half * 4:half * 4 + 4]

            def v_r(v, hh=hh, hlist=hlist):
                if X["dmm"].get(hh - 2):
                    v.wait_ge(C.sems["tE"], X["dmm"][hh - 2])
                last = None
                for i, r in enumerate(hlist):
                    last = v.tensor_scalar(
                        sc["rhsd"][hh % 2][:, i * 128:(i + 1) * 128],
                        tri_f16[:], sc["dta32"][:, r:r + 1], None, OP.mult)
                C.inc(last, "vE")
            sec("vector", v_r)
            v_rh = C.cnt["vE"]

            def t_d(t, hh=hh, v_rh=v_rh):
                C.wait(t, "vE", v_rh)
                key = f"pd{hh % 2}"
                C.bank_guard(t, key)
                half0 = (hh % 2) * 512
                for i in range(4):
                    mm = t.matmul(pdm[:, half0 + i * 128:half0 + (i + 1) * 128],
                                  su_f16[:],
                                  sc["rhsd"][hh % 2][:, i * 128:(i + 1) * 128],
                                  start=(i == 0), stop=(i == 3))
                C.inc(mm, "tE")
            sec("tensor", t_d)
            X["dmm"][hh] = C.cnt["tE"]

        def s_e(s, gg=gg):
            s.wait_ge(C.sems["tE"], X["dmm"][2 * gg + 1])
            if X["expfree"].get(gg - 2):
                C.wait(s, "vE", X["expfree"][gg - 2])
            C.inc(s.activation(sc["expd"][gg % 2][:], pdm[:, 0:1024], AF.Exp),
                  "sE")
            C.bank_mark("pd0", "sE")
            C.bank_mark("pd1", "sE")
        sec("scalar", s_e)
        exp_g = C.cnt["sE"]

        first_y2 = True
        for i, r in enumerate(heads):
            rg = X["base_r"] + r

            def v_m(v, r=r, i=i, gg=gg, rg=rg, exp_g=exp_g):
                C.wait(v, "sE", exp_g)
                if X["intra"].get(rg - 2):
                    v.wait_ge(C.sems["tE"], X["intra"][rg - 2])
                C.inc(v.scalar_tensor_tensor(
                    sc["mtil"][r % 2][:],
                    sc["expd"][gg % 2][:, i * 128:(i + 1) * 128],
                    sc["dtc"][:, r:r + 1], sc["cbtm"][:], OP.mult, OP.mult), "vE")
                if i == 7:
                    X["expfree"][gg] = C.cnt["vE"]
            sec("vector", v_m)
            v_mt = C.cnt["vE"]

            def t_i(t, r=r, i=i, first=first_y2, v_mt=v_mt):
                C.wait(t, "vE", v_mt)
                if first:
                    C.bank_guard(t, "pa1")
                C.inc(t.matmul(pa[1][:, i * 64:(i + 1) * 64], sc["mtil"][r % 2][:],
                               sc["xsT"][:, r * 64:(r + 1) * 64],
                               start=first, stop=False), "tE")
                X["intra"][rg] = C.cnt["tE"]
            sec("tensor", t_i)
            first_y2 = False

        def t_inter(t, g=g):
            C.bank_guard(t, "pa0")
            if X["hev"].get(g):
                t.wait_ge(C.sems["vE"], X["hev"][g])
            last = None
            for i, r in enumerate(heads):
                last = t.matmul(pa[0][:, i * 64:(i + 1) * 64],
                                bmcf[:, TT + ch * 128:TT + (ch + 1) * 128],
                                h_bf[:, r * 64:(r + 1) * 64],
                                start=(i == 0), stop=(i == 7))
            C.inc(last, "tE")
        sec("tensor", t_inter)
        t_int = C.cnt["tE"]

        def t_v(t, g=g, v_3=v_3):
            C.wait(t, "vE", v_3)
            key = f"pd{g % 2}"
            C.bank_guard(t, key)
            C.inc(t.matmul(pdm[:, (g % 2) * 512:(g % 2) * 512 + 512],
                           sc["vf16"][:], e2_f16[:, g * 512:(g + 1) * 512],
                           start=True, stop=True), "tE")
        sec("tensor", t_v)
        t_ve = C.cnt["tE"]
        if g == 7:
            X["tv_last"] = t_ve

        def v_grp(v, g=g, gg=gg, t_ve=t_ve, t_int=t_int):
            if X["dxs"].get(gg - 1):
                v.wait_ge(C.sems["tE"], X["dxs"][gg - 1])
            for i, r in enumerate(range(8 * g, 8 * g + 8)):
                v.tensor_scalar(sc["dxst"][:, i * 64:(i + 1) * 64],
                                sc["xsT"][:, r * 64:(r + 1) * 64],
                                dcol_bc[:, r:r + 1], None, OP.mult)
            v.wait_ge(C.sems["tE"], t_ve)
            v.tensor_copy(sc["vexp"][:], pdm[:, (g % 2) * 512:(g % 2) * 512 + 512])
            C.bank_mark(f"pd{g % 2}", "vE")
            v.wait_ge(C.sems["tE"], t_int)
            C.inc(v.scalar_tensor_tensor(sc["ytmp"][:], pa[0][:, 0:512], 1.0,
                                         sc["vexp"][:], OP.mult, OP.mult), "vE")
            C.bank_mark("pa0", "vE")
        sec("vector", v_grp)
        v_yt = C.cnt["vE"]

        def t_inj(t, gg=gg, v_yt=v_yt):
            C.wait(t, "vE", v_yt)
            t.matmul(pa[1][:, 0:512], ident_bf[:], sc["ytmp"][:], start=False,
                     stop=False)
            C.inc(t.matmul(pa[1][:, 0:512], ident_bf[:], sc["dxst"][:],
                           start=False, stop=True), "tE")
            X["dxs"][gg] = C.cnt["tE"]
        sec("tensor", t_inj)
        t_y2 = C.cnt["tE"]

        def v_yx(v, g=g, gg=gg, t_y2=t_y2):
            v.wait_ge(C.sems["tE"], t_y2)
            v.tensor_copy(ysb[:, g * 512:(g + 1) * 512], pa[1][:, 0:512])
            C.bank_mark("pa1", "vE")
            if X["st"].get(gg - 1):
                v.wait_ge(C.sems["tE"], X["st"][gg - 1])
            for i, r in enumerate(range(8 * g, 8 * g + 8)):
                v.tensor_scalar(sc["xtld"][:, i * 64:(i + 1) * 64],
                                sc["xsT"][:, r * 64:(r + 1) * 64],
                                sc["wtok"][:, r:r + 1], None, OP.mult)
            last = None
            for i, r in enumerate(range(8 * g, 8 * g + 8)):
                last = v.tensor_scalar(sc["hdec"][:, i * 64:(i + 1) * 64],
                                       h_bf[:, r * 64:(r + 1) * 64],
                                       sc["sbc"][:, r:r + 1], None, OP.mult)
            C.inc(last, "vE")
        sec("vector", v_yx)
        v_xt = C.cnt["vE"]

        def t_st(t, g=g, gg=gg, v_xt=v_xt):
            C.wait(t, "vE", v_xt)
            C.bank_guard(t, "pa2")
            for i in range(8):
                t.matmul(pa[2][:, i * 64:(i + 1) * 64], sc["bmt"][:],
                         sc["xtld"][:, i * 64:(i + 1) * 64],
                         start=(i == 0), stop=False)
            C.inc(t.matmul(pa[2][:, 0:512], ident_bf[:], sc["hdec"][:],
                           start=False, stop=True), "tE")
            X["st"][gg] = C.cnt["tE"]
        sec("tensor", t_st)
        t_hu = C.cnt["tE"]
        if g == 7:
            X["st_last"] = t_hu

        def v_h(v, g=g, t_hu=t_hu):
            v.wait_ge(C.sems["tE"], t_hu)
            C.inc(v.tensor_copy(h_bf[:, g * 512:(g + 1) * 512], pa[2][:, 0:512]),
                  "vE")
            C.bank_mark("pa2", "vE")
            X["hev"][g] = C.cnt["vE"]
        sec("vector", v_h)

    X["base_g"] += 8
    X["base_r"] += 64

    def v_tail(v):
        C.wait(v, "dx")
        v.tensor_tensor(ysb[:], ysb[:], zrd, OP.mult)
        _ln_stats(C, v, bnb, ysb[:], DS, smalls[:, 8:9], smalls[:, 9:10],
                  smalls, 12, want_meansq=True)
        C.inc(v.tensor_copy(smalls[:, 18:19], smalls[:, 9:10]), "vE")
    sec("vector", v_tail)
    v_t1 = C.cnt["vE"]
    X["zrd_free"][par] = v_t1

    def s_tail(s, v_t1=v_t1):
        C.wait(s, "vE", v_t1)
        if X["vt2"]:
            C.wait(s, "vE", X["vt2"])
        C.inc(_rstd_finish(s, smalls[:, 18:19], smalls[:, 10:11],
                           smalls[:, 11:12]), "sE")
    sec("scalar", s_tail)
    s_t1 = C.cnt["sE"]

    def v_tail2(v, s_t1=s_t1):
        C.wait(v, "sE", s_t1)
        if X["ytr"]:
            v.wait_ge(C.sems["tE"], X["ytr"])
        C.inc(v.scalar_tensor_tensor(sc["yhat_tok"][:], ysb[:], smalls[:, 10:11],
                                     nw_bc[:], OP.mult, OP.mult), "vE")
    sec("vector", v_tail2)
    v_t2 = C.cnt["vE"]
    X["vt2"] = v_t2

    for jb in range(4):
        def tf2(t, jb=jb, v_t2=v_t2):
            if jb == 0:
                C.wait(t, "vE", v_t2)
            key = f"pb{jb % 2}"
            C.bank_guard(t, key)
            for i in range(8):
                c = jb * 8 + i
                mm = t.transpose(pb[jb % 2][:, i * 128:(i + 1) * 128],
                                 sc["yhat_tok"][:, c * 128:(c + 1) * 128],
                                 ident_bf[:])
            C.inc(mm, "tE")
        sec("tensor", tf2)
        t_tr2 = C.cnt["tE"]

        def sg2(v, jb=jb, t_tr2=t_tr2):
            v.wait_ge(C.sems["tE"], t_tr2)
            dstap = yhatF[:, jb * 8 * TT:(jb * 8 + 8) * TT] \
                .rearrange("p (c t) -> p c t", t=TT)[:, :, ch * 128:(ch + 1) * 128]
            C.inc(v.tensor_copy(
                dstap, pb[jb % 2][:, 0:1024].rearrange("p (c t) -> p c t", t=128)),
                "vE")
            C.bank_mark(f"pb{jb % 2}", "vE")
        sec("vector", sg2)
    X["ytr"] = C.cnt["tE"]


def _e_outproj(nc, C, sec, tok0, w_out, yhatF, wbuf, x1stg, x1spill, pa):
    dw_at = {}
    mm_done = {}
    do_at = {}

    def pre(g):
        def f(gp, g=g):
            col0 = g * 256
            if g >= 2:
                gp.wait_ge(C.sems["tE"], mm_done[g - 2])
            dw_at[g] = C.dma(gp.dma_start(
                wbuf[g % 2][:, 0:32 * 256].rearrange("p (c f) -> p c f", f=256),
                w_out.rearrange("(c p) f -> p c f", p=128)[:, :, col0:col0 + 256]),
                "dwa" if g % 2 == 0 else "dwb")
        sec("gpsimd", f)

    pre(0)
    pre(1)
    for g in range(16):
        col0 = g * 256
        for b in range(NB):
            idx = g * NB + b

            def tf(t, g=g, b=b, idx=idx):
                if b == 0:
                    C.wait(t, "dwa" if g % 2 == 0 else "dwb")
                C.bank_guard(t, f"pa{idx % 3}")
                for c in range(32):
                    mm = t.matmul(pa[idx % 3][:, 0:256],
                                  yhatF[:, c * TT + b * 128:c * TT + (b + 1) * 128],
                                  wbuf[g % 2][:, c * 256:(c + 1) * 256],
                                  start=(c == 0), stop=(c == 31))
                C.inc(mm, "tE")
                mm_done[g] = C.cnt["tE"]
            sec("tensor", tf)
            t_mm = C.cnt["tE"]

            def sf(s, idx=idx, t_mm=t_mm):
                C.wait(s, "tE", t_mm)
                if idx >= 2:
                    C.wait(s, "doa" if idx % 2 == 0 else "dob")
                C.inc(s.activation(x1stg[idx % 2][:, 0:256], pa[idx % 3][:, 0:256],
                                   AF.Copy), "sE")
                C.bank_mark(f"pa{idx % 3}", "sE")
            sec("scalar", sf)
            s_x1 = C.cnt["sE"]

            def df(sy, b=b, col0=col0, idx=idx, s_x1=s_x1):
                C.wait(sy, "sE", s_x1)
                do_at[idx] = C.dma(sy.dma_start(
                    x1spill[tok0 + b * 128:tok0 + (b + 1) * 128, col0:col0 + 256],
                    x1stg[idx % 2][:, 0:256]), "doa" if idx % 2 == 0 else "dob")
            sec("sync", df)
        if g + 2 < 16:
            pre(g + 2)


def _f_prep(nc, C, sec, ffn_a, ffn_b, ffn_wq, ffn_keys, wbuf, awT, bwT,
            wqT, keysT, ident_bf, pa, pb):
    def d0(gp):
        for h in range(2):
            C.dma(gp.dma_start(
                wbuf[h][:, 0:8192].rearrange("p (j d) -> p j d", d=4096),
                ffn_a.rearrange("(j p) d -> p j d", p=128)[:, 2 * h:2 * h + 2, :]),
                "dw")
    sec("gpsimd", d0)
    dw_a = C.cnt["dw"]

    for cb in range(16):
        def tf(t, cb=cb):
            if cb == 0:
                C.wait(t, "dw", dw_a)
            key = f"pb{cb % 2}"
            C.bank_guard(t, key)
            for cc in range(2):
                c = cb * 2 + cc
                for j in range(4):
                    mm = t.transpose(
                        pb[cb % 2][:, cc * 512 + j * 128:cc * 512 + (j + 1) * 128],
                        wbuf[j // 2][:, (j % 2) * 4096 + c * 128:
                                     (j % 2) * 4096 + (c + 1) * 128],
                        ident_bf[:])
            C.inc(mm, "tE")
        sec("tensor", tf)
        t_tr = C.cnt["tE"]

        def sg(s, cb=cb, t_tr=t_tr):
            s.wait_ge(C.sems["tE"], t_tr)
            C.inc(s.activation(awT[:, cb * 1024:(cb + 1) * 1024],
                               pb[cb % 2][:, 0:1024], AF.Copy), "sE")
            C.bank_mark(f"pb{cb % 2}", "sE")
        sec("scalar", sg)
    t_a_done = C.cnt["tE"]

    def d1(gp, t_a_done=t_a_done):
        gp.wait_ge(C.sems["tE"], t_a_done)
        for h in range(2):
            C.dma(gp.dma_start(
                wbuf[h][:, 0:8192].rearrange("p (j n) -> p j n", n=512),
                ffn_b.rearrange("(j p) n -> p j n", p=128)[:, 16 * h:16 * h + 16, :]),
                "dw")
    sec("gpsimd", d1)
    dw_b = C.cnt["dw"]

    for nch in range(4):
        for jb in range(4):
            def tf2(t, nch=nch, jb=jb):
                if nch == 0 and jb == 0:
                    C.wait(t, "dw", dw_b)
                key = f"pb{jb % 2}"
                C.bank_guard(t, key)
                for i in range(8):
                    j = jb * 8 + i
                    mm = t.transpose(pb[jb % 2][:, i * 128:(i + 1) * 128],
                                     wbuf[j // 16][:, (j % 16) * 512 + nch * 128:
                                                   (j % 16) * 512 + (nch + 1) * 128],
                                     ident_bf[:])
                C.inc(mm, "tE")
            sec("tensor", tf2)
            t_tr2 = C.cnt["tE"]

            def sg2(s, nch=nch, jb=jb, t_tr2=t_tr2):
                s.wait_ge(C.sems["tE"], t_tr2)
                C.inc(s.activation(
                    bwT[:, nch * 4096 + jb * 1024:nch * 4096 + (jb + 1) * 1024],
                    pb[jb % 2][:, 0:1024], AF.Copy), "sE")
                C.bank_mark(f"pb{jb % 2}", "sE")
            sec("scalar", sg2)
    t_b_done = C.cnt["tE"]

    def d2(gp, t_b_done=t_b_done):
        gp.wait_ge(C.sems["tE"], t_b_done)
        C.dma(gp.dma_start(wbuf[0][0:16, 0:4096], ffn_wq), "dw")
        C.dma(gp.dma_start(wbuf[0][0:64, 4096:4112], ffn_keys), "dw")
    sec("gpsimd", d2)
    dw_q = C.cnt["dw"]

    for jb in range(4):
        def tf3(t, jb=jb):
            if jb == 0:
                C.wait(t, "dw", dw_q)
            key = f"pb{jb % 2}"
            C.bank_guard(t, key)
            for i in range(8):
                c = jb * 8 + i
                mm = t.transpose(pb[jb % 2][:, i * 16:(i + 1) * 16],
                                 wbuf[0][0:16, c * 128:(c + 1) * 128],
                                 ident_bf[0:16, 0:16])
            C.inc(mm, "tE")
        sec("tensor", tf3)
        t_tr3 = C.cnt["tE"]

        def sg3(s, jb=jb, t_tr3=t_tr3):
            s.wait_ge(C.sems["tE"], t_tr3)
            C.inc(s.activation(wqT[:, jb * 128:(jb + 1) * 128],
                               pb[jb % 2][:, 0:128], AF.Copy), "sE")
            C.bank_mark(f"pb{jb % 2}", "sE")
        sec("scalar", sg3)

    def tf4(t):
        C.bank_guard(t, "pb0")
        C.inc(t.transpose(pb[0][0:16, 0:64], wbuf[0][0:64, 4096:4112],
                          ident_bf[0:64, 0:64]), "tE")
    sec("tensor", tf4)
    t_k = C.cnt["tE"]

    def sg4(s, t_k=t_k):
        s.wait_ge(C.sems["tE"], t_k)
        C.inc(s.activation(keysT[:], pb[0][0:16, 0:64], AF.Copy), "sE")
        C.bank_mark("pb0", "sE")
    sec("scalar", sg4)


def _f_block(nc, C, sec, tok0, b, x_in, x1spill, out_t, xblk2, ysb2, x3f,
             x2F, awT, bwT, wqT, keysT, e512_bf, fscr, fscr2, gq, gwork,
             gate_bf, gateT, zhat, zhT, qtok, qT, bnb, smalls, rcols,
             ident_bf, ident_f32, pa, pdm, pb, do_at, Xf):
    t0b = tok0 + b * 128

    def d0(sy, b=b):
        if Xf["aq"]:
            sy.wait_ge(C.sems["tE"], Xf["aq"])
        if Xf["vln"]:
            sy.wait_ge(C.sems["vE"], Xf["vln"])
        C.dma(sy.dma_start(xblk2[:], x_in[t0b:t0b + 128, :]), "dx")
        C.dma(sy.dma_start(x2F[:], x1spill[t0b:t0b + 128, :]), "dx")
    sec("sync", d0)
    dx_b = C.cnt["dx"]

    def v0(v, dx_b=dx_b):
        C.wait(v, "dx")
        if Xf["pe_last"]:
            v.wait_ge(C.sems["tE"], Xf["pe_last"])
        C.wait(v, "do")
        v.tensor_tensor(ysb2[:], x2F[:, 0:H], xblk2[:], OP.add)
        _ln_stats(C, v, bnb, ysb2[:], H, smalls[:, 8:9], smalls[:, 9:10],
                  smalls, 12, want_meansq=False)
        C.inc(v.tensor_copy(smalls[:, 18:19], smalls[:, 9:10]), "vE")
    sec("vector", v0)
    v_ln = C.cnt["vE"]
    Xf["vln"] = v_ln

    def s0(s, v_ln=v_ln):
        C.wait(s, "vE", v_ln)
        C.inc(_rstd_finish(s, smalls[:, 18:19], smalls[:, 10:11],
                           smalls[:, 11:12]), "sE")
    sec("scalar", s0)
    s_ln = C.cnt["sE"]

    def v1(v, s_ln=s_ln):
        C.wait(v, "sE", s_ln)
        C.inc(v.tensor_scalar(ysb2[:], ysb2[:], smalls[:, 8:9], smalls[:, 10:11],
                              OP.subtract, OP.mult), "vE")
    sec("vector", v1)
    v_x2 = C.cnt["vE"]

    for jb in range(8):
        def tf(t, jb=jb, v_x2=v_x2):
            if jb == 0:
                C.wait(t, "vE", v_x2)
            key = f"pd{jb % 2}"
            C.bank_guard(t, key)
            half = (jb % 2) * 512
            for i in range(4):
                c = jb * 4 + i
                mm = t.transpose(pdm[:, half + i * 128:half + (i + 1) * 128],
                                 ysb2[:, c * 128:(c + 1) * 128], ident_f32[:])
            C.inc(mm, "tE")
        sec("tensor", tf)
        t_tr = C.cnt["tE"]

        def sg(s, jb=jb, t_tr=t_tr):
            s.wait_ge(C.sems["tE"], t_tr)
            C.inc(s.activation(x2F[:, jb * 512:(jb + 1) * 512],
                               pdm[:, (jb % 2) * 512:(jb % 2) * 512 + 512],
                               AF.Copy), "sE")
            C.bank_mark(f"pd{jb % 2}", "sE")
        sec("scalar", sg)
    s_x2f = C.cnt["sE"]

    def t_a(t, s_x2f=s_x2f):
        C.wait(t, "sE", s_x2f)
        C.bank_guard(t, "pa0")
        C.bank_guard(t, "pa1")
        last = None
        for c in range(32):
            t.matmul(pa[0][:, 0:512], x2F[:, c * 128:(c + 1) * 128],
                     awT[:, c * 512:(c + 1) * 512], start=(c == 0), stop=(c == 31))
            last = t.matmul(pa[1][:, 0:16], x2F[:, c * 128:(c + 1) * 128],
                            wqT[:, c * 16:(c + 1) * 16], start=(c == 0),
                            stop=(c == 31))
        C.inc(last, "tE")
    sec("tensor", t_a)
    t_aq = C.cnt["tE"]
    Xf["aq"] = t_aq

    def s_q(v, t_aq=t_aq):
        C.wait(v, "tE", t_aq)
        C.inc(v.tensor_copy(qtok[:], pa[1][:, 0:16]), "vE")
        C.bank_mark("pa1", "vE")
    sec("vector", s_q)
    s_qe = C.cnt["vE"]

    def t_q(t, s_qe=s_qe):
        C.wait(t, "vE", s_qe)
        C.bank_guard(t, "pb0")
        C.inc(t.transpose(pb[0][0:16, 0:128], qtok[:], ident_bf[:]), "tE")
    sec("tensor", t_q)
    t_qt = C.cnt["tE"]

    def s_q2(v, t_qt=t_qt):
        C.wait(v, "tE", t_qt)
        C.inc(v.tensor_copy(qT[:], pb[0][0:16, 0:128]), "vE")
        C.bank_mark("pb0", "vE")
    sec("vector", s_q2)
    s_qt = C.cnt["vE"]

    def t_sc(t, s_qt=s_qt):
        C.wait(t, "vE", s_qt)
        C.bank_guard(t, "pa2")
        C.inc(t.matmul(pa[2][:, 0:64], qT[:], keysT[:], start=True, stop=True),
              "tE")
    sec("tensor", t_sc)
    t_s = C.cnt["tE"]

    def v_r(v, t_s=t_s):
        C.wait(v, "tE", t_s)
        v.tensor_copy(gq[:], pa[2][:, 0:64])
        C.bank_mark("pa2", "vE")
        v.tensor_copy(gwork[:], gq[:])
        for it in range(15):
            v.reduce_max(rcols[0][:], gwork[:], axis=AX)
            if it == 0:
                v.tensor_copy(rcols[1][:], rcols[0][:])
            v.tensor_scalar(fscr2[:, 0:64], gwork[:], rcols[0][:], None, OP.is_ge)
            v.scalar_tensor_tensor(gwork[:], fscr2[:, 0:64], -1e30, gwork[:],
                                   OP.mult, OP.add)
        v.reduce_max(rcols[2][:], gwork[:], axis=AX)
        C.inc(v.tensor_scalar(rcols[1][:], rcols[1][:], -1.0, None, OP.mult), "vE")
    sec("vector", v_r)
    v_rt = C.cnt["vE"]

    def s_r(s, v_rt=v_rt):
        C.wait(s, "vE", v_rt)
        C.inc(s.activation(fscr[:, 0:64], gq[:], AF.Exp, bias=rcols[1][:]), "sE")
    sec("scalar", s_r)
    s_re = C.cnt["sE"]

    def v_g(v, s_re=s_re):
        C.wait(v, "sE", s_re)
        v.tensor_scalar(gwork[:], gq[:], rcols[2][:], None, OP.is_ge)
        v.tensor_tensor(gwork[:], gwork[:], fscr[:, 0:64], OP.mult)
        v.reduce_sum(rcols[3][:], gwork[:], axis=AX)
        v.reciprocal(rcols[0][:], rcols[3][:])
        C.inc(v.tensor_scalar(gate_bf[:], gwork[:], rcols[0][:], 2.0,
                              OP.mult, OP.mult), "vE")
    sec("vector", v_g)
    v_gt = C.cnt["vE"]

    def t_g(t, v_gt=v_gt):
        C.wait(t, "vE", v_gt)
        C.bank_guard(t, "pb1")
        C.inc(t.transpose(pb[1][0:64, 0:128], gate_bf[:], ident_bf[:]), "tE")
    sec("tensor", t_g)
    t_gt = C.cnt["tE"]

    def s_g(v, t_gt=t_gt):
        C.wait(v, "tE", t_gt)
        C.inc(v.tensor_copy(gateT[:], pb[1][0:64, 0:128]), "vE")
        C.bank_mark("pb1", "vE")
    sec("vector", s_g)
    s_gT = C.cnt["vE"]

    def t_ge(t, s_gT=s_gT):
        C.wait(t, "vE", s_gT)
        C.bank_guard(t, "pa2")
        C.inc(t.matmul(pa[2][:, 0:512], gateT[:], e512_bf[:], start=True,
                       stop=True), "tE")
    sec("tensor", t_ge)
    t_gx = C.cnt["tE"]

    def s_ge(v, t_gx=t_gx):
        C.wait(v, "tE", t_gx)
        C.inc(v.tensor_copy(fscr[:, 0:512], pa[2][:, 0:512]), "vE")
        C.bank_mark("pa2", "vE")
    sec("vector", s_ge)
    s_gx = C.cnt["vE"]

    def v_z(v, s_gx=s_gx):
        C.wait(v, "vE", s_gx)
        C.inc(v.scalar_tensor_tensor(zhat[:], pa[0][:, 0:512], 1.0,
                                     fscr[:, 0:512], OP.mult, OP.mult), "vE")
        C.bank_mark("pa0", "vE")
    sec("vector", v_z)
    v_zh = C.cnt["vE"]

    def t_zt(t, v_zh=v_zh):
        C.wait(t, "vE", v_zh)
        C.bank_guard(t, "pb0")
        last = None
        for i in range(4):
            last = t.transpose(pb[0][:, i * 128:(i + 1) * 128],
                               zhat[:, i * 128:(i + 1) * 128], ident_bf[:])
        C.inc(last, "tE")
    sec("tensor", t_zt)
    t_zT = C.cnt["tE"]

    def s_zt(v, t_zT=t_zT):
        C.wait(v, "tE", t_zT)
        C.inc(v.tensor_copy(zhT[:], pb[0][:, 0:512]), "vE")
        C.bank_mark("pb0", "vE")
    sec("vector", s_zt)
    s_zTT = C.cnt["vE"]

    for og in range(8):
        def t_b(t, og=og, s_zTT=s_zTT):
            if og == 0:
                C.wait(t, "vE", s_zTT)
            key = f"pd{og % 2}"
            C.bank_guard(t, key)
            half = (og % 2) * 512
            last = None
            for nch in range(4):
                last = t.matmul(pdm[:, half:half + 512],
                                zhT[:, nch * 128:(nch + 1) * 128],
                                bwT[:, nch * 4096 + og * 512:
                                    nch * 4096 + (og + 1) * 512],
                                start=(nch == 0), stop=(nch == 3))
            C.inc(last, "tE")
        sec("tensor", t_b)
        t_bo = C.cnt["tE"]
        Xf["pe_last"] = t_bo

        def v_o(v, og=og, t_bo=t_bo):
            C.wait(v, "tE", t_bo)
            half = (og % 2) * 512
            C.inc(v.scalar_tensor_tensor(x3f[:, og * 512:(og + 1) * 512],
                                         pdm[:, half:half + 512], 1.0,
                                         ysb2[:, og * 512:(og + 1) * 512],
                                         OP.mult, OP.add), "vE")
            C.bank_mark(f"pd{og % 2}", "vE")
        sec("vector", v_o)

    def v_l2(v):
        _ln_stats(C, v, bnb, x3f[:], H, smalls[:, 8:9], smalls[:, 9:10],
                  smalls, 12, want_meansq=False)
        C.inc(v.tensor_copy(smalls[:, 18:19], smalls[:, 9:10]), "vE")
    sec("vector", v_l2)
    v_l2a = C.cnt["vE"]

    def s_l2(s, v_l2a=v_l2a):
        C.wait(s, "vE", v_l2a)
        C.inc(_rstd_finish(s, smalls[:, 18:19], smalls[:, 10:11],
                           smalls[:, 11:12]), "sE")
    sec("scalar", s_l2)
    s_l2a = C.cnt["sE"]

    def v_out(v, b=b, s_l2a=s_l2a):
        C.wait(v, "sE", s_l2a)
        C.inc(v.tensor_scalar(x3f[:], x3f[:], smalls[:, 8:9], smalls[:, 10:11],
                              OP.subtract, OP.mult), "vE")
    sec("vector", v_out)
    v_ot = C.cnt["vE"]

    def d_out(sy, b=b, v_ot=v_ot):
        C.wait(sy, "vE", v_ot)
        do_at[b] = C.dma(sy.dma_start(out_t[t0b:t0b + 128, :], x3f[:]), "do")
    sec("sync", d_out)


def kernel(**inputs):
    _ = inputs.get("L0_attn_residual")
    x = np.ascontiguousarray(inputs["L0_attn_hidden_states"], dtype=np.float32)
    nc = build()
    base = {
        "w_in": np.ascontiguousarray(inputs["L0_attn_in_proj_w"], np.float32),
        "conv_w": np.ascontiguousarray(inputs["L0_attn_conv_w"], np.float32),
        "a_log": np.ascontiguousarray(inputs["L0_attn_A_log"], np.float32).reshape(1, NH),
        "dt_bias": np.ascontiguousarray(inputs["L0_attn_dt_bias"], np.float32).reshape(1, NH),
        "d_vec": np.ascontiguousarray(inputs["L0_attn_D"], np.float32).reshape(1, NH),
        "ln_w": np.ascontiguousarray(inputs["L0_attn_ln_w"], np.float32).reshape(1, H),
        "norm_w": np.ascontiguousarray(inputs["L0_attn_norm_w"], np.float32).reshape(1, DS),
        "mup": np.ascontiguousarray(inputs["L0_attn_mup_vector"], np.float32).reshape(1, PROJ),
        "w_out": np.ascontiguousarray(inputs["L0_attn_out_proj_w"], np.float32),
        "ffn_a": np.ascontiguousarray(inputs["L0_ffn_A_w"], np.float32),
        "ffn_b": np.ascontiguousarray(inputs["L0_ffn_B_w"], np.float32),
        "ffn_wq": np.ascontiguousarray(inputs["L0_ffn_Wq_w"], np.float32),
        "ffn_keys": np.ascontiguousarray(inputs["L0_ffn_keys"], np.float32),
    }
    in_maps = [dict(base, x=np.ascontiguousarray(x[c])) for c in range(NCORES)]
    res = run_bass_kernel_spmd(nc, in_maps, core_ids=list(range(NCORES)))
    return np.stack([res.results[c]["out"] for c in range(NCORES)], axis=0)


# version 06
# speedup vs baseline: 1.4099x; 1.1367x over previous
import contextlib
import numpy as np
import concourse.bass as bass
import concourse.mybir as mybir
from concourse.bass_utils import run_bass_kernel_spmd
from concourse.alu_op_type import AluOpType as OP

F32 = mybir.dt.float32
F16 = mybir.dt.float16
BF16 = mybir.dt.bfloat16
AF = mybir.ActivationFunctionType
AX = mybir.AxisListType.X

L = 2048
H = 4096
DS = 4096
PROJ = 8512
CONVD = 4352
NH = 64
TT = 1024
NB = 8
NT = L // TT
EPS = 1e-5
NCORES = 8
EXT = TT + 3

ALL_ENG = ["sync", "gpsimd", "tensor", "vector", "scalar"]

_DRAIN_OPS = frozenset({
    "tensor_scalar", "tensor_tensor", "scalar_tensor_tensor", "tensor_copy",
    "tensor_tensor_scan", "reduce_max", "reduce_sum", "tensor_reduce",
    "bn_stats", "bn_aggr", "reciprocal", "memset", "activation", "select",
    "copy",
})


class _AutoDrain:

    def __init__(self, eng):
        self._e = eng
        self._w = set()
        self._r = set()

    def __getattr__(self, name):
        attr = getattr(self._e, name)
        if name in _DRAIN_OPS:
            def w(*a, **k):
                aps = [x for x in list(a) + list(k.values())
                       if isinstance(x, bass.AP)]
                names = [ap.tensor.name for ap in aps]
                out, ins = set(names[:1]), set(names[1:])
                if (ins & self._w) or (out & self._w) or (out & self._r):
                    self._e.drain()
                    self._w.clear()
                    self._r.clear()
                self._w |= out
                self._r |= ins
                return attr(*a, **k)
            return w
        return attr


def bcast_ap(dram_ap, npart, cols):
    return bass.AP(dram_ap.tensor, 0, [[0, npart], [1, cols]])


class Ctx:

    def __init__(self, nc, es):
        self.nc = nc
        self.sems = {}
        self.cnt = {}
        for name in ["dw", "dwa", "dwb", "dx", "do", "doa", "dob", "tE", "vE",
                     "sE", "gE"]:
            self.sems[name] = es.enter_context(nc.semaphore(name))
            self.cnt[name] = 0
        self.bank_rd = {}

    def inc(self, instr, name, n=1):
        instr.then_inc(self.sems[name], n)
        self.cnt[name] += n
        return self.cnt[name]

    def dma(self, instr, name):
        return self.inc(instr, name, 16)

    def wait(self, eng, name, val=None):
        v = self.cnt[name] if val is None else val
        if v:
            eng.wait_ge(self.sems[name], v)

    def bank_guard(self, eng, key):
        if key in self.bank_rd:
            sem, val = self.bank_rd[key]
            if val:
                eng.wait_ge(self.sems[sem], val)

    def bank_mark(self, key, sem):
        self.bank_rd[key] = (sem, self.cnt[sem])


def build(debug=()):
    nc = bass.Bass("TRN2", target_bir_lowering=False)
    debug = set(debug)

    x_in = nc.dram_tensor("x", [L, H], F32, kind="ExternalInput").ap()
    w_in = nc.dram_tensor("w_in", [H, PROJ], F32, kind="ExternalInput").ap()
    conv_w = nc.dram_tensor("conv_w", [CONVD, 4], F32, kind="ExternalInput").ap()
    a_log = nc.dram_tensor("a_log", [1, NH], F32, kind="ExternalInput").ap()
    dt_bias = nc.dram_tensor("dt_bias", [1, NH], F32, kind="ExternalInput").ap()
    d_vec = nc.dram_tensor("d_vec", [1, NH], F32, kind="ExternalInput").ap()
    ln_w = nc.dram_tensor("ln_w", [1, H], F32, kind="ExternalInput").ap()
    norm_w = nc.dram_tensor("norm_w", [1, DS], F32, kind="ExternalInput").ap()
    mup = nc.dram_tensor("mup", [1, PROJ], F32, kind="ExternalInput").ap()
    w_out = nc.dram_tensor("w_out", [DS, H], F32, kind="ExternalInput").ap()
    ffn_a = nc.dram_tensor("ffn_a", [512, H], F32, kind="ExternalInput").ap()
    ffn_b = nc.dram_tensor("ffn_b", [H, 512], F32, kind="ExternalInput").ap()
    ffn_wq = nc.dram_tensor("ffn_wq", [16, H], F32, kind="ExternalInput").ap()
    ffn_keys = nc.dram_tensor("ffn_keys", [NH, 16], F32, kind="ExternalInput").ap()

    out_t = nc.dram_tensor("out", [L, H], F32, kind="ExternalOutput").ap()

    def scratch(name, shape, dtype):
        if name in debug:
            return nc.dram_tensor(name, shape, dtype, kind="ExternalOutput").ap()
        return nc.dram_tensor(name, shape, dtype).ap()

    zspill = scratch("zspill", [L, DS], BF16)
    xsspill = scratch("xsspill", [DS, L], BF16)
    x1spill = scratch("x1spill", [L, H], BF16)

    es = contextlib.ExitStack()
    with es:
        C = Ctx(nc, es)
        block = es.enter_context(nc.Block())

        def sec(eng_name, fn):
            if eng_name in ("vector", "scalar"):
                getattr(block, eng_name)(lambda e: fn(_AutoDrain(e)))
            else:
                getattr(block, eng_name)(fn)

        def barrier():
            snap = dict(C.cnt)
            for e in ALL_ENG:
                def f(eng, snap=snap):
                    for name, v in snap.items():
                        if v:
                            eng.wait_ge(C.sems[name], v)
                sec(e, f)

        A = lambda n, sh, dt: es.enter_context(nc.sbuf_tensor(n, sh, dt))

        ones_f32 = A("ones_f32", [128, 128], F32)
        ones_f16 = A("ones_f16", [128, 128], F16)
        ident_bf = A("ident_bf", [128, 128], BF16)
        ident_f32 = A("ident_f32", [128, 128], F32)
        tri_f16 = A("tri_f16", [128, 128], F16)
        su_f16 = A("su_f16", [128, 128], F16)
        tri_f32 = A("tri_f32", [128, 128], F32)
        e2_f16 = A("e2_f16", [64, DS], F16)
        e512_bf = A("e512_bf", [64, 512], BF16)
        dbias_bc = A("dbias_bc", [128, NH], F32)
        nega_bc = A("nega_bc", [128, NH], F32)
        dcol_bc = A("dcol_bc", [128, NH], F32)
        convwF = A("convwF", [128, 34 * 4], F32)
        iota_f = A("iota_f", [128, 128], F32)
        iota_c = A("iota_c", [128, 1], F32)
        smalls = A("smalls", [128, 32], F32)
        sm64 = A("sm64", [64, 16], F32)
        dtT = A("dtT", [128, NB * NH], F32)
        h_bf = A("h_bf", [128, DS], BF16)
        tailsave = A("tailsave", [128, 34 * 3], BF16)
        bmcf = A("bmcf", [128, 2 * TT], BF16)
        wbuf = [A(f"wbuf{i}", [128, 8192], BF16) for i in range(2)]
        bnb = [A(f"bnb{i}", [128, 24], F32) for i in range(2)]

        pa = [es.enter_context(nc.psum_tensor(f"pa{i}", [128, 512], F32)) for i in range(4)]
        pdm = es.enter_context(nc.psum_tensor("pdm", [128, 1024], F32))
        pb = [es.enter_context(nc.psum_tensor(f"pb{i}", [128, 1024], BF16)) for i in range(2)]

        with contextlib.ExitStack() as sp:
            e2scr = sp.enter_context(nc.sbuf_tensor("e2scr", [64, DS], F32))

            def setup_sync(s):
                C.dma(s.dma_start(convwF[:].rearrange("p (c k) -> p c k", k=4),
                                  conv_w.rearrange("(c p) k -> p c k", p=128)), "dx")
                C.dma(s.dma_start(dbias_bc[:], bcast_ap(dt_bias, 128, NH)), "dx")
                C.dma(s.dma_start(nega_bc[:], bcast_ap(a_log, 128, NH)), "dx")
                C.dma(s.dma_start(dcol_bc[:], bcast_ap(d_vec, 128, NH)), "dx")
            sec("sync", setup_sync)

            def setup_gp(g):
                g.iota(iota_f[:], [[1, 128]], base=0, channel_multiplier=0,
                       allow_small_or_imprecise_dtypes=True)
                g.iota(iota_c[:], [[0, 1]], base=0, channel_multiplier=1,
                       allow_small_or_imprecise_dtypes=True)
                C.inc(g.iota(e2scr[:, :].rearrange("p (a b) -> p a b", b=64),
                             [[1, 64], [0, 64]], base=0, channel_multiplier=0,
                             allow_small_or_imprecise_dtypes=True), "gE")
            sec("gpsimd", setup_gp)

            def setup_v(v):
                C.wait(v, "gE")
                C.wait(v, "dx")
                v.memset(ones_f32[:], 1.0)
                v.memset(ones_f16[:], 1.0)
                v.memset(h_bf[:], 0.0)
                v.memset(tailsave[:], 0.0)
                v.tensor_scalar(ident_f32[:], iota_f[:], iota_c[:], None, OP.is_equal)
                v.tensor_copy(ident_bf[:], ident_f32[:])
                v.tensor_scalar(tri_f32[:], iota_f[:], iota_c[:], None, OP.is_ge)
                v.tensor_copy(tri_f16[:], tri_f32[:])
                v.tensor_scalar(su_f16[:], iota_f[:], iota_c[:], None, OP.is_lt)
                v.tensor_scalar(e2scr[:, :], e2scr[:, :], iota_c[0:64, :],
                                None, OP.is_equal)
                C.inc(v.tensor_copy(e2_f16[:], e2scr[:, :]), "vE")
            sec("vector", setup_v)
            v_e2 = C.cnt["vE"]

            def setup_gp2(g):
                C.wait(g, "vE", v_e2)
                C.inc(g.iota(e2scr[:, 0:512].rearrange("p (a b) -> p a b", b=8),
                             [[1, 64], [0, 8]], base=0, channel_multiplier=0,
                             allow_small_or_imprecise_dtypes=True), "gE")
            sec("gpsimd", setup_gp2)

            def setup_v2(v):
                C.wait(v, "gE")
                v.tensor_scalar(e2scr[:, 0:512], e2scr[:, 0:512], iota_c[0:64, :],
                                None, OP.is_equal)
                C.inc(v.tensor_copy(e512_bf[:], e2scr[:, 0:512]), "vE")
            sec("vector", setup_v2)
            v_e5 = C.cnt["vE"]

            def setup_s(s):
                C.wait(s, "vE", v_e5)
                C.inc(s.activation(nega_bc[:], nega_bc[:], AF.Exp), "sE")
            sec("scalar", setup_s)

            def setup_v3(v):
                C.wait(v, "sE")
                C.inc(v.tensor_scalar(nega_bc[:], nega_bc[:], -1.0, None, OP.mult),
                      "vE")
            sec("vector", setup_v3)

            barrier()

        for tile in range(NT):
            tok0 = tile * TT

            with contextlib.ExitStack() as ph:
                Aph = lambda n, sh, dt: ph.enter_context(
                    nc.sbuf_tensor(f"{n}_t{tile}", sh, dt))
                xhatF = Aph("xhatF", [128, 32 * TT], BF16)

                with contextlib.ExitStack() as p0s:
                    A0 = lambda n, sh, dt: p0s.enter_context(
                        nc.sbuf_tensor(f"{n}_t{tile}", sh, dt))
                    lnw_bc = A0("lnw_bc", [128, H], F16)
                    xblk = [A0(f"xblk{i}", [128, H], F32) for i in range(2)]
                    xhat_tok = A0("xhat_tok", [128, H], BF16)
                    with nc.named_scope(f"P0_t{tile}"):
                        _p0(nc, C, sec, tok0, x_in, ln_w, lnw_bc, xblk, xhat_tok,
                            xhatF, bnb, smalls, ident_bf, pa, pb)
                        barrier()

                with contextlib.ExitStack() as a1s:
                    A1 = lambda n, sh, dt: a1s.enter_context(
                        nc.sbuf_tensor(f"{n}_t{tile}", sh, dt))
                    mup_bc = A1("mup_bc", [128, PROJ], F16)
                    xbct = [A1(f"xbct{i}", [128, EXT], BF16) for i in range(8)]
                    ztmp = A1("ztmp", [128, 512], F32)
                    bst = A1("bst", [128, 512], BF16)
                    accb = A1("accb", [128, TT], F32)
                    zstg = [A1(f"zstg{i}", [128, 512], BF16) for i in range(2)]
                    xstg = [A1(f"xstg{i}", [128, TT], BF16) for i in range(2)]

                    def mup_load(g):
                        C.dma(g.dma_start(mup_bc[:], bcast_ap(mup, 128, PROJ)), "dw")
                    sec("gpsimd", mup_load)

                    with nc.named_scope(f"A1_t{tile}"):
                        _a1(nc, C, sec, tok0, w_in, xhatF, wbuf, mup_bc, ztmp, bst,
                            xbct, accb, convwF, tailsave, bmcf, dtT, xstg, xsspill,
                            ident_bf, pa, pb)
                        barrier()
                    with nc.named_scope(f"A2_t{tile}"):
                        _a2(nc, C, sec, tok0, w_in, xhatF, wbuf, mup_bc, ztmp,
                            zstg, zspill, pa)
                        barrier()

            with contextlib.ExitStack() as ph:
                Aph = lambda n, sh, dt: ph.enter_context(
                    nc.sbuf_tensor(f"{n}_t{tile}", sh, dt))
                yhatF = Aph("yhatF", [128, 32 * TT], BF16)
                nw_bc = Aph("nw_bc", [128, DS], F16)
                ysb = Aph("ysb", [128, DS], BF16)
                sc = {n: Aph(n, sh, dt) for n, sh, dt in [
                    ("xsT", [128, DS], BF16),
                    ("yhat_tok", [128, DS], BF16),
                    ("vexp", [128, 512], F32),
                    ("ytmp", [128, 512], BF16),
                    ("dxst", [128, 512], BF16),
                    ("xtld", [128, 512], BF16),
                    ("hdec", [128, 512], BF16),
                    ("dtc", [128, NH], F32),
                    ("dta32", [128, NH], F32),
                    ("dta16", [128, NH], F16),
                    ("srow", [1, 64], F32),
                    ("sbc", [128, 64], F32),
                    ("t64a", [128, NH], F32),
                    ("t64b", [128, NH], F32),
                    ("t64c", [128, NH], F32),
                ]}
                sc["cbtm"] = [Aph(f"cbtm{i}", [128, 128], F32) for i in range(2)]
                sc["bmt"] = [Aph(f"bmt{i}", [128, 128], BF16) for i in range(2)]
                sc["vf16"] = [Aph(f"vf16{i}", [64, 128], F16) for i in range(2)]
                sc["expd"] = [Aph(f"expd{i}", [128, 1024], F16) for i in range(2)]
                sc["rhsd"] = [Aph(f"rhsd{i}", [128, 512], F16) for i in range(2)]
                sc["mtil"] = [Aph(f"mtil{i}", [128, 1024], BF16) for i in range(2)]
                sc["xsTd"] = [Aph(f"xsTd{i}", [128, 512], BF16) for i in range(2)]
                x1stg = [Aph(f"x1stg{i}", [128, 512], BF16) for i in range(2)]

                def nw_load(g):
                    C.dma(g.dma_start(nw_bc[:], bcast_ap(norm_w, 128, DS)), "dw")
                sec("gpsimd", nw_load)

                with nc.named_scope(f"SCAN_t{tile}"):
                    X = {"xrd_free": {0: 0, 1: 0}, "zrd_free": {0: 0, 1: 0},
                         "st_last": 0, "scum": 0, "tcu_prev": 0, "vyx_last": 0,
                         "front_v": 0, "vt2": 0, "ytr": 0, "base_g": 0,
                         "dmm": {}, "expfree": {}, "mfree": {}, "dxs": {},
                         "st": {}, "hev": {}}
                    fr = lambda ch: _scan_front(
                        nc, C, sec, tok0, ch, sc, wbuf, xsspill, zspill, dtT,
                        dbias_bc, nega_bc, bmcf, ones_f16, ones_f32, tri_f16,
                        su_f16, tri_f32, ident_bf, ident_f32, sm64, pa, pb, X)
                    fr(0)
                    for ch in range(NB):
                        _scan_groups(nc, C, sec, tok0, ch, sc, wbuf, dcol_bc,
                                     h_bf, bmcf, ysb, e2_f16, bnb, tri_f16,
                                     su_f16, ident_bf, pa, pdm, pb, X)
                        if ch + 1 < NB:
                            fr(ch + 1)
                        _scan_tail(nc, C, sec, tok0, ch, sc, nw_bc, ysb, yhatF,
                                   ident_bf, smalls, sm64, bnb, pa, pb, X)
                    barrier()
                with nc.named_scope(f"E_t{tile}"):
                    _e_outproj(nc, C, sec, tok0, w_out, yhatF, wbuf, x1stg,
                               x1spill, pa)
                    barrier()

            with contextlib.ExitStack() as ph:
                Aph = lambda n, sh, dt: ph.enter_context(
                    nc.sbuf_tensor(f"{n}_t{tile}", sh, dt))
                awT = Aph("awT", [128, 32 * 512], BF16)
                bwT = Aph("bwT", [128, 4 * H], BF16)
                wqT = Aph("wqT", [128, 32 * 16], BF16)
                keysT = Aph("keysT", [16, 64], BF16)
                x2F = Aph("x2F", [128, 32 * 128], BF16)
                x3f = Aph("x3f", [128, H], F32)
                xblk2 = Aph("xblk2", [128, H], F32)
                ysb2 = Aph("ysb2", [128, H], F32)
                fscr = Aph("fscr", [128, 512], F32)
                fscr2 = Aph("fscr2", [128, 64], F32)
                gq = Aph("gq", [128, 64], F32)
                gwork = Aph("gwork", [128, 64], F32)
                gate_bf = Aph("gate_bf", [128, 64], BF16)
                gateT = Aph("gateT", [64, 128], BF16)
                zhat = Aph("zhat", [128, 512], BF16)
                zhT = Aph("zhT", [128, 512], BF16)
                qtok = Aph("qtok", [128, 16], BF16)
                qT = Aph("qT", [16, 128], BF16)
                rcols = [Aph(f"rcol{i}", [128, 1], F32) for i in range(4)]

                with nc.named_scope(f"FP_t{tile}"):
                    _f_prep(nc, C, sec, ffn_a, ffn_b, ffn_wq, ffn_keys, wbuf,
                            awT, bwT, wqT, keysT, ident_bf, pa, pb)
                    barrier()
                do_at = {}
                Xf = {"aq": 0, "vln": 0, "pe_last": 0}
                with nc.named_scope(f"F_t{tile}"):
                    for b in range(NB):
                        _f_block(nc, C, sec, tok0, b, x_in, x1spill, out_t,
                                 xblk2, ysb2, x3f, x2F, awT, bwT, wqT, keysT,
                                 e512_bf, fscr, fscr2, gq, gwork, gate_bf, gateT,
                                 zhat, zhT, qtok, qT, bnb, smalls, rcols, ident_bf,
                                 ident_f32, pa, pdm, pb, do_at, Xf)
                    barrier()

        def fin(s):
            C.wait(s, "do")
            C.wait(s, "doa")
            C.wait(s, "dob")
        sec("sync", fin)

    return nc



def _ln_stats(C, v, bnb, src_ap, ncols, mu_col, m2_col, sm, c0, want_meansq):
    nst = ncols // 512
    for i in range(nst):
        v.bn_stats(bnb[i % 2][:, (i // 2) * 6:(i // 2) * 6 + 6],
                   src_ap[:, i * 512:(i + 1) * 512])
    v.bn_aggr(sm[:, c0:c0 + 2], bnb[0][:, 0:(nst // 2) * 6])
    v.bn_aggr(bnb[1][:, 18:20], bnb[1][:, 0:(nst // 2) * 6])
    mA, vA = sm[:, c0:c0 + 1], sm[:, c0 + 1:c0 + 2]
    mB, vB = bnb[1][:, 18:19], bnb[1][:, 19:20]
    t0, t1 = sm[:, c0 + 4:c0 + 5], sm[:, c0 + 5:c0 + 6]
    v.tensor_tensor(mu_col, mA, mB, OP.add)
    v.tensor_scalar(mu_col, mu_col, 0.5, None, OP.mult)
    v.tensor_tensor(t0, mA, mB, OP.subtract)
    v.tensor_tensor(t0, t0, t0, OP.mult)
    v.tensor_tensor(t1, vA, vB, OP.add)
    v.tensor_scalar(t1, t1, 0.5, None, OP.mult)
    v.scalar_tensor_tensor(m2_col, t0, 0.25, t1, OP.mult, OP.add)
    if want_meansq:
        v.tensor_tensor(t0, mu_col, mu_col, OP.mult)
        v.tensor_tensor(m2_col, m2_col, t0, OP.add)
    v.tensor_scalar(m2_col, m2_col, EPS, None, OP.add)


def _rstd_finish(s, src_col, dst_col, tmp_col):
    s.activation(tmp_col, src_col, AF.Ln)
    return s.activation(dst_col, tmp_col, AF.Exp, scale=-0.5)


def _p0(nc, C, sec, tok0, x_in, ln_w, lnw_bc, xblk, xhat_tok, xhatF, bnb,
        smalls, ident_bf, pa, pb):
    dx_at = {}
    v_stt = {}
    t_done = {}

    def g_lnw(g):
        C.dma(g.dma_start(lnw_bc[:], bcast_ap(ln_w, 128, H)), "dw")
    sec("gpsimd", g_lnw)
    dw_lnw = C.cnt["dw"]

    def d_pre(s):
        for b in range(2):
            dx_at[b] = C.dma(s.dma_start(
                xblk[b % 2][:], x_in[tok0 + b * 128:tok0 + (b + 1) * 128, :]), "dx")
    sec("sync", d_pre)

    for b in range(NB):
        def vf(v, b=b, dxf=None):
            C.wait(v, "dx")
            if b == 0:
                C.wait(v, "dw", dw_lnw)
            _ln_stats(C, v, bnb, xblk[b % 2][:], H, smalls[:, 8:9], smalls[:, 9:10],
                      smalls, 12, want_meansq=True)
            C.inc(v.tensor_copy(smalls[:, 18:19], smalls[:, 9:10]), "vE")
        sec("vector", vf)
        v_bn = C.cnt["vE"]

        def sf(s, b=b):
            C.wait(s, "vE", v_bn)
            C.inc(_rstd_finish(s, smalls[:, 18:19], smalls[:, 10:11],
                               smalls[:, 11:12]), "sE")
        sec("scalar", sf)
        s_rs = C.cnt["sE"]

        def vg(v, b=b):
            C.wait(v, "sE", s_rs)
            if b >= 1:
                C.wait(v, "tE", t_done[b - 1])
            C.inc(v.scalar_tensor_tensor(xhat_tok[:], xblk[b % 2][:],
                                         smalls[:, 10:11], lnw_bc[:],
                                         OP.mult, OP.mult), "vE")
        sec("vector", vg)
        v_stt[b] = C.cnt["vE"]

        if b + 2 < NB:
            def df(s, b=b):
                C.wait(s, "vE", v_stt[b])
                dx_at[b + 2] = C.dma(s.dma_start(
                    xblk[b % 2][:],
                    x_in[tok0 + (b + 2) * 128:tok0 + (b + 3) * 128, :]), "dx")
            sec("sync", df)

        for jb in range(4):
            def tf(t, b=b, jb=jb):
                if jb == 0:
                    C.wait(t, "vE", v_stt[b])
                key = f"pb{jb % 2}"
                C.bank_guard(t, key)
                for i in range(8):
                    c = jb * 8 + i
                    mm = t.transpose(pb[jb % 2][:, i * 128:(i + 1) * 128],
                                     xhat_tok[:, c * 128:(c + 1) * 128], ident_bf[:])
                C.inc(mm, "tE")
            sec("tensor", tf)
            t_tr = C.cnt["tE"]

            def sg(v, b=b, jb=jb, t_tr=t_tr):
                v.wait_ge(C.sems["tE"], t_tr)
                dstap = xhatF[:, jb * 8 * TT:(jb * 8 + 8) * TT] \
                    .rearrange("p (c t) -> p c t", t=TT)[:, :, b * 128:(b + 1) * 128]
                C.inc(v.tensor_copy(
                    dstap, pb[jb % 2][:, 0:1024].rearrange("p (c t) -> p c t", t=128)),
                    "vE")
                C.bank_mark(f"pb{jb % 2}", "vE")
            sec("vector", sg)
        t_done[b] = C.cnt["tE"]


def _a1(nc, C, sec, tok0, w_in, xhatF, wbuf, mup_bc, ztmp, bst, xbct, accb,
        convwF, tailsave, bmcf, dtT, xstg, xsspill, ident_bf, pa, pb):
    n_slab = 18
    dw_at = {}
    mm_done = {}
    ev_done = {}
    tg_at = {}
    conv_tap = {}
    spill_at = {}
    s_silu_at = {}

    def pre(g):
        def f(gp, g=g):
            col0 = 4096 + g * 256
            w = 64 if g == 17 else 256
            if g >= 2:
                gp.wait_ge(C.sems["tE"], mm_done[g - 2])
            dw_at[g] = C.dma(gp.dma_start(
                wbuf[g % 2][:, 0:32 * w].rearrange("p (c f) -> p c f", f=w),
                w_in.rearrange("(c p) f -> p c f", p=128)[:, :, col0:col0 + w]),
                "dwa" if g % 2 == 0 else "dwb")
        sec("gpsimd", f)

    pre(0)
    pre(1)

    for g in range(n_slab):
        col0 = 4096 + g * 256
        w = 64 if g == 17 else 256

        if g < 16:
            def gtail(gp, g=g):
                if 2 * g - 8 >= 0:
                    gp.wait_ge(C.sems["vE"], conv_tap[2 * g + 1 - 8])
                cc0 = 2 * g
                gp.tensor_copy(xbct[cc0 % 8][:, 0:3],
                               tailsave[:, cc0 * 3:cc0 * 3 + 3])
                C.inc(gp.tensor_copy(xbct[(cc0 + 1) % 8][:, 0:3],
                                     tailsave[:, (cc0 + 1) * 3:(cc0 + 1) * 3 + 3]),
                      "gE")
            sec("gpsimd", gtail)
        g_tail_in = C.cnt["gE"]

        for b in range(NB):
            idx = g * NB + b

            def tf(t, g=g, b=b, w=w, idx=idx):
                if b == 0:
                    C.wait(t, "dwa" if g % 2 == 0 else "dwb")
                C.bank_guard(t, f"pa{idx % 3}")
                for c in range(32):
                    mm = t.matmul(pa[idx % 3][:, 0:w],
                                  xhatF[:, c * TT + b * 128:c * TT + (b + 1) * 128],
                                  wbuf[g % 2][:, c * w:(c + 1) * w],
                                  start=(c == 0), stop=(c == 31))
                C.inc(mm, "tE")
                mm_done[g] = C.cnt["tE"]
            sec("tensor", tf)
            t_mm = C.cnt["tE"]

            def vf(v, g=g, b=b, w=w, col0=col0, idx=idx, t_mm=t_mm):
                C.wait(v, "tE", t_mm)
                if g < 17 and idx - 1 in tg_at:
                    C.wait(v, "tE", tg_at[idx - 1])
                if g == 17:
                    C.inc(v.tensor_tensor(dtT[:, b * 64:(b + 1) * 64],
                                          pa[idx % 3][:, 0:64],
                                          mup_bc[:, col0:col0 + 64], OP.mult), "vE")
                else:
                    C.inc(v.tensor_tensor(bst[:, 0:w], pa[idx % 3][:, 0:w],
                                          mup_bc[:, col0:col0 + w], OP.mult), "vE")
                C.bank_mark(f"pa{idx % 3}", "vE")
                ev_done[idx] = C.cnt["vE"]
            sec("vector", vf)
            v_ev = C.cnt["vE"]

            if g < 17:
                def tg(t, g=g, b=b, idx=idx, v_ev=v_ev):
                    C.wait(t, "vE", v_ev)
                    key = f"pb{idx % 2}"
                    C.bank_guard(t, key)
                    for i in range(2):
                        mm = t.transpose(pb[idx % 2][:, i * 128:(i + 1) * 128],
                                         bst[:, i * 128:(i + 1) * 128], ident_bf[:])
                    C.inc(mm, "tE")
                    tg_at[idx] = C.cnt["tE"]
                sec("tensor", tg)
                t_tr = C.cnt["tE"]

                def sg(v, g=g, b=b, idx=idx, t_tr=t_tr):
                    v.wait_ge(C.sems["tE"], t_tr)
                    last = None
                    for i in range(2):
                        cc = 2 * g + i
                        if cc < 32:
                            dst = xbct[cc % 8][:, 3 + b * 128:3 + (b + 1) * 128]
                        else:
                            dst = bmcf[:, (cc - 32) * TT + b * 128:
                                       (cc - 32) * TT + (b + 1) * 128]
                        last = v.tensor_copy(dst, pb[idx % 2][:, i * 128:(i + 1) * 128])
                    C.inc(last, "vE")
                    C.bank_mark(f"pb{idx % 2}", "vE")
                sec("vector", sg)

        if g + 2 < n_slab:
            pre(g + 2)

        if g < 16:
            for i in range(2):
                cc = 2 * g + i
                _conv_chunk(nc, C, sec, tok0, cc, xbct[cc % 8], accb, convwF,
                            tailsave, xstg, xsspill, bmcf, conv_tap, spill_at)
        elif g == 16:
            for i in range(2):
                cc = 32 + i
                _conv_chunk(nc, C, sec, tok0, cc, None, accb, convwF,
                            tailsave, xstg, xsspill, bmcf, conv_tap, spill_at)


def _conv_chunk(nc, C, sec, tok0, cc, buf, accb, convwF, tailsave, xstg,
                xsspill, bmcf, conv_tap, spill_at):
    if cc >= 32:
        data = bmcf[:, (cc - 32) * TT:(cc - 32 + 1) * TT]
    else:
        data = buf[:, 3:3 + TT]
    sE_now = C.cnt["sE"]
    vE_now = C.cnt["vE"]
    gE_now = C.cnt["gE"]

    def vf(v, cc=cc, data=data, sE_now=sE_now, vE_now=vE_now, gE_now=gE_now):
        C.wait(v, "sE", sE_now)
        C.wait(v, "vE", vE_now)
        C.wait(v, "gE", gE_now)
        w = lambda k: convwF[:, cc * 4 + k:cc * 4 + k + 1]
        if cc < 32:
            ext = buf
            v.tensor_scalar(accb[:], ext[:, 0:TT], w(0), None, OP.mult)
            for k in range(1, 3):
                v.scalar_tensor_tensor(accb[:], ext[:, k:k + TT], w(k), accb[:],
                                       OP.mult, OP.add)
            last = v.scalar_tensor_tensor(accb[:], ext[:, 3:3 + TT], w(3), accb[:],
                                          OP.mult, OP.add)
        else:
            ts3 = tailsave[:, cc * 3:cc * 3 + 3]
            v.tensor_scalar(accb[:], data[:], w(3), None, OP.mult)
            for k in range(3):
                sh = 3 - k
                v.scalar_tensor_tensor(accb[:, sh:TT], data[:, 0:TT - sh], w(k),
                                       accb[:, sh:TT], OP.mult, OP.add)
                last = v.scalar_tensor_tensor(accb[:, 0:sh], ts3[:, 3 - sh:3], w(k),
                                              accb[:, 0:sh], OP.mult, OP.add)
        C.inc(last, "vE")
        conv_tap[cc] = C.cnt["vE"]
    sec("vector", vf)
    v_tap = C.cnt["vE"]

    def gp2(g, cc=cc, data=data, v_tap=v_tap):
        C.wait(g, "vE", v_tap)
        C.inc(g.tensor_copy(tailsave[:, cc * 3:cc * 3 + 3], data[:, TT - 3:TT]), "gE")
    sec("gpsimd", gp2)
    g_tail = C.cnt["gE"]

    def sf(s, cc=cc, data=data, v_tap=v_tap, g_tail=g_tail):
        C.wait(s, "vE", v_tap)
        C.wait(s, "gE", g_tail)
        if cc < 32:
            if cc >= 2:
                C.wait(s, "doa" if cc % 2 == 0 else "dob")
            dst = xstg[cc % 2][:]
        else:
            dst = data
        C.inc(s.activation(dst, accb[:], AF.Silu), "sE")
    sec("scalar", sf)
    s_silu = C.cnt["sE"]

    if cc < 32:
        def df(sy, cc=cc, s_silu=s_silu):
            C.wait(sy, "sE", s_silu)
            spill_at[cc] = C.dma(sy.dma_start(
                xsspill[cc * 128:(cc + 1) * 128, tok0:tok0 + TT], xstg[cc % 2][:]),
                "doa" if cc % 2 == 0 else "dob")
        sec("sync", df)


def _a2(nc, C, sec, tok0, w_in, xhatF, wbuf, mup_bc, ztmp, zstg, zspill, pa):
    dw_at = {}
    mm_done = {}
    do_at = {}
    sz_at = {}

    def pre(g):
        def f(gp, g=g):
            col0 = g * 256
            if g >= 2:
                gp.wait_ge(C.sems["tE"], mm_done[g - 2])
            dw_at[g] = C.dma(gp.dma_start(
                wbuf[g % 2][:, 0:32 * 256].rearrange("p (c f) -> p c f", f=256),
                w_in.rearrange("(c p) f -> p c f", p=128)[:, :, col0:col0 + 256]),
                "dwa" if g % 2 == 0 else "dwb")
        sec("gpsimd", f)

    pre(0)
    pre(1)
    for g in range(16):
        col0 = g * 256
        for b in range(NB):
            idx = g * NB + b

            def tf(t, g=g, b=b, idx=idx):
                if b == 0:
                    C.wait(t, "dwa" if g % 2 == 0 else "dwb")
                C.bank_guard(t, f"pa{idx % 3}")
                for c in range(32):
                    mm = t.matmul(pa[idx % 3][:, 0:256],
                                  xhatF[:, c * TT + b * 128:c * TT + (b + 1) * 128],
                                  wbuf[g % 2][:, c * 256:(c + 1) * 256],
                                  start=(c == 0), stop=(c == 31))
                C.inc(mm, "tE")
                mm_done[g] = C.cnt["tE"]
            sec("tensor", tf)
            t_mm = C.cnt["tE"]

            def vf(v, col0=col0, idx=idx, t_mm=t_mm):
                C.wait(v, "tE", t_mm)
                if idx - 1 in sz_at:
                    C.wait(v, "sE", sz_at[idx - 1])
                C.inc(v.tensor_tensor(ztmp[:, 0:256], pa[idx % 3][:, 0:256],
                                      mup_bc[:, col0:col0 + 256], OP.mult), "vE")
                C.bank_mark(f"pa{idx % 3}", "vE")
            sec("vector", vf)
            v_ev = C.cnt["vE"]

            def sf(s, idx=idx, v_ev=v_ev):
                C.wait(s, "vE", v_ev)
                if idx >= 2:
                    C.wait(s, "doa" if idx % 2 == 0 else "dob")
                C.inc(s.activation(zstg[idx % 2][:, 0:256], ztmp[:, 0:256],
                                   AF.Silu), "sE")
                sz_at[idx] = C.cnt["sE"]
            sec("scalar", sf)
            s_z = C.cnt["sE"]

            def df(sy, b=b, col0=col0, idx=idx, s_z=s_z):
                C.wait(sy, "sE", s_z)
                do_at[idx] = C.dma(sy.dma_start(
                    zspill[tok0 + b * 128:tok0 + (b + 1) * 128, col0:col0 + 256],
                    zstg[idx % 2][:, 0:256]), "doa" if idx % 2 == 0 else "dob")
            sec("sync", df)
        if g + 2 < 16:
            pre(g + 2)


def _bc(ap, dims):
    return bass.AP(ap.tensor, ap.offset, dims)


def _scan_front(nc, C, sec, tok0, ch, sc, wbuf, xsspill, zspill, dtT,
                dbias_bc, nega_bc, bmcf, ones_f16, ones_f32, tri_f16,
                su_f16, tri_f32, ident_bf, ident_f32, sm64, pa, pb, X):
    t0c = tok0 + ch * 128
    par = ch % 2
    xrd = wbuf[0][:, par * 4096:par * 4096 + 4096]
    zrd = wbuf[1][:, par * 4096:par * 4096 + 4096]

    def d0(sy, par=par):
        if X["xrd_free"][par]:
            sy.wait_ge(C.sems["tE"], X["xrd_free"][par])
        if X["zrd_free"][par]:
            sy.wait_ge(C.sems["vE"], X["zrd_free"][par])
        C.dma(sy.dma_start(
            xrd.rearrange("p (c t) -> p c t", t=128),
            xsspill.rearrange("(c p) t -> p c t", p=128)[:, :, t0c:t0c + 128]), "dx")
        C.dma(sy.dma_start(zrd, zspill[t0c:t0c + 128, :]), "dx")
    sec("sync", d0)
    dxv = C.cnt["dx"]

    for jb in range(4):
        def tf(t, jb=jb, dxv=dxv):
            if jb == 0:
                C.wait(t, "dx", dxv)
            key = f"pb{jb % 2}"
            C.bank_guard(t, key)
            for i in range(8):
                c = jb * 8 + i
                mm = t.transpose(pb[jb % 2][:, i * 128:(i + 1) * 128],
                                 xrd[:, c * 128:(c + 1) * 128], ident_bf[:])
            C.inc(mm, "tE")
        sec("tensor", tf)
        t_tr = C.cnt["tE"]

        def sg(v, jb=jb, t_tr=t_tr):
            v.wait_ge(C.sems["tE"], t_tr)
            C.inc(v.tensor_copy(sc["xsT"][:, jb * 1024:(jb + 1) * 1024],
                                pb[jb % 2][:, 0:1024]), "vE")
            C.bank_mark(f"pb{jb % 2}", "vE")
        sec("vector", sg)
    X["xrd_free"][par] = C.cnt["tE"]

    def v_dt(v):
        dts = dtT[:, ch * 64:(ch + 1) * 64]
        if X["scum"]:
            C.wait(v, "sE", X["scum"])
        v.tensor_tensor(sc["t64a"][:], dts, dbias_bc[:], OP.add)
        C.inc(v.scalar_tensor_tensor(sc["t64b"][:], sc["t64a"][:], -1.0,
                                     sc["t64a"][:], OP.mult, OP.min), "vE")
    sec("vector", v_dt)
    v_1 = C.cnt["vE"]

    def s_dt(s, v_1=v_1):
        C.wait(s, "vE", v_1)
        s.activation(sc["t64c"][:], sc["t64b"][:], AF.Exp)
        C.inc(s.activation(sc["t64b"][:], sc["t64c"][:], AF.Ln, bias=1.0), "sE")
    sec("scalar", s_dt)
    s_1 = C.cnt["sE"]

    def v_dt2(v, s_1=s_1):
        C.wait(v, "sE", s_1)
        if X["tcu_prev"]:
            v.wait_ge(C.sems["tE"], X["tcu_prev"])
        v.tensor_scalar(sc["t64c"][:], sc["t64a"][:], 0.0, None, OP.max)
        v.tensor_tensor(sc["dtc"][:], sc["t64b"][:], sc["t64c"][:], OP.add)
        v.tensor_tensor(sc["dta32"][:], sc["dtc"][:], nega_bc[:], OP.mult)
        C.inc(v.tensor_copy(sc["dta16"][:], sc["dta32"][:]), "vE")
    sec("vector", v_dt2)
    v_2 = C.cnt["vE"]

    def t_cum(t, v_2=v_2):
        C.wait(t, "vE", v_2)
        C.bank_guard(t, "pa3")
        t.matmul(pa[3][0:64, 0:128], sc["dta16"][:], tri_f16[:], start=True,
                 stop=True)
        t.matmul(pa[3][:, 128:192], su_f16[:], sc["dta16"][:], start=True,
                 stop=True)
        C.inc(t.matmul(pa[3][0:64, 192:193], sc["dta16"][:], ones_f16[:, 0:1],
                       start=True, stop=True), "tE")
    sec("tensor", t_cum)
    t_cu = C.cnt["tE"]
    X["tcu_prev"] = t_cu

    def s_cum(s, t_cu=t_cu, par=par):
        C.wait(s, "tE", t_cu)
        if X["vyx_last"]:
            C.wait(s, "vE", X["vyx_last"])
        s.activation(sc["vf16"][par][:], pa[3][0:64, 0:128], AF.Exp)
        s.activation(sc["t64a"][:], pa[3][:, 128:192], AF.Exp)
        C.inc(s.activation(sm64[:, 2:3], pa[3][0:64, 192:193], AF.Exp), "sE")
        C.bank_mark("pa3", "sE")
    sec("scalar", s_cum)
    s_2 = C.cnt["sE"]
    X["scum"] = s_2

    def t_sb(t, s_2=s_2):
        C.wait(t, "sE", s_2)
        C.bank_guard(t, "pa3")
        C.inc(t.transpose(pa[3][0:1, 384:448], sm64[0:64, 2:3],
                          ident_f32[0:64, 0:64]), "tE")
    sec("tensor", t_sb)
    t_sb1 = C.cnt["tE"]

    def v_sb(v, t_sb1=t_sb1):
        v.wait_ge(C.sems["tE"], t_sb1)
        C.inc(v.tensor_copy(sc["srow"][:], pa[3][0:1, 384:448]), "vE")
    sec("vector", v_sb)
    v_sr = C.cnt["vE"]

    def t_sb2(t, v_sr=v_sr):
        C.wait(t, "vE", v_sr)
        C.inc(t.matmul(pa[3][:, 448:512], ones_f32[0:1, :], sc["srow"][:],
                       start=True, stop=True), "tE")
    sec("tensor", t_sb2)
    t_sb2a = C.cnt["tE"]

    def v_sb2(v, t_sb2a=t_sb2a):
        v.wait_ge(C.sems["tE"], t_sb2a)
        C.inc(v.tensor_copy(sc["sbc"][:], pa[3][:, 448:512]), "vE")
        C.bank_mark("pa3", "vE")
    sec("vector", v_sb2)

    def t_cb(t, par=par):
        C.wait(t, "vE")
        C.bank_guard(t, "pa3")
        t.matmul(pa[3][:, 256:384], bmcf[:, ch * 128:(ch + 1) * 128],
                 bmcf[:, TT + ch * 128:TT + (ch + 1) * 128], start=True, stop=True)
        C.bank_guard(t, "pb0")
        C.inc(t.transpose(pb[0][:, 0:128], bmcf[:, ch * 128:(ch + 1) * 128],
                          ident_bf[:]), "tE")
    sec("tensor", t_cb)
    t_cb_at = C.cnt["tE"]

    def v_cb(v, t_cb_at=t_cb_at, par=par):
        v.wait_ge(C.sems["tE"], t_cb_at)
        v.tensor_tensor(sc["cbtm"][par][:], pa[3][:, 256:384], tri_f32[:], OP.mult)
        C.bank_mark("pa3", "vE")
        C.inc(v.tensor_copy(sc["bmt"][par][:], pb[0][:, 0:128]), "vE")
        C.bank_mark("pb0", "vE")
    sec("vector", v_cb)
    X["front_v"] = C.cnt["vE"]


def _scan_groups(nc, C, sec, tok0, ch, sc, wbuf, dcol_bc, h_bf, bmcf, ysb,
                 e2_f16, bnb, tri_f16, su_f16, ident_bf, pa, pdm, pb, X):
    par = ch % 2
    zrd = wbuf[1][:, par * 4096:par * 4096 + 4096]
    front_v = X["front_v"]

    for g in range(8):
        gg = X["base_g"] + g
        heads = list(range(8 * g, 8 * g + 8))

        for half in range(2):
            hh = 2 * gg + half
            h0 = 8 * g + half * 4

            def v_r(v, hh=hh, h0=h0, front_v=front_v):
                C.wait(v, "vE", front_v)
                if X["dmm"].get(hh - 2):
                    v.wait_ge(C.sems["tE"], X["dmm"][hh - 2])
                C.inc(v.tensor_tensor(
                    sc["rhsd"][hh % 2][:].rearrange("p (a b) -> p a b", b=128),
                    _bc(tri_f16[:], [[128, 128], [0, 4], [1, 128]]),
                    _bc(sc["dta32"][:, h0:h0 + 4], [[64, 128], [1, 4], [0, 128]]),
                    OP.mult), "vE")
            sec("vector", v_r)
            v_rh = C.cnt["vE"]

            def t_d(t, hh=hh, v_rh=v_rh):
                C.wait(t, "vE", v_rh)
                key = f"pd{hh % 2}"
                C.bank_guard(t, key)
                half0 = (hh % 2) * 512
                for i in range(4):
                    mm = t.matmul(pdm[:, half0 + i * 128:half0 + (i + 1) * 128],
                                  su_f16[:],
                                  sc["rhsd"][hh % 2][:, i * 128:(i + 1) * 128],
                                  start=(i == 0), stop=(i == 3))
                C.inc(mm, "tE")
            sec("tensor", t_d)
            X["dmm"][hh] = C.cnt["tE"]

        def s_e(s, gg=gg):
            s.wait_ge(C.sems["tE"], X["dmm"][2 * gg + 1])
            if X["expfree"].get(gg - 2):
                C.wait(s, "vE", X["expfree"][gg - 2])
            C.inc(s.activation(sc["expd"][gg % 2][:], pdm[:, 0:1024], AF.Exp),
                  "sE")
            C.bank_mark("pd0", "sE")
            C.bank_mark("pd1", "sE")
        sec("scalar", s_e)
        exp_g = C.cnt["sE"]

        def v_gA(v, g=g, gg=gg, par=par, exp_g=exp_g):
            C.wait(v, "sE", exp_g)
            if X["mfree"].get(gg - 2):
                v.wait_ge(C.sems["tE"], X["mfree"][gg - 2])
            v.tensor_tensor(
                sc["mtil"][gg % 2][:].rearrange("p (a b) -> p a b", b=128),
                sc["expd"][gg % 2][:].rearrange("p (a b) -> p a b", b=128),
                _bc(sc["cbtm"][par][:], [[128, 128], [0, 8], [1, 128]]),
                OP.mult)
            v.tensor_tensor(
                sc["xsTd"][gg % 2][:].rearrange("p (a b) -> p a b", b=64),
                sc["xsT"][:, g * 512:(g + 1) * 512].rearrange("p (a b) -> p a b", b=64),
                _bc(sc["dtc"][:, g * 8:g * 8 + 8], [[64, 128], [1, 8], [0, 64]]),
                OP.mult)
            C.inc(v.tensor_tensor(
                sc["dxst"][:].rearrange("p (a b) -> p a b", b=64),
                sc["xsT"][:, g * 512:(g + 1) * 512].rearrange("p (a b) -> p a b", b=64),
                _bc(dcol_bc[:, g * 8:g * 8 + 8], [[64, 128], [1, 8], [0, 64]]),
                OP.mult), "vE")
        sec("vector", v_gA)
        v_ga = C.cnt["vE"]
        X["expfree"][gg] = v_ga

        def t_iv(t, g=g, gg=gg, par=par, v_ga=v_ga):
            C.wait(t, "vE", v_ga)
            C.bank_guard(t, "pa1")
            for i in range(8):
                t.matmul(pa[1][:, i * 64:(i + 1) * 64],
                         sc["mtil"][gg % 2][:, i * 128:(i + 1) * 128],
                         sc["xsTd"][gg % 2][:, i * 64:(i + 1) * 64],
                         start=(i == 0), stop=False)
            C.bank_guard(t, "pa0")
            if X["hev"].get(g):
                t.wait_ge(C.sems["vE"], X["hev"][g])
            for i, r in enumerate(heads):
                mm = t.matmul(pa[0][:, i * 64:(i + 1) * 64],
                              bmcf[:, TT + ch * 128:TT + (ch + 1) * 128],
                              h_bf[:, r * 64:(r + 1) * 64],
                              start=(i == 0), stop=(i == 7))
            C.inc(mm, "tE")
            key = f"pd{g % 2}"
            C.bank_guard(t, key)
            C.inc(t.matmul(pdm[:, (g % 2) * 512:(g % 2) * 512 + 512],
                           sc["vf16"][par][:], e2_f16[:, g * 512:(g + 1) * 512],
                           start=True, stop=True), "tE")
        sec("tensor", t_iv)
        t_ve = C.cnt["tE"]
        t_int = t_ve - 1
        X["mfree"][gg] = t_int

        def v_gB(v, g=g, t_ve=t_ve, t_int=t_int):
            v.wait_ge(C.sems["tE"], t_ve)
            v.tensor_copy(sc["vexp"][:], pdm[:, (g % 2) * 512:(g % 2) * 512 + 512])
            C.bank_mark(f"pd{g % 2}", "vE")
            C.inc(v.scalar_tensor_tensor(sc["ytmp"][:], pa[0][:, 0:512], 1.0,
                                         sc["vexp"][:], OP.mult, OP.mult), "vE")
            C.bank_mark("pa0", "vE")
        sec("vector", v_gB)
        v_yt = C.cnt["vE"]

        def t_inj(t, gg=gg, v_yt=v_yt):
            C.wait(t, "vE", v_yt)
            t.matmul(pa[1][:, 0:512], ident_bf[:], sc["ytmp"][:], start=False,
                     stop=False)
            C.inc(t.matmul(pa[1][:, 0:512], ident_bf[:], sc["dxst"][:],
                           start=False, stop=True), "tE")
            X["dxs"][gg] = C.cnt["tE"]
        sec("tensor", t_inj)
        t_y2 = C.cnt["tE"]

        def v_gC(v, g=g, gg=gg, par=par, t_y2=t_y2):
            v.wait_ge(C.sems["tE"], t_y2)
            v.scalar_tensor_tensor(ysb[:, g * 512:(g + 1) * 512], pa[1][:, 0:512],
                                   1.0, zrd[:, g * 512:(g + 1) * 512],
                                   OP.mult, OP.mult)
            C.bank_mark("pa1", "vE")
            v.bn_stats(bnb[g % 2][:, (g // 2) * 6:(g // 2) * 6 + 6],
                       ysb[:, g * 512:(g + 1) * 512])
            if X["st"].get(gg - 1):
                v.wait_ge(C.sems["tE"], X["st"][gg - 1])
            v.tensor_tensor(
                sc["xtld"][:].rearrange("p (a b) -> p a b", b=64),
                sc["xsTd"][gg % 2][:].rearrange("p (a b) -> p a b", b=64),
                _bc(sc["t64a"][:, g * 8:g * 8 + 8], [[64, 128], [1, 8], [0, 64]]),
                OP.mult)
            C.inc(v.tensor_tensor(
                sc["hdec"][:].rearrange("p (a b) -> p a b", b=64),
                h_bf[:, g * 512:(g + 1) * 512].rearrange("p (a b) -> p a b", b=64),
                _bc(sc["sbc"][:, g * 8:g * 8 + 8], [[64, 128], [1, 8], [0, 64]]),
                OP.mult), "vE")
        sec("vector", v_gC)
        v_xt = C.cnt["vE"]

        def t_st(t, g=g, gg=gg, par=par, v_xt=v_xt):
            C.wait(t, "vE", v_xt)
            C.bank_guard(t, "pa2")
            for i in range(8):
                t.matmul(pa[2][:, i * 64:(i + 1) * 64], sc["bmt"][par][:],
                         sc["xtld"][:, i * 64:(i + 1) * 64],
                         start=(i == 0), stop=False)
            C.inc(t.matmul(pa[2][:, 0:512], ident_bf[:], sc["hdec"][:],
                           start=False, stop=True), "tE")
            X["st"][gg] = C.cnt["tE"]
        sec("tensor", t_st)
        t_hu = C.cnt["tE"]
        if g == 7:
            X["st_last"] = t_hu

        def v_h(v, g=g, t_hu=t_hu):
            v.wait_ge(C.sems["tE"], t_hu)
            C.inc(v.tensor_copy(h_bf[:, g * 512:(g + 1) * 512], pa[2][:, 0:512]),
                  "vE")
            C.bank_mark("pa2", "vE")
            X["hev"][g] = C.cnt["vE"]
        sec("vector", v_h)

    X["vyx_last"] = C.cnt["vE"]
    X["zrd_free"][par] = C.cnt["vE"]
    X["base_g"] += 8


def _scan_tail(nc, C, sec, tok0, ch, sc, nw_bc, ysb, yhatF, ident_bf, smalls,
               sm64, bnb, pa, pb, X):
    def v_tail(v):
        v.bn_aggr(smalls[:, 12:14], bnb[0][:, 0:24])
        v.bn_aggr(bnb[1][:, 18:20], bnb[1][:, 0:24])
        mA, vA = smalls[:, 12:13], smalls[:, 13:14]
        mB, vB = bnb[1][:, 18:19], bnb[1][:, 19:20]
        t0, t1 = smalls[:, 16:17], smalls[:, 17:18]
        v.tensor_tensor(smalls[:, 8:9], mA, mB, OP.add)
        v.tensor_scalar(smalls[:, 8:9], smalls[:, 8:9], 0.5, None, OP.mult)
        v.tensor_tensor(t0, mA, mB, OP.subtract)
        v.tensor_tensor(t0, t0, t0, OP.mult)
        v.tensor_tensor(t1, vA, vB, OP.add)
        v.tensor_scalar(t1, t1, 0.5, None, OP.mult)
        v.scalar_tensor_tensor(smalls[:, 9:10], t0, 0.25, t1, OP.mult, OP.add)
        v.tensor_tensor(t0, smalls[:, 8:9], smalls[:, 8:9], OP.mult)
        v.tensor_tensor(smalls[:, 9:10], smalls[:, 9:10], t0, OP.add)
        v.tensor_scalar(smalls[:, 9:10], smalls[:, 9:10], EPS, None, OP.add)
        C.inc(v.tensor_copy(smalls[:, 18:19], smalls[:, 9:10]), "vE")
    sec("vector", v_tail)
    v_t1 = C.cnt["vE"]

    def s_tail(s, v_t1=v_t1):
        C.wait(s, "vE", v_t1)
        if X["vt2"]:
            C.wait(s, "vE", X["vt2"])
        C.inc(_rstd_finish(s, smalls[:, 18:19], smalls[:, 10:11],
                           smalls[:, 11:12]), "sE")
    sec("scalar", s_tail)
    s_t1 = C.cnt["sE"]

    def v_tail2(v, s_t1=s_t1):
        C.wait(v, "sE", s_t1)
        if X["ytr"]:
            v.wait_ge(C.sems["tE"], X["ytr"])
        C.inc(v.scalar_tensor_tensor(sc["yhat_tok"][:], ysb[:], smalls[:, 10:11],
                                     nw_bc[:], OP.mult, OP.mult), "vE")
    sec("vector", v_tail2)
    v_t2 = C.cnt["vE"]
    X["vt2"] = v_t2

    for jb in range(4):
        def tf2(t, jb=jb, v_t2=v_t2):
            if jb == 0:
                C.wait(t, "vE", v_t2)
            key = f"pb{jb % 2}"
            C.bank_guard(t, key)
            for i in range(8):
                c = jb * 8 + i
                mm = t.transpose(pb[jb % 2][:, i * 128:(i + 1) * 128],
                                 sc["yhat_tok"][:, c * 128:(c + 1) * 128],
                                 ident_bf[:])
            C.inc(mm, "tE")
        sec("tensor", tf2)
        t_tr2 = C.cnt["tE"]

        def sg2(v, jb=jb, t_tr2=t_tr2):
            v.wait_ge(C.sems["tE"], t_tr2)
            dstap = yhatF[:, jb * 8 * TT:(jb * 8 + 8) * TT] \
                .rearrange("p (c t) -> p c t", t=TT)[:, :, ch * 128:(ch + 1) * 128]
            C.inc(v.tensor_copy(
                dstap, pb[jb % 2][:, 0:1024].rearrange("p (c t) -> p c t", t=128)),
                "vE")
            C.bank_mark(f"pb{jb % 2}", "vE")
        sec("vector", sg2)
    X["ytr"] = C.cnt["tE"]


def _e_outproj(nc, C, sec, tok0, w_out, yhatF, wbuf, x1stg, x1spill, pa):
    dw_at = {}
    mm_done = {}
    do_at = {}

    def pre(g):
        def f(gp, g=g):
            col0 = g * 256
            if g >= 2:
                gp.wait_ge(C.sems["tE"], mm_done[g - 2])
            dw_at[g] = C.dma(gp.dma_start(
                wbuf[g % 2][:, 0:32 * 256].rearrange("p (c f) -> p c f", f=256),
                w_out.rearrange("(c p) f -> p c f", p=128)[:, :, col0:col0 + 256]),
                "dwa" if g % 2 == 0 else "dwb")
        sec("gpsimd", f)

    pre(0)
    pre(1)
    for g in range(16):
        col0 = g * 256
        for b in range(NB):
            idx = g * NB + b

            def tf(t, g=g, b=b, idx=idx):
                if b == 0:
                    C.wait(t, "dwa" if g % 2 == 0 else "dwb")
                C.bank_guard(t, f"pa{idx % 3}")
                for c in range(32):
                    mm = t.matmul(pa[idx % 3][:, 0:256],
                                  yhatF[:, c * TT + b * 128:c * TT + (b + 1) * 128],
                                  wbuf[g % 2][:, c * 256:(c + 1) * 256],
                                  start=(c == 0), stop=(c == 31))
                C.inc(mm, "tE")
                mm_done[g] = C.cnt["tE"]
            sec("tensor", tf)
            t_mm = C.cnt["tE"]

            def sf(s, idx=idx, t_mm=t_mm):
                C.wait(s, "tE", t_mm)
                if idx >= 2:
                    C.wait(s, "doa" if idx % 2 == 0 else "dob")
                C.inc(s.activation(x1stg[idx % 2][:, 0:256], pa[idx % 3][:, 0:256],
                                   AF.Copy), "sE")
                C.bank_mark(f"pa{idx % 3}", "sE")
            sec("scalar", sf)
            s_x1 = C.cnt["sE"]

            def df(sy, b=b, col0=col0, idx=idx, s_x1=s_x1):
                C.wait(sy, "sE", s_x1)
                do_at[idx] = C.dma(sy.dma_start(
                    x1spill[tok0 + b * 128:tok0 + (b + 1) * 128, col0:col0 + 256],
                    x1stg[idx % 2][:, 0:256]), "doa" if idx % 2 == 0 else "dob")
            sec("sync", df)
        if g + 2 < 16:
            pre(g + 2)


def _f_prep(nc, C, sec, ffn_a, ffn_b, ffn_wq, ffn_keys, wbuf, awT, bwT,
            wqT, keysT, ident_bf, pa, pb):
    def d0(gp):
        for h in range(2):
            C.dma(gp.dma_start(
                wbuf[h][:, 0:8192].rearrange("p (j d) -> p j d", d=4096),
                ffn_a.rearrange("(j p) d -> p j d", p=128)[:, 2 * h:2 * h + 2, :]),
                "dw")
    sec("gpsimd", d0)
    dw_a = C.cnt["dw"]

    for cb in range(16):
        def tf(t, cb=cb):
            if cb == 0:
                C.wait(t, "dw", dw_a)
            key = f"pb{cb % 2}"
            C.bank_guard(t, key)
            for cc in range(2):
                c = cb * 2 + cc
                for j in range(4):
                    mm = t.transpose(
                        pb[cb % 2][:, cc * 512 + j * 128:cc * 512 + (j + 1) * 128],
                        wbuf[j // 2][:, (j % 2) * 4096 + c * 128:
                                     (j % 2) * 4096 + (c + 1) * 128],
                        ident_bf[:])
            C.inc(mm, "tE")
        sec("tensor", tf)
        t_tr = C.cnt["tE"]

        def sg(s, cb=cb, t_tr=t_tr):
            s.wait_ge(C.sems["tE"], t_tr)
            C.inc(s.activation(awT[:, cb * 1024:(cb + 1) * 1024],
                               pb[cb % 2][:, 0:1024], AF.Copy), "sE")
            C.bank_mark(f"pb{cb % 2}", "sE")
        sec("scalar", sg)
    t_a_done = C.cnt["tE"]

    def d1(gp, t_a_done=t_a_done):
        gp.wait_ge(C.sems["tE"], t_a_done)
        for h in range(2):
            C.dma(gp.dma_start(
                wbuf[h][:, 0:8192].rearrange("p (j n) -> p j n", n=512),
                ffn_b.rearrange("(j p) n -> p j n", p=128)[:, 16 * h:16 * h + 16, :]),
                "dw")
    sec("gpsimd", d1)
    dw_b = C.cnt["dw"]

    for nch in range(4):
        for jb in range(4):
            def tf2(t, nch=nch, jb=jb):
                if nch == 0 and jb == 0:
                    C.wait(t, "dw", dw_b)
                key = f"pb{jb % 2}"
                C.bank_guard(t, key)
                for i in range(8):
                    j = jb * 8 + i
                    mm = t.transpose(pb[jb % 2][:, i * 128:(i + 1) * 128],
                                     wbuf[j // 16][:, (j % 16) * 512 + nch * 128:
                                                   (j % 16) * 512 + (nch + 1) * 128],
                                     ident_bf[:])
                C.inc(mm, "tE")
            sec("tensor", tf2)
            t_tr2 = C.cnt["tE"]

            def sg2(s, nch=nch, jb=jb, t_tr2=t_tr2):
                s.wait_ge(C.sems["tE"], t_tr2)
                C.inc(s.activation(
                    bwT[:, nch * 4096 + jb * 1024:nch * 4096 + (jb + 1) * 1024],
                    pb[jb % 2][:, 0:1024], AF.Copy), "sE")
                C.bank_mark(f"pb{jb % 2}", "sE")
            sec("scalar", sg2)
    t_b_done = C.cnt["tE"]

    def d2(gp, t_b_done=t_b_done):
        gp.wait_ge(C.sems["tE"], t_b_done)
        C.dma(gp.dma_start(wbuf[0][0:16, 0:4096], ffn_wq), "dw")
        C.dma(gp.dma_start(wbuf[0][0:64, 4096:4112], ffn_keys), "dw")
    sec("gpsimd", d2)
    dw_q = C.cnt["dw"]

    for jb in range(4):
        def tf3(t, jb=jb):
            if jb == 0:
                C.wait(t, "dw", dw_q)
            key = f"pb{jb % 2}"
            C.bank_guard(t, key)
            for i in range(8):
                c = jb * 8 + i
                mm = t.transpose(pb[jb % 2][:, i * 16:(i + 1) * 16],
                                 wbuf[0][0:16, c * 128:(c + 1) * 128],
                                 ident_bf[0:16, 0:16])
            C.inc(mm, "tE")
        sec("tensor", tf3)
        t_tr3 = C.cnt["tE"]

        def sg3(s, jb=jb, t_tr3=t_tr3):
            s.wait_ge(C.sems["tE"], t_tr3)
            C.inc(s.activation(wqT[:, jb * 128:(jb + 1) * 128],
                               pb[jb % 2][:, 0:128], AF.Copy), "sE")
            C.bank_mark(f"pb{jb % 2}", "sE")
        sec("scalar", sg3)

    def tf4(t):
        C.bank_guard(t, "pb0")
        C.inc(t.transpose(pb[0][0:16, 0:64], wbuf[0][0:64, 4096:4112],
                          ident_bf[0:64, 0:64]), "tE")
    sec("tensor", tf4)
    t_k = C.cnt["tE"]

    def sg4(s, t_k=t_k):
        s.wait_ge(C.sems["tE"], t_k)
        C.inc(s.activation(keysT[:], pb[0][0:16, 0:64], AF.Copy), "sE")
        C.bank_mark("pb0", "sE")
    sec("scalar", sg4)


def _f_block(nc, C, sec, tok0, b, x_in, x1spill, out_t, xblk2, ysb2, x3f,
             x2F, awT, bwT, wqT, keysT, e512_bf, fscr, fscr2, gq, gwork,
             gate_bf, gateT, zhat, zhT, qtok, qT, bnb, smalls, rcols,
             ident_bf, ident_f32, pa, pdm, pb, do_at, Xf):
    t0b = tok0 + b * 128

    def d0(sy, b=b):
        if Xf["aq"]:
            sy.wait_ge(C.sems["tE"], Xf["aq"])
        if Xf["vln"]:
            sy.wait_ge(C.sems["vE"], Xf["vln"])
        C.dma(sy.dma_start(xblk2[:], x_in[t0b:t0b + 128, :]), "dx")
        C.dma(sy.dma_start(x2F[:], x1spill[t0b:t0b + 128, :]), "dx")
    sec("sync", d0)
    dx_b = C.cnt["dx"]

    def v0(v, dx_b=dx_b):
        C.wait(v, "dx")
        if Xf["pe_last"]:
            v.wait_ge(C.sems["tE"], Xf["pe_last"])
        C.wait(v, "do")
        v.tensor_tensor(ysb2[:], x2F[:, 0:H], xblk2[:], OP.add)
        _ln_stats(C, v, bnb, ysb2[:], H, smalls[:, 8:9], smalls[:, 9:10],
                  smalls, 12, want_meansq=False)
        C.inc(v.tensor_copy(smalls[:, 18:19], smalls[:, 9:10]), "vE")
    sec("vector", v0)
    v_ln = C.cnt["vE"]
    Xf["vln"] = v_ln

    def s0(s, v_ln=v_ln):
        C.wait(s, "vE", v_ln)
        C.inc(_rstd_finish(s, smalls[:, 18:19], smalls[:, 10:11],
                           smalls[:, 11:12]), "sE")
    sec("scalar", s0)
    s_ln = C.cnt["sE"]

    def v1(v, s_ln=s_ln):
        C.wait(v, "sE", s_ln)
        C.inc(v.tensor_scalar(ysb2[:], ysb2[:], smalls[:, 8:9], smalls[:, 10:11],
                              OP.subtract, OP.mult), "vE")
    sec("vector", v1)
    v_x2 = C.cnt["vE"]

    for jb in range(8):
        def tf(t, jb=jb, v_x2=v_x2):
            if jb == 0:
                C.wait(t, "vE", v_x2)
            key = f"pd{jb % 2}"
            C.bank_guard(t, key)
            half = (jb % 2) * 512
            for i in range(4):
                c = jb * 4 + i
                mm = t.transpose(pdm[:, half + i * 128:half + (i + 1) * 128],
                                 ysb2[:, c * 128:(c + 1) * 128], ident_f32[:])
            C.inc(mm, "tE")
        sec("tensor", tf)
        t_tr = C.cnt["tE"]

        def sg(s, jb=jb, t_tr=t_tr):
            s.wait_ge(C.sems["tE"], t_tr)
            C.inc(s.activation(x2F[:, jb * 512:(jb + 1) * 512],
                               pdm[:, (jb % 2) * 512:(jb % 2) * 512 + 512],
                               AF.Copy), "sE")
            C.bank_mark(f"pd{jb % 2}", "sE")
        sec("scalar", sg)
    s_x2f = C.cnt["sE"]

    def t_a(t, s_x2f=s_x2f):
        C.wait(t, "sE", s_x2f)
        C.bank_guard(t, "pa0")
        C.bank_guard(t, "pa1")
        last = None
        for c in range(32):
            t.matmul(pa[0][:, 0:512], x2F[:, c * 128:(c + 1) * 128],
                     awT[:, c * 512:(c + 1) * 512], start=(c == 0), stop=(c == 31))
            last = t.matmul(pa[1][:, 0:16], x2F[:, c * 128:(c + 1) * 128],
                            wqT[:, c * 16:(c + 1) * 16], start=(c == 0),
                            stop=(c == 31))
        C.inc(last, "tE")
    sec("tensor", t_a)
    t_aq = C.cnt["tE"]
    Xf["aq"] = t_aq

    def s_q(v, t_aq=t_aq):
        C.wait(v, "tE", t_aq)
        C.inc(v.tensor_copy(qtok[:], pa[1][:, 0:16]), "vE")
        C.bank_mark("pa1", "vE")
    sec("vector", s_q)
    s_qe = C.cnt["vE"]

    def t_q(t, s_qe=s_qe):
        C.wait(t, "vE", s_qe)
        C.bank_guard(t, "pb0")
        C.inc(t.transpose(pb[0][0:16, 0:128], qtok[:], ident_bf[:]), "tE")
    sec("tensor", t_q)
    t_qt = C.cnt["tE"]

    def s_q2(v, t_qt=t_qt):
        C.wait(v, "tE", t_qt)
        C.inc(v.tensor_copy(qT[:], pb[0][0:16, 0:128]), "vE")
        C.bank_mark("pb0", "vE")
    sec("vector", s_q2)
    s_qt = C.cnt["vE"]

    def t_sc(t, s_qt=s_qt):
        C.wait(t, "vE", s_qt)
        C.bank_guard(t, "pa2")
        C.inc(t.matmul(pa[2][:, 0:64], qT[:], keysT[:], start=True, stop=True),
              "tE")
    sec("tensor", t_sc)
    t_s = C.cnt["tE"]

    def v_r(v, t_s=t_s):
        C.wait(v, "tE", t_s)
        v.tensor_copy(gq[:], pa[2][:, 0:64])
        C.bank_mark("pa2", "vE")
        v.tensor_copy(gwork[:], gq[:])
        for it in range(15):
            v.reduce_max(rcols[0][:], gwork[:], axis=AX)
            if it == 0:
                v.tensor_copy(rcols[1][:], rcols[0][:])
            v.tensor_scalar(fscr2[:, 0:64], gwork[:], rcols[0][:], None, OP.is_ge)
            v.scalar_tensor_tensor(gwork[:], fscr2[:, 0:64], -1e30, gwork[:],
                                   OP.mult, OP.add)
        v.reduce_max(rcols[2][:], gwork[:], axis=AX)
        C.inc(v.tensor_scalar(rcols[1][:], rcols[1][:], -1.0, None, OP.mult), "vE")
    sec("vector", v_r)
    v_rt = C.cnt["vE"]

    def s_r(s, v_rt=v_rt):
        C.wait(s, "vE", v_rt)
        C.inc(s.activation(fscr[:, 0:64], gq[:], AF.Exp, bias=rcols[1][:]), "sE")
    sec("scalar", s_r)
    s_re = C.cnt["sE"]

    def v_g(v, s_re=s_re):
        C.wait(v, "sE", s_re)
        v.tensor_scalar(gwork[:], gq[:], rcols[2][:], None, OP.is_ge)
        v.tensor_tensor(gwork[:], gwork[:], fscr[:, 0:64], OP.mult)
        v.reduce_sum(rcols[3][:], gwork[:], axis=AX)
        v.reciprocal(rcols[0][:], rcols[3][:])
        C.inc(v.tensor_scalar(gate_bf[:], gwork[:], rcols[0][:], 2.0,
                              OP.mult, OP.mult), "vE")
    sec("vector", v_g)
    v_gt = C.cnt["vE"]

    def t_g(t, v_gt=v_gt):
        C.wait(t, "vE", v_gt)
        C.bank_guard(t, "pb1")
        C.inc(t.transpose(pb[1][0:64, 0:128], gate_bf[:], ident_bf[:]), "tE")
    sec("tensor", t_g)
    t_gt = C.cnt["tE"]

    def s_g(v, t_gt=t_gt):
        C.wait(v, "tE", t_gt)
        C.inc(v.tensor_copy(gateT[:], pb[1][0:64, 0:128]), "vE")
        C.bank_mark("pb1", "vE")
    sec("vector", s_g)
    s_gT = C.cnt["vE"]

    def t_ge(t, s_gT=s_gT):
        C.wait(t, "vE", s_gT)
        C.bank_guard(t, "pa2")
        C.inc(t.matmul(pa[2][:, 0:512], gateT[:], e512_bf[:], start=True,
                       stop=True), "tE")
    sec("tensor", t_ge)
    t_gx = C.cnt["tE"]

    def s_ge(v, t_gx=t_gx):
        C.wait(v, "tE", t_gx)
        C.inc(v.tensor_copy(fscr[:, 0:512], pa[2][:, 0:512]), "vE")
        C.bank_mark("pa2", "vE")
    sec("vector", s_ge)
    s_gx = C.cnt["vE"]

    def v_z(v, s_gx=s_gx):
        C.wait(v, "vE", s_gx)
        C.inc(v.scalar_tensor_tensor(zhat[:], pa[0][:, 0:512], 1.0,
                                     fscr[:, 0:512], OP.mult, OP.mult), "vE")
        C.bank_mark("pa0", "vE")
    sec("vector", v_z)
    v_zh = C.cnt["vE"]

    def t_zt(t, v_zh=v_zh):
        C.wait(t, "vE", v_zh)
        C.bank_guard(t, "pb0")
        last = None
        for i in range(4):
            last = t.transpose(pb[0][:, i * 128:(i + 1) * 128],
                               zhat[:, i * 128:(i + 1) * 128], ident_bf[:])
        C.inc(last, "tE")
    sec("tensor", t_zt)
    t_zT = C.cnt["tE"]

    def s_zt(v, t_zT=t_zT):
        C.wait(v, "tE", t_zT)
        C.inc(v.tensor_copy(zhT[:], pb[0][:, 0:512]), "vE")
        C.bank_mark("pb0", "vE")
    sec("vector", s_zt)
    s_zTT = C.cnt["vE"]

    for og in range(8):
        def t_b(t, og=og, s_zTT=s_zTT):
            if og == 0:
                C.wait(t, "vE", s_zTT)
            key = f"pd{og % 2}"
            C.bank_guard(t, key)
            half = (og % 2) * 512
            last = None
            for nch in range(4):
                last = t.matmul(pdm[:, half:half + 512],
                                zhT[:, nch * 128:(nch + 1) * 128],
                                bwT[:, nch * 4096 + og * 512:
                                    nch * 4096 + (og + 1) * 512],
                                start=(nch == 0), stop=(nch == 3))
            C.inc(last, "tE")
        sec("tensor", t_b)
        t_bo = C.cnt["tE"]
        Xf["pe_last"] = t_bo

        def v_o(v, og=og, t_bo=t_bo):
            C.wait(v, "tE", t_bo)
            half = (og % 2) * 512
            C.inc(v.scalar_tensor_tensor(x3f[:, og * 512:(og + 1) * 512],
                                         pdm[:, half:half + 512], 1.0,
                                         ysb2[:, og * 512:(og + 1) * 512],
                                         OP.mult, OP.add), "vE")
            C.bank_mark(f"pd{og % 2}", "vE")
        sec("vector", v_o)

    def v_l2(v):
        _ln_stats(C, v, bnb, x3f[:], H, smalls[:, 8:9], smalls[:, 9:10],
                  smalls, 12, want_meansq=False)
        C.inc(v.tensor_copy(smalls[:, 18:19], smalls[:, 9:10]), "vE")
    sec("vector", v_l2)
    v_l2a = C.cnt["vE"]

    def s_l2(s, v_l2a=v_l2a):
        C.wait(s, "vE", v_l2a)
        C.inc(_rstd_finish(s, smalls[:, 18:19], smalls[:, 10:11],
                           smalls[:, 11:12]), "sE")
    sec("scalar", s_l2)
    s_l2a = C.cnt["sE"]

    def v_out(v, b=b, s_l2a=s_l2a):
        C.wait(v, "sE", s_l2a)
        C.inc(v.tensor_scalar(x3f[:], x3f[:], smalls[:, 8:9], smalls[:, 10:11],
                              OP.subtract, OP.mult), "vE")
    sec("vector", v_out)
    v_ot = C.cnt["vE"]

    def d_out(sy, b=b, v_ot=v_ot):
        C.wait(sy, "vE", v_ot)
        do_at[b] = C.dma(sy.dma_start(out_t[t0b:t0b + 128, :], x3f[:]), "do")
    sec("sync", d_out)


def kernel(**inputs):
    _ = inputs.get("L0_attn_residual")
    x = np.ascontiguousarray(inputs["L0_attn_hidden_states"], dtype=np.float32)
    nc = build()
    base = {
        "w_in": np.ascontiguousarray(inputs["L0_attn_in_proj_w"], np.float32),
        "conv_w": np.ascontiguousarray(inputs["L0_attn_conv_w"], np.float32),
        "a_log": np.ascontiguousarray(inputs["L0_attn_A_log"], np.float32).reshape(1, NH),
        "dt_bias": np.ascontiguousarray(inputs["L0_attn_dt_bias"], np.float32).reshape(1, NH),
        "d_vec": np.ascontiguousarray(inputs["L0_attn_D"], np.float32).reshape(1, NH),
        "ln_w": np.ascontiguousarray(inputs["L0_attn_ln_w"], np.float32).reshape(1, H),
        "norm_w": np.ascontiguousarray(inputs["L0_attn_norm_w"], np.float32).reshape(1, DS),
        "mup": np.ascontiguousarray(inputs["L0_attn_mup_vector"], np.float32).reshape(1, PROJ),
        "w_out": np.ascontiguousarray(inputs["L0_attn_out_proj_w"], np.float32),
        "ffn_a": np.ascontiguousarray(inputs["L0_ffn_A_w"], np.float32),
        "ffn_b": np.ascontiguousarray(inputs["L0_ffn_B_w"], np.float32),
        "ffn_wq": np.ascontiguousarray(inputs["L0_ffn_Wq_w"], np.float32),
        "ffn_keys": np.ascontiguousarray(inputs["L0_ffn_keys"], np.float32),
    }
    in_maps = [dict(base, x=np.ascontiguousarray(x[c])) for c in range(NCORES)]
    res = run_bass_kernel_spmd(nc, in_maps, core_ids=list(range(NCORES)))
    return np.stack([res.results[c]["out"] for c in range(NCORES)], axis=0)


# version 07
# speedup vs baseline: 1.4485x; 1.0273x over previous
import contextlib
import numpy as np
import concourse.bass as bass
import concourse.mybir as mybir
from concourse.bass_utils import run_bass_kernel_spmd
from concourse.alu_op_type import AluOpType as OP

F32 = mybir.dt.float32
F16 = mybir.dt.float16
BF16 = mybir.dt.bfloat16
AF = mybir.ActivationFunctionType
AX = mybir.AxisListType.X

L = 2048
H = 4096
DS = 4096
PROJ = 8512
CONVD = 4352
NH = 64
TT = 1024
NB = 8
NT = L // TT
EPS = 1e-5
NCORES = 8
EXT = TT + 3

ALL_ENG = ["sync", "gpsimd", "tensor", "vector", "scalar"]

_DRAIN_OPS = frozenset({
    "tensor_scalar", "tensor_tensor", "scalar_tensor_tensor", "tensor_copy",
    "tensor_tensor_scan", "reduce_max", "reduce_sum", "tensor_reduce",
    "bn_stats", "bn_aggr", "reciprocal", "memset", "activation", "select",
    "copy",
})


class _AutoDrain:

    def __init__(self, eng):
        self._e = eng
        self._w = set()
        self._r = set()

    def __getattr__(self, name):
        attr = getattr(self._e, name)
        if name in _DRAIN_OPS:
            def w(*a, **k):
                aps = [x for x in list(a) + list(k.values())
                       if isinstance(x, bass.AP)]
                names = [ap.tensor.name for ap in aps]
                out, ins = set(names[:1]), set(names[1:])
                if (ins & self._w) or (out & self._w) or (out & self._r):
                    self._e.drain()
                    self._w.clear()
                    self._r.clear()
                self._w |= out
                self._r |= ins
                return attr(*a, **k)
            return w
        return attr


def bcast_ap(dram_ap, npart, cols):
    return bass.AP(dram_ap.tensor, 0, [[0, npart], [1, cols]])


class Ctx:

    def __init__(self, nc, es):
        self.nc = nc
        self.sems = {}
        self.cnt = {}
        for name in ["dw", "dwa", "dwb", "dx", "do", "doa", "dob", "tE", "vE",
                     "sE", "gE"]:
            self.sems[name] = es.enter_context(nc.semaphore(name))
            self.cnt[name] = 0
        self.bank_rd = {}

    def inc(self, instr, name, n=1):
        instr.then_inc(self.sems[name], n)
        self.cnt[name] += n
        return self.cnt[name]

    def dma(self, instr, name):
        return self.inc(instr, name, 16)

    def wait(self, eng, name, val=None):
        v = self.cnt[name] if val is None else val
        if v:
            eng.wait_ge(self.sems[name], v)

    def bank_guard(self, eng, key):
        if key in self.bank_rd:
            sem, val = self.bank_rd[key]
            if val:
                eng.wait_ge(self.sems[sem], val)

    def bank_mark(self, key, sem):
        self.bank_rd[key] = (sem, self.cnt[sem])


def build(debug=()):
    nc = bass.Bass("TRN2", target_bir_lowering=False)
    debug = set(debug)

    x_in = nc.dram_tensor("x", [L, H], F32, kind="ExternalInput").ap()
    w_in = nc.dram_tensor("w_in", [H, PROJ], F32, kind="ExternalInput").ap()
    conv_w = nc.dram_tensor("conv_w", [CONVD, 4], F32, kind="ExternalInput").ap()
    a_log = nc.dram_tensor("a_log", [1, NH], F32, kind="ExternalInput").ap()
    dt_bias = nc.dram_tensor("dt_bias", [1, NH], F32, kind="ExternalInput").ap()
    d_vec = nc.dram_tensor("d_vec", [1, NH], F32, kind="ExternalInput").ap()
    ln_w = nc.dram_tensor("ln_w", [1, H], F32, kind="ExternalInput").ap()
    norm_w = nc.dram_tensor("norm_w", [1, DS], F32, kind="ExternalInput").ap()
    mup = nc.dram_tensor("mup", [1, PROJ], F32, kind="ExternalInput").ap()
    w_out = nc.dram_tensor("w_out", [DS, H], F32, kind="ExternalInput").ap()
    ffn_a = nc.dram_tensor("ffn_a", [512, H], F32, kind="ExternalInput").ap()
    ffn_b = nc.dram_tensor("ffn_b", [H, 512], F32, kind="ExternalInput").ap()
    ffn_wq = nc.dram_tensor("ffn_wq", [16, H], F32, kind="ExternalInput").ap()
    ffn_keys = nc.dram_tensor("ffn_keys", [NH, 16], F32, kind="ExternalInput").ap()

    out_t = nc.dram_tensor("out", [L, H], F32, kind="ExternalOutput").ap()

    def scratch(name, shape, dtype):
        if name in debug:
            return nc.dram_tensor(name, shape, dtype, kind="ExternalOutput").ap()
        return nc.dram_tensor(name, shape, dtype).ap()

    zspill = scratch("zspill", [L, DS], BF16)
    xsspill = scratch("xsspill", [DS, L], BF16)
    x1spill = scratch("x1spill", [L, H], BF16)

    es = contextlib.ExitStack()
    with es:
        C = Ctx(nc, es)
        block = es.enter_context(nc.Block())

        def sec(eng_name, fn):
            if eng_name in ("vector", "scalar"):
                getattr(block, eng_name)(lambda e: fn(_AutoDrain(e)))
            else:
                getattr(block, eng_name)(fn)

        def barrier():
            snap = dict(C.cnt)
            for e in ALL_ENG:
                def f(eng, snap=snap):
                    for name, v in snap.items():
                        if v:
                            eng.wait_ge(C.sems[name], v)
                sec(e, f)

        A = lambda n, sh, dt: es.enter_context(nc.sbuf_tensor(n, sh, dt))

        ones_f32 = A("ones_f32", [128, 128], F32)
        ones_f16 = A("ones_f16", [128, 128], F16)
        ident_bf = A("ident_bf", [128, 128], BF16)
        ident_f32 = A("ident_f32", [128, 128], F32)
        tri_f16 = A("tri_f16", [128, 128], F16)
        su_f16 = A("su_f16", [128, 128], F16)
        tri_f32 = A("tri_f32", [128, 128], F32)
        e2_f16 = A("e2_f16", [64, DS], F16)
        e512_bf = A("e512_bf", [64, 512], BF16)
        dbias_bc = A("dbias_bc", [128, NH], F32)
        nega_bc = A("nega_bc", [128, NH], F32)
        dcol_bc = A("dcol_bc", [128, NH], F32)
        convwF = A("convwF", [128, 34 * 4], F32)
        iota_f = A("iota_f", [128, 128], F32)
        iota_c = A("iota_c", [128, 1], F32)
        smalls = A("smalls", [128, 32], F32)
        sm64 = A("sm64", [64, 16], F32)
        dtT = A("dtT", [128, NB * NH], F32)
        h_bf = A("h_bf", [128, DS], BF16)
        tailsave = A("tailsave", [128, 34 * 3], BF16)
        bmcf = A("bmcf", [128, 2 * TT], BF16)
        wbuf = [A(f"wbuf{i}", [128, 8192], BF16) for i in range(2)]
        bnb = [A(f"bnb{i}", [128, 24], F32) for i in range(2)]

        pa = [es.enter_context(nc.psum_tensor(f"pa{i}", [128, 512], F32)) for i in range(4)]
        pdm = es.enter_context(nc.psum_tensor("pdm", [128, 1024], F32))
        pb = [es.enter_context(nc.psum_tensor(f"pb{i}", [128, 1024], BF16)) for i in range(2)]

        with contextlib.ExitStack() as sp:
            e2scr = sp.enter_context(nc.sbuf_tensor("e2scr", [64, DS], F32))

            def setup_sync(s):
                C.dma(s.dma_start(convwF[:].rearrange("p (c k) -> p c k", k=4),
                                  conv_w.rearrange("(c p) k -> p c k", p=128)), "dx")
                C.dma(s.dma_start(dbias_bc[:], bcast_ap(dt_bias, 128, NH)), "dx")
                C.dma(s.dma_start(nega_bc[:], bcast_ap(a_log, 128, NH)), "dx")
                C.dma(s.dma_start(dcol_bc[:], bcast_ap(d_vec, 128, NH)), "dx")
            sec("sync", setup_sync)

            def setup_gp(g):
                g.iota(iota_f[:], [[1, 128]], base=0, channel_multiplier=0,
                       allow_small_or_imprecise_dtypes=True)
                g.iota(iota_c[:], [[0, 1]], base=0, channel_multiplier=1,
                       allow_small_or_imprecise_dtypes=True)
                C.inc(g.iota(e2scr[:, :].rearrange("p (a b) -> p a b", b=64),
                             [[1, 64], [0, 64]], base=0, channel_multiplier=0,
                             allow_small_or_imprecise_dtypes=True), "gE")
            sec("gpsimd", setup_gp)

            def setup_v(v):
                C.wait(v, "gE")
                C.wait(v, "dx")
                v.memset(ones_f32[:], 1.0)
                v.memset(ones_f16[:], 1.0)
                v.memset(h_bf[:], 0.0)
                v.memset(tailsave[:], 0.0)
                v.tensor_scalar(ident_f32[:], iota_f[:], iota_c[:], None, OP.is_equal)
                v.tensor_copy(ident_bf[:], ident_f32[:])
                v.tensor_scalar(tri_f32[:], iota_f[:], iota_c[:], None, OP.is_ge)
                v.tensor_copy(tri_f16[:], tri_f32[:])
                v.tensor_scalar(su_f16[:], iota_f[:], iota_c[:], None, OP.is_lt)
                v.tensor_scalar(e2scr[:, :], e2scr[:, :], iota_c[0:64, :],
                                None, OP.is_equal)
                C.inc(v.tensor_copy(e2_f16[:], e2scr[:, :]), "vE")
            sec("vector", setup_v)
            v_e2 = C.cnt["vE"]

            def setup_gp2(g):
                C.wait(g, "vE", v_e2)
                C.inc(g.iota(e2scr[:, 0:512].rearrange("p (a b) -> p a b", b=8),
                             [[1, 64], [0, 8]], base=0, channel_multiplier=0,
                             allow_small_or_imprecise_dtypes=True), "gE")
            sec("gpsimd", setup_gp2)

            def setup_v2(v):
                C.wait(v, "gE")
                v.tensor_scalar(e2scr[:, 0:512], e2scr[:, 0:512], iota_c[0:64, :],
                                None, OP.is_equal)
                C.inc(v.tensor_copy(e512_bf[:], e2scr[:, 0:512]), "vE")
            sec("vector", setup_v2)
            v_e5 = C.cnt["vE"]

            def setup_s(s):
                C.wait(s, "vE", v_e5)
                C.inc(s.activation(nega_bc[:], nega_bc[:], AF.Exp), "sE")
            sec("scalar", setup_s)

            def setup_v3(v):
                C.wait(v, "sE")
                C.inc(v.tensor_scalar(nega_bc[:], nega_bc[:], -1.0, None, OP.mult),
                      "vE")
            sec("vector", setup_v3)

            barrier()

        for tile in range(NT):
            tok0 = tile * TT

            with contextlib.ExitStack() as ph:
                Aph = lambda n, sh, dt: ph.enter_context(
                    nc.sbuf_tensor(f"{n}_t{tile}", sh, dt))
                xhatF = Aph("xhatF", [128, 32 * TT], BF16)

                with contextlib.ExitStack() as p0s:
                    A0 = lambda n, sh, dt: p0s.enter_context(
                        nc.sbuf_tensor(f"{n}_t{tile}", sh, dt))
                    lnw_bc = A0("lnw_bc", [128, H], F16)
                    xblk = [A0(f"xblk{i}", [128, H], F32) for i in range(2)]
                    xhat_tok = A0("xhat_tok", [128, H], BF16)
                    with nc.named_scope(f"P0_t{tile}"):
                        _p0(nc, C, sec, tok0, x_in, ln_w, lnw_bc, xblk, xhat_tok,
                            xhatF, bnb, smalls, ident_bf, pa, pb)
                        barrier()

                with contextlib.ExitStack() as a1s:
                    A1 = lambda n, sh, dt: a1s.enter_context(
                        nc.sbuf_tensor(f"{n}_t{tile}", sh, dt))
                    mup_bc = A1("mup_bc", [128, PROJ], F16)
                    xbct = [A1(f"xbct{i}", [128, EXT], BF16) for i in range(8)]
                    ztmp = A1("ztmp", [128, 512], F32)
                    bst = A1("bst", [128, 512], BF16)
                    accb = A1("accb", [128, TT], F32)
                    zstg = [A1(f"zstg{i}", [128, 512], BF16) for i in range(2)]
                    xstg = [A1(f"xstg{i}", [128, TT], BF16) for i in range(2)]

                    def mup_load(g):
                        C.dma(g.dma_start(mup_bc[:], bcast_ap(mup, 128, PROJ)), "dw")
                    sec("gpsimd", mup_load)

                    with nc.named_scope(f"A1_t{tile}"):
                        _a1(nc, C, sec, tok0, w_in, xhatF, wbuf, mup_bc, ztmp, bst,
                            xbct, accb, convwF, tailsave, bmcf, dtT, xstg, xsspill,
                            ident_bf, pa, pb)
                        barrier()
                    with nc.named_scope(f"A2_t{tile}"):
                        _a2(nc, C, sec, tok0, w_in, xhatF, wbuf, mup_bc, ztmp,
                            zstg, zspill, pa)
                        barrier()

            with contextlib.ExitStack() as ph:
                Aph = lambda n, sh, dt: ph.enter_context(
                    nc.sbuf_tensor(f"{n}_t{tile}", sh, dt))
                yhatF = Aph("yhatF", [128, 32 * TT], BF16)
                nw_bc = Aph("nw_bc", [128, DS], F16)
                ysb = Aph("ysb", [128, DS], BF16)
                sc = {n: Aph(n, sh, dt) for n, sh, dt in [
                    ("xsT", [128, DS], BF16),
                    ("yhat_tok", [128, DS], BF16),
                    ("vexp", [128, 512], F32),
                    ("ytmp", [128, 512], BF16),
                    ("dxst", [128, 512], BF16),
                    ("yA", [128, 512], BF16),
                    ("xtld", [128, 512], BF16),
                    ("hdec", [128, 512], BF16),
                    ("dtc", [128, NH], F32),
                    ("dta32", [128, NH], F32),
                    ("dta16", [128, NH], F16),
                    ("srow", [1, 64], F32),
                    ("sbc", [128, 64], F32),
                    ("t64a", [128, NH], F32),
                    ("t64b", [128, NH], F32),
                    ("t64c", [128, NH], F32),
                ]}
                sc["cbtm"] = [Aph(f"cbtm{i}", [128, 128], F32) for i in range(2)]
                sc["bmt"] = [Aph(f"bmt{i}", [128, 128], BF16) for i in range(2)]
                sc["vf16"] = [Aph(f"vf16{i}", [64, 128], F16) for i in range(2)]
                sc["expd"] = [Aph(f"expd{i}", [128, 1024], F16) for i in range(2)]
                sc["rhsd"] = [Aph(f"rhsd{i}", [128, 512], F16) for i in range(2)]
                sc["mtil"] = [Aph(f"mtil{i}", [128, 1024], BF16) for i in range(2)]
                sc["xsTd"] = [Aph(f"xsTd{i}", [128, 512], BF16) for i in range(2)]
                x1stg = [Aph(f"x1stg{i}", [128, 512], BF16) for i in range(2)]

                def nw_load(g):
                    C.dma(g.dma_start(nw_bc[:], bcast_ap(norm_w, 128, DS)), "dw")
                sec("gpsimd", nw_load)

                with nc.named_scope(f"SCAN_t{tile}"):
                    X = {"xrd_free": {0: 0, 1: 0}, "zrd_free": {0: 0, 1: 0},
                         "st_last": 0, "scum": 0, "tcu_prev": 0, "vyx_last": 0,
                         "front_v": 0, "vt2": 0, "ytr": 0, "base_g": 0,
                         "dmm": {}, "expfree": {}, "mfree": {}, "dxs": {},
                         "st": {}, "hev": {}}
                    fr = lambda ch: _scan_front(
                        nc, C, sec, tok0, ch, sc, wbuf, xsspill, zspill, dtT,
                        dbias_bc, nega_bc, bmcf, ones_f16, ones_f32, tri_f16,
                        su_f16, tri_f32, ident_bf, ident_f32, sm64, pa, pb, X)
                    fr(0)
                    for ch in range(NB):
                        _scan_groups(nc, C, sec, tok0, ch, sc, wbuf, dcol_bc,
                                     h_bf, bmcf, ysb, e2_f16, bnb, tri_f16,
                                     su_f16, ident_bf, pa, pdm, pb, X)
                        if ch + 1 < NB:
                            fr(ch + 1)
                        _scan_tail(nc, C, sec, tok0, ch, sc, nw_bc, ysb, yhatF,
                                   ident_bf, smalls, sm64, bnb, pa, pb, X)
                    barrier()
                with nc.named_scope(f"E_t{tile}"):
                    _e_outproj(nc, C, sec, tok0, w_out, yhatF, wbuf, x1stg,
                               x1spill, pa)
                    barrier()

            with contextlib.ExitStack() as ph:
                Aph = lambda n, sh, dt: ph.enter_context(
                    nc.sbuf_tensor(f"{n}_t{tile}", sh, dt))
                awT = Aph("awT", [128, 32 * 512], BF16)
                bwT = Aph("bwT", [128, 4 * H], BF16)
                wqT = Aph("wqT", [128, 32 * 16], BF16)
                keysT = Aph("keysT", [16, 64], BF16)
                x2F = Aph("x2F", [128, 32 * 128], BF16)
                x2tok = Aph("x2tok", [128, H], BF16)
                x3f = Aph("x3f", [128, H], F32)
                xblk2 = Aph("xblk2", [128, H], F32)
                ysb2 = Aph("ysb2", [128, H], F32)
                fscr = Aph("fscr", [128, 512], F32)
                fscr2 = Aph("fscr2", [128, 64], F32)
                gq = Aph("gq", [128, 64], F32)
                gwork = Aph("gwork", [128, 64], F32)
                gate_bf = Aph("gate_bf", [128, 64], BF16)
                gateT = Aph("gateT", [64, 128], BF16)
                zhat = Aph("zhat", [128, 512], BF16)
                zhT = Aph("zhT", [128, 512], BF16)
                qtok = Aph("qtok", [128, 16], BF16)
                qT = Aph("qT", [16, 128], BF16)
                rcols = [Aph(f"rcol{i}", [128, 1], F32) for i in range(4)]

                with nc.named_scope(f"FP_t{tile}"):
                    _f_prep(nc, C, sec, ffn_a, ffn_b, ffn_wq, ffn_keys, wbuf,
                            awT, bwT, wqT, keysT, ident_bf, pa, pb)
                    barrier()
                do_at = {}
                Xf = {"aq": 0, "x2Fev": 0, "x2tr": 0}
                with nc.named_scope(f"F_t{tile}"):
                    for b in range(NB):
                        _f_block(nc, C, sec, tok0, b, x_in, x1spill, out_t,
                                 xblk2, ysb2, x3f, x2F, x2tok, awT, bwT, wqT, keysT,
                                 e512_bf, fscr, fscr2, gq, gwork, gate_bf, gateT,
                                 zhat, zhT, qtok, qT, bnb, smalls, rcols, ident_bf,
                                 ident_f32, pa, pdm, pb, do_at, Xf)
                    barrier()

        def fin(s):
            C.wait(s, "do")
            C.wait(s, "doa")
            C.wait(s, "dob")
        sec("sync", fin)

    return nc



def _ln_stats(C, v, bnb, src_ap, ncols, mu_col, m2_col, sm, c0, want_meansq):
    nst = ncols // 512
    for i in range(nst):
        v.bn_stats(bnb[i % 2][:, (i // 2) * 6:(i // 2) * 6 + 6],
                   src_ap[:, i * 512:(i + 1) * 512])
    v.bn_aggr(sm[:, c0:c0 + 2], bnb[0][:, 0:(nst // 2) * 6])
    v.bn_aggr(bnb[1][:, 18:20], bnb[1][:, 0:(nst // 2) * 6])
    mA, vA = sm[:, c0:c0 + 1], sm[:, c0 + 1:c0 + 2]
    mB, vB = bnb[1][:, 18:19], bnb[1][:, 19:20]
    t0, t1 = sm[:, c0 + 4:c0 + 5], sm[:, c0 + 5:c0 + 6]
    v.tensor_tensor(mu_col, mA, mB, OP.add)
    v.tensor_scalar(mu_col, mu_col, 0.5, None, OP.mult)
    v.tensor_tensor(t0, mA, mB, OP.subtract)
    v.tensor_tensor(t0, t0, t0, OP.mult)
    v.tensor_tensor(t1, vA, vB, OP.add)
    v.tensor_scalar(t1, t1, 0.5, None, OP.mult)
    v.scalar_tensor_tensor(m2_col, t0, 0.25, t1, OP.mult, OP.add)
    if want_meansq:
        v.tensor_tensor(t0, mu_col, mu_col, OP.mult)
        v.tensor_tensor(m2_col, m2_col, t0, OP.add)
    v.tensor_scalar(m2_col, m2_col, EPS, None, OP.add)


def _rstd_finish(s, src_col, dst_col, tmp_col):
    s.activation(tmp_col, src_col, AF.Ln)
    return s.activation(dst_col, tmp_col, AF.Exp, scale=-0.5)


def _p0(nc, C, sec, tok0, x_in, ln_w, lnw_bc, xblk, xhat_tok, xhatF, bnb,
        smalls, ident_bf, pa, pb):
    dx_at = {}
    v_stt = {}
    t_done = {}

    def g_lnw(g):
        C.dma(g.dma_start(lnw_bc[:], bcast_ap(ln_w, 128, H)), "dw")
    sec("gpsimd", g_lnw)
    dw_lnw = C.cnt["dw"]

    def d_pre(s):
        for b in range(2):
            dx_at[b] = C.dma(s.dma_start(
                xblk[b % 2][:], x_in[tok0 + b * 128:tok0 + (b + 1) * 128, :]), "dx")
    sec("sync", d_pre)

    for b in range(NB):
        def vf(v, b=b, dxf=None):
            C.wait(v, "dx")
            if b == 0:
                C.wait(v, "dw", dw_lnw)
            _ln_stats(C, v, bnb, xblk[b % 2][:], H, smalls[:, 8:9], smalls[:, 9:10],
                      smalls, 12, want_meansq=True)
            C.inc(v.tensor_copy(smalls[:, 18:19], smalls[:, 9:10]), "vE")
        sec("vector", vf)
        v_bn = C.cnt["vE"]

        def sf(s, b=b):
            C.wait(s, "vE", v_bn)
            C.inc(_rstd_finish(s, smalls[:, 18:19], smalls[:, 10:11],
                               smalls[:, 11:12]), "sE")
        sec("scalar", sf)
        s_rs = C.cnt["sE"]

        def vg(v, b=b):
            C.wait(v, "sE", s_rs)
            if b >= 1:
                C.wait(v, "tE", t_done[b - 1])
            C.inc(v.scalar_tensor_tensor(xhat_tok[:], xblk[b % 2][:],
                                         smalls[:, 10:11], lnw_bc[:],
                                         OP.mult, OP.mult), "vE")
        sec("vector", vg)
        v_stt[b] = C.cnt["vE"]

        if b + 2 < NB:
            def df(s, b=b):
                C.wait(s, "vE", v_stt[b])
                dx_at[b + 2] = C.dma(s.dma_start(
                    xblk[b % 2][:],
                    x_in[tok0 + (b + 2) * 128:tok0 + (b + 3) * 128, :]), "dx")
            sec("sync", df)

        for jb in range(4):
            def tf(t, b=b, jb=jb):
                if jb == 0:
                    C.wait(t, "vE", v_stt[b])
                key = f"pb{jb % 2}"
                C.bank_guard(t, key)
                for i in range(8):
                    c = jb * 8 + i
                    mm = t.transpose(pb[jb % 2][:, i * 128:(i + 1) * 128],
                                     xhat_tok[:, c * 128:(c + 1) * 128], ident_bf[:])
                C.inc(mm, "tE")
            sec("tensor", tf)
            t_tr = C.cnt["tE"]

            def sg(v, b=b, jb=jb, t_tr=t_tr):
                v.wait_ge(C.sems["tE"], t_tr)
                dstap = xhatF[:, jb * 8 * TT:(jb * 8 + 8) * TT] \
                    .rearrange("p (c t) -> p c t", t=TT)[:, :, b * 128:(b + 1) * 128]
                C.inc(v.tensor_copy(
                    dstap, pb[jb % 2][:, 0:1024].rearrange("p (c t) -> p c t", t=128)),
                    "vE")
                C.bank_mark(f"pb{jb % 2}", "vE")
            sec("vector", sg)
        t_done[b] = C.cnt["tE"]


def _a1(nc, C, sec, tok0, w_in, xhatF, wbuf, mup_bc, ztmp, bst, xbct, accb,
        convwF, tailsave, bmcf, dtT, xstg, xsspill, ident_bf, pa, pb):
    n_slab = 18
    dw_at = {}
    mm_done = {}
    ev_done = {}
    tg_at = {}
    conv_tap = {}
    spill_at = {}
    s_silu_at = {}

    def pre(g):
        def f(gp, g=g):
            col0 = 4096 + g * 256
            w = 64 if g == 17 else 256
            if g >= 2:
                gp.wait_ge(C.sems["tE"], mm_done[g - 2])
            dw_at[g] = C.dma(gp.dma_start(
                wbuf[g % 2][:, 0:32 * w].rearrange("p (c f) -> p c f", f=w),
                w_in.rearrange("(c p) f -> p c f", p=128)[:, :, col0:col0 + w]),
                "dwa" if g % 2 == 0 else "dwb")
        sec("gpsimd", f)

    pre(0)
    pre(1)

    for g in range(n_slab):
        col0 = 4096 + g * 256
        w = 64 if g == 17 else 256

        if g < 16:
            def gtail(gp, g=g):
                if 2 * g - 8 >= 0:
                    gp.wait_ge(C.sems["vE"], conv_tap[2 * g + 1 - 8])
                cc0 = 2 * g
                gp.tensor_copy(xbct[cc0 % 8][:, 0:3],
                               tailsave[:, cc0 * 3:cc0 * 3 + 3])
                C.inc(gp.tensor_copy(xbct[(cc0 + 1) % 8][:, 0:3],
                                     tailsave[:, (cc0 + 1) * 3:(cc0 + 1) * 3 + 3]),
                      "gE")
            sec("gpsimd", gtail)
        g_tail_in = C.cnt["gE"]

        for b in range(NB):
            idx = g * NB + b

            def tf(t, g=g, b=b, w=w, idx=idx):
                if b == 0:
                    C.wait(t, "dwa" if g % 2 == 0 else "dwb")
                C.bank_guard(t, f"pa{idx % 3}")
                for c in range(32):
                    mm = t.matmul(pa[idx % 3][:, 0:w],
                                  xhatF[:, c * TT + b * 128:c * TT + (b + 1) * 128],
                                  wbuf[g % 2][:, c * w:(c + 1) * w],
                                  start=(c == 0), stop=(c == 31))
                C.inc(mm, "tE")
                mm_done[g] = C.cnt["tE"]
            sec("tensor", tf)
            t_mm = C.cnt["tE"]

            def vf(v, g=g, b=b, w=w, col0=col0, idx=idx, t_mm=t_mm):
                C.wait(v, "tE", t_mm)
                if g < 17 and idx - 1 in tg_at:
                    C.wait(v, "tE", tg_at[idx - 1])
                if g == 17:
                    C.inc(v.tensor_tensor(dtT[:, b * 64:(b + 1) * 64],
                                          pa[idx % 3][:, 0:64],
                                          mup_bc[:, col0:col0 + 64], OP.mult), "vE")
                else:
                    C.inc(v.tensor_tensor(bst[:, 0:w], pa[idx % 3][:, 0:w],
                                          mup_bc[:, col0:col0 + w], OP.mult), "vE")
                C.bank_mark(f"pa{idx % 3}", "vE")
                ev_done[idx] = C.cnt["vE"]
            sec("vector", vf)
            v_ev = C.cnt["vE"]

            if g < 17:
                def tg(t, g=g, b=b, idx=idx, v_ev=v_ev):
                    C.wait(t, "vE", v_ev)
                    key = f"pb{idx % 2}"
                    C.bank_guard(t, key)
                    for i in range(2):
                        mm = t.transpose(pb[idx % 2][:, i * 128:(i + 1) * 128],
                                         bst[:, i * 128:(i + 1) * 128], ident_bf[:])
                    C.inc(mm, "tE")
                    tg_at[idx] = C.cnt["tE"]
                sec("tensor", tg)
                t_tr = C.cnt["tE"]

                def sg(v, g=g, b=b, idx=idx, t_tr=t_tr):
                    v.wait_ge(C.sems["tE"], t_tr)
                    last = None
                    for i in range(2):
                        cc = 2 * g + i
                        if cc < 32:
                            dst = xbct[cc % 8][:, 3 + b * 128:3 + (b + 1) * 128]
                        else:
                            dst = bmcf[:, (cc - 32) * TT + b * 128:
                                       (cc - 32) * TT + (b + 1) * 128]
                        last = v.tensor_copy(dst, pb[idx % 2][:, i * 128:(i + 1) * 128])
                    C.inc(last, "vE")
                    C.bank_mark(f"pb{idx % 2}", "vE")
                sec("vector", sg)

        if g + 2 < n_slab:
            pre(g + 2)

        if g < 16:
            for i in range(2):
                cc = 2 * g + i
                _conv_chunk(nc, C, sec, tok0, cc, xbct[cc % 8], accb, convwF,
                            tailsave, xstg, xsspill, bmcf, conv_tap, spill_at)
        elif g == 16:
            for i in range(2):
                cc = 32 + i
                _conv_chunk(nc, C, sec, tok0, cc, None, accb, convwF,
                            tailsave, xstg, xsspill, bmcf, conv_tap, spill_at)


def _conv_chunk(nc, C, sec, tok0, cc, buf, accb, convwF, tailsave, xstg,
                xsspill, bmcf, conv_tap, spill_at):
    if cc >= 32:
        data = bmcf[:, (cc - 32) * TT:(cc - 32 + 1) * TT]
    else:
        data = buf[:, 3:3 + TT]
    sE_now = C.cnt["sE"]
    vE_now = C.cnt["vE"]
    gE_now = C.cnt["gE"]

    def vf(v, cc=cc, data=data, sE_now=sE_now, vE_now=vE_now, gE_now=gE_now):
        C.wait(v, "sE", sE_now)
        C.wait(v, "vE", vE_now)
        C.wait(v, "gE", gE_now)
        w = lambda k: convwF[:, cc * 4 + k:cc * 4 + k + 1]
        if cc < 32:
            ext = buf
            v.tensor_scalar(accb[:], ext[:, 0:TT], w(0), None, OP.mult)
            for k in range(1, 3):
                v.scalar_tensor_tensor(accb[:], ext[:, k:k + TT], w(k), accb[:],
                                       OP.mult, OP.add)
            last = v.scalar_tensor_tensor(accb[:], ext[:, 3:3 + TT], w(3), accb[:],
                                          OP.mult, OP.add)
        else:
            ts3 = tailsave[:, cc * 3:cc * 3 + 3]
            v.tensor_scalar(accb[:], data[:], w(3), None, OP.mult)
            for k in range(3):
                sh = 3 - k
                v.scalar_tensor_tensor(accb[:, sh:TT], data[:, 0:TT - sh], w(k),
                                       accb[:, sh:TT], OP.mult, OP.add)
                last = v.scalar_tensor_tensor(accb[:, 0:sh], ts3[:, 3 - sh:3], w(k),
                                              accb[:, 0:sh], OP.mult, OP.add)
        C.inc(last, "vE")
        conv_tap[cc] = C.cnt["vE"]
    sec("vector", vf)
    v_tap = C.cnt["vE"]

    def gp2(g, cc=cc, data=data, v_tap=v_tap):
        C.wait(g, "vE", v_tap)
        C.inc(g.tensor_copy(tailsave[:, cc * 3:cc * 3 + 3], data[:, TT - 3:TT]), "gE")
    sec("gpsimd", gp2)
    g_tail = C.cnt["gE"]

    def sf(s, cc=cc, data=data, v_tap=v_tap, g_tail=g_tail):
        C.wait(s, "vE", v_tap)
        C.wait(s, "gE", g_tail)
        if cc < 32:
            if cc >= 2:
                C.wait(s, "doa" if cc % 2 == 0 else "dob")
            dst = xstg[cc % 2][:]
        else:
            dst = data
        C.inc(s.activation(dst, accb[:], AF.Silu), "sE")
    sec("scalar", sf)
    s_silu = C.cnt["sE"]

    if cc < 32:
        def df(sy, cc=cc, s_silu=s_silu):
            C.wait(sy, "sE", s_silu)
            spill_at[cc] = C.dma(sy.dma_start(
                xsspill[cc * 128:(cc + 1) * 128, tok0:tok0 + TT], xstg[cc % 2][:]),
                "doa" if cc % 2 == 0 else "dob")
        sec("sync", df)


def _a2(nc, C, sec, tok0, w_in, xhatF, wbuf, mup_bc, ztmp, zstg, zspill, pa):
    dw_at = {}
    mm_done = {}
    do_at = {}
    sz_at = {}

    def pre(g):
        def f(gp, g=g):
            col0 = g * 256
            if g >= 2:
                gp.wait_ge(C.sems["tE"], mm_done[g - 2])
            dw_at[g] = C.dma(gp.dma_start(
                wbuf[g % 2][:, 0:32 * 256].rearrange("p (c f) -> p c f", f=256),
                w_in.rearrange("(c p) f -> p c f", p=128)[:, :, col0:col0 + 256]),
                "dwa" if g % 2 == 0 else "dwb")
        sec("gpsimd", f)

    pre(0)
    pre(1)
    for g in range(16):
        col0 = g * 256
        for b in range(NB):
            idx = g * NB + b

            def tf(t, g=g, b=b, idx=idx):
                if b == 0:
                    C.wait(t, "dwa" if g % 2 == 0 else "dwb")
                C.bank_guard(t, f"pa{idx % 3}")
                for c in range(32):
                    mm = t.matmul(pa[idx % 3][:, 0:256],
                                  xhatF[:, c * TT + b * 128:c * TT + (b + 1) * 128],
                                  wbuf[g % 2][:, c * 256:(c + 1) * 256],
                                  start=(c == 0), stop=(c == 31))
                C.inc(mm, "tE")
                mm_done[g] = C.cnt["tE"]
            sec("tensor", tf)
            t_mm = C.cnt["tE"]

            def vf(v, col0=col0, idx=idx, t_mm=t_mm):
                C.wait(v, "tE", t_mm)
                if idx - 1 in sz_at:
                    C.wait(v, "sE", sz_at[idx - 1])
                C.inc(v.tensor_tensor(ztmp[:, 0:256], pa[idx % 3][:, 0:256],
                                      mup_bc[:, col0:col0 + 256], OP.mult), "vE")
                C.bank_mark(f"pa{idx % 3}", "vE")
            sec("vector", vf)
            v_ev = C.cnt["vE"]

            def sf(s, idx=idx, v_ev=v_ev):
                C.wait(s, "vE", v_ev)
                if idx >= 2:
                    C.wait(s, "doa" if idx % 2 == 0 else "dob")
                C.inc(s.activation(zstg[idx % 2][:, 0:256], ztmp[:, 0:256],
                                   AF.Silu), "sE")
                sz_at[idx] = C.cnt["sE"]
            sec("scalar", sf)
            s_z = C.cnt["sE"]

            def df(sy, b=b, col0=col0, idx=idx, s_z=s_z):
                C.wait(sy, "sE", s_z)
                do_at[idx] = C.dma(sy.dma_start(
                    zspill[tok0 + b * 128:tok0 + (b + 1) * 128, col0:col0 + 256],
                    zstg[idx % 2][:, 0:256]), "doa" if idx % 2 == 0 else "dob")
            sec("sync", df)
        if g + 2 < 16:
            pre(g + 2)


def _bc(ap, dims):
    return bass.AP(ap.tensor, ap.offset, dims)


def _scan_front(nc, C, sec, tok0, ch, sc, wbuf, xsspill, zspill, dtT,
                dbias_bc, nega_bc, bmcf, ones_f16, ones_f32, tri_f16,
                su_f16, tri_f32, ident_bf, ident_f32, sm64, pa, pb, X):
    t0c = tok0 + ch * 128
    par = ch % 2
    xrd = wbuf[0][:, par * 4096:par * 4096 + 4096]
    zrd = wbuf[1][:, par * 4096:par * 4096 + 4096]

    def d0(sy, par=par):
        if X["xrd_free"][par]:
            sy.wait_ge(C.sems["tE"], X["xrd_free"][par])
        if X["zrd_free"][par]:
            sy.wait_ge(C.sems["vE"], X["zrd_free"][par])
        C.dma(sy.dma_start(
            xrd.rearrange("p (c t) -> p c t", t=128),
            xsspill.rearrange("(c p) t -> p c t", p=128)[:, :, t0c:t0c + 128]), "dx")
        C.dma(sy.dma_start(zrd, zspill[t0c:t0c + 128, :]), "dx")
    sec("sync", d0)
    dxv = C.cnt["dx"]

    for jb in range(4):
        def tf(t, jb=jb, dxv=dxv):
            if jb == 0:
                C.wait(t, "dx", dxv)
            key = f"pb{jb % 2}"
            C.bank_guard(t, key)
            for i in range(8):
                c = jb * 8 + i
                mm = t.transpose(pb[jb % 2][:, i * 128:(i + 1) * 128],
                                 xrd[:, c * 128:(c + 1) * 128], ident_bf[:])
            C.inc(mm, "tE")
        sec("tensor", tf)
        t_tr = C.cnt["tE"]

        def sg(v, jb=jb, t_tr=t_tr):
            v.wait_ge(C.sems["tE"], t_tr)
            C.inc(v.tensor_copy(sc["xsT"][:, jb * 1024:(jb + 1) * 1024],
                                pb[jb % 2][:, 0:1024]), "vE")
            C.bank_mark(f"pb{jb % 2}", "vE")
        sec("vector", sg)
    X["xrd_free"][par] = C.cnt["tE"]

    def v_dt(v):
        dts = dtT[:, ch * 64:(ch + 1) * 64]
        if X["scum"]:
            C.wait(v, "sE", X["scum"])
        v.tensor_tensor(sc["t64a"][:], dts, dbias_bc[:], OP.add)
        C.inc(v.scalar_tensor_tensor(sc["t64b"][:], sc["t64a"][:], -1.0,
                                     sc["t64a"][:], OP.mult, OP.min), "vE")
    sec("vector", v_dt)
    v_1 = C.cnt["vE"]

    def s_dt(s, v_1=v_1):
        C.wait(s, "vE", v_1)
        s.activation(sc["t64c"][:], sc["t64b"][:], AF.Exp)
        C.inc(s.activation(sc["t64b"][:], sc["t64c"][:], AF.Ln, bias=1.0), "sE")
    sec("scalar", s_dt)
    s_1 = C.cnt["sE"]

    def v_dt2(v, s_1=s_1):
        C.wait(v, "sE", s_1)
        if X["tcu_prev"]:
            v.wait_ge(C.sems["tE"], X["tcu_prev"])
        v.tensor_scalar(sc["t64c"][:], sc["t64a"][:], 0.0, None, OP.max)
        v.tensor_tensor(sc["dtc"][:], sc["t64b"][:], sc["t64c"][:], OP.add)
        v.tensor_tensor(sc["dta32"][:], sc["dtc"][:], nega_bc[:], OP.mult)
        C.inc(v.tensor_copy(sc["dta16"][:], sc["dta32"][:]), "vE")
    sec("vector", v_dt2)
    v_2 = C.cnt["vE"]

    def t_cum(t, v_2=v_2):
        C.wait(t, "vE", v_2)
        C.bank_guard(t, "pa3")
        t.matmul(pa[3][0:64, 0:128], sc["dta16"][:], tri_f16[:], start=True,
                 stop=True)
        t.matmul(pa[3][:, 128:192], su_f16[:], sc["dta16"][:], start=True,
                 stop=True)
        C.inc(t.matmul(pa[3][0:1, 192:256], ones_f16[:, 0:1], sc["dta16"][:],
                       start=True, stop=True), "tE")
    sec("tensor", t_cum)
    t_cu = C.cnt["tE"]
    X["tcu_prev"] = t_cu

    def s_cum(s, t_cu=t_cu, par=par):
        C.wait(s, "tE", t_cu)
        if X["vyx_last"]:
            C.wait(s, "vE", X["vyx_last"])
        s.activation(sc["vf16"][par][:], pa[3][0:64, 0:128], AF.Exp)
        C.inc(s.activation(sc["t64a"][:], pa[3][:, 128:192], AF.Exp), "sE")
        C.bank_mark("pa3", "sE")
    sec("scalar", s_cum)
    s_2 = C.cnt["sE"]
    X["scum"] = s_2

    def v_sb(v, t_cu=t_cu):
        v.wait_ge(C.sems["tE"], t_cu)
        C.inc(v.tensor_copy(sc["srow"][:], pa[3][0:1, 192:256]), "vE")
    sec("vector", v_sb)
    v_sr = C.cnt["vE"]

    def t_sb2(t, v_sr=v_sr):
        C.wait(t, "vE", v_sr)
        C.bank_guard(t, "pa3")
        C.inc(t.matmul(pa[3][:, 448:512], ones_f32[0:1, :], sc["srow"][:],
                       start=True, stop=True), "tE")
    sec("tensor", t_sb2)
    t_sb2a = C.cnt["tE"]

    def s_sb(s, t_sb2a=t_sb2a):
        s.wait_ge(C.sems["tE"], t_sb2a)
        C.inc(s.activation(sc["sbc"][:], pa[3][:, 448:512], AF.Exp), "sE")
        C.bank_mark("pa3", "sE")
    sec("scalar", s_sb)

    def t_cb(t, par=par):
        C.wait(t, "vE")
        C.bank_guard(t, "pa3")
        t.matmul(pa[3][:, 256:384], bmcf[:, ch * 128:(ch + 1) * 128],
                 bmcf[:, TT + ch * 128:TT + (ch + 1) * 128], start=True, stop=True)
        C.bank_guard(t, "pb0")
        C.inc(t.transpose(pb[0][:, 0:128], bmcf[:, ch * 128:(ch + 1) * 128],
                          ident_bf[:]), "tE")
    sec("tensor", t_cb)
    t_cb_at = C.cnt["tE"]

    def v_cb(v, t_cb_at=t_cb_at, par=par):
        v.wait_ge(C.sems["tE"], t_cb_at)
        v.tensor_tensor(sc["cbtm"][par][:], pa[3][:, 256:384], tri_f32[:], OP.mult)
        C.bank_mark("pa3", "vE")
        C.inc(v.tensor_copy(sc["bmt"][par][:], pb[0][:, 0:128]), "vE")
        C.bank_mark("pb0", "vE")
    sec("vector", v_cb)
    X["front_v"] = C.cnt["vE"]


def _scan_groups(nc, C, sec, tok0, ch, sc, wbuf, dcol_bc, h_bf, bmcf, ysb,
                 e2_f16, bnb, tri_f16, su_f16, ident_bf, pa, pdm, pb, X):
    par = ch % 2
    zrd = wbuf[1][:, par * 4096:par * 4096 + 4096]
    front_v = X["front_v"]

    for g in range(8):
        gg = X["base_g"] + g
        heads = list(range(8 * g, 8 * g + 8))

        for half in range(2):
            hh = 2 * gg + half
            h0 = 8 * g + half * 4

            def v_r(v, hh=hh, h0=h0, front_v=front_v):
                C.wait(v, "vE", front_v)
                if X["dmm"].get(hh - 2):
                    v.wait_ge(C.sems["tE"], X["dmm"][hh - 2])
                C.inc(v.tensor_tensor(
                    sc["rhsd"][hh % 2][:].rearrange("p (a b) -> p a b", b=128),
                    _bc(tri_f16[:], [[128, 128], [0, 4], [1, 128]]),
                    _bc(sc["dta32"][:, h0:h0 + 4], [[64, 128], [1, 4], [0, 128]]),
                    OP.mult), "vE")
            sec("vector", v_r)
            v_rh = C.cnt["vE"]

            def t_d(t, hh=hh, v_rh=v_rh):
                C.wait(t, "vE", v_rh)
                key = f"pd{hh % 2}"
                C.bank_guard(t, key)
                half0 = (hh % 2) * 512
                for i in range(4):
                    mm = t.matmul(pdm[:, half0 + i * 128:half0 + (i + 1) * 128],
                                  su_f16[:],
                                  sc["rhsd"][hh % 2][:, i * 128:(i + 1) * 128],
                                  start=(i == 0), stop=(i == 3))
                C.inc(mm, "tE")
            sec("tensor", t_d)
            X["dmm"][hh] = C.cnt["tE"]

        def s_e(s, gg=gg):
            s.wait_ge(C.sems["tE"], X["dmm"][2 * gg + 1])
            if X["expfree"].get(gg - 2):
                C.wait(s, "vE", X["expfree"][gg - 2])
            C.inc(s.activation(sc["expd"][gg % 2][:], pdm[:, 0:1024], AF.Exp),
                  "sE")
            C.bank_mark("pd0", "sE")
            C.bank_mark("pd1", "sE")
        sec("scalar", s_e)
        exp_g = C.cnt["sE"]

        def v_gA(v, g=g, gg=gg, par=par, exp_g=exp_g):
            C.wait(v, "sE", exp_g)
            if X["mfree"].get(gg - 2):
                v.wait_ge(C.sems["tE"], X["mfree"][gg - 2])
            v.tensor_tensor(
                sc["mtil"][gg % 2][:].rearrange("p (a b) -> p a b", b=128),
                sc["expd"][gg % 2][:].rearrange("p (a b) -> p a b", b=128),
                _bc(sc["cbtm"][par][:], [[128, 128], [0, 8], [1, 128]]),
                OP.mult)
            v.tensor_tensor(
                sc["xsTd"][gg % 2][:].rearrange("p (a b) -> p a b", b=64),
                sc["xsT"][:, g * 512:(g + 1) * 512].rearrange("p (a b) -> p a b", b=64),
                _bc(sc["dtc"][:, g * 8:g * 8 + 8], [[64, 128], [1, 8], [0, 64]]),
                OP.mult)
            C.inc(v.tensor_tensor(
                sc["dxst"][:].rearrange("p (a b) -> p a b", b=64),
                sc["xsT"][:, g * 512:(g + 1) * 512].rearrange("p (a b) -> p a b", b=64),
                _bc(dcol_bc[:, g * 8:g * 8 + 8], [[64, 128], [1, 8], [0, 64]]),
                OP.mult), "vE")
        sec("vector", v_gA)
        v_ga = C.cnt["vE"]
        X["expfree"][gg] = v_ga

        def t_iv(t, g=g, gg=gg, par=par, v_ga=v_ga):
            C.wait(t, "vE", v_ga)
            C.bank_guard(t, "pa1")
            for i in range(8):
                t.matmul(pa[1][:, i * 64:(i + 1) * 64],
                         sc["mtil"][gg % 2][:, i * 128:(i + 1) * 128],
                         sc["xsTd"][gg % 2][:, i * 64:(i + 1) * 64],
                         start=(i == 0), stop=(i == 7))
            C.bank_guard(t, "pa0")
            if X["hev"].get(g):
                t.wait_ge(C.sems["vE"], X["hev"][g])
            for i, r in enumerate(heads):
                mm = t.matmul(pa[0][:, i * 64:(i + 1) * 64],
                              bmcf[:, TT + ch * 128:TT + (ch + 1) * 128],
                              h_bf[:, r * 64:(r + 1) * 64],
                              start=(i == 0), stop=(i == 7))
            C.inc(mm, "tE")
            key = f"pd{g % 2}"
            C.bank_guard(t, key)
            C.inc(t.matmul(pdm[:, (g % 2) * 512:(g % 2) * 512 + 512],
                           sc["vf16"][par][:], e2_f16[:, g * 512:(g + 1) * 512],
                           start=True, stop=True), "tE")
        sec("tensor", t_iv)
        t_ve = C.cnt["tE"]
        t_int = t_ve - 1
        X["mfree"][gg] = t_int

        def v_gB(v, g=g, t_ve=t_ve, t_int=t_int):
            v.wait_ge(C.sems["tE"], t_ve)
            v.tensor_copy(sc["vexp"][:], pdm[:, (g % 2) * 512:(g % 2) * 512 + 512])
            C.bank_mark(f"pd{g % 2}", "vE")
            C.inc(v.scalar_tensor_tensor(sc["ytmp"][:], pa[0][:, 0:512], 1.0,
                                         sc["vexp"][:], OP.mult, OP.mult), "vE")
            C.bank_mark("pa0", "vE")
        sec("vector", v_gB)
        v_yt = C.cnt["vE"]

        def v_gC(v, g=g, gg=gg, par=par, t_int=t_int):
            v.scalar_tensor_tensor(sc["yA"][:], pa[1][:, 0:512], 1.0,
                                   sc["ytmp"][:], OP.mult, OP.add)
            C.bank_mark("pa1", "vE")
            v.scalar_tensor_tensor(sc["yA"][:], sc["dxst"][:], 1.0, sc["yA"][:],
                                   OP.mult, OP.add)
            v.tensor_tensor(ysb[:, g * 512:(g + 1) * 512], sc["yA"][:],
                            zrd[:, g * 512:(g + 1) * 512], OP.mult)
            v.bn_stats(bnb[g % 2][:, (g // 2) * 6:(g // 2) * 6 + 6],
                       ysb[:, g * 512:(g + 1) * 512])
            if X["st"].get(gg - 1):
                v.wait_ge(C.sems["tE"], X["st"][gg - 1])
            v.tensor_tensor(
                sc["xtld"][:].rearrange("p (a b) -> p a b", b=64),
                sc["xsTd"][gg % 2][:].rearrange("p (a b) -> p a b", b=64),
                _bc(sc["t64a"][:, g * 8:g * 8 + 8], [[64, 128], [1, 8], [0, 64]]),
                OP.mult)
            C.inc(v.tensor_tensor(
                sc["hdec"][:].rearrange("p (a b) -> p a b", b=64),
                h_bf[:, g * 512:(g + 1) * 512].rearrange("p (a b) -> p a b", b=64),
                _bc(sc["sbc"][:, g * 8:g * 8 + 8], [[64, 128], [1, 8], [0, 64]]),
                OP.mult), "vE")
        sec("vector", v_gC)
        v_xt = C.cnt["vE"]

        def t_st(t, g=g, gg=gg, par=par, v_xt=v_xt):
            C.wait(t, "vE", v_xt)
            C.bank_guard(t, "pa2")
            for i in range(8):
                t.matmul(pa[2][:, i * 64:(i + 1) * 64], sc["bmt"][par][:],
                         sc["xtld"][:, i * 64:(i + 1) * 64],
                         start=(i == 0), stop=False)
            C.inc(t.matmul(pa[2][:, 0:512], ident_bf[:], sc["hdec"][:],
                           start=False, stop=True), "tE")
            X["st"][gg] = C.cnt["tE"]
        sec("tensor", t_st)
        t_hu = C.cnt["tE"]
        if g == 7:
            X["st_last"] = t_hu

        def v_h(v, g=g, t_hu=t_hu):
            v.wait_ge(C.sems["tE"], t_hu)
            C.inc(v.tensor_copy(h_bf[:, g * 512:(g + 1) * 512], pa[2][:, 0:512]),
                  "vE")
            C.bank_mark("pa2", "vE")
            X["hev"][g] = C.cnt["vE"]
        sec("vector", v_h)

    X["vyx_last"] = C.cnt["vE"]
    X["zrd_free"][par] = C.cnt["vE"]
    X["base_g"] += 8


def _scan_tail(nc, C, sec, tok0, ch, sc, nw_bc, ysb, yhatF, ident_bf, smalls,
               sm64, bnb, pa, pb, X):
    def v_tail(v):
        v.bn_aggr(smalls[:, 12:14], bnb[0][:, 0:24])
        v.bn_aggr(bnb[1][:, 18:20], bnb[1][:, 0:24])
        mA, vA = smalls[:, 12:13], smalls[:, 13:14]
        mB, vB = bnb[1][:, 18:19], bnb[1][:, 19:20]
        t0, t1 = smalls[:, 16:17], smalls[:, 17:18]
        v.tensor_tensor(smalls[:, 8:9], mA, mB, OP.add)
        v.tensor_scalar(smalls[:, 8:9], smalls[:, 8:9], 0.5, None, OP.mult)
        v.tensor_tensor(t0, mA, mB, OP.subtract)
        v.tensor_tensor(t0, t0, t0, OP.mult)
        v.tensor_tensor(t1, vA, vB, OP.add)
        v.tensor_scalar(t1, t1, 0.5, None, OP.mult)
        v.scalar_tensor_tensor(smalls[:, 9:10], t0, 0.25, t1, OP.mult, OP.add)
        v.tensor_tensor(t0, smalls[:, 8:9], smalls[:, 8:9], OP.mult)
        v.tensor_tensor(smalls[:, 9:10], smalls[:, 9:10], t0, OP.add)
        v.tensor_scalar(smalls[:, 9:10], smalls[:, 9:10], EPS, None, OP.add)
        C.inc(v.tensor_copy(smalls[:, 18:19], smalls[:, 9:10]), "vE")
    sec("vector", v_tail)
    v_t1 = C.cnt["vE"]

    def s_tail(s, v_t1=v_t1):
        C.wait(s, "vE", v_t1)
        if X["vt2"]:
            C.wait(s, "vE", X["vt2"])
        C.inc(_rstd_finish(s, smalls[:, 18:19], smalls[:, 10:11],
                           smalls[:, 11:12]), "sE")
    sec("scalar", s_tail)
    s_t1 = C.cnt["sE"]

    def v_tail2(v, s_t1=s_t1):
        C.wait(v, "sE", s_t1)
        if X["ytr"]:
            v.wait_ge(C.sems["tE"], X["ytr"])
        C.inc(v.scalar_tensor_tensor(sc["yhat_tok"][:], ysb[:], smalls[:, 10:11],
                                     nw_bc[:], OP.mult, OP.mult), "vE")
    sec("vector", v_tail2)
    v_t2 = C.cnt["vE"]
    X["vt2"] = v_t2

    for jb in range(4):
        def tf2(t, jb=jb, v_t2=v_t2):
            if jb == 0:
                C.wait(t, "vE", v_t2)
            key = f"pb{jb % 2}"
            C.bank_guard(t, key)
            for i in range(8):
                c = jb * 8 + i
                mm = t.transpose(pb[jb % 2][:, i * 128:(i + 1) * 128],
                                 sc["yhat_tok"][:, c * 128:(c + 1) * 128],
                                 ident_bf[:])
            C.inc(mm, "tE")
        sec("tensor", tf2)
        t_tr2 = C.cnt["tE"]

        def sg2(v, jb=jb, t_tr2=t_tr2):
            v.wait_ge(C.sems["tE"], t_tr2)
            dstap = yhatF[:, jb * 8 * TT:(jb * 8 + 8) * TT] \
                .rearrange("p (c t) -> p c t", t=TT)[:, :, ch * 128:(ch + 1) * 128]
            C.inc(v.tensor_copy(
                dstap, pb[jb % 2][:, 0:1024].rearrange("p (c t) -> p c t", t=128)),
                "vE")
            C.bank_mark(f"pb{jb % 2}", "vE")
        sec("vector", sg2)
    X["ytr"] = C.cnt["tE"]


def _e_outproj(nc, C, sec, tok0, w_out, yhatF, wbuf, x1stg, x1spill, pa):
    dw_at = {}
    mm_done = {}
    do_at = {}

    def pre(g):
        def f(gp, g=g):
            col0 = g * 256
            if g >= 2:
                gp.wait_ge(C.sems["tE"], mm_done[g - 2])
            dw_at[g] = C.dma(gp.dma_start(
                wbuf[g % 2][:, 0:32 * 256].rearrange("p (c f) -> p c f", f=256),
                w_out.rearrange("(c p) f -> p c f", p=128)[:, :, col0:col0 + 256]),
                "dwa" if g % 2 == 0 else "dwb")
        sec("gpsimd", f)

    pre(0)
    pre(1)
    for g in range(16):
        col0 = g * 256
        for b in range(NB):
            idx = g * NB + b

            def tf(t, g=g, b=b, idx=idx):
                if b == 0:
                    C.wait(t, "dwa" if g % 2 == 0 else "dwb")
                C.bank_guard(t, f"pa{idx % 3}")
                for c in range(32):
                    mm = t.matmul(pa[idx % 3][:, 0:256],
                                  yhatF[:, c * TT + b * 128:c * TT + (b + 1) * 128],
                                  wbuf[g % 2][:, c * 256:(c + 1) * 256],
                                  start=(c == 0), stop=(c == 31))
                C.inc(mm, "tE")
                mm_done[g] = C.cnt["tE"]
            sec("tensor", tf)
            t_mm = C.cnt["tE"]

            def sf(v, idx=idx, t_mm=t_mm):
                v.wait_ge(C.sems["tE"], t_mm)
                if idx >= 2:
                    C.wait(v, "doa" if idx % 2 == 0 else "dob")
                C.inc(v.tensor_copy(x1stg[idx % 2][:, 0:256], pa[idx % 3][:, 0:256]),
                      "vE")
                C.bank_mark(f"pa{idx % 3}", "vE")
            sec("vector", sf)
            s_x1 = C.cnt["vE"]

            def df(sy, b=b, col0=col0, idx=idx, s_x1=s_x1):
                C.wait(sy, "vE", s_x1)
                do_at[idx] = C.dma(sy.dma_start(
                    x1spill[tok0 + b * 128:tok0 + (b + 1) * 128, col0:col0 + 256],
                    x1stg[idx % 2][:, 0:256]), "doa" if idx % 2 == 0 else "dob")
            sec("sync", df)
        if g + 2 < 16:
            pre(g + 2)


def _f_prep(nc, C, sec, ffn_a, ffn_b, ffn_wq, ffn_keys, wbuf, awT, bwT,
            wqT, keysT, ident_bf, pa, pb):
    def d0(gp):
        for h in range(2):
            C.dma(gp.dma_start(
                wbuf[h][:, 0:8192].rearrange("p (j d) -> p j d", d=4096),
                ffn_a.rearrange("(j p) d -> p j d", p=128)[:, 2 * h:2 * h + 2, :]),
                "dw")
    sec("gpsimd", d0)
    dw_a = C.cnt["dw"]

    for cb in range(16):
        def tf(t, cb=cb):
            if cb == 0:
                C.wait(t, "dw", dw_a)
            key = f"pb{cb % 2}"
            C.bank_guard(t, key)
            for cc in range(2):
                c = cb * 2 + cc
                for j in range(4):
                    mm = t.transpose(
                        pb[cb % 2][:, cc * 512 + j * 128:cc * 512 + (j + 1) * 128],
                        wbuf[j // 2][:, (j % 2) * 4096 + c * 128:
                                     (j % 2) * 4096 + (c + 1) * 128],
                        ident_bf[:])
            C.inc(mm, "tE")
        sec("tensor", tf)
        t_tr = C.cnt["tE"]

        def sg(s, cb=cb, t_tr=t_tr):
            s.wait_ge(C.sems["tE"], t_tr)
            C.inc(s.activation(awT[:, cb * 1024:(cb + 1) * 1024],
                               pb[cb % 2][:, 0:1024], AF.Copy), "sE")
            C.bank_mark(f"pb{cb % 2}", "sE")
        sec("scalar", sg)
    t_a_done = C.cnt["tE"]

    def d1(gp, t_a_done=t_a_done):
        gp.wait_ge(C.sems["tE"], t_a_done)
        for h in range(2):
            C.dma(gp.dma_start(
                wbuf[h][:, 0:8192].rearrange("p (j n) -> p j n", n=512),
                ffn_b.rearrange("(j p) n -> p j n", p=128)[:, 16 * h:16 * h + 16, :]),
                "dw")
    sec("gpsimd", d1)
    dw_b = C.cnt["dw"]

    for nch in range(4):
        for jb in range(4):
            def tf2(t, nch=nch, jb=jb):
                if nch == 0 and jb == 0:
                    C.wait(t, "dw", dw_b)
                key = f"pb{jb % 2}"
                C.bank_guard(t, key)
                for i in range(8):
                    j = jb * 8 + i
                    mm = t.transpose(pb[jb % 2][:, i * 128:(i + 1) * 128],
                                     wbuf[j // 16][:, (j % 16) * 512 + nch * 128:
                                                   (j % 16) * 512 + (nch + 1) * 128],
                                     ident_bf[:])
                C.inc(mm, "tE")
            sec("tensor", tf2)
            t_tr2 = C.cnt["tE"]

            def sg2(s, nch=nch, jb=jb, t_tr2=t_tr2):
                s.wait_ge(C.sems["tE"], t_tr2)
                C.inc(s.activation(
                    bwT[:, nch * 4096 + jb * 1024:nch * 4096 + (jb + 1) * 1024],
                    pb[jb % 2][:, 0:1024], AF.Copy), "sE")
                C.bank_mark(f"pb{jb % 2}", "sE")
            sec("scalar", sg2)
    t_b_done = C.cnt["tE"]

    def d2(gp, t_b_done=t_b_done):
        gp.wait_ge(C.sems["tE"], t_b_done)
        C.dma(gp.dma_start(wbuf[0][0:16, 0:4096], ffn_wq), "dw")
        C.dma(gp.dma_start(wbuf[0][0:64, 4096:4112], ffn_keys), "dw")
    sec("gpsimd", d2)
    dw_q = C.cnt["dw"]

    for jb in range(4):
        def tf3(t, jb=jb):
            if jb == 0:
                C.wait(t, "dw", dw_q)
            key = f"pb{jb % 2}"
            C.bank_guard(t, key)
            for i in range(8):
                c = jb * 8 + i
                mm = t.transpose(pb[jb % 2][:, i * 16:(i + 1) * 16],
                                 wbuf[0][0:16, c * 128:(c + 1) * 128],
                                 ident_bf[0:16, 0:16])
            C.inc(mm, "tE")
        sec("tensor", tf3)
        t_tr3 = C.cnt["tE"]

        def sg3(s, jb=jb, t_tr3=t_tr3):
            s.wait_ge(C.sems["tE"], t_tr3)
            C.inc(s.activation(wqT[:, jb * 128:(jb + 1) * 128],
                               pb[jb % 2][:, 0:128], AF.Copy), "sE")
            C.bank_mark(f"pb{jb % 2}", "sE")
        sec("scalar", sg3)

    def tf4(t):
        C.bank_guard(t, "pb0")
        C.inc(t.transpose(pb[0][0:16, 0:64], wbuf[0][0:64, 4096:4112],
                          ident_bf[0:64, 0:64]), "tE")
    sec("tensor", tf4)
    t_k = C.cnt["tE"]

    def sg4(s, t_k=t_k):
        s.wait_ge(C.sems["tE"], t_k)
        C.inc(s.activation(keysT[:], pb[0][0:16, 0:64], AF.Copy), "sE")
        C.bank_mark("pb0", "sE")
    sec("scalar", sg4)


def _f_block(nc, C, sec, tok0, b, x_in, x1spill, out_t, xblk2, ysb2, x3f,
             x2F, x2tok, awT, bwT, wqT, keysT, e512_bf, fscr, fscr2, gq, gwork,
             gate_bf, gateT, zhat, zhT, qtok, qT, bnb, smalls, rcols,
             ident_bf, ident_f32, pa, pdm, pb, do_at, Xf):
    t0b = tok0 + b * 128

    def d0(sy, b=b):
        if Xf["aq"]:
            sy.wait_ge(C.sems["tE"], Xf["aq"])
        if Xf["x2Fev"]:
            sy.wait_ge(C.sems["vE"], Xf["x2Fev"])
        C.dma(sy.dma_start(xblk2[:], x_in[t0b:t0b + 128, :]), "dx")
        C.dma(sy.dma_start(x2F[:], x1spill[t0b:t0b + 128, :]), "dx")
    sec("sync", d0)
    dx_b = C.cnt["dx"]

    def v0(v, dx_b=dx_b):
        C.wait(v, "dx")
        v.tensor_tensor(ysb2[:], x2F[:, 0:H], xblk2[:], OP.add)
        _ln_stats(C, v, bnb, ysb2[:], H, smalls[:, 8:9], smalls[:, 9:10],
                  smalls, 12, want_meansq=False)
        C.inc(v.tensor_copy(smalls[:, 18:19], smalls[:, 9:10]), "vE")
    sec("vector", v0)
    v_ln = C.cnt["vE"]

    def s0(s, v_ln=v_ln):
        C.wait(s, "vE", v_ln)
        C.inc(_rstd_finish(s, smalls[:, 18:19], smalls[:, 10:11],
                           smalls[:, 11:12]), "sE")
    sec("scalar", s0)
    s_ln = C.cnt["sE"]

    def v1(v, s_ln=s_ln):
        C.wait(v, "sE", s_ln)
        if Xf["x2tr"]:
            v.wait_ge(C.sems["tE"], Xf["x2tr"])
        v.tensor_scalar(ysb2[:], ysb2[:], smalls[:, 8:9], smalls[:, 10:11],
                        OP.subtract, OP.mult)
        C.inc(v.tensor_copy(x2tok[:], ysb2[:]), "vE")
    sec("vector", v1)
    v_x2 = C.cnt["vE"]

    for jb in range(4):
        def tf(t, jb=jb, v_x2=v_x2):
            if jb == 0:
                C.wait(t, "vE", v_x2)
            C.bank_guard(t, f"pb{jb % 2}")
            for i in range(8):
                c = jb * 8 + i
                mm = t.transpose(pb[jb % 2][:, i * 128:(i + 1) * 128],
                                 x2tok[:, c * 128:(c + 1) * 128], ident_bf[:])
            C.inc(mm, "tE")
        sec("tensor", tf)
        t_tr = C.cnt["tE"]

        def sg(v, jb=jb, t_tr=t_tr):
            v.wait_ge(C.sems["tE"], t_tr)
            C.inc(v.tensor_copy(x2F[:, jb * 1024:(jb + 1) * 1024],
                                pb[jb % 2][:, 0:1024]), "vE")
            C.bank_mark(f"pb{jb % 2}", "vE")
        sec("vector", sg)
    Xf["x2tr"] = C.cnt["tE"]
    v_x2f = C.cnt["vE"]
    Xf["x2Fev"] = v_x2f

    zi = 0 if b % 2 == 0 else 3
    def t_a(t, v_x2f=v_x2f, zi=zi):
        C.wait(t, "vE", v_x2f)
        C.bank_guard(t, f"pa{zi}")
        C.bank_guard(t, "pa1")
        last = None
        for c in range(32):
            t.matmul(pa[zi][:, 0:512], x2F[:, c * 128:(c + 1) * 128],
                     awT[:, c * 512:(c + 1) * 512], start=(c == 0), stop=(c == 31))
            last = t.matmul(pa[1][:, 0:16], x2F[:, c * 128:(c + 1) * 128],
                            wqT[:, c * 16:(c + 1) * 16], start=(c == 0),
                            stop=(c == 31))
        C.inc(last, "tE")
    sec("tensor", t_a)
    t_aq = C.cnt["tE"]
    Xf["aq"] = t_aq

    def s_q(v, t_aq=t_aq):
        C.wait(v, "tE", t_aq)
        C.inc(v.tensor_copy(qtok[:], pa[1][:, 0:16]), "vE")
        C.bank_mark("pa1", "vE")
    sec("vector", s_q)
    s_qe = C.cnt["vE"]

    def t_q(t, s_qe=s_qe):
        C.wait(t, "vE", s_qe)
        C.bank_guard(t, "pb0")
        C.inc(t.transpose(pb[0][0:16, 0:128], qtok[:], ident_bf[:]), "tE")
    sec("tensor", t_q)
    t_qt = C.cnt["tE"]

    def s_q2(v, t_qt=t_qt):
        C.wait(v, "tE", t_qt)
        C.inc(v.tensor_copy(qT[:], pb[0][0:16, 0:128]), "vE")
        C.bank_mark("pb0", "vE")
    sec("vector", s_q2)
    s_qt = C.cnt["vE"]

    def t_sc(t, s_qt=s_qt):
        C.wait(t, "vE", s_qt)
        C.bank_guard(t, "pa2")
        C.inc(t.matmul(pa[2][:, 0:64], qT[:], keysT[:], start=True, stop=True),
              "tE")
    sec("tensor", t_sc)
    t_s = C.cnt["tE"]

    def v_r(v, t_s=t_s):
        C.wait(v, "tE", t_s)
        v.tensor_copy(gq[:], pa[2][:, 0:64])
        C.bank_mark("pa2", "vE")
        v.tensor_copy(gwork[:], gq[:])
        for it in range(15):
            v.reduce_max(rcols[0][:], gwork[:], axis=AX)
            if it == 0:
                v.tensor_copy(rcols[1][:], rcols[0][:])
            v.tensor_scalar(fscr2[:, 0:64], gwork[:], rcols[0][:], None, OP.is_ge)
            v.scalar_tensor_tensor(gwork[:], fscr2[:, 0:64], -1e30, gwork[:],
                                   OP.mult, OP.add)
        v.reduce_max(rcols[2][:], gwork[:], axis=AX)
        C.inc(v.tensor_scalar(rcols[1][:], rcols[1][:], -1.0, None, OP.mult), "vE")
    sec("vector", v_r)
    v_rt = C.cnt["vE"]

    def s_r(s, v_rt=v_rt):
        C.wait(s, "vE", v_rt)
        C.inc(s.activation(fscr[:, 0:64], gq[:], AF.Exp, bias=rcols[1][:]), "sE")
    sec("scalar", s_r)
    s_re = C.cnt["sE"]

    def v_g(v, s_re=s_re):
        C.wait(v, "sE", s_re)
        v.tensor_scalar(gwork[:], gq[:], rcols[2][:], None, OP.is_ge)
        v.tensor_tensor(gwork[:], gwork[:], fscr[:, 0:64], OP.mult)
        v.reduce_sum(rcols[3][:], gwork[:], axis=AX)
        v.reciprocal(rcols[0][:], rcols[3][:])
        C.inc(v.tensor_scalar(gate_bf[:], gwork[:], rcols[0][:], 2.0,
                              OP.mult, OP.mult), "vE")
    sec("vector", v_g)
    v_gt = C.cnt["vE"]

    def t_g(t, v_gt=v_gt):
        C.wait(t, "vE", v_gt)
        C.bank_guard(t, "pb1")
        C.inc(t.transpose(pb[1][0:64, 0:128], gate_bf[:], ident_bf[:]), "tE")
    sec("tensor", t_g)
    t_gt = C.cnt["tE"]

    def s_g(v, t_gt=t_gt):
        C.wait(v, "tE", t_gt)
        C.inc(v.tensor_copy(gateT[:], pb[1][0:64, 0:128]), "vE")
        C.bank_mark("pb1", "vE")
    sec("vector", s_g)
    s_gT = C.cnt["vE"]

    def t_ge(t, s_gT=s_gT):
        C.wait(t, "vE", s_gT)
        C.bank_guard(t, "pa2")
        C.inc(t.matmul(pa[2][:, 0:512], gateT[:], e512_bf[:], start=True,
                       stop=True), "tE")
    sec("tensor", t_ge)
    t_gx = C.cnt["tE"]

    def s_ge(v, t_gx=t_gx):
        C.wait(v, "tE", t_gx)
        C.inc(v.tensor_copy(fscr[:, 0:512], pa[2][:, 0:512]), "vE")
        C.bank_mark("pa2", "vE")
    sec("vector", s_ge)
    s_gx = C.cnt["vE"]

    def v_z(v, s_gx=s_gx, zi=zi):
        C.wait(v, "vE", s_gx)
        C.inc(v.scalar_tensor_tensor(zhat[:], pa[zi][:, 0:512], 1.0,
                                     fscr[:, 0:512], OP.mult, OP.mult), "vE")
        C.bank_mark(f"pa{zi}", "vE")
    sec("vector", v_z)
    v_zh = C.cnt["vE"]

    def t_zt(t, v_zh=v_zh):
        C.wait(t, "vE", v_zh)
        C.bank_guard(t, "pb0")
        last = None
        for i in range(4):
            last = t.transpose(pb[0][:, i * 128:(i + 1) * 128],
                               zhat[:, i * 128:(i + 1) * 128], ident_bf[:])
        C.inc(last, "tE")
    sec("tensor", t_zt)
    t_zT = C.cnt["tE"]

    def s_zt(v, t_zT=t_zT):
        C.wait(v, "tE", t_zT)
        C.inc(v.tensor_copy(zhT[:], pb[0][:, 0:512]), "vE")
        C.bank_mark("pb0", "vE")
    sec("vector", s_zt)
    s_zTT = C.cnt["vE"]

    for og in range(8):
        def t_b(t, og=og, s_zTT=s_zTT):
            if og == 0:
                C.wait(t, "vE", s_zTT)
            key = f"pd{og % 2}"
            C.bank_guard(t, key)
            half = (og % 2) * 512
            last = None
            for nch in range(4):
                last = t.matmul(pdm[:, half:half + 512],
                                zhT[:, nch * 128:(nch + 1) * 128],
                                bwT[:, nch * 4096 + og * 512:
                                    nch * 4096 + (og + 1) * 512],
                                start=(nch == 0), stop=(nch == 3))
            C.inc(last, "tE")
        sec("tensor", t_b)
        t_bo = C.cnt["tE"]

        def v_o(v, og=og, t_bo=t_bo):
            C.wait(v, "tE", t_bo)
            if og == 0:
                C.wait(v, "do")
            half = (og % 2) * 512
            C.inc(v.scalar_tensor_tensor(x3f[:, og * 512:(og + 1) * 512],
                                         pdm[:, half:half + 512], 1.0,
                                         ysb2[:, og * 512:(og + 1) * 512],
                                         OP.mult, OP.add), "vE")
            C.bank_mark(f"pd{og % 2}", "vE")
        sec("vector", v_o)

    def v_l2(v):
        _ln_stats(C, v, bnb, x3f[:], H, smalls[:, 8:9], smalls[:, 9:10],
                  smalls, 12, want_meansq=False)
        C.inc(v.tensor_copy(smalls[:, 18:19], smalls[:, 9:10]), "vE")
    sec("vector", v_l2)
    v_l2a = C.cnt["vE"]

    def s_l2(s, v_l2a=v_l2a):
        C.wait(s, "vE", v_l2a)
        C.inc(_rstd_finish(s, smalls[:, 18:19], smalls[:, 10:11],
                           smalls[:, 11:12]), "sE")
    sec("scalar", s_l2)
    s_l2a = C.cnt["sE"]

    def v_out(v, b=b, s_l2a=s_l2a):
        C.wait(v, "sE", s_l2a)
        C.inc(v.tensor_scalar(x3f[:], x3f[:], smalls[:, 8:9], smalls[:, 10:11],
                              OP.subtract, OP.mult), "vE")
    sec("vector", v_out)
    v_ot = C.cnt["vE"]

    def d_out(sy, b=b, v_ot=v_ot):
        C.wait(sy, "vE", v_ot)
        do_at[b] = C.dma(sy.dma_start(out_t[t0b:t0b + 128, :], x3f[:]), "do")
    sec("sync", d_out)


def kernel(**inputs):
    _ = inputs.get("L0_attn_residual")
    x = np.ascontiguousarray(inputs["L0_attn_hidden_states"], dtype=np.float32)
    nc = build()
    base = {
        "w_in": np.ascontiguousarray(inputs["L0_attn_in_proj_w"], np.float32),
        "conv_w": np.ascontiguousarray(inputs["L0_attn_conv_w"], np.float32),
        "a_log": np.ascontiguousarray(inputs["L0_attn_A_log"], np.float32).reshape(1, NH),
        "dt_bias": np.ascontiguousarray(inputs["L0_attn_dt_bias"], np.float32).reshape(1, NH),
        "d_vec": np.ascontiguousarray(inputs["L0_attn_D"], np.float32).reshape(1, NH),
        "ln_w": np.ascontiguousarray(inputs["L0_attn_ln_w"], np.float32).reshape(1, H),
        "norm_w": np.ascontiguousarray(inputs["L0_attn_norm_w"], np.float32).reshape(1, DS),
        "mup": np.ascontiguousarray(inputs["L0_attn_mup_vector"], np.float32).reshape(1, PROJ),
        "w_out": np.ascontiguousarray(inputs["L0_attn_out_proj_w"], np.float32),
        "ffn_a": np.ascontiguousarray(inputs["L0_ffn_A_w"], np.float32),
        "ffn_b": np.ascontiguousarray(inputs["L0_ffn_B_w"], np.float32),
        "ffn_wq": np.ascontiguousarray(inputs["L0_ffn_Wq_w"], np.float32),
        "ffn_keys": np.ascontiguousarray(inputs["L0_ffn_keys"], np.float32),
    }
    in_maps = [dict(base, x=np.ascontiguousarray(x[c])) for c in range(NCORES)]
    res = run_bass_kernel_spmd(nc, in_maps, core_ids=list(range(NCORES)))
    return np.stack([res.results[c]["out"] for c in range(NCORES)], axis=0)


# version 08
# speedup vs baseline: 1.4700x; 1.0148x over previous
import contextlib
import numpy as np
import concourse.bass as bass
import concourse.mybir as mybir
from concourse.bass_utils import run_bass_kernel_spmd
from concourse.alu_op_type import AluOpType as OP

F32 = mybir.dt.float32
F16 = mybir.dt.float16
BF16 = mybir.dt.bfloat16
AF = mybir.ActivationFunctionType
AX = mybir.AxisListType.X

L = 2048
H = 4096
DS = 4096
PROJ = 8512
CONVD = 4352
NH = 64
TT = 1024
NB = 8
NT = L // TT
EPS = 1e-5
NCORES = 8
EXT = TT + 3

ALL_ENG = ["sync", "gpsimd", "tensor", "vector", "scalar"]

_DRAIN_OPS = frozenset({
    "tensor_scalar", "tensor_tensor", "scalar_tensor_tensor", "tensor_copy",
    "tensor_tensor_scan", "reduce_max", "reduce_sum", "tensor_reduce",
    "bn_stats", "bn_aggr", "reciprocal", "memset", "activation", "select",
    "copy",
})


class _AutoDrain:

    def __init__(self, eng):
        self._e = eng
        self._w = set()
        self._r = set()

    def __getattr__(self, name):
        attr = getattr(self._e, name)
        if name in _DRAIN_OPS:
            def w(*a, **k):
                aps = [x for x in list(a) + list(k.values())
                       if isinstance(x, bass.AP)]
                names = [ap.tensor.name for ap in aps]
                out, ins = set(names[:1]), set(names[1:])
                if (ins & self._w) or (out & self._w) or (out & self._r):
                    self._e.drain()
                    self._w.clear()
                    self._r.clear()
                self._w |= out
                self._r |= ins
                return attr(*a, **k)
            return w
        return attr


def bcast_ap(dram_ap, npart, cols):
    return bass.AP(dram_ap.tensor, 0, [[0, npart], [1, cols]])


class Ctx:

    def __init__(self, nc, es):
        self.nc = nc
        self.sems = {}
        self.cnt = {}
        for name in ["dw", "dwa", "dwb", "dx", "do", "doa", "dob", "tE", "vE",
                     "sE", "gE"]:
            self.sems[name] = es.enter_context(nc.semaphore(name))
            self.cnt[name] = 0
        self.bank_rd = {}

    def inc(self, instr, name, n=1):
        instr.then_inc(self.sems[name], n)
        self.cnt[name] += n
        return self.cnt[name]

    def dma(self, instr, name):
        return self.inc(instr, name, 16)

    def wait(self, eng, name, val=None):
        v = self.cnt[name] if val is None else val
        if v:
            eng.wait_ge(self.sems[name], v)

    def bank_guard(self, eng, key):
        if key in self.bank_rd:
            sem, val = self.bank_rd[key]
            if val:
                eng.wait_ge(self.sems[sem], val)

    def bank_mark(self, key, sem):
        self.bank_rd[key] = (sem, self.cnt[sem])


def build(debug=()):
    nc = bass.Bass("TRN2", target_bir_lowering=False)
    debug = set(debug)

    x_in = nc.dram_tensor("x", [L, H], F32, kind="ExternalInput").ap()
    w_in = nc.dram_tensor("w_in", [H, PROJ], F32, kind="ExternalInput").ap()
    conv_w = nc.dram_tensor("conv_w", [CONVD, 4], F32, kind="ExternalInput").ap()
    a_log = nc.dram_tensor("a_log", [1, NH], F32, kind="ExternalInput").ap()
    dt_bias = nc.dram_tensor("dt_bias", [1, NH], F32, kind="ExternalInput").ap()
    d_vec = nc.dram_tensor("d_vec", [1, NH], F32, kind="ExternalInput").ap()
    ln_w = nc.dram_tensor("ln_w", [1, H], F32, kind="ExternalInput").ap()
    norm_w = nc.dram_tensor("norm_w", [1, DS], F32, kind="ExternalInput").ap()
    mup = nc.dram_tensor("mup", [1, PROJ], F32, kind="ExternalInput").ap()
    w_out = nc.dram_tensor("w_out", [DS, H], F32, kind="ExternalInput").ap()
    ffn_a = nc.dram_tensor("ffn_a", [512, H], F32, kind="ExternalInput").ap()
    ffn_b = nc.dram_tensor("ffn_b", [H, 512], F32, kind="ExternalInput").ap()
    ffn_wq = nc.dram_tensor("ffn_wq", [16, H], F32, kind="ExternalInput").ap()
    ffn_keys = nc.dram_tensor("ffn_keys", [NH, 16], F32, kind="ExternalInput").ap()

    out_t = nc.dram_tensor("out", [L, H], F32, kind="ExternalOutput").ap()

    def scratch(name, shape, dtype):
        if name in debug:
            return nc.dram_tensor(name, shape, dtype, kind="ExternalOutput").ap()
        return nc.dram_tensor(name, shape, dtype).ap()

    zspill = scratch("zspill", [L, DS], BF16)
    xsspill = scratch("xsspill", [DS, L], BF16)
    x1spill = scratch("x1spill", [L, H], BF16)

    es = contextlib.ExitStack()
    with es:
        C = Ctx(nc, es)
        block = es.enter_context(nc.Block())

        def sec(eng_name, fn):
            if eng_name in ("vector", "scalar"):
                getattr(block, eng_name)(lambda e: fn(_AutoDrain(e)))
            else:
                getattr(block, eng_name)(fn)

        def barrier():
            snap = dict(C.cnt)
            for e in ALL_ENG:
                def f(eng, snap=snap):
                    for name, v in snap.items():
                        if v:
                            eng.wait_ge(C.sems[name], v)
                sec(e, f)

        A = lambda n, sh, dt: es.enter_context(nc.sbuf_tensor(n, sh, dt))

        ones_f32 = A("ones_f32", [128, 128], F32)
        ones_f16 = A("ones_f16", [128, 128], F16)
        ident_bf = A("ident_bf", [128, 128], BF16)
        ident_f32 = A("ident_f32", [128, 128], F32)
        tri_f16 = A("tri_f16", [128, 128], F16)
        su_f16 = A("su_f16", [128, 128], F16)
        tri_f32 = A("tri_f32", [128, 128], F32)
        e2_f16 = A("e2_f16", [64, DS], F16)
        e512_bf = A("e512_bf", [64, 512], BF16)
        dbias_bc = A("dbias_bc", [128, NH], F32)
        nega_bc = A("nega_bc", [128, NH], F32)
        dcol_bc = A("dcol_bc", [128, NH], F32)
        convwF = A("convwF", [128, 34 * 4], F32)
        iota_f = A("iota_f", [128, 128], F32)
        iota_c = A("iota_c", [128, 1], F32)
        smalls = A("smalls", [128, 32], F32)
        sm64 = A("sm64", [64, 16], F32)
        dtT = A("dtT", [128, NB * NH], F32)
        h_bf = A("h_bf", [128, DS], BF16)
        tailsave = A("tailsave", [128, 34 * 3], BF16)
        bmcf = A("bmcf", [128, 2 * TT], BF16)
        wbuf = [A(f"wbuf{i}", [128, 8192], BF16) for i in range(2)]
        bnb = [A(f"bnb{i}", [128, 24], F32) for i in range(2)]

        pa = [es.enter_context(nc.psum_tensor(f"pa{i}", [128, 512], F32)) for i in range(4)]
        pdm = es.enter_context(nc.psum_tensor("pdm", [128, 1024], F32))
        pb = [es.enter_context(nc.psum_tensor(f"pb{i}", [128, 1024], BF16)) for i in range(2)]

        with contextlib.ExitStack() as sp:
            e2scr = sp.enter_context(nc.sbuf_tensor("e2scr", [64, DS], F32))

            def setup_sync(s):
                C.dma(s.dma_start(convwF[:].rearrange("p (c k) -> p c k", k=4),
                                  conv_w.rearrange("(c p) k -> p c k", p=128)), "dx")
                C.dma(s.dma_start(dbias_bc[:], bcast_ap(dt_bias, 128, NH)), "dx")
                C.dma(s.dma_start(nega_bc[:], bcast_ap(a_log, 128, NH)), "dx")
                C.dma(s.dma_start(dcol_bc[:], bcast_ap(d_vec, 128, NH)), "dx")
            sec("sync", setup_sync)

            def setup_gp(g):
                g.iota(iota_f[:], [[1, 128]], base=0, channel_multiplier=0,
                       allow_small_or_imprecise_dtypes=True)
                g.iota(iota_c[:], [[0, 1]], base=0, channel_multiplier=1,
                       allow_small_or_imprecise_dtypes=True)
                C.inc(g.iota(e2scr[:, :].rearrange("p (a b) -> p a b", b=64),
                             [[1, 64], [0, 64]], base=0, channel_multiplier=0,
                             allow_small_or_imprecise_dtypes=True), "gE")
            sec("gpsimd", setup_gp)

            def setup_v(v):
                C.wait(v, "gE")
                C.wait(v, "dx")
                v.memset(ones_f32[:], 1.0)
                v.memset(ones_f16[:], 1.0)
                v.memset(h_bf[:], 0.0)
                v.memset(tailsave[:], 0.0)
                v.tensor_scalar(ident_f32[:], iota_f[:], iota_c[:], None, OP.is_equal)
                v.tensor_copy(ident_bf[:], ident_f32[:])
                v.tensor_scalar(tri_f32[:], iota_f[:], iota_c[:], None, OP.is_ge)
                v.tensor_copy(tri_f16[:], tri_f32[:])
                v.tensor_scalar(su_f16[:], iota_f[:], iota_c[:], None, OP.is_lt)
                v.tensor_scalar(e2scr[:, :], e2scr[:, :], iota_c[0:64, :],
                                None, OP.is_equal)
                C.inc(v.tensor_copy(e2_f16[:], e2scr[:, :]), "vE")
            sec("vector", setup_v)
            v_e2 = C.cnt["vE"]

            def setup_gp2(g):
                C.wait(g, "vE", v_e2)
                C.inc(g.iota(e2scr[:, 0:512].rearrange("p (a b) -> p a b", b=8),
                             [[1, 64], [0, 8]], base=0, channel_multiplier=0,
                             allow_small_or_imprecise_dtypes=True), "gE")
            sec("gpsimd", setup_gp2)

            def setup_v2(v):
                C.wait(v, "gE")
                v.tensor_scalar(e2scr[:, 0:512], e2scr[:, 0:512], iota_c[0:64, :],
                                None, OP.is_equal)
                C.inc(v.tensor_copy(e512_bf[:], e2scr[:, 0:512]), "vE")
            sec("vector", setup_v2)
            v_e5 = C.cnt["vE"]

            def setup_s(s):
                C.wait(s, "vE", v_e5)
                C.inc(s.activation(nega_bc[:], nega_bc[:], AF.Exp), "sE")
            sec("scalar", setup_s)

            def setup_v3(v):
                C.wait(v, "sE")
                C.inc(v.tensor_scalar(nega_bc[:], nega_bc[:], -1.0, None, OP.mult),
                      "vE")
            sec("vector", setup_v3)

            barrier()

        for tile in range(NT):
            tok0 = tile * TT

            with contextlib.ExitStack() as ph:
                Aph = lambda n, sh, dt: ph.enter_context(
                    nc.sbuf_tensor(f"{n}_t{tile}", sh, dt))
                xhatF = Aph("xhatF", [128, 32 * TT], BF16)

                with contextlib.ExitStack() as p0s:
                    A0 = lambda n, sh, dt: p0s.enter_context(
                        nc.sbuf_tensor(f"{n}_t{tile}", sh, dt))
                    lnw_bc = A0("lnw_bc", [128, H], F16)
                    xblk = [A0(f"xblk{i}", [128, H], F32) for i in range(2)]
                    xhat_tok = A0("xhat_tok", [128, H], BF16)
                    with nc.named_scope(f"P0_t{tile}"):
                        _p0(nc, C, sec, tok0, x_in, ln_w, lnw_bc, xblk, xhat_tok,
                            xhatF, bnb, smalls, ident_bf, pa, pb)
                        barrier()

                with contextlib.ExitStack() as a1s:
                    A1 = lambda n, sh, dt: a1s.enter_context(
                        nc.sbuf_tensor(f"{n}_t{tile}", sh, dt))
                    mup_bc = A1("mup_bc", [128, PROJ], F16)
                    xbct = [A1(f"xbct{i}", [128, EXT], BF16) for i in range(8)]
                    ztmp = A1("ztmp", [128, 512], F32)
                    bst = A1("bst", [128, 512], BF16)
                    accb = A1("accb", [128, TT], F32)
                    zstg = [A1(f"zstg{i}", [128, 512], BF16) for i in range(2)]
                    xstg = [A1(f"xstg{i}", [128, TT], BF16) for i in range(2)]

                    def mup_load(g):
                        C.dma(g.dma_start(mup_bc[:], bcast_ap(mup, 128, PROJ)), "dw")
                    sec("gpsimd", mup_load)

                    with nc.named_scope(f"A1_t{tile}"):
                        _a1(nc, C, sec, tok0, w_in, xhatF, wbuf, mup_bc, ztmp, bst,
                            xbct, accb, convwF, tailsave, bmcf, dtT, xstg, xsspill,
                            ident_bf, pa, pb)
                        barrier()
                    with nc.named_scope(f"A2_t{tile}"):
                        _a2(nc, C, sec, tok0, w_in, xhatF, wbuf, mup_bc, ztmp,
                            zstg, zspill, pa)
                        barrier()

            with contextlib.ExitStack() as ph:
                Aph = lambda n, sh, dt: ph.enter_context(
                    nc.sbuf_tensor(f"{n}_t{tile}", sh, dt))
                yhatF = Aph("yhatF", [128, 32 * TT], BF16)
                nw_bc = Aph("nw_bc", [128, DS], F16)
                ysb = Aph("ysb", [128, DS], BF16)
                sc = {n: Aph(n, sh, dt) for n, sh, dt in [
                    ("xsT", [128, DS], BF16),
                    ("yhat_tok", [128, DS], BF16),
                    ("vexp", [128, 512], F32),
                    ("ytmp", [128, 512], BF16),
                    ("dxst", [128, 512], BF16),
                    ("yA", [128, 512], BF16),
                    ("xtld", [128, 512], BF16),
                    ("hdec", [128, 512], BF16),
                    ("dtc", [128, NH], F32),
                    ("dta32", [128, NH], F32),
                    ("dta16", [128, NH], F16),
                    ("srow", [1, 64], F32),
                    ("sbc", [128, 64], F32),
                    ("t64a", [128, NH], F32),
                    ("t64b", [128, NH], F32),
                    ("t64c", [128, NH], F32),
                ]}
                sc["cbtm"] = [Aph(f"cbtm{i}", [128, 128], F32) for i in range(2)]
                sc["bmt"] = [Aph(f"bmt{i}", [128, 128], BF16) for i in range(2)]
                sc["vf16"] = [Aph(f"vf16{i}", [64, 128], F16) for i in range(2)]
                sc["expd"] = [Aph(f"expd{i}", [128, 1024], F16) for i in range(2)]
                sc["rhsd"] = [Aph(f"rhsd{i}", [128, 512], F16) for i in range(2)]
                sc["mtil"] = [Aph(f"mtil{i}", [128, 1024], BF16) for i in range(2)]
                sc["xsTd"] = [Aph(f"xsTd{i}", [128, 512], BF16) for i in range(2)]
                x1stg = [Aph(f"x1stg{i}", [128, 512], BF16) for i in range(2)]

                def nw_load(g):
                    C.dma(g.dma_start(nw_bc[:], bcast_ap(norm_w, 128, DS)), "dw")
                sec("gpsimd", nw_load)

                with nc.named_scope(f"SCAN_t{tile}"):
                    X = {"xrd_free": {0: 0, 1: 0}, "zrd_free": {0: 0, 1: 0},
                         "st_last": 0, "scum": 0, "tcu_prev": 0, "vyx_last": 0,
                         "front_v": 0, "vt2": 0, "ytr": 0, "base_g": 0,
                         "dmm": {}, "expfree": {}, "mfree": {}, "dxs": {},
                         "st": {}, "hev": {}}
                    fr = lambda ch: _scan_front(
                        nc, C, sec, tok0, ch, sc, wbuf, xsspill, zspill, dtT,
                        dbias_bc, nega_bc, bmcf, ones_f16, ones_f32, tri_f16,
                        su_f16, tri_f32, ident_bf, ident_f32, sm64, pa, pb, X)
                    fr(0)
                    for ch in range(NB):
                        _scan_groups(nc, C, sec, tok0, ch, sc, wbuf, dcol_bc,
                                     h_bf, bmcf, ysb, e2_f16, bnb, tri_f16,
                                     su_f16, ident_bf, pa, pdm, pb, X)
                        if ch + 1 < NB:
                            fr(ch + 1)
                        _scan_tail(nc, C, sec, tok0, ch, sc, nw_bc, ysb, yhatF,
                                   ident_bf, smalls, sm64, bnb, pa, pb, X)
                    barrier()
                with nc.named_scope(f"E_t{tile}"):
                    _e_outproj(nc, C, sec, tok0, w_out, yhatF, wbuf, x1stg,
                               x1spill, pa)
                    barrier()

            with contextlib.ExitStack() as ph:
                Aph = lambda n, sh, dt: ph.enter_context(
                    nc.sbuf_tensor(f"{n}_t{tile}", sh, dt))
                awT = Aph("awT", [128, 32 * 512], BF16)
                bwT = Aph("bwT", [128, 4 * H], BF16)
                mT = Aph("mT", [128, 32 * 64], BF16)
                keysT = Aph("keysT", [16, 64], BF16)
                x2F = Aph("x2F", [128, 32 * 128], BF16)
                x2tok = Aph("x2tok", [128, H], BF16)
                x3f = Aph("x3f", [128, H], F32)
                xblk2 = Aph("xblk2", [128, H], F32)
                ysb2 = Aph("ysb2", [128, H], F32)
                fscr = Aph("fscr", [128, 512], F32)
                fscr2 = Aph("fscr2", [128, 64], F32)
                gq = Aph("gq", [128, 64], F32)
                gwork = Aph("gwork", [128, 64], F32)
                zhat = Aph("zhat", [128, 512], BF16)
                zhT = Aph("zhT", [128, 512], BF16)
                fcmp = Aph("fcmp", [128, 4096], BF16)
                rcols = [Aph(f"rcol{i}", [128, 1], F32) for i in range(4)]

                with nc.named_scope(f"FP_t{tile}"):
                    _f_prep(nc, C, sec, ffn_a, ffn_b, ffn_wq, ffn_keys, wbuf,
                            awT, bwT, mT, keysT, ident_bf, pa, pb)
                    barrier()
                do_at = {}
                Xf = {"aq": 0, "x2Fev": 0, "x2tr": 0}
                with nc.named_scope(f"F_t{tile}"):
                    for b in range(NB):
                        _f_block(nc, C, sec, tok0, b, x_in, x1spill, out_t,
                                 xblk2, ysb2, x3f, x2F, x2tok, awT, bwT, mT,
                                 fscr, fscr2, gq, gwork, fcmp,
                                 zhat, zhT, bnb, smalls, rcols, ident_bf,
                                 ident_f32, pa, pdm, pb, do_at, Xf)
                    barrier()

        def fin(s):
            C.wait(s, "do")
            C.wait(s, "doa")
            C.wait(s, "dob")
        sec("sync", fin)

    return nc



def _ln_stats(C, v, bnb, src_ap, ncols, mu_col, m2_col, sm, c0, want_meansq):
    nst = ncols // 512
    for i in range(nst):
        v.bn_stats(bnb[i % 2][:, (i // 2) * 6:(i // 2) * 6 + 6],
                   src_ap[:, i * 512:(i + 1) * 512])
    v.bn_aggr(sm[:, c0:c0 + 2], bnb[0][:, 0:(nst // 2) * 6])
    v.bn_aggr(bnb[1][:, 18:20], bnb[1][:, 0:(nst // 2) * 6])
    mA, vA = sm[:, c0:c0 + 1], sm[:, c0 + 1:c0 + 2]
    mB, vB = bnb[1][:, 18:19], bnb[1][:, 19:20]
    t0, t1 = sm[:, c0 + 4:c0 + 5], sm[:, c0 + 5:c0 + 6]
    v.tensor_tensor(mu_col, mA, mB, OP.add)
    v.tensor_scalar(mu_col, mu_col, 0.5, None, OP.mult)
    v.tensor_tensor(t0, mA, mB, OP.subtract)
    v.tensor_tensor(t0, t0, t0, OP.mult)
    v.tensor_tensor(t1, vA, vB, OP.add)
    v.tensor_scalar(t1, t1, 0.5, None, OP.mult)
    v.scalar_tensor_tensor(m2_col, t0, 0.25, t1, OP.mult, OP.add)
    if want_meansq:
        v.tensor_tensor(t0, mu_col, mu_col, OP.mult)
        v.tensor_tensor(m2_col, m2_col, t0, OP.add)
    v.tensor_scalar(m2_col, m2_col, EPS, None, OP.add)


def _rstd_finish(s, src_col, dst_col, tmp_col):
    s.activation(tmp_col, src_col, AF.Ln)
    return s.activation(dst_col, tmp_col, AF.Exp, scale=-0.5)


def _p0(nc, C, sec, tok0, x_in, ln_w, lnw_bc, xblk, xhat_tok, xhatF, bnb,
        smalls, ident_bf, pa, pb):
    dx_at = {}
    v_stt = {}
    t_done = {}

    def g_lnw(g):
        C.dma(g.dma_start(lnw_bc[:], bcast_ap(ln_w, 128, H)), "dw")
    sec("gpsimd", g_lnw)
    dw_lnw = C.cnt["dw"]

    def d_pre(s):
        for b in range(2):
            dx_at[b] = C.dma(s.dma_start(
                xblk[b % 2][:], x_in[tok0 + b * 128:tok0 + (b + 1) * 128, :]), "dx")
    sec("sync", d_pre)

    for b in range(NB):
        def vf(v, b=b, dxf=None):
            C.wait(v, "dx")
            if b == 0:
                C.wait(v, "dw", dw_lnw)
            _ln_stats(C, v, bnb, xblk[b % 2][:], H, smalls[:, 8:9], smalls[:, 9:10],
                      smalls, 12, want_meansq=True)
            C.inc(v.tensor_copy(smalls[:, 18:19], smalls[:, 9:10]), "vE")
        sec("vector", vf)
        v_bn = C.cnt["vE"]

        def sf(s, b=b):
            C.wait(s, "vE", v_bn)
            C.inc(_rstd_finish(s, smalls[:, 18:19], smalls[:, 10:11],
                               smalls[:, 11:12]), "sE")
        sec("scalar", sf)
        s_rs = C.cnt["sE"]

        def vg(v, b=b):
            C.wait(v, "sE", s_rs)
            if b >= 1:
                C.wait(v, "tE", t_done[b - 1])
            C.inc(v.scalar_tensor_tensor(xhat_tok[:], xblk[b % 2][:],
                                         smalls[:, 10:11], lnw_bc[:],
                                         OP.mult, OP.mult), "vE")
        sec("vector", vg)
        v_stt[b] = C.cnt["vE"]

        if b + 2 < NB:
            def df(s, b=b):
                C.wait(s, "vE", v_stt[b])
                dx_at[b + 2] = C.dma(s.dma_start(
                    xblk[b % 2][:],
                    x_in[tok0 + (b + 2) * 128:tok0 + (b + 3) * 128, :]), "dx")
            sec("sync", df)

        for jb in range(4):
            def tf(t, b=b, jb=jb):
                if jb == 0:
                    C.wait(t, "vE", v_stt[b])
                key = f"pb{jb % 2}"
                C.bank_guard(t, key)
                for i in range(8):
                    c = jb * 8 + i
                    mm = t.transpose(pb[jb % 2][:, i * 128:(i + 1) * 128],
                                     xhat_tok[:, c * 128:(c + 1) * 128], ident_bf[:])
                C.inc(mm, "tE")
            sec("tensor", tf)
            t_tr = C.cnt["tE"]

            def sg(v, b=b, jb=jb, t_tr=t_tr):
                v.wait_ge(C.sems["tE"], t_tr)
                dstap = xhatF[:, jb * 8 * TT:(jb * 8 + 8) * TT] \
                    .rearrange("p (c t) -> p c t", t=TT)[:, :, b * 128:(b + 1) * 128]
                C.inc(v.tensor_copy(
                    dstap, pb[jb % 2][:, 0:1024].rearrange("p (c t) -> p c t", t=128)),
                    "vE")
                C.bank_mark(f"pb{jb % 2}", "vE")
            sec("vector", sg)
        t_done[b] = C.cnt["tE"]


def _a1(nc, C, sec, tok0, w_in, xhatF, wbuf, mup_bc, ztmp, bst, xbct, accb,
        convwF, tailsave, bmcf, dtT, xstg, xsspill, ident_bf, pa, pb):
    n_slab = 18
    dw_at = {}
    mm_done = {}
    ev_done = {}
    tg_at = {}
    conv_tap = {}
    spill_at = {}
    s_silu_at = {}

    def pre(g):
        def f(gp, g=g):
            col0 = 4096 + g * 256
            w = 64 if g == 17 else 256
            if g >= 2:
                gp.wait_ge(C.sems["tE"], mm_done[g - 2])
            dw_at[g] = C.dma(gp.dma_start(
                wbuf[g % 2][:, 0:32 * w].rearrange("p (c f) -> p c f", f=w),
                w_in.rearrange("(c p) f -> p c f", p=128)[:, :, col0:col0 + w]),
                "dwa" if g % 2 == 0 else "dwb")
        sec("gpsimd", f)

    pre(0)
    pre(1)

    for g in range(n_slab):
        col0 = 4096 + g * 256
        w = 64 if g == 17 else 256

        if g < 16:
            def gtail(gp, g=g):
                if 2 * g - 8 >= 0:
                    gp.wait_ge(C.sems["vE"], conv_tap[2 * g + 1 - 8])
                cc0 = 2 * g
                gp.tensor_copy(xbct[cc0 % 8][:, 0:3],
                               tailsave[:, cc0 * 3:cc0 * 3 + 3])
                C.inc(gp.tensor_copy(xbct[(cc0 + 1) % 8][:, 0:3],
                                     tailsave[:, (cc0 + 1) * 3:(cc0 + 1) * 3 + 3]),
                      "gE")
            sec("gpsimd", gtail)
        g_tail_in = C.cnt["gE"]

        for b in range(NB):
            idx = g * NB + b

            def tf(t, g=g, b=b, w=w, idx=idx):
                if b == 0:
                    C.wait(t, "dwa" if g % 2 == 0 else "dwb")
                C.bank_guard(t, f"pa{idx % 3}")
                for c in range(32):
                    mm = t.matmul(pa[idx % 3][:, 0:w],
                                  xhatF[:, c * TT + b * 128:c * TT + (b + 1) * 128],
                                  wbuf[g % 2][:, c * w:(c + 1) * w],
                                  start=(c == 0), stop=(c == 31))
                C.inc(mm, "tE")
                mm_done[g] = C.cnt["tE"]
            sec("tensor", tf)
            t_mm = C.cnt["tE"]

            def vf(v, g=g, b=b, w=w, col0=col0, idx=idx, t_mm=t_mm):
                C.wait(v, "tE", t_mm)
                if g < 17 and idx - 1 in tg_at:
                    C.wait(v, "tE", tg_at[idx - 1])
                if g == 17:
                    C.inc(v.tensor_tensor(dtT[:, b * 64:(b + 1) * 64],
                                          pa[idx % 3][:, 0:64],
                                          mup_bc[:, col0:col0 + 64], OP.mult), "vE")
                else:
                    C.inc(v.tensor_tensor(bst[:, 0:w], pa[idx % 3][:, 0:w],
                                          mup_bc[:, col0:col0 + w], OP.mult), "vE")
                C.bank_mark(f"pa{idx % 3}", "vE")
                ev_done[idx] = C.cnt["vE"]
            sec("vector", vf)
            v_ev = C.cnt["vE"]

            if g < 17:
                def tg(t, g=g, b=b, idx=idx, v_ev=v_ev):
                    C.wait(t, "vE", v_ev)
                    key = f"pb{idx % 2}"
                    C.bank_guard(t, key)
                    for i in range(2):
                        mm = t.transpose(pb[idx % 2][:, i * 128:(i + 1) * 128],
                                         bst[:, i * 128:(i + 1) * 128], ident_bf[:])
                    C.inc(mm, "tE")
                    tg_at[idx] = C.cnt["tE"]
                sec("tensor", tg)
                t_tr = C.cnt["tE"]

                def sg(v, g=g, b=b, idx=idx, t_tr=t_tr):
                    v.wait_ge(C.sems["tE"], t_tr)
                    last = None
                    for i in range(2):
                        cc = 2 * g + i
                        if cc < 32:
                            dst = xbct[cc % 8][:, 3 + b * 128:3 + (b + 1) * 128]
                        else:
                            dst = bmcf[:, (cc - 32) * TT + b * 128:
                                       (cc - 32) * TT + (b + 1) * 128]
                        last = v.tensor_copy(dst, pb[idx % 2][:, i * 128:(i + 1) * 128])
                    C.inc(last, "vE")
                    C.bank_mark(f"pb{idx % 2}", "vE")
                sec("vector", sg)

        if g + 2 < n_slab:
            pre(g + 2)

        if g < 16:
            for i in range(2):
                cc = 2 * g + i
                _conv_chunk(nc, C, sec, tok0, cc, xbct[cc % 8], accb, convwF,
                            tailsave, xstg, xsspill, bmcf, conv_tap, spill_at)
        elif g == 16:
            for i in range(2):
                cc = 32 + i
                _conv_chunk(nc, C, sec, tok0, cc, None, accb, convwF,
                            tailsave, xstg, xsspill, bmcf, conv_tap, spill_at)


def _conv_chunk(nc, C, sec, tok0, cc, buf, accb, convwF, tailsave, xstg,
                xsspill, bmcf, conv_tap, spill_at):
    if cc >= 32:
        data = bmcf[:, (cc - 32) * TT:(cc - 32 + 1) * TT]
    else:
        data = buf[:, 3:3 + TT]
    sE_now = C.cnt["sE"]
    vE_now = C.cnt["vE"]
    gE_now = C.cnt["gE"]

    def vf(v, cc=cc, data=data, sE_now=sE_now, vE_now=vE_now, gE_now=gE_now):
        C.wait(v, "sE", sE_now)
        C.wait(v, "vE", vE_now)
        C.wait(v, "gE", gE_now)
        w = lambda k: convwF[:, cc * 4 + k:cc * 4 + k + 1]
        if cc < 32:
            ext = buf
            v.tensor_scalar(accb[:], ext[:, 0:TT], w(0), None, OP.mult)
            for k in range(1, 3):
                v.scalar_tensor_tensor(accb[:], ext[:, k:k + TT], w(k), accb[:],
                                       OP.mult, OP.add)
            last = v.scalar_tensor_tensor(accb[:], ext[:, 3:3 + TT], w(3), accb[:],
                                          OP.mult, OP.add)
        else:
            ts3 = tailsave[:, cc * 3:cc * 3 + 3]
            v.tensor_scalar(accb[:], data[:], w(3), None, OP.mult)
            for k in range(3):
                sh = 3 - k
                v.scalar_tensor_tensor(accb[:, sh:TT], data[:, 0:TT - sh], w(k),
                                       accb[:, sh:TT], OP.mult, OP.add)
                last = v.scalar_tensor_tensor(accb[:, 0:sh], ts3[:, 3 - sh:3], w(k),
                                              accb[:, 0:sh], OP.mult, OP.add)
        C.inc(last, "vE")
        conv_tap[cc] = C.cnt["vE"]
    sec("vector", vf)
    v_tap = C.cnt["vE"]

    def gp2(g, cc=cc, data=data, v_tap=v_tap):
        C.wait(g, "vE", v_tap)
        C.inc(g.tensor_copy(tailsave[:, cc * 3:cc * 3 + 3], data[:, TT - 3:TT]), "gE")
    sec("gpsimd", gp2)
    g_tail = C.cnt["gE"]

    def sf(s, cc=cc, data=data, v_tap=v_tap, g_tail=g_tail):
        C.wait(s, "vE", v_tap)
        C.wait(s, "gE", g_tail)
        if cc < 32:
            if cc >= 2:
                C.wait(s, "doa" if cc % 2 == 0 else "dob")
            dst = xstg[cc % 2][:]
        else:
            dst = data
        C.inc(s.activation(dst, accb[:], AF.Silu), "sE")
    sec("scalar", sf)
    s_silu = C.cnt["sE"]

    if cc < 32:
        def df(sy, cc=cc, s_silu=s_silu):
            C.wait(sy, "sE", s_silu)
            spill_at[cc] = C.dma(sy.dma_start(
                xsspill[cc * 128:(cc + 1) * 128, tok0:tok0 + TT], xstg[cc % 2][:]),
                "doa" if cc % 2 == 0 else "dob")
        sec("sync", df)


def _a2(nc, C, sec, tok0, w_in, xhatF, wbuf, mup_bc, ztmp, zstg, zspill, pa):
    dw_at = {}
    mm_done = {}
    do_at = {}
    sz_at = {}

    def pre(g):
        def f(gp, g=g):
            col0 = g * 256
            if g >= 2:
                gp.wait_ge(C.sems["tE"], mm_done[g - 2])
            dw_at[g] = C.dma(gp.dma_start(
                wbuf[g % 2][:, 0:32 * 256].rearrange("p (c f) -> p c f", f=256),
                w_in.rearrange("(c p) f -> p c f", p=128)[:, :, col0:col0 + 256]),
                "dwa" if g % 2 == 0 else "dwb")
        sec("gpsimd", f)

    pre(0)
    pre(1)
    for g in range(16):
        col0 = g * 256
        for b in range(NB):
            idx = g * NB + b

            def tf(t, g=g, b=b, idx=idx):
                if b == 0:
                    C.wait(t, "dwa" if g % 2 == 0 else "dwb")
                C.bank_guard(t, f"pa{idx % 3}")
                for c in range(32):
                    mm = t.matmul(pa[idx % 3][:, 0:256],
                                  xhatF[:, c * TT + b * 128:c * TT + (b + 1) * 128],
                                  wbuf[g % 2][:, c * 256:(c + 1) * 256],
                                  start=(c == 0), stop=(c == 31))
                C.inc(mm, "tE")
                mm_done[g] = C.cnt["tE"]
            sec("tensor", tf)
            t_mm = C.cnt["tE"]

            def vf(v, col0=col0, idx=idx, t_mm=t_mm):
                C.wait(v, "tE", t_mm)
                if idx - 1 in sz_at:
                    C.wait(v, "sE", sz_at[idx - 1])
                C.inc(v.tensor_tensor(ztmp[:, 0:256], pa[idx % 3][:, 0:256],
                                      mup_bc[:, col0:col0 + 256], OP.mult), "vE")
                C.bank_mark(f"pa{idx % 3}", "vE")
            sec("vector", vf)
            v_ev = C.cnt["vE"]

            def sf(s, idx=idx, v_ev=v_ev):
                C.wait(s, "vE", v_ev)
                if idx >= 2:
                    C.wait(s, "doa" if idx % 2 == 0 else "dob")
                C.inc(s.activation(zstg[idx % 2][:, 0:256], ztmp[:, 0:256],
                                   AF.Silu), "sE")
                sz_at[idx] = C.cnt["sE"]
            sec("scalar", sf)
            s_z = C.cnt["sE"]

            def df(sy, b=b, col0=col0, idx=idx, s_z=s_z):
                C.wait(sy, "sE", s_z)
                do_at[idx] = C.dma(sy.dma_start(
                    zspill[tok0 + b * 128:tok0 + (b + 1) * 128, col0:col0 + 256],
                    zstg[idx % 2][:, 0:256]), "doa" if idx % 2 == 0 else "dob")
            sec("sync", df)
        if g + 2 < 16:
            pre(g + 2)


def _bc(ap, dims):
    return bass.AP(ap.tensor, ap.offset, dims)


def _scan_front(nc, C, sec, tok0, ch, sc, wbuf, xsspill, zspill, dtT,
                dbias_bc, nega_bc, bmcf, ones_f16, ones_f32, tri_f16,
                su_f16, tri_f32, ident_bf, ident_f32, sm64, pa, pb, X):
    t0c = tok0 + ch * 128
    par = ch % 2
    xrd = wbuf[0][:, par * 4096:par * 4096 + 4096]
    zrd = wbuf[1][:, par * 4096:par * 4096 + 4096]

    def d0(sy, par=par):
        if X["xrd_free"][par]:
            sy.wait_ge(C.sems["tE"], X["xrd_free"][par])
        if X["zrd_free"][par]:
            sy.wait_ge(C.sems["vE"], X["zrd_free"][par])
        C.dma(sy.dma_start(
            xrd.rearrange("p (c t) -> p c t", t=128),
            xsspill.rearrange("(c p) t -> p c t", p=128)[:, :, t0c:t0c + 128]), "dx")
        C.dma(sy.dma_start(zrd, zspill[t0c:t0c + 128, :]), "dx")
    sec("sync", d0)
    dxv = C.cnt["dx"]

    for jb in range(4):
        def tf(t, jb=jb, dxv=dxv):
            if jb == 0:
                C.wait(t, "dx", dxv)
            key = f"pb{jb % 2}"
            C.bank_guard(t, key)
            for i in range(8):
                c = jb * 8 + i
                mm = t.transpose(pb[jb % 2][:, i * 128:(i + 1) * 128],
                                 xrd[:, c * 128:(c + 1) * 128], ident_bf[:])
            C.inc(mm, "tE")
        sec("tensor", tf)
        t_tr = C.cnt["tE"]

        def sg(v, jb=jb, t_tr=t_tr):
            v.wait_ge(C.sems["tE"], t_tr)
            C.inc(v.tensor_copy(sc["xsT"][:, jb * 1024:(jb + 1) * 1024],
                                pb[jb % 2][:, 0:1024]), "vE")
            C.bank_mark(f"pb{jb % 2}", "vE")
        sec("vector", sg)
    X["xrd_free"][par] = C.cnt["tE"]

    def v_dt(v):
        dts = dtT[:, ch * 64:(ch + 1) * 64]
        if X["scum"]:
            C.wait(v, "sE", X["scum"])
        v.tensor_tensor(sc["t64a"][:], dts, dbias_bc[:], OP.add)
        C.inc(v.scalar_tensor_tensor(sc["t64b"][:], sc["t64a"][:], -1.0,
                                     sc["t64a"][:], OP.mult, OP.min), "vE")
    sec("vector", v_dt)
    v_1 = C.cnt["vE"]

    def s_dt(s, v_1=v_1):
        C.wait(s, "vE", v_1)
        s.activation(sc["t64c"][:], sc["t64b"][:], AF.Exp)
        C.inc(s.activation(sc["t64b"][:], sc["t64c"][:], AF.Ln, bias=1.0), "sE")
    sec("scalar", s_dt)
    s_1 = C.cnt["sE"]

    def v_dt2(v, s_1=s_1):
        C.wait(v, "sE", s_1)
        if X["tcu_prev"]:
            v.wait_ge(C.sems["tE"], X["tcu_prev"])
        v.tensor_scalar(sc["t64c"][:], sc["t64a"][:], 0.0, None, OP.max)
        v.tensor_tensor(sc["dtc"][:], sc["t64b"][:], sc["t64c"][:], OP.add)
        v.tensor_tensor(sc["dta32"][:], sc["dtc"][:], nega_bc[:], OP.mult)
        C.inc(v.tensor_copy(sc["dta16"][:], sc["dta32"][:]), "vE")
    sec("vector", v_dt2)
    v_2 = C.cnt["vE"]

    def t_cum(t, v_2=v_2):
        C.wait(t, "vE", v_2)
        C.bank_guard(t, "pa3")
        t.matmul(pa[3][0:64, 0:128], sc["dta16"][:], tri_f16[:], start=True,
                 stop=True)
        t.matmul(pa[3][:, 128:192], su_f16[:], sc["dta16"][:], start=True,
                 stop=True)
        C.inc(t.matmul(pa[3][0:1, 192:256], ones_f16[:, 0:1], sc["dta16"][:],
                       start=True, stop=True), "tE")
    sec("tensor", t_cum)
    t_cu = C.cnt["tE"]
    X["tcu_prev"] = t_cu

    def s_cum(s, t_cu=t_cu, par=par):
        C.wait(s, "tE", t_cu)
        if X["vyx_last"]:
            C.wait(s, "vE", X["vyx_last"])
        s.activation(sc["vf16"][par][:], pa[3][0:64, 0:128], AF.Exp)
        C.inc(s.activation(sc["t64a"][:], pa[3][:, 128:192], AF.Exp), "sE")
        C.bank_mark("pa3", "sE")
    sec("scalar", s_cum)
    s_2 = C.cnt["sE"]
    X["scum"] = s_2

    def v_sb(v, t_cu=t_cu):
        v.wait_ge(C.sems["tE"], t_cu)
        C.inc(v.tensor_copy(sc["srow"][:], pa[3][0:1, 192:256]), "vE")
    sec("vector", v_sb)
    v_sr = C.cnt["vE"]

    def t_sb2(t, v_sr=v_sr):
        C.wait(t, "vE", v_sr)
        C.bank_guard(t, "pa3")
        C.inc(t.matmul(pa[3][:, 448:512], ones_f32[0:1, :], sc["srow"][:],
                       start=True, stop=True), "tE")
    sec("tensor", t_sb2)
    t_sb2a = C.cnt["tE"]

    def s_sb(s, t_sb2a=t_sb2a):
        s.wait_ge(C.sems["tE"], t_sb2a)
        C.inc(s.activation(sc["sbc"][:], pa[3][:, 448:512], AF.Exp), "sE")
        C.bank_mark("pa3", "sE")
    sec("scalar", s_sb)

    def t_cb(t, par=par):
        C.wait(t, "vE")
        C.bank_guard(t, "pa3")
        t.matmul(pa[3][:, 256:384], bmcf[:, ch * 128:(ch + 1) * 128],
                 bmcf[:, TT + ch * 128:TT + (ch + 1) * 128], start=True, stop=True)
        C.bank_guard(t, "pb0")
        C.inc(t.transpose(pb[0][:, 0:128], bmcf[:, ch * 128:(ch + 1) * 128],
                          ident_bf[:]), "tE")
    sec("tensor", t_cb)
    t_cb_at = C.cnt["tE"]

    def v_cb(v, t_cb_at=t_cb_at, par=par):
        v.wait_ge(C.sems["tE"], t_cb_at)
        v.tensor_tensor(sc["cbtm"][par][:], pa[3][:, 256:384], tri_f32[:], OP.mult)
        C.bank_mark("pa3", "vE")
        C.inc(v.tensor_copy(sc["bmt"][par][:], pb[0][:, 0:128]), "vE")
        C.bank_mark("pb0", "vE")
    sec("vector", v_cb)
    X["front_v"] = C.cnt["vE"]


def _scan_groups(nc, C, sec, tok0, ch, sc, wbuf, dcol_bc, h_bf, bmcf, ysb,
                 e2_f16, bnb, tri_f16, su_f16, ident_bf, pa, pdm, pb, X):
    par = ch % 2
    zrd = wbuf[1][:, par * 4096:par * 4096 + 4096]
    front_v = X["front_v"]

    for g in range(8):
        gg = X["base_g"] + g
        heads = list(range(8 * g, 8 * g + 8))

        for half in range(2):
            hh = 2 * gg + half
            h0 = 8 * g + half * 4

            def v_r(v, hh=hh, h0=h0, front_v=front_v):
                C.wait(v, "vE", front_v)
                if X["dmm"].get(hh - 2):
                    v.wait_ge(C.sems["tE"], X["dmm"][hh - 2])
                C.inc(v.tensor_tensor(
                    sc["rhsd"][hh % 2][:].rearrange("p (a b) -> p a b", b=128),
                    _bc(tri_f16[:], [[128, 128], [0, 4], [1, 128]]),
                    _bc(sc["dta32"][:, h0:h0 + 4], [[64, 128], [1, 4], [0, 128]]),
                    OP.mult), "vE")
            sec("vector", v_r)
            v_rh = C.cnt["vE"]

            def t_d(t, hh=hh, v_rh=v_rh):
                C.wait(t, "vE", v_rh)
                key = f"pd{hh % 2}"
                C.bank_guard(t, key)
                half0 = (hh % 2) * 512
                for i in range(4):
                    mm = t.matmul(pdm[:, half0 + i * 128:half0 + (i + 1) * 128],
                                  su_f16[:],
                                  sc["rhsd"][hh % 2][:, i * 128:(i + 1) * 128],
                                  start=(i == 0), stop=(i == 3))
                C.inc(mm, "tE")
            sec("tensor", t_d)
            X["dmm"][hh] = C.cnt["tE"]

        def s_e(s, gg=gg):
            s.wait_ge(C.sems["tE"], X["dmm"][2 * gg + 1])
            if X["expfree"].get(gg - 2):
                C.wait(s, "vE", X["expfree"][gg - 2])
            C.inc(s.activation(sc["expd"][gg % 2][:], pdm[:, 0:1024], AF.Exp),
                  "sE")
            C.bank_mark("pd0", "sE")
            C.bank_mark("pd1", "sE")
        sec("scalar", s_e)
        exp_g = C.cnt["sE"]

        def v_gA(v, g=g, gg=gg, par=par, exp_g=exp_g):
            C.wait(v, "sE", exp_g)
            if X["mfree"].get(gg - 2):
                v.wait_ge(C.sems["tE"], X["mfree"][gg - 2])
            v.tensor_tensor(
                sc["mtil"][gg % 2][:].rearrange("p (a b) -> p a b", b=128),
                sc["expd"][gg % 2][:].rearrange("p (a b) -> p a b", b=128),
                _bc(sc["cbtm"][par][:], [[128, 128], [0, 8], [1, 128]]),
                OP.mult)
            v.tensor_tensor(
                sc["xsTd"][gg % 2][:].rearrange("p (a b) -> p a b", b=64),
                sc["xsT"][:, g * 512:(g + 1) * 512].rearrange("p (a b) -> p a b", b=64),
                _bc(sc["dtc"][:, g * 8:g * 8 + 8], [[64, 128], [1, 8], [0, 64]]),
                OP.mult)
            C.inc(v.tensor_tensor(
                sc["dxst"][:].rearrange("p (a b) -> p a b", b=64),
                sc["xsT"][:, g * 512:(g + 1) * 512].rearrange("p (a b) -> p a b", b=64),
                _bc(dcol_bc[:, g * 8:g * 8 + 8], [[64, 128], [1, 8], [0, 64]]),
                OP.mult), "vE")
        sec("vector", v_gA)
        v_ga = C.cnt["vE"]
        X["expfree"][gg] = v_ga

        def t_iv(t, g=g, gg=gg, par=par, v_ga=v_ga):
            C.wait(t, "vE", v_ga)
            C.bank_guard(t, "pa1")
            for i in range(8):
                t.matmul(pa[1][:, i * 64:(i + 1) * 64],
                         sc["mtil"][gg % 2][:, i * 128:(i + 1) * 128],
                         sc["xsTd"][gg % 2][:, i * 64:(i + 1) * 64],
                         start=(i == 0), stop=(i == 7))
            C.bank_guard(t, "pa0")
            if X["hev"].get(g):
                t.wait_ge(C.sems["vE"], X["hev"][g])
            for i, r in enumerate(heads):
                mm = t.matmul(pa[0][:, i * 64:(i + 1) * 64],
                              bmcf[:, TT + ch * 128:TT + (ch + 1) * 128],
                              h_bf[:, r * 64:(r + 1) * 64],
                              start=(i == 0), stop=(i == 7))
            C.inc(mm, "tE")
            key = f"pd{g % 2}"
            C.bank_guard(t, key)
            C.inc(t.matmul(pdm[:, (g % 2) * 512:(g % 2) * 512 + 512],
                           sc["vf16"][par][:], e2_f16[:, g * 512:(g + 1) * 512],
                           start=True, stop=True), "tE")
        sec("tensor", t_iv)
        t_ve = C.cnt["tE"]
        t_int = t_ve - 1
        X["mfree"][gg] = t_int

        def v_gB(v, g=g, t_ve=t_ve, t_int=t_int):
            v.wait_ge(C.sems["tE"], t_ve)
            v.tensor_copy(sc["vexp"][:], pdm[:, (g % 2) * 512:(g % 2) * 512 + 512])
            C.bank_mark(f"pd{g % 2}", "vE")
            C.inc(v.scalar_tensor_tensor(sc["ytmp"][:], pa[0][:, 0:512], 1.0,
                                         sc["vexp"][:], OP.mult, OP.mult), "vE")
            C.bank_mark("pa0", "vE")
        sec("vector", v_gB)
        v_yt = C.cnt["vE"]

        def v_gC(v, g=g, gg=gg, par=par, t_int=t_int):
            v.scalar_tensor_tensor(sc["yA"][:], pa[1][:, 0:512], 1.0,
                                   sc["ytmp"][:], OP.mult, OP.add)
            C.bank_mark("pa1", "vE")
            v.scalar_tensor_tensor(sc["yA"][:], sc["dxst"][:], 1.0, sc["yA"][:],
                                   OP.mult, OP.add)
            v.tensor_tensor(ysb[:, g * 512:(g + 1) * 512], sc["yA"][:],
                            zrd[:, g * 512:(g + 1) * 512], OP.mult)
            v.bn_stats(bnb[g % 2][:, (g // 2) * 6:(g // 2) * 6 + 6],
                       ysb[:, g * 512:(g + 1) * 512])
            if X["st"].get(gg - 1):
                v.wait_ge(C.sems["tE"], X["st"][gg - 1])
            v.tensor_tensor(
                sc["xtld"][:].rearrange("p (a b) -> p a b", b=64),
                sc["xsTd"][gg % 2][:].rearrange("p (a b) -> p a b", b=64),
                _bc(sc["t64a"][:, g * 8:g * 8 + 8], [[64, 128], [1, 8], [0, 64]]),
                OP.mult)
            C.inc(v.tensor_tensor(
                sc["hdec"][:].rearrange("p (a b) -> p a b", b=64),
                h_bf[:, g * 512:(g + 1) * 512].rearrange("p (a b) -> p a b", b=64),
                _bc(sc["sbc"][:, g * 8:g * 8 + 8], [[64, 128], [1, 8], [0, 64]]),
                OP.mult), "vE")
        sec("vector", v_gC)
        v_xt = C.cnt["vE"]

        def t_st(t, g=g, gg=gg, par=par, v_xt=v_xt):
            C.wait(t, "vE", v_xt)
            C.bank_guard(t, "pa2")
            for i in range(8):
                t.matmul(pa[2][:, i * 64:(i + 1) * 64], sc["bmt"][par][:],
                         sc["xtld"][:, i * 64:(i + 1) * 64],
                         start=(i == 0), stop=False)
            C.inc(t.matmul(pa[2][:, 0:512], ident_bf[:], sc["hdec"][:],
                           start=False, stop=True), "tE")
            X["st"][gg] = C.cnt["tE"]
        sec("tensor", t_st)
        t_hu = C.cnt["tE"]
        if g == 7:
            X["st_last"] = t_hu

        def v_h(v, g=g, t_hu=t_hu):
            v.wait_ge(C.sems["tE"], t_hu)
            C.inc(v.tensor_copy(h_bf[:, g * 512:(g + 1) * 512], pa[2][:, 0:512]),
                  "vE")
            C.bank_mark("pa2", "vE")
            X["hev"][g] = C.cnt["vE"]
        sec("vector", v_h)

    X["vyx_last"] = C.cnt["vE"]
    X["zrd_free"][par] = C.cnt["vE"]
    X["base_g"] += 8


def _scan_tail(nc, C, sec, tok0, ch, sc, nw_bc, ysb, yhatF, ident_bf, smalls,
               sm64, bnb, pa, pb, X):
    def v_tail(v):
        v.bn_aggr(smalls[:, 12:14], bnb[0][:, 0:24])
        v.bn_aggr(bnb[1][:, 18:20], bnb[1][:, 0:24])
        mA, vA = smalls[:, 12:13], smalls[:, 13:14]
        mB, vB = bnb[1][:, 18:19], bnb[1][:, 19:20]
        t0, t1 = smalls[:, 16:17], smalls[:, 17:18]
        v.tensor_tensor(smalls[:, 8:9], mA, mB, OP.add)
        v.tensor_scalar(smalls[:, 8:9], smalls[:, 8:9], 0.5, None, OP.mult)
        v.tensor_tensor(t0, mA, mB, OP.subtract)
        v.tensor_tensor(t0, t0, t0, OP.mult)
        v.tensor_tensor(t1, vA, vB, OP.add)
        v.tensor_scalar(t1, t1, 0.5, None, OP.mult)
        v.scalar_tensor_tensor(smalls[:, 9:10], t0, 0.25, t1, OP.mult, OP.add)
        v.tensor_tensor(t0, smalls[:, 8:9], smalls[:, 8:9], OP.mult)
        v.tensor_tensor(smalls[:, 9:10], smalls[:, 9:10], t0, OP.add)
        v.tensor_scalar(smalls[:, 9:10], smalls[:, 9:10], EPS, None, OP.add)
        C.inc(v.tensor_copy(smalls[:, 18:19], smalls[:, 9:10]), "vE")
    sec("vector", v_tail)
    v_t1 = C.cnt["vE"]

    def s_tail(s, v_t1=v_t1):
        C.wait(s, "vE", v_t1)
        if X["vt2"]:
            C.wait(s, "vE", X["vt2"])
        C.inc(_rstd_finish(s, smalls[:, 18:19], smalls[:, 10:11],
                           smalls[:, 11:12]), "sE")
    sec("scalar", s_tail)
    s_t1 = C.cnt["sE"]

    def v_tail2(v, s_t1=s_t1):
        C.wait(v, "sE", s_t1)
        if X["ytr"]:
            v.wait_ge(C.sems["tE"], X["ytr"])
        C.inc(v.scalar_tensor_tensor(sc["yhat_tok"][:], ysb[:], smalls[:, 10:11],
                                     nw_bc[:], OP.mult, OP.mult), "vE")
    sec("vector", v_tail2)
    v_t2 = C.cnt["vE"]
    X["vt2"] = v_t2

    for jb in range(4):
        def tf2(t, jb=jb, v_t2=v_t2):
            if jb == 0:
                C.wait(t, "vE", v_t2)
            key = f"pb{jb % 2}"
            C.bank_guard(t, key)
            for i in range(8):
                c = jb * 8 + i
                mm = t.transpose(pb[jb % 2][:, i * 128:(i + 1) * 128],
                                 sc["yhat_tok"][:, c * 128:(c + 1) * 128],
                                 ident_bf[:])
            C.inc(mm, "tE")
        sec("tensor", tf2)
        t_tr2 = C.cnt["tE"]

        def sg2(v, jb=jb, t_tr2=t_tr2):
            v.wait_ge(C.sems["tE"], t_tr2)
            dstap = yhatF[:, jb * 8 * TT:(jb * 8 + 8) * TT] \
                .rearrange("p (c t) -> p c t", t=TT)[:, :, ch * 128:(ch + 1) * 128]
            C.inc(v.tensor_copy(
                dstap, pb[jb % 2][:, 0:1024].rearrange("p (c t) -> p c t", t=128)),
                "vE")
            C.bank_mark(f"pb{jb % 2}", "vE")
        sec("vector", sg2)
    X["ytr"] = C.cnt["tE"]


def _e_outproj(nc, C, sec, tok0, w_out, yhatF, wbuf, x1stg, x1spill, pa):
    dw_at = {}
    mm_done = {}
    do_at = {}

    def pre(g):
        def f(gp, g=g):
            col0 = g * 256
            if g >= 2:
                gp.wait_ge(C.sems["tE"], mm_done[g - 2])
            dw_at[g] = C.dma(gp.dma_start(
                wbuf[g % 2][:, 0:32 * 256].rearrange("p (c f) -> p c f", f=256),
                w_out.rearrange("(c p) f -> p c f", p=128)[:, :, col0:col0 + 256]),
                "dwa" if g % 2 == 0 else "dwb")
        sec("gpsimd", f)

    pre(0)
    pre(1)
    for g in range(16):
        col0 = g * 256
        for b in range(NB):
            idx = g * NB + b

            def tf(t, g=g, b=b, idx=idx):
                if b == 0:
                    C.wait(t, "dwa" if g % 2 == 0 else "dwb")
                C.bank_guard(t, f"pa{idx % 3}")
                for c in range(32):
                    mm = t.matmul(pa[idx % 3][:, 0:256],
                                  yhatF[:, c * TT + b * 128:c * TT + (b + 1) * 128],
                                  wbuf[g % 2][:, c * 256:(c + 1) * 256],
                                  start=(c == 0), stop=(c == 31))
                C.inc(mm, "tE")
                mm_done[g] = C.cnt["tE"]
            sec("tensor", tf)
            t_mm = C.cnt["tE"]

            def sf(v, idx=idx, t_mm=t_mm):
                v.wait_ge(C.sems["tE"], t_mm)
                if idx >= 2:
                    C.wait(v, "doa" if idx % 2 == 0 else "dob")
                C.inc(v.tensor_copy(x1stg[idx % 2][:, 0:256], pa[idx % 3][:, 0:256]),
                      "vE")
                C.bank_mark(f"pa{idx % 3}", "vE")
            sec("vector", sf)
            s_x1 = C.cnt["vE"]

            def df(sy, b=b, col0=col0, idx=idx, s_x1=s_x1):
                C.wait(sy, "vE", s_x1)
                do_at[idx] = C.dma(sy.dma_start(
                    x1spill[tok0 + b * 128:tok0 + (b + 1) * 128, col0:col0 + 256],
                    x1stg[idx % 2][:, 0:256]), "doa" if idx % 2 == 0 else "dob")
            sec("sync", df)
        if g + 2 < 16:
            pre(g + 2)


def _f_prep(nc, C, sec, ffn_a, ffn_b, ffn_wq, ffn_keys, wbuf, awT, bwT,
            mT, keysT, ident_bf, pa, pb):
    def d0(gp):
        for h in range(2):
            C.dma(gp.dma_start(
                wbuf[h][:, 0:8192].rearrange("p (j d) -> p j d", d=4096),
                ffn_a.rearrange("(j p) d -> p j d", p=128)[:, 2 * h:2 * h + 2, :]),
                "dw")
    sec("gpsimd", d0)
    dw_a = C.cnt["dw"]

    for cb in range(16):
        def tf(t, cb=cb):
            if cb == 0:
                C.wait(t, "dw", dw_a)
            key = f"pb{cb % 2}"
            C.bank_guard(t, key)
            for cc in range(2):
                c = cb * 2 + cc
                for j in range(4):
                    mm = t.transpose(
                        pb[cb % 2][:, cc * 512 + j * 128:cc * 512 + (j + 1) * 128],
                        wbuf[j // 2][:, (j % 2) * 4096 + c * 128:
                                     (j % 2) * 4096 + (c + 1) * 128],
                        ident_bf[:])
            C.inc(mm, "tE")
        sec("tensor", tf)
        t_tr = C.cnt["tE"]

        def sg(s, cb=cb, t_tr=t_tr):
            s.wait_ge(C.sems["tE"], t_tr)
            C.inc(s.activation(awT[:, cb * 1024:(cb + 1) * 1024],
                               pb[cb % 2][:, 0:1024], AF.Copy), "sE")
            C.bank_mark(f"pb{cb % 2}", "sE")
        sec("scalar", sg)
    t_a_done = C.cnt["tE"]

    def d1(gp, t_a_done=t_a_done):
        gp.wait_ge(C.sems["tE"], t_a_done)
        for h in range(2):
            C.dma(gp.dma_start(
                wbuf[h][:, 0:8192].rearrange("p (j n) -> p j n", n=512),
                ffn_b.rearrange("(j p) n -> p j n", p=128)[:, 16 * h:16 * h + 16, :]),
                "dw")
    sec("gpsimd", d1)
    dw_b = C.cnt["dw"]

    for nch in range(4):
        for jb in range(4):
            def tf2(t, nch=nch, jb=jb):
                if nch == 0 and jb == 0:
                    C.wait(t, "dw", dw_b)
                key = f"pb{jb % 2}"
                C.bank_guard(t, key)
                for i in range(8):
                    j = jb * 8 + i
                    mm = t.transpose(pb[jb % 2][:, i * 128:(i + 1) * 128],
                                     wbuf[j // 16][:, (j % 16) * 512 + nch * 128:
                                                   (j % 16) * 512 + (nch + 1) * 128],
                                     ident_bf[:])
                C.inc(mm, "tE")
            sec("tensor", tf2)
            t_tr2 = C.cnt["tE"]

            def sg2(s, nch=nch, jb=jb, t_tr2=t_tr2):
                s.wait_ge(C.sems["tE"], t_tr2)
                C.inc(s.activation(
                    bwT[:, nch * 4096 + jb * 1024:nch * 4096 + (jb + 1) * 1024],
                    pb[jb % 2][:, 0:1024], AF.Copy), "sE")
                C.bank_mark(f"pb{jb % 2}", "sE")
            sec("scalar", sg2)
    t_b_done = C.cnt["tE"]

    def d2(gp, t_b_done=t_b_done):
        gp.wait_ge(C.sems["tE"], t_b_done)
        C.dma(gp.dma_start(wbuf[0][0:16, 0:4096], ffn_wq), "dw")
        C.dma(gp.dma_start(wbuf[0][0:64, 4096:4112], ffn_keys), "dw")
    sec("gpsimd", d2)
    dw_q = C.cnt["dw"]

    def tf4(t):
        C.wait(t, "dw", dw_q)
        C.bank_guard(t, "pb0")
        C.inc(t.transpose(pb[0][0:16, 0:64], wbuf[0][0:64, 4096:4112],
                          ident_bf[0:64, 0:64]), "tE")
    sec("tensor", tf4)
    t_k = C.cnt["tE"]

    def sg4(s, t_k=t_k):
        s.wait_ge(C.sems["tE"], t_k)
        C.inc(s.activation(keysT[:], pb[0][0:16, 0:64], AF.Copy), "sE")
        C.bank_mark("pb0", "sE")
    sec("scalar", sg4)
    s_k = C.cnt["sE"]

    for mb in range(4):
        def tm(t, mb=mb, s_k=s_k):
            if mb == 0:
                C.wait(t, "sE", s_k)
            C.bank_guard(t, f"pa{mb % 2}")
            for i in range(8):
                c = mb * 8 + i
                mm = t.matmul(pa[mb % 2][:, i * 64:(i + 1) * 64],
                              wbuf[0][0:16, c * 128:(c + 1) * 128],
                              keysT[:], start=True, stop=True)
            C.inc(mm, "tE")
        sec("tensor", tm)
        t_m = C.cnt["tE"]

        def vm(v, mb=mb, t_m=t_m):
            v.wait_ge(C.sems["tE"], t_m)
            C.inc(v.tensor_copy(mT[:, mb * 512:(mb + 1) * 512],
                                pa[mb % 2][:, 0:512]), "vE")
            C.bank_mark(f"pa{mb % 2}", "vE")
        sec("vector", vm)


def _f_block(nc, C, sec, tok0, b, x_in, x1spill, out_t, xblk2, ysb2, x3f,
             x2F, x2tok, awT, bwT, mT, fscr, fscr2, gq, gwork, fcmp,
             zhat, zhT, bnb, smalls, rcols,
             ident_bf, ident_f32, pa, pdm, pb, do_at, Xf):
    t0b = tok0 + b * 128

    def d0(sy, b=b):
        if Xf["aq"]:
            sy.wait_ge(C.sems["tE"], Xf["aq"])
        if Xf["x2Fev"]:
            sy.wait_ge(C.sems["vE"], Xf["x2Fev"])
        C.dma(sy.dma_start(xblk2[:], x_in[t0b:t0b + 128, :]), "dx")
        C.dma(sy.dma_start(x2F[:], x1spill[t0b:t0b + 128, :]), "dx")
    sec("sync", d0)
    dx_b = C.cnt["dx"]

    def v0(v, dx_b=dx_b):
        C.wait(v, "dx")
        v.tensor_tensor(ysb2[:], x2F[:, 0:H], xblk2[:], OP.add)
        _ln_stats(C, v, bnb, ysb2[:], H, smalls[:, 8:9], smalls[:, 9:10],
                  smalls, 12, want_meansq=False)
        C.inc(v.tensor_copy(smalls[:, 18:19], smalls[:, 9:10]), "vE")
    sec("vector", v0)
    v_ln = C.cnt["vE"]

    def s0(s, v_ln=v_ln):
        C.wait(s, "vE", v_ln)
        C.inc(_rstd_finish(s, smalls[:, 18:19], smalls[:, 10:11],
                           smalls[:, 11:12]), "sE")
    sec("scalar", s0)
    s_ln = C.cnt["sE"]

    def v1(v, s_ln=s_ln):
        C.wait(v, "sE", s_ln)
        if Xf["x2tr"]:
            v.wait_ge(C.sems["tE"], Xf["x2tr"])
        v.tensor_scalar(ysb2[:], ysb2[:], smalls[:, 8:9], smalls[:, 10:11],
                        OP.subtract, OP.mult)
        C.inc(v.tensor_copy(x2tok[:], ysb2[:]), "vE")
    sec("vector", v1)
    v_x2 = C.cnt["vE"]

    for jb in range(4):
        def tf(t, jb=jb, v_x2=v_x2):
            if jb == 0:
                C.wait(t, "vE", v_x2)
            C.bank_guard(t, f"pb{jb % 2}")
            for i in range(8):
                c = jb * 8 + i
                mm = t.transpose(pb[jb % 2][:, i * 128:(i + 1) * 128],
                                 x2tok[:, c * 128:(c + 1) * 128], ident_bf[:])
            C.inc(mm, "tE")
        sec("tensor", tf)
        t_tr = C.cnt["tE"]

        def sg(v, jb=jb, t_tr=t_tr):
            v.wait_ge(C.sems["tE"], t_tr)
            C.inc(v.tensor_copy(x2F[:, jb * 1024:(jb + 1) * 1024],
                                pb[jb % 2][:, 0:1024]), "vE")
            C.bank_mark(f"pb{jb % 2}", "vE")
        sec("vector", sg)
    Xf["x2tr"] = C.cnt["tE"]
    v_x2f = C.cnt["vE"]
    Xf["x2Fev"] = v_x2f

    zi = 0 if b % 2 == 0 else 3
    def t_a(t, v_x2f=v_x2f, zi=zi):
        C.wait(t, "vE", v_x2f)
        C.bank_guard(t, f"pa{zi}")
        C.bank_guard(t, "pa1")
        last = None
        for c in range(32):
            t.matmul(pa[zi][:, 0:512], x2F[:, c * 128:(c + 1) * 128],
                     awT[:, c * 512:(c + 1) * 512], start=(c == 0), stop=(c == 31))
            last = t.matmul(pa[1][:, 0:64], x2F[:, c * 128:(c + 1) * 128],
                            mT[:, c * 64:(c + 1) * 64], start=(c == 0),
                            stop=(c == 31))
        C.inc(last, "tE")
    sec("tensor", t_a)
    t_aq = C.cnt["tE"]
    Xf["aq"] = t_aq

    def v_sc(v, t_aq=t_aq):
        C.wait(v, "tE", t_aq)
        v.tensor_copy(gq[:], pa[1][:, 0:64])
        C.bank_mark("pa1", "vE")
        v.tensor_tensor(
            fcmp[:].rearrange("p (a b) -> p a b", b=64),
            _bc(gq[:], [[64, 128], [0, 64], [1, 64]]),
            _bc(gq[:], [[64, 128], [1, 64], [0, 64]]),
            OP.is_gt)
        v.reduce_sum(fscr2[:], fcmp[:].rearrange("p (a b) -> p a b", b=64),
                     axis=AX)
        v.reduce_max(rcols[0][:], gq[:], axis=AX)
        C.inc(v.tensor_scalar(rcols[1][:], rcols[0][:], -1.0, None, OP.mult),
              "vE")
    sec("vector", v_sc)
    v_rt = C.cnt["vE"]

    def s_r(s, v_rt=v_rt):
        C.wait(s, "vE", v_rt)
        C.inc(s.activation(fscr[:, 0:64], gq[:], AF.Exp, bias=rcols[1][:]), "sE")
    sec("scalar", s_r)
    s_re = C.cnt["sE"]

    def v_g(v, s_re=s_re):
        C.wait(v, "sE", s_re)
        v.tensor_scalar(gwork[:], fscr2[:], 15.0, None, OP.is_le)
        v.tensor_tensor(gwork[:], gwork[:], fscr[:, 0:64], OP.mult)
        v.reduce_sum(rcols[3][:], gwork[:], axis=AX)
        v.reciprocal(rcols[0][:], rcols[3][:])
        C.inc(v.tensor_scalar(gwork[:], gwork[:], rcols[0][:], 2.0,
                              OP.mult, OP.mult), "vE")
    sec("vector", v_g)

    def v_z(v, zi=zi):
        C.inc(v.scalar_tensor_tensor(
            zhat[:].rearrange("p (a b) -> p a b", b=8),
            pa[zi][:, 0:512].rearrange("p (a b) -> p a b", b=8), 1.0,
            _bc(gwork[:], [[64, 128], [1, 64], [0, 8]]),
            OP.mult, OP.mult), "vE")
        C.bank_mark(f"pa{zi}", "vE")
    sec("vector", v_z)
    v_zh = C.cnt["vE"]

    def t_zt(t, v_zh=v_zh):
        C.wait(t, "vE", v_zh)
        C.bank_guard(t, "pb0")
        last = None
        for i in range(4):
            last = t.transpose(pb[0][:, i * 128:(i + 1) * 128],
                               zhat[:, i * 128:(i + 1) * 128], ident_bf[:])
        C.inc(last, "tE")
    sec("tensor", t_zt)
    t_zT = C.cnt["tE"]

    def s_zt(v, t_zT=t_zT):
        C.wait(v, "tE", t_zT)
        C.inc(v.tensor_copy(zhT[:], pb[0][:, 0:512]), "vE")
        C.bank_mark("pb0", "vE")
    sec("vector", s_zt)
    s_zTT = C.cnt["vE"]

    for og in range(8):
        def t_b(t, og=og, s_zTT=s_zTT):
            if og == 0:
                C.wait(t, "vE", s_zTT)
            key = f"pd{og % 2}"
            C.bank_guard(t, key)
            half = (og % 2) * 512
            last = None
            for nch in range(4):
                last = t.matmul(pdm[:, half:half + 512],
                                zhT[:, nch * 128:(nch + 1) * 128],
                                bwT[:, nch * 4096 + og * 512:
                                    nch * 4096 + (og + 1) * 512],
                                start=(nch == 0), stop=(nch == 3))
            C.inc(last, "tE")
        sec("tensor", t_b)
        t_bo = C.cnt["tE"]

        def v_o(v, og=og, t_bo=t_bo):
            C.wait(v, "tE", t_bo)
            if og == 0:
                C.wait(v, "do")
            half = (og % 2) * 512
            C.inc(v.scalar_tensor_tensor(x3f[:, og * 512:(og + 1) * 512],
                                         pdm[:, half:half + 512], 1.0,
                                         ysb2[:, og * 512:(og + 1) * 512],
                                         OP.mult, OP.add), "vE")
            C.bank_mark(f"pd{og % 2}", "vE")
        sec("vector", v_o)

    def v_l2(v):
        _ln_stats(C, v, bnb, x3f[:], H, smalls[:, 8:9], smalls[:, 9:10],
                  smalls, 12, want_meansq=False)
        C.inc(v.tensor_copy(smalls[:, 18:19], smalls[:, 9:10]), "vE")
    sec("vector", v_l2)
    v_l2a = C.cnt["vE"]

    def s_l2(s, v_l2a=v_l2a):
        C.wait(s, "vE", v_l2a)
        C.inc(_rstd_finish(s, smalls[:, 18:19], smalls[:, 10:11],
                           smalls[:, 11:12]), "sE")
    sec("scalar", s_l2)
    s_l2a = C.cnt["sE"]

    def v_out(v, b=b, s_l2a=s_l2a):
        C.wait(v, "sE", s_l2a)
        C.inc(v.tensor_scalar(x3f[:], x3f[:], smalls[:, 8:9], smalls[:, 10:11],
                              OP.subtract, OP.mult), "vE")
    sec("vector", v_out)
    v_ot = C.cnt["vE"]

    def d_out(sy, b=b, v_ot=v_ot):
        C.wait(sy, "vE", v_ot)
        do_at[b] = C.dma(sy.dma_start(out_t[t0b:t0b + 128, :], x3f[:]), "do")
    sec("sync", d_out)


def kernel(**inputs):
    _ = inputs.get("L0_attn_residual")
    x = np.ascontiguousarray(inputs["L0_attn_hidden_states"], dtype=np.float32)
    nc = build()
    base = {
        "w_in": np.ascontiguousarray(inputs["L0_attn_in_proj_w"], np.float32),
        "conv_w": np.ascontiguousarray(inputs["L0_attn_conv_w"], np.float32),
        "a_log": np.ascontiguousarray(inputs["L0_attn_A_log"], np.float32).reshape(1, NH),
        "dt_bias": np.ascontiguousarray(inputs["L0_attn_dt_bias"], np.float32).reshape(1, NH),
        "d_vec": np.ascontiguousarray(inputs["L0_attn_D"], np.float32).reshape(1, NH),
        "ln_w": np.ascontiguousarray(inputs["L0_attn_ln_w"], np.float32).reshape(1, H),
        "norm_w": np.ascontiguousarray(inputs["L0_attn_norm_w"], np.float32).reshape(1, DS),
        "mup": np.ascontiguousarray(inputs["L0_attn_mup_vector"], np.float32).reshape(1, PROJ),
        "w_out": np.ascontiguousarray(inputs["L0_attn_out_proj_w"], np.float32),
        "ffn_a": np.ascontiguousarray(inputs["L0_ffn_A_w"], np.float32),
        "ffn_b": np.ascontiguousarray(inputs["L0_ffn_B_w"], np.float32),
        "ffn_wq": np.ascontiguousarray(inputs["L0_ffn_Wq_w"], np.float32),
        "ffn_keys": np.ascontiguousarray(inputs["L0_ffn_keys"], np.float32),
    }
    in_maps = [dict(base, x=np.ascontiguousarray(x[c])) for c in range(NCORES)]
    res = run_bass_kernel_spmd(nc, in_maps, core_ids=list(range(NCORES)))
    return np.stack([res.results[c]["out"] for c in range(NCORES)], axis=0)
